# version 02
# baseline (speedup 1.0000x reference)
import numpy as np
import ml_dtypes
from contextlib import ExitStack
import concourse.bass as bass
import concourse.mybir as mybir
from concourse.bass_utils import run_bass_kernel_spmd

F32 = mybir.dt.float32
BF16 = mybir.dt.bfloat16
I32 = mybir.dt.int32
AF = mybir.ActivationFunctionType
ALU = mybir.AluOpType
AX = mybir.AxisListType

T = 2048
D = 2048
NH = 32
NPAIR = 16
C = 64
NCHUNK = T // C
GRP = 512
NGRP = T // GRP
CPG = GRP // C
NEG = -1.0e30
ATTN_UPTO = 3
ATTN_NHEADS = 16
EPS_LN = 1e-5
EPS_DN = 1e-6


def host_consts():
    c = {}
    c["ident"] = np.eye(128, dtype=np.float32)
    c["identF"] = np.eye(128, dtype=np.float32)
    ii = np.zeros((128, 64), np.float32)
    ii[0:64] = np.eye(64)
    ii[64:128] = np.eye(64)
    c["eye2"] = ii
    c["ones_col"] = np.ones((128, 1), np.float32)
    c["ones_row"] = np.ones((1, 128), np.float32)
    s4 = np.zeros((4, 128), np.float32)
    for k in range(4):
        s4[k, 32 * k:32 * (k + 1)] = 1.0
    c["sel4_c"] = s4
    c["neg128_row"] = np.full((1, 128), -128.0, np.float32)
    c["ones_row_fc"] = np.ones((1, 128), np.float32)
    c["ones128_c"] = np.ones((128, 128), np.float32)
    bd = np.zeros((128, 2), np.float32)
    bd[0:64, 0] = 1.0
    bd[64:128, 1] = 1.0
    c["bdones_c"] = bd
    ind2 = np.zeros((2, 128), np.float32)
    ind2[0, 0:64] = 1.0
    ind2[1, 64:128] = 1.0
    c["ind2_c"] = ind2
    sel = np.zeros((NPAIR, 32, 128), np.float32)
    for p in range(NPAIR):
        sel[p, 2 * p, 0:64] = 1.0
        sel[p, 2 * p + 1, 64:128] = 1.0
    c["selP"] = np.ascontiguousarray(sel.transpose(1, 0, 2).reshape(32, NPAIR * 128))
    c["selPn"] = np.ascontiguousarray((-sel).transpose(1, 0, 2).reshape(32, NPAIR * 128))
    ohLn = np.zeros((NPAIR, 32, 64), np.float32)
    ohLB = np.zeros((NPAIR, 32, 64), np.float32)
    for p in range(NPAIR):
        ohLn[p, p, :] = -1.0
        ohLB[p, 16 + p, :] = 1.0
    c["ohLn"] = np.ascontiguousarray(ohLn.transpose(1, 0, 2).reshape(32, NPAIR * 64))
    c["ohLB"] = np.ascontiguousarray(ohLB.transpose(1, 0, 2).reshape(32, NPAIR * 64))
    j = np.arange(64)
    mstrict = np.where(j[:, None] < j[None, :], 0.0, NEG).astype(np.float32)
    mincl = np.where(j[:, None] <= j[None, :], 0.0, NEG).astype(np.float32)
    mstrictT = np.where(j[:, None] > j[None, :], 0.0, NEG).astype(np.float32)
    c["mask_E1"] = np.concatenate([mstrict, mstrict], 0)
    c["mask_E2"] = np.concatenate([mincl, mincl], 0)
    c["mask_E1T"] = np.concatenate([mstrictT, mstrictT], 0)
    return c


def host_prep(inputs):
    h = {}
    asc = np.ascontiguousarray
    wq = inputs["L0_attn_w_qkvz"]
    rows = []
    for off in (0, 64, 128, 192):
        for hh in range(NH):
            rows.append(np.arange(hh * 256 + off, hh * 256 + off + 64))
    perm = np.concatenate(rows)
    h["Wqkvzp"] = asc(wq[perm].T)
    wba = inputs["L0_attn_w_ba"]
    h["Wb"] = asc(wba[0::2].T)
    h["Wa"] = asc(wba[1::2].T)
    h["convw"] = asc(inputs["L0_attn_conv_w"][:, 0, :])
    h["A_log"] = asc(np.tile(inputs["L0_attn_A_log"].reshape(1, 32), (4, 1)).reshape(128, 1))
    h["dt_bias"] = asc(np.tile(inputs["L0_attn_dt_bias"].reshape(1, 32), (4, 1)).reshape(128, 1))
    h["normw"] = asc(inputs["L0_attn_norm_weight"].reshape(1, 64))
    h["Woutp"] = asc(inputs["L0_attn_w_out"].T)
    h["Wg0"] = asc(inputs["L0_ffn_w_gate"].T)
    h["Wu0"] = asc(inputs["L0_ffn_w_up"].T)
    h["Wd0"] = asc(inputs["L0_ffn_w_down"].T)
    for nm in ("wq", "wk", "wv", "wo"):
        h[f"C{nm}"] = asc(inputs[f"L1_attn_{nm}_codes"])
        h[f"S{nm}"] = asc(inputs[f"L1_attn_{nm}_scales"])
    h["bq"] = asc(inputs["L1_attn_bq"].reshape(16, 128).T)
    h["bk"] = asc(inputs["L1_attn_bk"].reshape(16, 128).T)
    h["bv"] = asc(inputs["L1_attn_bv"].reshape(16, 128).T)
    h["bo"] = asc(inputs["L1_attn_bo"].reshape(16, 128).T)
    h["Wg1"] = asc(inputs["L1_ffn_wq_gate"].T)
    h["Wu1"] = asc(inputs["L1_ffn_wq_up"].T)
    h["Wd1"] = asc(inputs["L1_ffn_wq_down"].T)
    h["sg1"] = asc(inputs["L1_ffn_s_gate"].reshape(44, 128).T)
    h["su1"] = asc(inputs["L1_ffn_s_up"].reshape(44, 128).T)
    h["sd1"] = asc(inputs["L1_ffn_s_down"].reshape(16, 128).T)
    return h


class Ctr:
    def __init__(self, ctx, nc, name):
        self.h = ctx.enter_context(nc.semaphore(name))
        self.n = 0


class Prog:
    ENGINES = ("sync", "gpsimd", "tensor", "vector", "scalar")

    def __init__(self, nc, ctx):
        self.nc = nc
        self.ctx = ctx
        self.streams = {e: [] for e in self.ENGINES}
        self.sems = {}
        self.swdge = {}
        self._holders = []
        self.sync_runs = {}

    def sem(self, name):
        if name not in self.sems:
            self.sems[name] = Ctr(self.ctx, self.nc, name)
        return self.sems[name]

    def n(self, name):
        return self.sem(name).n

    def _close_sync_runs(self):
        for nm, rec in self.sync_runs.items():
            if rec["open"]:
                s = self.sems[nm]
                v = s.n
                self.streams["sync"].append(lambda e, s=s, v=v: e.wait_ge(s.h, v))
                rec["bounds"].append(v)
                rec["open"] = False

    def milestone(self, eng, main_sem, ms_sem, k=1):
        self.wait(eng, main_sem, self.n(main_sem))
        s = self.sem(ms_sem)
        s.n += k
        self.streams[eng].append(lambda e, s=s, k=k: e.sem_inc(s.h, k))
        return s.n

    def op(self, eng, fn, inc=None, inck=1):
        if eng == "sync":
            self._close_sync_runs()
        if inc is not None:
            if isinstance(inc, str):
                inc = [(inc, inck)]
            hs = []
            ret = None
            for nm, k in inc:
                s = self.sem(nm)
                s.n += k
                hs.append((s.h, k))
                ret = s.n

            def run(e, fn=fn, hs=hs):
                i = fn(e)
                for h, k in hs:
                    i = i.then_inc(h, k)
                return i
            self.streams[eng].append(run)
            return ret
        self.streams[eng].append(lambda e, fn=fn: fn(e))
        return None

    def dma(self, eng, out, in_, inc):
        if eng == "sync":
            s = self.sem(inc)
            s.n += 16
            rec = self.sync_runs.setdefault(inc, {"bounds": [], "open": False})
            rec["open"] = True
            self.streams[eng].append(
                lambda e, o=out, i=in_, s=s: e.dma_start(out=o, in_=i).then_inc(s.h, 16))
            return s.n
        if eng == "gpsimd":
            s = self.sem(inc)
            s.n += 16
            v = s.n

            def run(e, o=out, i=in_, s=s, v=v):
                e.dma_start(out=o, in_=i).then_inc(s.h, 16)
                return e.wait_ge(s.h, v)
            self.streams[eng].append(run)
            return v
        return self.op(eng, lambda e, o=out, i=in_: e.dma_start(out=o, in_=i), inc=inc, inck=16)

    def wait(self, eng, sem, val=None):
        if eng == "sync":
            self._close_sync_runs()
        s = self.sem(sem)
        v = s.n if val is None else val
        if v > 0:
            if sem in self.sync_runs and eng != "sync":
                holder = [v]
                self._holders.append((sem, holder))
                self.streams[eng].append(lambda e, s=s, h=holder: e.wait_ge(s.h, h[0]))
            else:
                self.streams[eng].append(lambda e, s=s, v=v: e.wait_ge(s.h, v))
        return v

    def barrier(self, engines=None):
        engines = list(engines or self.ENGINES)
        if "sync" in engines:
            self._close_sync_runs()
        vals = {}
        for e in engines:
            s = self.sem("ph_" + e)
            s.n += 1
            vals[e] = s.n
            self.streams[e].append(lambda en, s=s: en.drain().then_inc(s.h, 1))
        for e in engines:
            for f in engines:
                if f != e:
                    sf = self.sem("ph_" + f)
                    self.streams[e].append(lambda en, s=sf, v=vals[f]: en.wait_ge(s.h, v))
        return vals

    def finalize(self, block):
        self._close_sync_runs()
        for sem, holder in self._holders:
            bounds = self.sync_runs[sem]["bounds"]
            v = holder[0]
            holder[0] = next((b for b in bounds if b >= v), bounds[-1] if bounds else v)
        for name in self.ENGINES:
            stream = self.streams[name]

            def runner(eng, stream=stream):
                for fn in stream:
                    fn(eng)
            getattr(block, name)(runner)


def register_float_consts(prog, tl, values=(EPS_DN, EPS_LN, 1.0 / 64.0, 0.08838834764831845,
                                             1.0 / 128.0, 1.0 / D)):
    nc = prog.nc
    tile = tl["fconst"]
    for i, v in enumerate(values):
        prog.op("vector", lambda e, i=i, v=v: e.memset(tile[:, i:i + 1], v), inc="dvS")
        nc.const_aps.register(F32, v, tile[:, i:i + 1])
    prog.wait("vector", "dvS", prog.n("dvS"))


def emit_scan(prog, tl):
    nc = prog.nc
    cs = tl["consts"]
    for p in range(NPAIR):
        prog.op("vector", lambda e, p=p: e.memset(tl["state"][p][:], 0.0), inc="dvS")
        n = prog.op("vector", lambda e, p=p: e.memset(tl["state_b"][p][:], 0.0), inc="dvS")
        tl["state_dv"][p] = n
    prog.op("tensor", lambda e: e.matmul(tl["psumD"][:, 0:64], tl["ones_row_b"][:], tl["normw_b"][:],
                                         start=True, stop=True), inc="mmS")
    prog.wait("vector", "mmS", prog.n("mmS"))
    prog.op("vector", lambda e: e.tensor_copy(tl["normwB"][:], tl["psumD"][:, 0:64]), inc="dvS")
    prog.barrier()
    for g in range(NGRP):
        for i in range(48):
            prog.dma("sync", tl["qk_g"][i][:],
                     bass.AP(tl["qkvhat_scr"].tensor, 128 * i * T + 512 * g, [[T, 128], [1, 512]]),
                     inc="grp_ld")
        for i in range(16):
            prog.dma("sync", tl["sigz_g"][i][:],
                     bass.AP(tl["sigz_scr"].tensor, 128 * i * T + 512 * g, [[T, 128], [1, 512]]),
                     inc="grp_ld")
        n = prog.n("grp_ld")
        prog.wait("tensor", "grp_ld", n)
        prog.wait("vector", "grp_ld", n)
        for lc in range(CPG):
            emit_chunk(prog, tl, g * CPG + lc, lc)
        emit_wout_group(prog, tl, g)
        prog.barrier(("tensor", "vector", "scalar", "sync", "gpsimd"))


def emit_chunk(prog, tl, cidx, lc):
    nc = prog.nc
    cs = tl["consts"]
    ts = slice(lc * C, lc * C + C)
    pct = tl["pct"]
    ws = tl["ws"]
    IA = tl["ident_s"][0:64, 0:64]
    IB = tl["ident_s"][64:128, 64:128]
    k_t = tl["qk_g"]
    kof, vof = 16, 32
    psm = tl["psum_sm"]
    pse = tl["psum_e"]
    ppw = [tl["psumA"], tl["psumB"], tl["psumC"], tl["psumD"]]

    def mm(fn, inc=None):
        return prog.op("tensor", fn, inc=inc)

    def vv(fn, inc=None):
        return prog.op("vector", fn, inc=inc)

    def ss(fn, inc=None):
        return prog.op("scalar", fn, inc=inc)

    prog.wait("sync", "mmS", prog.n("mmS"))
    prog.wait("sync", "dvS", prog.n("dvS"))

    def pct_src(name):
        return bass.AP(tl["stats_scr"][name].tensor, cidx * C, [[2 * T, 16], [T, 2], [1, 64]])

    def pc_src(name, rep):
        if rep:
            return bass.AP(tl["stats_scr"][name].tensor, cidx * C + C - 1,
                           [[T, 2], [0, 64], [2 * T, 16], [1, 1]])
        return bass.AP(tl["stats_scr"][name].tensor, cidx * C, [[T, 2], [1, 64], [2 * T, 16], [1, 1]])

    prog.dma("sync", pct[0:16, :], pct_src("lam"), inc="pc_ld")
    prog.dma("sync", pct[16:32, :], pct_src("lamb"), inc="pc_ld")
    pcT = tl["pcT_in"]
    for si, (nm, rep) in enumerate((("negbg", False), ("beta", False), ("gam", False),
                                    ("w", False), ("gamC", True))):
        srcnm = "gam" if nm == "gamC" else nm
        if rep:
            ap = bass.AP(tl["stats_scr"][srcnm].tensor, cidx * C + C - 1,
                         [[2 * T, 16], [T, 2], [0, 64], [1, 1]])
        else:
            ap = bass.AP(tl["stats_scr"][srcnm].tensor, cidx * C, [[2 * T, 16], [T, 2], [1, 64]])
        if rep:
            prog.dma("sync", pcT[16 * si:16 * (si + 1), :].rearrange("p (a b) -> p a b", b=1), ap,
                     inc="pc_ld")
        else:
            prog.dma("sync", pcT[16 * si:16 * (si + 1), :], ap, inc="pc_ld")
    npc = prog.n("pc_ld")
    prog.wait("tensor", "pc_ld", npc)
    prog.wait("tensor", "dvS", prog.n("dvS"))
    nmm0 = prog.op("tensor", lambda e: e.matmul(tl["psumA"][:, 384:464], pcT[:],
                                                cs["identF_s"][0:80, 0:80], start=True, stop=True),
                   inc="mmS")
    prog.wait("vector", "mmS", nmm0)
    npcv = prog.op("vector", lambda e: e.tensor_copy(tl["pcall"][:], tl["psumA"][:, 384:464]),
                   inc="dvS")
    prog.wait("tensor", "dvS", npcv)
    prog.wait("vector", "pc_ld", npc)
    pcall = tl["pcall"]
    pcs = {nm: pcall[:, 16 * si:16 * (si + 1)]
           for si, nm in enumerate(("negbg", "beta", "gam", "w", "gamC"))}

    st1_dv = [0] * NPAIR
    st1_ev = [0] * NPAIR
    lamblk = tl["lam_w"][:, 64 * cidx:64 * cidx + 64]
    lambblk = tl["lamb_w"][:, 64 * cidx:64 * cidx + 64]

    for p in range(NPAIR):
        kt, qt, vt = k_t[kof + p], k_t[p], k_t[vof + p]
        KT_A, KT_B = kt[0:64, ts], kt[64:128, ts]
        QT_A, QT_B = qt[0:64, ts], qt[64:128, ts]
        w = ws[p]
        bank1 = psm if p % 2 == 0 else pse
        pg = bank1[:, 0:128]
        pv = bank1[:, 320:448]
        pe = bank1[:, 128:320]
        if p >= 2:
            prog.wait("tensor", "st1ev", st1_ev[p - 2])
        mm(lambda e, pg=pg, A=KT_A: e.matmul(pg[0:64, 0:64], A, A, start=True, stop=True))
        mm(lambda e, pg=pg, A=KT_A, Q=QT_A: e.matmul(pg[0:64, 64:128], A, Q, start=True, stop=True))
        mm(lambda e, pg=pg, B=KT_B: e.matmul(pg[64:128, 0:64], B, B, start=True, stop=True))
        mm(lambda e, pg=pg, B=KT_B, Q=QT_B: e.matmul(pg[64:128, 64:128], B, Q, start=True, stop=True))
        sel = cs["selP_s"][:, 128 * p:128 * (p + 1)]
        seln = cs["selPn_s"][:, 128 * p:128 * (p + 1)]
        ohn = cs["ohLn_s"][:, 64 * p:64 * (p + 1)]
        ohb = cs["ohLB_s"][:, 64 * p:64 * (p + 1)]
        mm(lambda e, pe=pe, s=sel, b=lambblk: e.matmul(pe[:, 0:64], s, b, start=True, stop=False))
        mm(lambda e, pe=pe, o=ohn: e.matmul(pe[:, 0:64], pct[:], o, start=False, stop=True))
        mm(lambda e, pe=pe, s=sel, b=lamblk: e.matmul(pe[:, 64:128], s, b, start=True, stop=False))
        mm(lambda e, pe=pe, o=ohn: e.matmul(pe[:, 64:128], pct[:], o, start=False, stop=True))
        mm(lambda e, pe=pe, s=seln, b=lamblk: e.matmul(pe[:, 128:192], s, b, start=True, stop=False))
        mm(lambda e, pe=pe, o=ohb: e.matmul(pe[:, 128:192], pct[:], o, start=False, stop=True))
        mm(lambda e, pv=pv, vt=vt: e.matmul(pv[0:64, 0:64], vt[0:64, ts], IA, start=True, stop=True))
        mm(lambda e, pv=pv, vt=vt: e.matmul(pv[64:128, 0:64], vt[64:128, ts], IB, start=True, stop=True))
        mm(lambda e, pv=pv, kt=kt: e.matmul(pv[0:64, 64:128], kt[0:64, ts], IA, start=True, stop=True))
        nmm = mm(lambda e, pv=pv, kt=kt: e.matmul(pv[64:128, 64:128], kt[64:128, ts], IB,
                                                  start=True, stop=True), inc="mmS")
        prog.wait("vector", "mmS", nmm)
        vv(lambda e, w=w, pe=pe: e.tensor_add(w["e1f"][:], pe[:, 0:64], cs["mask_E1_s"][:]))
        vv(lambda e, w=w, pe=pe: e.tensor_add(w["e2f"][:], pe[:, 64:128], cs["mask_E2_s"][:]))
        nv = vv(lambda e, w=w, pe=pe: e.tensor_add(w["e1tf"][:], pe[:, 128:192], cs["mask_E1T_s"][:]),
                inc="dvS")
        prog.wait("scalar", "dvS", nv)
        ss(lambda e, w=w: e.activation(w["E1"][:], w["e1f"][:], AF.Exp))
        ss(lambda e, w=w: e.activation(w["E2"][:], w["e2f"][:], AF.Exp))
        na = ss(lambda e, w=w: e.activation(w["E1T"][:], w["e1tf"][:], AF.Exp), inc="acS")
        prog.wait("vector", "acS", na)
        vv(lambda e, w=w, pg=pg: e.tensor_mul(w["NT"][:], pg[:, 0:64], w["E1"][:]))
        vv(lambda e, w=w, pg=pg: e.tensor_mul(w["NTT"][:], pg[:, 0:64], w["E1T"][:]))
        vv(lambda e, w=w, pg=pg: e.tensor_mul(w["G"][:], pg[:, 64:128], w["E2"][:]))
        pbeta = pcs["beta"][:, p:p + 1]
        pw_ = pcs["w"][:, p:p + 1]
        vv(lambda e, w=w, pv=pv, s=pbeta: e.tensor_scalar_mul(w["bV"][:], pv[:, 0:64], s))
        nw = vv(lambda e, w=w, pv=pv, s=pw_: e.tensor_scalar_mul(w["W"][:], pv[:, 64:128], s),
                inc="dvS")
        prog.wait("vector", "dvS", nw)
        st1_dv[p] = prog.op("vector",
                            lambda e, w=w: e.tensor_sub(w["PPT"][:], cs["eye2_s"][:], w["NT"][:]),
                            inc="dvS")
        st1_ev[p] = prog.milestone("vector", "dvS", "st1ev")

    seq = [("P2", "NTT", "NT"), ("P2T", "NT", "NTT"), ("P4", "P2T", "P2"), ("P4T", "P2", "P2T"),
           ("P8", "P4T", "P4"), ("P8T", "P4", "P4T"), ("P16", "P8T", "P8"), ("P16T", "P8", "P8T"),
           ("P32T", "P16", "P16T")]
    sq_ev = {}
    bank_free = {}
    for si, (dst, lh, rh) in enumerate(seq):
        for p in range(NPAIR):
            w = ws[p]
            bi = p % 4
            bank = ppw[bi][:, 0:64]
            if bi in bank_free:
                prog.wait("tensor", "sq_ev", bank_free[bi])
            if si <= 1:
                prog.wait("tensor", "dvS", st1_dv[p])
            else:
                prog.wait("tensor", "sq_ev", sq_ev[(si - 1, p)])
            mm(lambda e, b=bank, w=w, lh=lh, rh=rh: e.matmul(b[0:64, :], w[lh][0:64, :], w[rh][0:64, :],
                                                             start=True, stop=True))
            nmm = mm(lambda e, b=bank, w=w, lh=lh, rh=rh: e.matmul(b[64:128, :], w[lh][64:128, :],
                                                                   w[rh][64:128, :], start=True, stop=True),
                     inc="mmS")
            prog.wait("vector", "mmS", nmm)
            n = vv(lambda e, w=w, b=bank, dst=dst: e.tensor_copy(w[dst][:], b[:]), inc="sq_ev")
            sq_ev[(si, p)] = n
            bank_free[bi] = n

    pr_ev = dict((p, 0) for p in range(NPAIR))
    bank_free3 = {}
    for fi, fk in enumerate(("P2T", "P4T", "P8T", "P16T", "P32T")):
        for p in range(NPAIR):
            w = ws[p]
            bi = p % 4
            bank = ppw[bi][:, 64:128]
            if bi in bank_free3:
                prog.wait("tensor", "pr_ev", bank_free3[bi])
            prog.wait("tensor", "sq_ev", sq_ev[({0: 1, 1: 3, 2: 5, 3: 7, 4: 8}[fi], p)])
            if fi == 0:
                prog.wait("tensor", "dvS", st1_dv[p])
            else:
                prog.wait("tensor", "pr_ev", pr_ev[p])
            mm(lambda e, b=bank, w=w, fk=fk: e.matmul(b[0:64, :], w[fk][0:64, :], w["PPT"][0:64, :],
                                                      start=True, stop=True))
            nmm = mm(lambda e, b=bank, w=w, fk=fk: e.matmul(b[64:128, :], w[fk][64:128, :],
                                                            w["PPT"][64:128, :], start=True, stop=True),
                     inc="mmS")
            prog.wait("vector", "mmS", nmm)
            if fi == 0:
                prog.wait("vector", "dvS", st1_dv[p])
            else:
                prog.wait("vector", "pr_ev", pr_ev[p])
            n = vv(lambda e, w=w, b=bank: e.tensor_add(w["PPT"][:], b[:], w["PPT"][:]), inc="pr_ev")
            pr_ev[p] = n
            bank_free3[bi] = n

    st4_free = {}
    for p in range(NPAIR):
        w = ws[p]
        kt, qt = k_t[kof + p], k_t[p]
        KT_A, KT_B = kt[0:64, ts], kt[64:128, ts]
        QT_A, QT_B = qt[0:64, ts], qt[64:128, ts]
        S = tl["state"][p]
        Sb = tl["state_b"][p]
        bi = p % 4
        bank = ppw[bi]
        pcn = bank[:, 128:192]
        pcq = bank[:, 192:256]
        pdl = bank[:, 256:320]
        pom = bank[:, 320:384]
        psu = bank[:, 384:448]
        pof = bank[:, 448:512]
        if bi in st4_free:
            prog.wait("tensor", "st4ev", st4_free[bi])
        prog.wait("tensor", "dvS", tl["state_dv"][p])
        mm(lambda e, A=KT_A, S=Sb, pcn=pcn: e.matmul(pcn[0:64, :], A, S[0:64, :], start=True, stop=True))
        mm(lambda e, B=KT_B, S=Sb, pcn=pcn: e.matmul(pcn[64:128, :], B, S[64:128, :], start=True, stop=True))
        mm(lambda e, A=QT_A, S=Sb, pcq=pcq: e.matmul(pcq[0:64, :], A, S[0:64, :], start=True, stop=True))
        nmm = mm(lambda e, B=QT_B, S=Sb, pcq=pcq: e.matmul(pcq[64:128, :], B, S[64:128, :],
                                                           start=True, stop=True), inc="mmS")
        prog.wait("vector", "mmS", nmm)
        prog.wait("vector", "dvS", st1_dv[p])
        pnegbg = pcs["negbg"][:, p:p + 1]
        pgam = pcs["gam"][:, p:p + 1]
        vv(lambda e, w=w, pcn=pcn, s=pnegbg: e.scalar_tensor_tensor(w["c"][:], pcn[:], s,
                                                                    w["bV"][:], ALU.mult, ALU.add))
        nv = vv(lambda e, w=w, pcq=pcq, s=pgam: e.tensor_scalar_mul(w["CSQ"][:], pcq[:], s),
                inc="dvS")
        prog.wait("tensor", "dvS", nv)
        prog.wait("tensor", "pr_ev", pr_ev[p])
        mm(lambda e, w=w, pdl=pdl: e.matmul(pdl[0:64, :], w["PPT"][0:64, :], w["c"][0:64, :],
                                            start=True, stop=True))
        nmm = mm(lambda e, w=w, pdl=pdl: e.matmul(pdl[64:128, :], w["PPT"][64:128, :], w["c"][64:128, :],
                                                  start=True, stop=True), inc="mmS")
        prog.wait("vector", "mmS", nmm)
        nv = vv(lambda e, w=w, pdl=pdl: e.tensor_copy(w["delta"][:], pdl[:]), inc="dvS")
        prog.wait("tensor", "dvS", nv)
        mm(lambda e, w=w, pom=pom: e.matmul(pom[0:64, :], w["G"][0:64, :], w["delta"][0:64, :],
                                            start=True, stop=True))
        mm(lambda e, w=w, pom=pom: e.matmul(pom[64:128, :], w["G"][64:128, :], w["delta"][64:128, :],
                                            start=True, stop=True))
        mm(lambda e, w=w, psu=psu: e.matmul(psu[0:64, :], w["W"][0:64, :], w["delta"][0:64, :],
                                            start=True, stop=True))
        nmm = mm(lambda e, w=w, psu=psu: e.matmul(psu[64:128, :], w["W"][64:128, :], w["delta"][64:128, :],
                                                  start=True, stop=True), inc="mmS")
        prog.wait("vector", "mmS", nmm)
        vv(lambda e, w=w, pom=pom: e.tensor_add(w["orow"][:], pom[:], w["CSQ"][:]))
        pgamC = pcs["gamC"][:, p:p + 1]
        nst = vv(lambda e, S=S, psu=psu, s=pgamC: e.scalar_tensor_tensor(S[:], S[:], s,
                                                                        psu[:], ALU.mult, ALU.add), inc="dvS")
        prog.wait("vector", "dvS", nst)
        ns = vv(lambda e, S=S, Sb=Sb: e.tensor_copy(Sb[:], S[:]), inc="dvS")
        tl["state_dv"][p] = ns
        prog.wait("scalar", "dvS", ns)
        na = ss(lambda e, w=w: e.activation(w["osq"][:], w["orow"][:], AF.Square), inc="acS")
        prog.wait("vector", "acS", na)
        nv = vv(lambda e, w=w: e.reduce_sum(w["var"][:], w["osq"][:], axis=AX.X), inc="dvS")
        prog.wait("scalar", "dvS", nv)
        na = ss(lambda e, w=w: e.activation(w["rstd"][:], w["var"][:], AF.Sqrt, scale=1.0 / 64.0,
                                            bias=EPS_DN), inc="acS")
        prog.wait("vector", "acS", na)
        nv = vv(lambda e, w=w: e.reciprocal(w["rinv"][:], w["rstd"][:]), inc="dvS")
        prog.wait("vector", "dvS", nv)
        nv = vv(lambda e, w=w: e.scalar_tensor_tensor(w["on"][:], w["orow"][:], w["rinv"][:, 0:1],
                                                      tl["normwB"][:], ALU.mult, ALU.mult), inc="dvS")
        prog.wait("tensor", "dvS", nv)
        mm(lambda e, w=w, pof=pof: e.matmul(pof[0:64, :], w["on"][0:64, :], IA, start=True, stop=True))
        nmm = mm(lambda e, w=w, pof=pof: e.matmul(pof[64:128, :], w["on"][64:128, :], IB,
                                                  start=True, stop=True), inc="mmS")
        prog.wait("vector", "mmS", nmm)
        n4 = vv(lambda e, p=p, pof=pof: e.tensor_mul(tl["o_g"][p][:, ts], pof[:], tl["sigz_g"][p][:, ts]),
                inc="st4ev")
        st4_free[bi] = n4


def emit_wout_group(prog, tl, g):
    nc = prog.nc
    ps = [tl["psumA"], tl["psumB"]]
    xb = tl["xbuf"]
    og = tl["o_g"]
    x_in = tl["x_in"]
    x1a = tl["x1a_scr"]
    wop = tl["Woutp"]
    xl = {}
    for dt in range(4):
        xl[dt] = prog.dma("sync", xb[dt][:],
                          bass.AP(x_in.tensor, 128 * dt * T + 512 * g, [[T, 128], [1, 512]]),
                          inc="x_ld")
    prog.wait("tensor", "st4ev", prog.n("st4ev"))
    wo_ev = {}
    for dt in range(16):
        buf = tl["wbuf"][dt % 3]
        if dt >= 3:
            prog.wait("gpsimd", "wofree", prog.n("wofree") - 2)
        prog.dma("gpsimd", buf[:].rearrange("p (a b) -> p a b", b=128),
                 bass.AP(wop.tensor, dt * 128, [[D, 128], [128 * D, 16], [1, 128]]),
                 inc="wodma")
        prog.wait("tensor", "wodma", prog.n("wodma"))
        bank = ps[dt % 2]
        if dt >= 2:
            prog.wait("tensor", "dvS", wo_ev[dt - 2])
        for kt in range(16):
            prog.op("tensor", lambda e, b=bank, buf=buf, kt=kt, og=og:
                    e.matmul(b[:], buf[:, 128 * kt:128 * (kt + 1)], og[kt][:],
                             start=(kt == 0), stop=(kt == 15)),
                    inc="mmS" if kt == 15 else None)
        nmm = prog.n("mmS")
        prog.milestone("tensor", "mmS", "wofree")
        prog.wait("vector", "mmS", nmm)
        prog.wait("vector", "x_ld", xl[dt])
        nv = prog.op("vector", lambda e, b=bank, t_=xb[dt % 4]: e.tensor_add(
            t_[:], b[:], t_[:]), inc="dvS")
        wo_ev[dt] = nv
        prog.wait("sync", "dvS", nv)
        prog.dma("sync", bass.AP(x1a.tensor, 128 * dt * T + 512 * g, [[T, 128], [1, 512]]),
                 xb[dt % 4][:], inc="x1a_st")
        if dt + 4 < 16:
            prog.wait("sync", "x1a_st", prog.n("x1a_st"))
            xl[dt + 4] = prog.dma("sync", xb[dt % 4][:],
                                  bass.AP(x_in.tensor, 128 * (dt + 4) * T + 512 * g,
                                          [[T, 128], [1, 512]]), inc="x_ld")


def emit_ln(prog, tl, src_ap, xn_dst, label):
    nc = prog.nc
    pn = tl["psumC"]
    pb = tl["psumD"]
    lr = tl["lnrow"]
    xb16 = tl["xb16"]
    sqb = tl["sqb"]
    onescb = tl["ones_col_b"]
    onesrb = tl["ones_row_b"]
    ev_g = {}
    for g in range(NGRP):
        if g >= 1:
            prog.wait("gpsimd", "dvS", ev_g[g - 1])
        for kt in range(16):
            prog.dma("gpsimd", xb16[kt][:],
                     bass.AP(src_ap.tensor, 128 * kt * T + 512 * g, [[T, 128], [1, 512]]),
                     inc=label + "ld")
        nld = prog.n(label + "ld")
        prog.wait("scalar", label + "ld", nld)
        prog.wait("tensor", label + "ld", nld)
        if g >= 1:
            prog.wait("tensor", "dvS", ev_g[g - 1])
        for kt in range(16):
            prog.op("scalar", lambda e, kt=kt: e.activation(sqb[kt][:], xb16[kt][:], AF.Square),
                    inc="acS" if kt == 15 else None)
        na = prog.n("acS")
        for kt in range(16):
            prog.op("tensor", lambda e, kt=kt: e.matmul(pn[0:1, 0:512], onescb[:],
                                                        xb16[kt][:], start=(kt == 0), stop=(kt == 15)))
        prog.wait("tensor", "acS", na)
        for kt in range(16):
            prog.op("tensor", lambda e, kt=kt: e.matmul(pb[0:1, 0:512], onescb[:],
                                                        sqb[kt][:], start=(kt == 0), stop=(kt == 15)),
                    inc="mmS" if kt == 15 else None)
        nmm = prog.n("mmS")
        prog.wait("vector", "mmS", nmm)

        def vv(fn, inc="dvS"):
            return prog.op("vector", fn, inc=inc)

        def ss(fn, inc="acS"):
            return prog.op("scalar", fn, inc=inc)

        vv(lambda e: e.tensor_scalar_mul(lr[0][:], pn[0:1, 0:512], 1.0 / D))
        nv = vv(lambda e: e.tensor_scalar_mul(lr[1][:], pb[0:1, 0:512], 1.0 / D))
        prog.wait("vector", "dvS", nv)
        nv = vv(lambda e: e.tensor_mul(lr[2][:], lr[0][:], lr[0][:]))
        prog.wait("vector", "dvS", nv)
        nv = vv(lambda e: e.tensor_sub(lr[1][:], lr[1][:], lr[2][:]))
        prog.wait("scalar", "dvS", nv)
        na = ss(lambda e: e.activation(lr[2][:], lr[1][:], AF.Sqrt, bias=EPS_LN))
        prog.wait("vector", "acS", na)
        nv = vv(lambda e: e.reciprocal(lr[1][:], lr[2][:]))
        prog.wait("vector", "dvS", nv)
        nv = vv(lambda e: e.tensor_mul(lr[2][:], lr[0][:], lr[1][:]))
        prog.wait("vector", "dvS", nv)
        nv = vv(lambda e: e.tensor_copy(lr[3][:], lr[1][:]))
        prog.wait("vector", "dvS", nv)
        nv = vv(lambda e: e.tensor_copy(lr[4][:], lr[2][:]))
        prog.wait("tensor", "dvS", nv)
        prog.op("tensor", lambda e: e.matmul(pn[:, 0:512], onesrb[:], lr[3][:],
                                             start=True, stop=True))
        nmm = prog.op("tensor", lambda e: e.matmul(pb[:, 0:512], onesrb[:], lr[4][:],
                                                   start=True, stop=True), inc="mmS")
        prog.wait("vector", "mmS", nmm)
        for kt in range(16):
            vv(lambda e, kt=kt: e.tensor_mul(sqb[kt][:], xb16[kt][:], pn[:, 0:512]),
               inc=None)
        nv = vv(lambda e: e.tensor_copy(lr[0][:], lr[0][:]))
        prog.wait("vector", "dvS", nv)
        last = None
        for kt in range(16):
            last = vv(lambda e, kt=kt, g=g: e.tensor_sub(
                xn_dst[:, 2048 * kt + 512 * g:2048 * kt + 512 * (g + 1)],
                sqb[kt][:], pb[:, 0:512]), inc="dvS")
        ev_g[g] = last
    prog.barrier(("tensor", "vector", "scalar", "gpsimd"))


def emit_stats(prog, tl):
    nc = prog.nc
    xn = tl["xn"]
    ps = [tl["psumA"], tl["psumB"]]
    st = tl["st"]
    stg = tl["st_stage"]
    ev = {}
    for wi, (which, wt) in enumerate((("b", tl["wb"]), ("a", tl["wa"]))):
        for g in range(NGRP):
            i = wi * NGRP + g
            bank = ps[i % 2]
            if i >= 2:
                prog.wait("tensor", "sb_mv", 16 * (i - 1))
            for kt in range(16):
                prog.op("tensor", lambda e, b=bank, w=wt, kt=kt, g=g:
                        e.matmul(b[0:32, 0:512], w[:, 32 * kt:32 * (kt + 1)],
                                 xn[:, 2048 * kt + 512 * g: 2048 * kt + 512 * (g + 1)],
                                 start=(kt == 0), stop=(kt == 15)),
                        inc="mmS" if kt == 15 else None)
            n = prog.n("mmS")
            prog.wait("vector", "mmS", n)
            prog.wait("vector", "sb_mv", 16 * i)
            nv = prog.op("vector", lambda e, b=bank: e.tensor_copy(stg[:], b[0:32, 0:512]), inc="dvS")
            prog.wait("sync", "dvS", nv)
            dst = st["braw"] if which == "b" else st["araw"]
            prog.dma("sync", dst[32 * g:32 * (g + 1), :], stg[:], inc="sb_mv")
    prog.wait("scalar", "sb_mv", prog.n("sb_mv"))
    prog.wait("vector", "sb_mv", prog.n("sb_mv"))

    def vv(fn, inc="dvS"):
        return prog.op("vector", fn, inc=inc)

    def ss(fn, inc="acS"):
        return prog.op("scalar", fn, inc=inc)

    braw, araw = st["braw"], st["araw"]
    dtb = tl["dt_bias"][:, 0:1]
    na = ss(lambda e: e.activation(st["t0"][:], araw[:], AF.Abs, bias=dtb))
    prog.wait("scalar", "acS", na)
    na = ss(lambda e: e.activation(st["t1"][:], st["t0"][:], AF.Exp, scale=-1.0))
    prog.wait("scalar", "acS", na)
    na = ss(lambda e: e.activation(st["t1"][:], st["t1"][:], AF.Ln, bias=1.0))
    prog.wait("scalar", "acS", na)
    na = ss(lambda e: e.activation(st["t2"][:], araw[:], AF.Relu, bias=dtb))
    prog.wait("vector", "acS", na)
    nv = vv(lambda e: e.tensor_add(st["sp"][:], st["t1"][:], st["t2"][:]))
    prog.wait("scalar", "dvS", nv)
    na = ss(lambda e: e.activation(tl["negexpA"][:], tl["A_log"][:], AF.Exp))
    prog.wait("vector", "acS", na)
    prog.wait("vector", "dvS", nv)
    nea = tl["negexpA"][:, 0:1]
    nv = vv(lambda e, s=nea: e.tensor_scalar(st["lam"][:], st["sp"][:], s, -1.0, ALU.mult, ALU.mult))
    l3 = st["lam"][:].rearrange("p (c k) -> p c k", k=64)
    for sh in (1, 2, 4, 8, 16, 32):
        prog.wait("vector", "dvS", nv)
        nv = vv(lambda e, sh=sh, l3=l3: e.tensor_add(l3[:, :, sh:64], l3[:, :, sh:64], l3[:, :, 0:64 - sh]))
    ss(lambda e: e.activation(st["beta"][:], braw[:], AF.Sigmoid))
    na = ss(lambda e: e.activation(st["t0"][:], braw[:], AF.Abs))
    prog.wait("scalar", "acS", na)
    na = ss(lambda e: e.activation(st["t1"][:], st["t0"][:], AF.Exp, scale=-1.0))
    prog.wait("scalar", "acS", na)
    na = ss(lambda e: e.activation(st["t1"][:], st["t1"][:], AF.Ln, bias=1.0))
    prog.wait("scalar", "acS", na)
    prog.wait("scalar", "dvS", prog.n("dvS"))
    na = ss(lambda e: e.activation(st["t2"][:], braw[:], AF.Relu, scale=-1.0))
    prog.wait("vector", "acS", na)
    nv2 = vv(lambda e: e.tensor_add(st["t2"][:], st["t1"][:], st["t2"][:]))
    prog.wait("vector", "dvS", nv2)
    nv2 = vv(lambda e: e.tensor_sub(st["lamb"][:], st["lam"][:], st["t2"][:]))
    prog.wait("scalar", "dvS", prog.n("dvS"))
    na = ss(lambda e: e.activation(st["gam"][:], st["lam"][:], AF.Exp))
    prog.wait("vector", "acS", na)
    nv3 = vv(lambda e: e.tensor_mul(st["t0"][:], st["beta"][:], st["gam"][:]))
    prog.wait("vector", "dvS", nv3)
    nv3 = vv(lambda e: e.tensor_scalar_mul(st["negbg"][:], st["t0"][:], -1.0))
    _lamh = st["lam"] if not hasattr(st["lam"], "tensor") else st["lam"].tensor
    lamCsrc = bass.AP(_lamh, 63, [[512, 128], [64, 8], [0, 64]])
    nv4 = vv(lambda e, s=lamCsrc, l3=l3: e.tensor_tensor(
        st["t1"][:].rearrange("p (c k) -> p c k", k=64), s, l3[:, :, :], ALU.subtract))
    prog.wait("scalar", "dvS", nv4)
    na = ss(lambda e: e.activation(st["w"][:], st["t1"][:], AF.Exp))
    prog.wait("sync", "acS", na)
    prog.wait("sync", "dvS", prog.n("dvS"))
    for nm in ("lam", "lamb", "beta", "negbg", "gam", "w"):
        for tb in range(4):
            prog.dma("sync", bass.AP(tl["stats_scr"][nm].tensor, tb * 512, [[T, 32], [1, 512]]),
                     st[nm][32 * tb:32 * (tb + 1), :], inc="st_scr")
    prog.wait("sync", "st_scr", prog.n("st_scr"))
    prog.dma("sync", tl["lam_w"][:], tl["stats_scr"]["lam"][:], inc="st_scr")
    prog.dma("sync", tl["lamb_w"][:], tl["stats_scr"]["lamb"][:], inc="st_scr")
    prog.wait("tensor", "st_scr", prog.n("st_scr"))
    prog.barrier(("tensor", "vector", "scalar", "sync"))


def emit_qkvz(prog, tl):
    nc = prog.nc
    xn = tl["xn"]
    wbuf = tl["wbuf"]
    ps = [tl["psumA"], tl["psumB"]]
    pn = tl["psumC"]
    pb = tl["psumD"]
    raw = tl["qkraw"]
    hat = tl["qkhat"]
    sq = tl["sqtile"]
    prog.op("vector", lambda e: e.memset(raw[:, 0:3], 0.0), inc="dvS")
    prog.wait("tensor", "dvS", prog.n("dvS"))
    evac_g = {}
    for dt in range(64):
        sec = dt // 16
        buf = wbuf[dt % 3]
        if dt >= 3:
            prog.wait("gpsimd", "wfree", dt - 2)
        prog.dma("gpsimd", buf[:].rearrange("p (a b) -> p a b", b=128),
                 bass.AP(tl["Wqkvzp"].tensor, dt * 128, [[8192, 128], [128 * 8192, 16], [1, 128]]),
                 inc="wqdma")
        prog.wait("tensor", "wqdma", prog.n("wqdma"))
        nhs = prog.n("hat_st")
        prog.wait("vector", "hat_st", nhs)
        prog.wait("scalar", "hat_st", nhs)
        for g in range(NGRP):
            i = dt * NGRP + g
            bank = ps[i % 2]
            if i >= 2:
                prog.wait("tensor", "dvS", evac_g[i - 2])
            for kt in range(16):
                prog.op("tensor", lambda e, b=bank, buf=buf, kt=kt, g=g:
                        e.matmul(b[:], buf[:, 128 * kt:128 * (kt + 1)],
                                 xn[:, 2048 * kt + 512 * g: 2048 * kt + 512 * (g + 1)],
                                 start=(kt == 0), stop=(kt == 15)),
                        inc="mmS" if kt == 15 else None)
            n = prog.n("mmS")
            if g == NGRP - 1:
                prog.milestone("tensor", "mmS", "wfree")
            if sec == 3:
                prog.wait("scalar", "mmS", n)
                sN = prog.op("scalar", lambda e, b=bank, g=g:
                             e.activation(hat[:, 512 * g:512 * (g + 1)], b[:], AF.Sigmoid), inc="acS")
                evac_g[i] = prog.n("dvS")
                prog.wait("tensor", "acS", sN)
                prog.wait("sync", "acS", sN)
                prog.dma("sync", tl["sigz_scr"][128 * (dt - 48):128 * (dt - 47), 512 * g:512 * (g + 1)],
                         hat[:, 512 * g:512 * (g + 1)], inc="hat_st")
            else:
                prog.wait("vector", "mmS", n)
                evac_g[i] = prog.op("vector", lambda e, b=bank, g=g:
                                    e.tensor_copy(raw[:, 3 + 512 * g:3 + 512 * (g + 1)], b[:]), inc="dvS")
        if sec == 3:
            prog.barrier(("tensor", "vector", "scalar", "sync"))
            continue
        cwc = tl["convw_s"][:, 4 * dt:4 * dt + 4]
        nv = prog.n("dvS")
        prog.wait("vector", "dvS", nv)
        nv = prog.op("vector", lambda e, c=cwc: e.tensor_scalar_mul(hat[:], raw[:, 3:2051], c[:, 3:4]),
                     inc="dvS")
        for tap in (2, 1, 0):
            prog.wait("vector", "dvS", nv)
            nv = prog.op("vector", lambda e, tap=tap, c=cwc: e.scalar_tensor_tensor(
                hat[:], raw[:, tap:tap + 2048], c[:, tap:tap + 1], hat[:], ALU.mult, ALU.add), inc="dvS")
        if sec == 2:
            prog.wait("sync", "dvS", nv)
            prog.dma("sync", tl["qkvhat_scr"][128 * dt:128 * (dt + 1), :], hat[:], inc="hat_st")
        else:
            prog.wait("scalar", "dvS", nv)
            sN = prog.op("scalar", lambda e: e.activation(sq[:], hat[:], AF.Square), inc="acS")
            prog.wait("tensor", "acS", sN)
            for g in range(NGRP):
                gsl = slice(512 * g, 512 * (g + 1))
                nmm = prog.op("tensor", lambda e, gsl=gsl: e.matmul(
                    pn[0:2, 0:512], tl["bdones"][:], sq[:, gsl], start=True, stop=True), inc="mmS")
                prog.wait("scalar", "mmS", nmm)
                sN = prog.op("scalar", lambda e: e.activation(tl["nrm2"][:], pn[0:2, 0:512], AF.Sqrt,
                                                              bias=EPS_DN), inc="acS")
                prog.wait("vector", "acS", sN)
                nv = prog.op("vector", lambda e: e.reciprocal(tl["nrm"][:], tl["nrm2"][:]), inc="dvS")
                if sec == 0:
                    prog.wait("vector", "dvS", nv)
                    nv = prog.op("vector", lambda e: e.tensor_scalar_mul(tl["nrm"][:], tl["nrm"][:], 0.125),
                                 inc="dvS")
                prog.wait("tensor", "dvS", nv)
                nmm = prog.op("tensor", lambda e: e.matmul(pb[:, 0:512], tl["ind2"][:], tl["nrm"][:],
                                                           start=True, stop=True), inc="mmS")
                prog.wait("vector", "mmS", nmm)
                nv = prog.op("vector", lambda e, gsl=gsl: e.tensor_mul(hat[:, gsl], hat[:, gsl], pb[:, 0:512]),
                             inc="dvS")
                prog.wait("tensor", "dvS", nv)
            prog.wait("sync", "dvS", prog.n("dvS"))
            prog.dma("sync", tl["qkvhat_scr"][128 * dt:128 * (dt + 1), :], hat[:], inc="hat_st")
        prog.barrier(("tensor", "vector", "scalar", "sync"))
    prog.wait("sync", "hat_st", prog.n("hat_st"))
    prog.barrier()


def emit_ffn(prog, tl, cfg):
    nc = prog.nc
    hid = cfg["hid"]
    nht = hid // 128
    xn = tl["xn"]
    ps = [tl["psumA"], tl["psumB"], tl["psumC"], tl["psumD"]]
    gat = tl["qkhat"]
    sil = tl["sqtile"]
    lbl = cfg["lbl"]
    ev = {}
    for ht in range(nht):
        buf = tl["wbuf"][ht % 3]
        buf2 = tl["wbuf2"][ht % 3]
        if ht >= 3:
            prog.wait("gpsimd", lbl + "f1free", ht - 2)
        prog.dma("gpsimd", buf[:].rearrange("p (a b) -> p a b", b=128),
                 bass.AP(cfg["Wg"].tensor, ht * 128, [[hid, 128], [128 * hid, 16], [1, 128]]),
                 inc=lbl + "f1dma")
        prog.dma("gpsimd", buf2[:].rearrange("p (a b) -> p a b", b=128),
                 bass.AP(cfg["Wu"].tensor, ht * 128, [[hid, 128], [128 * hid, 16], [1, 128]]),
                 inc=lbl + "f1dma")
        prog.wait("tensor", lbl + "f1dma", prog.n(lbl + "f1dma"))
        for g in range(NGRP):
            i = ht * NGRP + g
            bg = ps[(2 * i) % 4]
            bu = ps[(2 * i + 1) % 4]
            qz = cfg.get("quant")
            if i >= 2:
                prog.wait("tensor", "dvS", ev[i - 2])
            for kt in range(16):
                prog.op("tensor", lambda e, b=bg, buf=buf, kt=kt, g=g:
                        e.matmul(b[:], buf[:, 128 * kt:128 * (kt + 1)],
                                 xn[:, 2048 * kt + 512 * g:2048 * kt + 512 * (g + 1)],
                                 start=(kt == 0), stop=(kt == 15 and not qz)))
            if qz:
                prog.op("tensor", lambda e, b=bg, g=g: e.matmul(
                    b[:], tl["neg128_row_b"][:], qz["rs"][:, 512 * g:512 * (g + 1)],
                    start=False, stop=True))
            for kt in range(16):
                prog.op("tensor", lambda e, b=bu, buf=buf2, kt=kt, g=g:
                        e.matmul(b[:], buf[:, 128 * kt:128 * (kt + 1)],
                                 xn[:, 2048 * kt + 512 * g:2048 * kt + 512 * (g + 1)],
                                 start=(kt == 0), stop=(kt == 15 and not qz)),
                        inc="mmS" if (kt == 15 and not qz) else None)
            if qz:
                prog.op("tensor", lambda e, b=bu, g=g: e.matmul(
                    b[:], tl["neg128_row_b"][:], qz["rs"][:, 512 * g:512 * (g + 1)],
                    start=False, stop=True), inc="mmS")
            if g == NGRP - 1:
                prog.milestone("tensor", "mmS", lbl + "f1free")
            nmm = prog.n("mmS")
            gsl = slice(512 * g, 512 * (g + 1))
            qz = cfg.get("quant")
            if qz:
                pass
            prog.wait("scalar", "mmS", nmm)
            if qz:
                sgc = qz["sg"][:, ht:ht + 1]
                suc = qz["su"][:, ht:ht + 1]
                na = prog.op("scalar", lambda e, b=bg, gsl=gsl, s=sgc: e.activation(
                    sil[:, gsl], b[:], AF.Silu, scale=s), inc="acS")
                prog.wait("vector", "acS", na)
                nv0 = prog.op("vector", lambda e, b=bu, gsl=gsl, s=suc: e.tensor_scalar_mul(
                    tl["upt"][:, gsl], b[:], s), inc="dvS")
                prog.wait("vector", "dvS", nv0)
                ev[i] = prog.op("vector", lambda e, gsl=gsl: e.tensor_mul(
                    gat[:, gsl], sil[:, gsl], tl["upt"][:, gsl]), inc="dvS")
            else:
                na = prog.op("scalar", lambda e, b=bg, gsl=gsl: e.activation(sil[:, gsl], b[:], AF.Silu),
                             inc="acS")
                prog.wait("vector", "acS", na)
                ev[i] = prog.op("vector", lambda e, b=bu, gsl=gsl: e.tensor_mul(gat[:, gsl], sil[:, gsl], b[:]),
                                inc="dvS")
        prog.wait("sync", "dvS", ev[ht * NGRP + NGRP - 1])
        prog.dma("sync", cfg["gated"][128 * ht:128 * (ht + 1), :], gat[:], inc=lbl + "f1st")
        prog.wait("vector", lbl + "f1st", prog.n(lbl + "f1st"))
    prog.barrier(("tensor", "vector", "scalar", "sync", "gpsimd"))
    for i in range(nht):
        buf = tl["wbuf"][i % 3]
        if i >= 3:
            prog.wait("gpsimd", lbl + "wdst", 16 * (i - 2))
        prog.dma("gpsimd", buf[:], cfg["Wd"][128 * i:128 * (i + 1), :], inc=lbl + "wddma")
        prog.wait("sync", lbl + "wddma", prog.n(lbl + "wddma"))
        prog.dma("sync", cfg["wd_scr"][128 * i:128 * (i + 1), :], buf[:], inc=lbl + "wdst")
    prog.wait("gpsimd", lbl + "wdst", prog.n(lbl + "wdst"))
    prog.barrier(("tensor", "vector", "scalar", "sync", "gpsimd"))


def emit_ffn_f2(prog, tl, cfg):
    nc = prog.nc
    hid = cfg["hid"]
    nht = hid // 128
    ps = [tl["psumA"], tl["psumB"], tl["psumC"], tl["psumD"]]
    lbl = cfg["lbl"]
    gatg = tl["gatg"]
    gwb = tl["gwb"]
    xb = tl["xbuf"]
    qz = cfg.get("quant")
    for g in range(NGRP):
        for i in range(nht):
            prog.dma("sync", gatg[i][:],
                     bass.AP(cfg["gated"].tensor, 128 * i * T + 512 * g, [[T, 128], [1, 512]]),
                     inc=lbl + "f2gld")
        for dtile in range(16):
            prog.dma("sync", xb[dtile][:],
                     bass.AP(cfg["res_in"].tensor, 128 * dtile * T + 512 * g, [[T, 128], [1, 512]]),
                     inc=lbl + "f2gld")
        n = prog.n(lbl + "f2gld")
        prog.wait("tensor", lbl + "f2gld", n)
        prog.wait("vector", lbl + "f2gld", n)
        if qz:
            for i in range(nht):
                prog.op("tensor", lambda e, i=i: e.matmul(
                    tl["psum_e"][0:1, 0:512], tl["ones_col_b"][:], gatg[i][:],
                    start=(i == 0), stop=(i == nht - 1)), inc="mmS" if i == nht - 1 else None)
            nms = prog.n("mmS")
            prog.wait("vector", "mmS", nms)
            nv = prog.op("vector", lambda e: e.tensor_copy(tl["gsum_b"][:], tl["psum_e"][0:1, 0:512]),
                         inc="dvS")
            prog.wait("tensor", "dvS", nv)
        ev2 = {}
        for dtile in range(16):
            wb1 = gwb[dtile % 2]
            if dtile >= 2:
                prog.wait("sync", lbl + "f2free", (g * 16 + dtile - 1))
            prog.dma("sync", wb1[:, 0:128 * nht].rearrange("p (a b) -> p a b", b=128),
                     bass.AP(cfg["wd_scr"].tensor, dtile * 128, [[D, 128], [128 * D, nht], [1, 128]]),
                     inc=lbl + "f2wld")
            prog.wait("tensor", lbl + "f2wld", prog.n(lbl + "f2wld"))
            bank = ps[dtile % 2]
            if dtile >= 2:
                prog.wait("tensor", "dvS", ev2[dtile - 2])
            for i in range(nht):
                prog.op("tensor", lambda e, b=bank, bs=wb1, iw=i, gg=gatg[i], i=i:
                        e.matmul(b[:], bs[:, 128 * iw:128 * (iw + 1)], gg[:],
                                 start=(i == 0), stop=(i == nht - 1 and not qz)),
                        inc="mmS" if (i == nht - 1 and not qz) else None)
            if qz:
                prog.op("tensor", lambda e, b=bank: e.matmul(
                    b[:], tl["neg128_row_b"][:], tl["gsum_b"][:], start=False, stop=True), inc="mmS")
            nmm = prog.n("mmS")
            prog.milestone("tensor", "mmS", lbl + "f2free")
            prog.wait("vector", "mmS", nmm)
            if qz:
                sdc = qz["sd"][:, dtile:dtile + 1]
                nv = prog.op("vector", lambda e, b=bank, t_=xb[dtile], s=sdc: e.scalar_tensor_tensor(
                    t_[:], b[:], s, t_[:], ALU.mult, ALU.add), inc="dvS")
            else:
                nv = prog.op("vector", lambda e, b=bank, t_=xb[dtile]: e.tensor_add(
                    t_[:], b[:], t_[:]), inc="dvS")
            ev2[dtile] = nv
            prog.wait("sync", "dvS", nv)
            prog.dma("sync", bass.AP(cfg["out"].tensor, 128 * dtile * T + 512 * g, [[T, 128], [1, 512]]),
                     xb[dtile][:], inc="fout")
        prog.barrier(("tensor", "vector", "sync"))
    prog.wait("gpsimd", "fout", prog.n("fout"))


CMAP = [("ident_s", "ident", "BF16", [128, 128]), ("identF_s", "identF", "F32", [128, 128]),
        ("sel4", "sel4_c", "F32", [4, 128]), ("ones_row_f", "ones_row_fc", "F32", [1, 128]),
        ("ones128", "ones128_c", "BF16", [128, 128]),
        ("neg128_row_b", "neg128_row", "BF16", [1, 128]),
        ("eye2_s", "eye2", "BF16", [128, 64]),
        ("ones_col_b", "ones_col", "BF16", [128, 1]), ("ones_row_b", "ones_row", "BF16", [1, 128]),
        ("bdones", "bdones_c", "BF16", [128, 2]), ("ind2", "ind2_c", "F32", [2, 128]),
        ("selP_s", "selP", "F32", [32, NPAIR * 128]), ("selPn_s", "selPn", "F32", [32, NPAIR * 128]),
        ("ohLn_s", "ohLn", "F32", [32, NPAIR * 64]), ("ohLB_s", "ohLB", "F32", [32, NPAIR * 64]),
        ("mask_E1_s", "mask_E1", "F32", [128, 64]), ("mask_E2_s", "mask_E2", "F32", [128, 64]),
        ("mask_E1T_s", "mask_E1T", "F32", [128, 64])]


def l0_dram(nc):
    dram = {}
    dram["x_in"] = nc.dram_tensor("x_in", [D, T], F32, kind="ExternalInput").ap()
    specs = [("Wqkvzp", [D, 8192], F32), ("Wb", [D, 32], F32), ("Wa", [D, 32], F32),
             ("convw", [6144, 4], F32), ("A_log", [128, 1], F32), ("dt_bias", [128, 1], F32),
             ("normw", [1, 64], F32), ("Woutp", [D, D], F32),
             ("Wg0", [D, 8192], F32), ("Wu0", [D, 8192], F32), ("Wd0", [8192, D], F32)]
    for nm, shape, dt in specs:
        dram[nm] = nc.dram_tensor(nm, shape, dt, kind="ExternalInput").ap()
    for sname, dname, dt, shape in CMAP:
        dram[dname] = nc.dram_tensor(dname, shape, F32, kind="ExternalInput").ap()
    dram["x1_out"] = nc.dram_tensor("x1_out", [D, T], F32, kind="ExternalOutput").ap()
    return dram


def l0_scratch(nc, tag=""):
    scr = {}
    scr["qkvhat"] = nc.dram_tensor(f"qkvhat_scr{tag}", [6144, T], BF16, kind="Internal").ap()
    scr["sigz"] = nc.dram_tensor(f"sigz_scr{tag}", [D, T], BF16, kind="Internal").ap()
    scr["x1a"] = nc.dram_tensor(f"x1a_scr{tag}", [D, T], F32, kind="Internal").ap()
    scr["xn"] = nc.dram_tensor(f"xn_scr{tag}", [D, T], BF16, kind="Internal").ap()
    scr["gated"] = nc.dram_tensor(f"gated_scr{tag}", [8192, T], BF16, kind="Internal").ap()
    scr["wd_bf"] = nc.dram_tensor(f"wd_bf_scr{tag}", [8192, D], BF16, kind="Internal").ap()
    for nm in ("lam", "lamb", "beta", "negbg", "gam", "w"):
        scr[nm] = nc.dram_tensor(f"stat_{nm}{tag}", [32, T], F32, kind="Internal").ap()
    return scr


def alloc_common(nc, E, tl, dram):
    tl["consts"] = {}
    for sname, dname, dts, shape in CMAP:
        h = E(nc.sbuf_tensor(sname, shape, BF16 if dts == "BF16" else F32))
        tl["consts"][sname] = h
        tl[sname] = h
    tl["fconst"] = E(nc.sbuf_tensor("fconst", [128, 8], F32))
    tl["wbuf"] = [E(nc.sbuf_tensor(f"wbuf{i}", [128, 2048], BF16)) for i in range(3)]
    tl["psumA"] = E(nc.psum_tensor("psA", [128, 512], F32))
    tl["psumB"] = E(nc.psum_tensor("psB", [128, 512], F32))
    tl["psumC"] = E(nc.psum_tensor("psC", [128, 512], F32))
    tl["psumD"] = E(nc.psum_tensor("psD", [128, 512], F32))
    tl["psum_sm"] = E(nc.psum_tensor("psSM", [128, 512], F32))
    tl["psum_e"] = E(nc.psum_tensor("psE", [128, 512], F32))
    tl["A_log"] = E(nc.sbuf_tensor("A_log_s", [128, 1], F32))
    tl["dt_bias"] = E(nc.sbuf_tensor("dt_bias_s", [128, 1], F32))
    tl["negexpA"] = E(nc.sbuf_tensor("negexpA", [128, 1], F32))
    tl["normw_b"] = E(nc.sbuf_tensor("normw_b", [1, 64], BF16))
    tl["normwB"] = E(nc.sbuf_tensor("normwB", [128, 64], F32))
    tl["convw_s"] = E(nc.sbuf_tensor("convw_s", [128, 48 * 4], F32))
    tl["lnrow"] = ([E(nc.sbuf_tensor(f"lnrow{i}", [1, 512], F32)) for i in range(3)]
                   + [E(nc.sbuf_tensor(f"lnrow{i}", [1, 512], BF16)) for i in (3, 4)])


def emit_const_loads(prog, tl, dram):
    for sname, dname, dts, shape in CMAP:
        if dts == "BF16":
            prog.dma("gpsimd", tl["consts"][sname][:], dram[dname][:], inc="c_ldg")
        else:
            prog.dma("sync", tl["consts"][sname][:], dram[dname][:], inc="c_ld")
    prog.dma("sync", tl["convw_s"][:].rearrange("p (a b) -> p a b", b=4),
             bass.AP(dram["convw"].tensor, 0, [[4, 128], [128 * 4, 48], [1, 4]]), inc="c_ld")
    prog.dma("sync", tl["A_log"][:], dram["A_log"][:], inc="c_ld")
    prog.dma("sync", tl["dt_bias"][:], dram["dt_bias"][:], inc="c_ld")
    prog.dma("gpsimd", tl["normw_b"][:], dram["normw"][:], inc="c_ldg")
    for e in ("tensor", "vector", "scalar"):
        prog.wait(e, "c_ld", prog.n("c_ld"))
        prog.wait(e, "c_ldg", prog.n("c_ldg"))


def emit_layer0(prog, tl_c, nc, ctx, dram, scr, x_in_ap, x_out_ap):
    E = ctx.enter_context
    tl = dict(tl_c)
    tl["x_in"] = x_in_ap
    tl["Wqkvzp"] = dram["Wqkvzp"]
    tl["Woutp"] = dram["Woutp"]
    tl["qkvhat_scr"] = scr["qkvhat"]
    tl["sigz_scr"] = scr["sigz"]
    tl["x1a_scr"] = scr["x1a"]
    tl["stats_scr"] = {nm: scr[nm] for nm in ("lam", "lamb", "beta", "negbg", "gam", "w")}
    tl["lam_w"] = E(nc.sbuf_tensor("lam_w", [32, 2048], F32))
    tl["lamb_w"] = E(nc.sbuf_tensor("lamb_w", [32, 2048], F32))
    st = {}
    tl["st"] = st

    with ExitStack() as phb:
        Eb = phb.enter_context
        tl["xn"] = Eb(nc.sbuf_tensor("xn0", [128, 16 * 2048], BF16))
        for nm in ("braw", "araw", "t0", "t1", "t2", "sp", "lam", "lamb", "beta",
                   "negbg", "gam", "w"):
            st[nm] = Eb(nc.sbuf_tensor(f"st_{nm}", [128, 512], F32))
        tl["st_stage"] = Eb(nc.sbuf_tensor("st_stage", [32, 512], F32))
        tl["wb"] = Eb(nc.sbuf_tensor("wb_s", [128, 16 * 32], BF16))
        tl["wa"] = Eb(nc.sbuf_tensor("wa_s", [128, 16 * 32], BF16))
        prog.dma("gpsimd", tl["wb"][:].rearrange("p (a b) -> p a b", b=32),
                 bass.AP(dram["Wb"].tensor, 0, [[32, 128], [128 * 32, 16], [1, 32]]), inc="c_ldg")
        prog.dma("gpsimd", tl["wa"][:].rearrange("p (a b) -> p a b", b=32),
                 bass.AP(dram["Wa"].tensor, 0, [[32, 128], [128 * 32, 16], [1, 32]]), inc="c_ldg")
        prog.wait("tensor", "c_ldg", prog.n("c_ldg"))
        with ExitStack() as phln:
            El = phln.enter_context
            tl["xb16"] = [El(nc.sbuf_tensor(f"xb16_{i}", [128, 512], BF16)) for i in range(16)]
            tl["sqb"] = [El(nc.sbuf_tensor(f"sqb{i}", [128, 512], BF16)) for i in range(16)]
            emit_ln(prog, tl, x_in_ap, tl["xn"][:], "ln0")
            prog.barrier()
        tl["qkraw"] = Eb(nc.sbuf_tensor("qkraw", [128, 2051], BF16))
        tl["qkhat"] = Eb(nc.sbuf_tensor("qkhat", [128, 2048], BF16))
        tl["sqtile"] = Eb(nc.sbuf_tensor("sqtile", [128, 2048], BF16))
        tl["nrm2"] = Eb(nc.sbuf_tensor("nrm2", [2, 512], F32))
        tl["nrm"] = Eb(nc.sbuf_tensor("nrm", [2, 512], F32))
        prog.wait("sync", "dvS", prog.n("dvS"))
        for kt in range(16):
            prog.dma("sync", scr["xn"][128 * kt:128 * (kt + 1), :],
                     tl["xn"][:, 2048 * kt:2048 * (kt + 1)], inc="xn_st")
        emit_stats(prog, tl)
        emit_qkvz(prog, tl)
        prog.wait("sync", "xn_st", prog.n("xn_st"))
        prog.barrier()

    with ExitStack() as phc:
        Ec = phc.enter_context
        tl["qk_g"] = [Ec(nc.sbuf_tensor(f"qkg{i}", [128, GRP], BF16)) for i in range(48)]
        tl["sigz_g"] = [Ec(nc.sbuf_tensor(f"szg{i}", [128, GRP], BF16)) for i in range(16)]
        tl["o_g"] = [Ec(nc.sbuf_tensor(f"og{i}", [128, GRP], BF16)) for i in range(16)]
        tl["state"] = [Ec(nc.sbuf_tensor(f"Sst{i}", [128, 64], F32)) for i in range(NPAIR)]
        tl["state_b"] = [Ec(nc.sbuf_tensor(f"Sstb{i}", [128, 64], BF16)) for i in range(NPAIR)]
        tl["state_dv"] = [0] * NPAIR
        tl["xbuf"] = [Ec(nc.sbuf_tensor(f"xbufC{i}", [128, 512], F32)) for i in range(4)]
        tl["pct"] = Ec(nc.sbuf_tensor("pct", [32, 128], F32))
        tl["pcT_in"] = Ec(nc.sbuf_tensor("pcT_in", [80, 128], F32))
        tl["pcall"] = Ec(nc.sbuf_tensor("pcall", [128, 80], F32))
        wn_pp = ("NT", "NTT", "G", "P2", "P2T", "P4", "P4T", "P8", "P8T",
                 "P16", "P16T", "P32T", "PPT", "bV", "W")
        wn_s4b = ("c", "CSQ", "delta", "on")
        wn_s4f = ("orow", "osq")
        wn_s2f = ("e1f", "e2f", "e1tf")
        wn_s2b = ("E1", "E2", "E1T")
        slot4 = []
        for s in range(4):
            d = {}
            for nm in wn_s4b:
                d[nm] = Ec(nc.sbuf_tensor(f"s4_{s}_{nm}", [128, 64], BF16))
            for nm in wn_s4f:
                d[nm] = Ec(nc.sbuf_tensor(f"s4_{s}_{nm}", [128, 64], F32))
            for nm in ("var", "rstd", "rinv"):
                d[nm] = Ec(nc.sbuf_tensor(f"s4_{s}_{nm}", [128, 1], F32))
            slot4.append(d)
        slot2 = []
        for s in range(2):
            d = {}
            for nm in wn_s2f:
                d[nm] = Ec(nc.sbuf_tensor(f"s2_{s}_{nm}", [128, 64], F32))
            for nm in wn_s2b:
                d[nm] = Ec(nc.sbuf_tensor(f"s2_{s}_{nm}", [128, 64], BF16))
            slot2.append(d)
        ws = []
        for p in range(NPAIR):
            w = {}
            for nm in wn_pp:
                w[nm] = Ec(nc.sbuf_tensor(f"w{p}_{nm}", [128, 64], BF16))
            w.update(slot4[p % 4])
            w.update(slot2[p % 2])
            ws.append(w)
        tl["ws"] = ws
        emit_scan(prog, tl)
        prog.barrier()

    with ExitStack() as phd:
        Ed = phd.enter_context
        tl["xn"] = Ed(nc.sbuf_tensor("xn1", [128, 16 * 2048], BF16))
        tl["qkhat"] = Ed(nc.sbuf_tensor("gatt", [128, 2048], BF16))
        tl["wbuf2"] = [Ed(nc.sbuf_tensor(f"wbuf2_{i}", [128, 2048], BF16)) for i in range(3)]
        tl["sqtile"] = Ed(nc.sbuf_tensor("silt", [128, 2048], BF16))
        for kt in range(16):
            prog.dma("gpsimd", tl["xn"][:, 2048 * kt:2048 * (kt + 1)],
                     scr["xn"][128 * kt:128 * (kt + 1), :], inc="xn_ld")
        prog.wait("tensor", "xn_ld", prog.n("xn_ld"))
        cfg0 = dict(Wg=dram["Wg0"], Wu=dram["Wu0"], Wd=dram["Wd0"],
                    wd_scr=scr["wd_bf"], gated=scr["gated"], hid=8192,
                    res_in=scr["x1a"], out=x_out_ap, lbl="f0")
        emit_ffn(prog, tl, cfg0)
    with ExitStack() as phe:
        Ee = phe.enter_context
        tl["gatg"] = [Ee(nc.sbuf_tensor(f"gatg{i}", [128, 512], BF16)) for i in range(64)]
        tl["gwb"] = [Ee(nc.sbuf_tensor(f"gwb{i}", [128, 8192], BF16)) for i in range(2)]
        tl["xbuf"] = [Ee(nc.sbuf_tensor(f"xbufF{i}", [128, 512], F32)) for i in range(16)]
        emit_ffn_f2(prog, tl, cfg0)
    prog.barrier()


def build_layer0():
    nc = bass.Bass("TRN2", target_bir_lowering=False)
    dram = l0_dram(nc)
    scr = l0_scratch(nc)
    with ExitStack() as ctx:
        E = ctx.enter_context
        tl = {}
        alloc_common(nc, E, tl, dram)
        prog = Prog(nc, ctx)
        register_float_consts(prog, tl)
        emit_const_loads(prog, tl, dram)
        prog.barrier()
        emit_layer0(prog, tl, nc, ctx, dram, scr, dram["x_in"], dram["x1_out"])
        prog.finalize(E(nc.Block()))
    return nc


def l0_in_map(inputs, h, cs, b):
    m = {"x_in": np.ascontiguousarray(inputs["L0_attn_hidden_states"][b].T)}
    for nm in ("Wqkvzp", "Wb", "Wa", "convw", "A_log", "dt_bias", "Woutp"):
        m[nm] = h[nm]
    m["normw"] = h["normw"]
    m["Wg0"], m["Wu0"], m["Wd0"] = h["Wg0"], h["Wu0"], h["Wd0"]
    for sname, dname, dts, shape in CMAP:
        m[dname] = cs[dname]
    return m


def kernel(**inputs):
    return kernel_fused(**inputs)


def kernel_unfused(**inputs):
    h = host_prep(inputs)
    cs = host_consts()
    nc0 = build_layer0()
    in0 = [l0_in_map(inputs, h, cs, b) for b in range(8)]
    r0 = run_bass_kernel_spmd(nc0, in0, core_ids=list(range(8)))
    nc1 = build_layer1()
    in1 = [l1_in_map(h, cs, r0.results[b]["x1_out"]) for b in range(8)]
    r1 = run_bass_kernel_spmd(nc1, in1, core_ids=list(range(8)))
    out = np.stack([r1.results[b]["x2_out"].T for b in range(8)])
    return np.ascontiguousarray(out)


def emit_rowsum(prog, tl, rs_b):
    xn = tl["xn"]
    pn = tl["psumC"]
    onescb = tl["ones_col_b"]
    for g in range(NGRP):
        if g >= 1:
            prog.wait("tensor", "dvS", prog.n("dvS"))
        for kt in range(16):
            prog.op("tensor", lambda e, kt=kt, g=g: e.matmul(
                pn[0:1, 0:512], onescb[:],
                xn[:, 2048 * kt + 512 * g:2048 * kt + 512 * (g + 1)],
                start=(kt == 0), stop=(kt == 15)), inc="mmS" if kt == 15 else None)
        n = prog.n("mmS")
        prog.wait("vector", "mmS", n)
        nv = prog.op("vector", lambda e, rs=rs_b, g=g: e.tensor_copy(rs[:, 512 * g:512 * (g + 1)],
                                                                     pn[0:1, 0:512]), inc="dvS")
    prog.wait("tensor", "dvS", nv)


def emit_wscale_build(prog, tl, codes_ap, sc_ap, wsc, head_or_dt, lbl):
    h = head_or_dt
    cbuf = tl["codebuf"]
    scs = tl["sc_s"]
    pbanks = (tl["psum_sm"], tl["psum_e"])
    prog.wait("gpsimd", "dvS", tl["wsc_free"][0])
    prog.dma("gpsimd", cbuf[:].rearrange("p (a b) -> p a b", b=128),
             bass.AP(codes_ap.tensor, h * 128, [[2048, 128], [128 * 2048, 16], [1, 128]]),
             inc=lbl + "cdma")
    prog.wait("sync", "mmS", prog.n("mmS"))
    prog.dma("sync", scs[:].rearrange("p (a b) -> p a b", b=128),
             bass.AP(sc_ap.tensor, h * 128, [[2048, 4], [4 * 2048, 16], [1, 128]]),
             inc=lbl + "sdma")
    prog.wait("tensor", lbl + "sdma", prog.n(lbl + "sdma"))
    prog.wait("tensor", lbl + "cdma", prog.n(lbl + "cdma"))
    for kt in range(16):
        pb = pbanks[kt % 2]
        par = (kt // 2) % 2
        if kt >= 2:
            prog.wait("tensor", "dvS", tl["wsc_ev"][kt - 2])
        nmm = prog.op("tensor", lambda e, kt=kt, par=par, pb=pb: e.matmul(
            pb[:, 128 * par:128 * (par + 1)], tl["sel4"][:], scs[:, 128 * kt:128 * (kt + 1)],
            start=True, stop=True), inc="mmS")
        prog.wait("vector", "mmS", nmm)
        tl["wsc_ev"][kt] = prog.op("vector", lambda e, kt=kt, par=par, pb=pb, w=wsc, c=cbuf: e.tensor_mul(
            w[:, 128 * kt:128 * (kt + 1)], c[:, 128 * kt:128 * (kt + 1)],
            pb[:, 128 * par:128 * (par + 1)]), inc="dvS")
    tl["wsc_free"][0] = tl["wsc_ev"][15]
    return tl["wsc_ev"][15]


def emit_attn1(prog, tl, dram, scr, upto=3, nheads=16):
    nc = prog.nc
    xn = tl["xn"]
    ps = [tl["psumA"], tl["psumB"]]
    pso = tl["psumC"]
    pse = tl["psum_e"]
    IA = tl["ident_s"]
    onescb = tl["ones_col_b"]
    onesrb = tl["ones_row_b"]
    for h in range(nheads):
        if upto <= 0:
            wev = emit_wscale_build(prog, tl, dram["Cwq"], dram["Swq"], tl["wsc"], h, "at")
            prog.wait("sync", "dvS", wev)
            prog.dma("sync", scr["o1"][128 * h:128 * (h + 1), :], tl["wsc"][:], inc="o_st")
            prog.wait("vector", "o_st", prog.n("o_st"))
            prog.barrier(("tensor", "vector", "scalar", "sync", "gpsimd"))
            continue
        qkv = {}
        for mi, mat in enumerate(("q", "k", "v")):
            wsc = tl["wsc"]
            wev = emit_wscale_build(prog, tl, dram[f"Cw{mat}"], dram[f"Sw{mat}"], wsc, h, "at")
            prog.wait("tensor", "dvS", wev)
            dst = tl[f"{mat}h"]
            bias = tl["bqkv"][mat][:, h:h + 1]
            prog.wait("tensor", "dvS", prog.n("dvS"))
            ev = None
            for g in range(NGRP):
                bank = ps[g % 2]
                if g >= 2:
                    prog.wait("tensor", "dvS", ev_g[g - 2])
                if g == 0:
                    ev_g = {}
                for kt in range(16):
                    prog.op("tensor", lambda e, b=bank, w=wsc, kt=kt, g=g: e.matmul(
                        b[:], w[:, 128 * kt:128 * (kt + 1)],
                        xn[:, 2048 * kt + 512 * g:2048 * kt + 512 * (g + 1)],
                        start=(kt == 0), stop=(kt == 15)), inc="mmS" if kt == 15 else None)
                nmm = prog.n("mmS")
                prog.wait("vector", "mmS", nmm)
                ev_g[g] = prog.op("vector", lambda e, b=bank, g=g, d=dst, bi=bias: e.tensor_scalar_add(
                    d[:, 512 * g:512 * (g + 1)], b[:], bi), inc="dvS")
            prog.wait("vector", "dvS", ev_g[3])
        if upto <= 1:
            nv = prog.n("dvS")
            prog.wait("sync", "dvS", nv)
            prog.dma("sync", scr["o1"][128 * h:128 * (h + 1), :], tl["vh"][:], inc="o_st")
            prog.wait("vector", "o_st", prog.n("o_st"))
            prog.barrier(("tensor", "vector", "scalar", "sync", "gpsimd"))
            continue
        vt = tl["vh"]
        vbanks = (tl["psumD"], tl["psum_sm"])
        for kt in range(16):
            vb = vbanks[kt % 2]
            if kt >= 2:
                prog.wait("tensor", "dvS", tl["vt_ev"][kt - 2])
            nmm = prog.op("tensor", lambda e, kt=kt, vb=vb, vt=vt: e.matmul(
                vb[:, 0:128], vt[:, 128 * kt:128 * (kt + 1)], IA[:],
                start=True, stop=True), inc="mmS")
            prog.wait("vector", "mmS", nmm)
            tl["vt_ev"][kt] = prog.op("vector", lambda e, kt=kt, vb=vb: e.tensor_copy(
                tl["VT"][kt][:], vb[:, 0:128]), inc="dvS")
        for mi, mat in enumerate(("q", "k")):
            src = tl[f"{mat}h"]
            prog.wait("scalar", "dvS", prog.n("dvS"))
            na = prog.op("scalar", lambda e, s=src: e.activation(tl["sqh"][:], s[:], AF.Square),
                         inc="acS")
            prog.wait("tensor", "acS", na)
            mdst = tl["mxq"] if mi == 0 else tl["mxk"]
            for g in range(NGRP):
                if g >= 1:
                    prog.wait("tensor", "dvS", prog.n("dvS"))
                prog.op("tensor", lambda e, g=g: e.matmul(
                    pse[:, 0:512], tl["ones128"][:], tl["sqh"][:, 512 * g:512 * (g + 1)],
                    start=True, stop=True), inc="mmS")
                nmm = prog.n("mmS")
                prog.wait("vector", "mmS", nmm)
                if g == 0:
                    prog.op("vector", lambda e, m=mdst: e.reduce_max(m[:, 0:1], pse[:, 0:512], axis=AX.X),
                            inc="dvS")
                else:
                    prog.wait("vector", "dvS", prog.n("dvS"))
                    prog.op("vector", lambda e, m=mdst: e.reduce_max(m[:, 1:2], pse[:, 0:512], axis=AX.X),
                            inc="dvS")
                    prog.wait("vector", "dvS", prog.n("dvS"))
                    prog.op("vector", lambda e, m=mdst: e.tensor_max(m[:, 0:1], m[:, 0:1], m[:, 1:2]),
                            inc="dvS")
        prog.wait("vector", "dvS", prog.n("dvS"))
        nv = prog.op("vector", lambda e: e.tensor_mul(tl["mxq"][:, 2:3], tl["mxq"][:, 0:1],
                                                      tl["mxk"][:, 0:1]), inc="dvS")
        prog.wait("scalar", "dvS", nv)
        na = prog.op("scalar", lambda e: e.activation(tl["mxq"][:, 3:4], tl["mxq"][:, 2:3], AF.Sqrt,
                                                      scale=1.0 / 128.0), inc="acS")
        prog.wait("vector", "acS", na)
        nv = prog.op("vector", lambda e: e.tensor_scalar(tl["negB"][:], tl["mxq"][:, 3:4], -1.0, -1.0,
                                                         ALU.mult, ALU.add), inc="dvS")
        prog.wait("scalar", "dvS", nv)
        prog.wait("tensor", "dvS", nv)
        if upto <= 2:
            nv = prog.n("dvS")
            prog.wait("sync", "dvS", nv)
            prog.dma("sync", scr["o1"][128 * h:128 * (h + 1), :], tl["vh"][:], inc="o_st")
            prog.wait("vector", "o_st", prog.n("o_st"))
            prog.barrier(("tensor", "vector", "scalar", "sync", "gpsimd"))
            continue
        for g in range(NGRP):
            qsl = slice(512 * g, 512 * (g + 1))
            for kt in range(16):
                par = kt % 2
                bank = ps[par]
                if kt >= 2:
                    prog.wait("tensor", "acS", tl["pt_ev"][kt - 2])
                nmm = prog.op("tensor", lambda e, b=bank, kt=kt, qsl=qsl: e.matmul(
                    b[:], tl["kh"][:, 128 * kt:128 * (kt + 1)], tl["qh"][:, qsl],
                    start=True, stop=True), inc="mmS")
                prog.wait("scalar", "mmS", nmm)
                tl["pt_ev"][kt] = prog.op("scalar", lambda e, b=bank, kt=kt: e.activation(
                    tl["PT"][kt][:], b[:], AF.Exp, bias=tl["negB"][:, 0:1], scale=0.08838834764831845),
                    inc="acS")
            na = prog.n("acS")
            prog.wait("tensor", "acS", na)
            for kt in range(16):
                prog.op("tensor", lambda e, kt=kt: e.matmul(
                    pse[0:1, 0:512], onescb[:], tl["PT"][kt][:],
                    start=(kt == 0), stop=(kt == 15)), inc=None)
            for kt in range(16):
                prog.op("tensor", lambda e, kt=kt: e.matmul(
                    pso[:], tl["VT"][kt][:], tl["PT"][kt][:],
                    start=(kt == 0), stop=(kt == 15)), inc="mmS" if kt == 15 else None)
            nmm = prog.n("mmS")
            prog.wait("vector", "mmS", nmm)
            nv = prog.op("vector", lambda e: e.reciprocal(tl["rrow"][:], pse[0:1, 0:512]), inc="dvS")
            prog.wait("vector", "dvS", nv)
            nv = prog.op("vector", lambda e: e.tensor_copy(tl["rrow_b"][:], tl["rrow"][:]), inc="dvS")
            prog.wait("tensor", "dvS", nv)
            nmm = prog.op("tensor", lambda e: e.matmul(tl["psumD"][:, 0:512], onesrb[:], tl["rrow_b"][:],
                                                       start=True, stop=True), inc="mmS")
            prog.wait("vector", "mmS", nmm)
            nv = prog.op("vector", lambda e: e.tensor_copy(tl["rb"][:], tl["psumD"][:, 0:512]), inc="dvS")
            prog.wait("vector", "dvS", nv)
            nv = prog.op("vector", lambda e, qsl=qsl: e.tensor_mul(tl["oh"][:, qsl], pso[:], tl["rb"][:]),
                         inc="dvS")
            prog.wait("tensor", "dvS", nv)
            prog.wait("scalar", "dvS", nv)
        nv = prog.n("dvS")
        prog.wait("sync", "dvS", nv)
        prog.dma("sync", scr["o1"][128 * h:128 * (h + 1), :], tl["oh"][:], inc="o_st")
        prog.wait("vector", "o_st", prog.n("o_st"))
        prog.barrier(("tensor", "vector", "scalar", "sync", "gpsimd"))


def emit_wo_residual(prog, tl, dram, scr, x_in_ap):
    nc = prog.nc
    ps = [tl["psumA"], tl["psumB"]]
    for g in range(NGRP):
        for i in range(16):
            prog.dma("sync", tl["o_g1"][i][:],
                     bass.AP(scr["o1"].tensor, 128 * i * T + 512 * g, [[T, 128], [1, 512]]),
                     inc="wo1_ld")
        n = prog.n("wo1_ld")
        prog.wait("tensor", "wo1_ld", n)
        ev2 = {}
        for dt in range(16):
            wev = emit_wscale_build(prog, tl, dram["Cwo"], dram["Swo"], tl["wsc"], dt, "wo")
            prog.wait("tensor", "dvS", wev)
            bank = ps[dt % 2]
            if dt >= 2:
                prog.wait("tensor", "dvS", ev2[dt - 2])
            for kt in range(16):
                prog.op("tensor", lambda e, b=bank, kt=kt: e.matmul(
                    b[:], tl["wsc"][:, 128 * kt:128 * (kt + 1)], tl["o_g1"][kt][:],
                    start=(kt == 0), stop=(kt == 15)), inc="mmS" if kt == 15 else None)
            nmm = prog.n("mmS")
            xb = tl["xbuf"][dt % 4]
            nx = prog.dma("sync", xb[:],
                          bass.AP(x_in_ap.tensor, 128 * dt * T + 512 * g, [[T, 128], [1, 512]]),
                          inc="x1_ld")
            prog.wait("vector", "mmS", nmm)
            prog.wait("vector", "x1_ld", nx)
            nv = prog.op("vector", lambda e, b=bank, xb=xb, dt=dt: e.scalar_tensor_tensor(
                xb[:], b[:], tl["bqkv"]["o"][:, dt:dt + 1] if False else tl["bo_s"][:, dt:dt + 1],
                xb[:], ALU.add, ALU.add), inc="dvS")
            ev2[dt] = nv
            prog.wait("sync", "dvS", nv)
            prog.dma("sync", bass.AP(scr["x1a1"].tensor, 128 * dt * T + 512 * g, [[T, 128], [1, 512]]),
                     xb[:], inc="x1a1_st")
        prog.barrier(("tensor", "vector", "scalar", "sync", "gpsimd"))


def l1_dram(nc):
    dram = {}
    dram["x1_in"] = nc.dram_tensor("x1_in", [D, T], F32, kind="ExternalInput").ap()
    for nm in ("q", "k", "v", "o"):
        dram[f"Cw{nm}"] = nc.dram_tensor(f"Cw{nm}", [D, D], F32, kind="ExternalInput").ap()
        dram[f"Sw{nm}"] = nc.dram_tensor(f"Sw{nm}", [64, D], F32, kind="ExternalInput").ap()
    for nm in ("bq", "bk", "bv", "bo"):
        dram[nm] = nc.dram_tensor(nm, [128, 16], F32, kind="ExternalInput").ap()
    dram["Wg1"] = nc.dram_tensor("Wg1", [D, 5632], I32, kind="ExternalInput").ap()
    dram["Wu1"] = nc.dram_tensor("Wu1", [D, 5632], I32, kind="ExternalInput").ap()
    dram["Wd1"] = nc.dram_tensor("Wd1", [5632, D], I32, kind="ExternalInput").ap()
    dram["sg1"] = nc.dram_tensor("sg1", [128, 44], F32, kind="ExternalInput").ap()
    dram["su1"] = nc.dram_tensor("su1", [128, 44], F32, kind="ExternalInput").ap()
    dram["sd1"] = nc.dram_tensor("sd1", [128, 16], F32, kind="ExternalInput").ap()
    for sname, dname, dt, shape in CMAP:
        dram[dname] = nc.dram_tensor(dname, shape, F32, kind="ExternalInput").ap()
    dram["x2_out"] = nc.dram_tensor("x2_out", [D, T], F32, kind="ExternalOutput").ap()
    return dram


def l1_scratch(nc, tag=""):
    scr = {}
    scr["o1"] = nc.dram_tensor(f"o1_scr{tag}", [D, T], BF16, kind="Internal").ap()
    scr["x1a1"] = nc.dram_tensor(f"x1a1_scr{tag}", [D, T], F32, kind="Internal").ap()
    scr["xn1"] = nc.dram_tensor(f"xn1_scr{tag}", [D, T], BF16, kind="Internal").ap()
    scr["gated1"] = nc.dram_tensor(f"gated1_scr{tag}", [5632, T], BF16, kind="Internal").ap()
    scr["wd_bf1"] = nc.dram_tensor(f"wd_bf1_scr{tag}", [5632, D], BF16, kind="Internal").ap()
    return scr


def emit_layer1(prog, tl_c, nc, ctx, dram, scr, x_in_ap, x_out_ap, parts=("attn", "ffn")):
    E = ctx.enter_context
    tl = dict(tl_c)
    tl["bqkv"] = {}
    for nm in ("q", "k", "v"):
        tl["bqkv"][nm] = E(nc.sbuf_tensor(f"b{nm}_s", [128, 16], F32))
        prog.dma("sync", tl["bqkv"][nm][:], dram[f"b{nm}"][:], inc="c_ld")
    tl["bo_s"] = E(nc.sbuf_tensor("bo_s", [128, 16], F32))
    prog.dma("sync", tl["bo_s"][:], dram["bo"][:], inc="c_ld")
    tl["sg1_s"] = E(nc.sbuf_tensor("sg1_s", [128, 44], F32))
    tl["su1_s"] = E(nc.sbuf_tensor("su1_s", [128, 44], F32))
    tl["sd1_s"] = E(nc.sbuf_tensor("sd1_s", [128, 16], F32))
    for nm in ("sg1", "su1", "sd1"):
        prog.dma("sync", tl[nm + "_s"][:], dram[nm][:], inc="c_ld")
    for e in ("tensor", "vector", "scalar"):
        prog.wait(e, "c_ld", prog.n("c_ld"))
    prog.barrier(("tensor", "vector", "scalar", "sync", "gpsimd"))

    with ExitStack() as pha:
        Ea = pha.enter_context
        tl["xn"] = Ea(nc.sbuf_tensor("xn1r", [128, 16 * 2048], BF16))
        tl["rs_b"] = Ea(nc.sbuf_tensor("rs_b", [1, 2048], BF16))
        with ExitStack() as phln:
            El = phln.enter_context
            tl["xb16"] = [El(nc.sbuf_tensor(f"l1xb{i}", [128, 512], BF16)) for i in range(16)]
            tl["sqb"] = [El(nc.sbuf_tensor(f"l1sq{i}", [128, 512], BF16)) for i in range(16)]
            emit_ln(prog, tl, x_in_ap, tl["xn"][:], "ln1")
            prog.barrier()
        emit_rowsum(prog, tl, tl["rs_b"])
        if "attn" in parts and ExitStack:
         with ExitStack() as phb:
            Eb = phb.enter_context
            tl["codebuf"] = Eb(nc.sbuf_tensor("codebuf", [128, 2048], BF16))
            tl["sc_s"] = Eb(nc.sbuf_tensor("sc_s", [4, 2048], F32))
            tl["wsc"] = Eb(nc.sbuf_tensor("wsc", [128, 2048], BF16))
            tl["wsc_ev"] = [0] * 16
            tl["wsc_free"] = [0]
            tl["qh"] = Eb(nc.sbuf_tensor("qh", [128, 2048], BF16))
            tl["kh"] = Eb(nc.sbuf_tensor("kh", [128, 2048], BF16))
            tl["vh"] = Eb(nc.sbuf_tensor("vh", [128, 2048], BF16))
            tl["VT"] = [Eb(nc.sbuf_tensor(f"VTt{i}", [128, 128], BF16)) for i in range(16)]
            tl["vt_ev"] = [0] * 16
            tl["PT"] = [Eb(nc.sbuf_tensor(f"PTt{i}", [128, 512], BF16)) for i in range(16)]
            tl["pt_ev"] = [0] * 16
            tl["sqh"] = Eb(nc.sbuf_tensor("sqh", [128, 2048], BF16))
            tl["mxq"] = Eb(nc.sbuf_tensor("mxq", [128, 4], F32))
            tl["mxk"] = Eb(nc.sbuf_tensor("mxk", [128, 4], F32))
            tl["negB"] = Eb(nc.sbuf_tensor("negB", [128, 1], F32))
            tl["rrow"] = Eb(nc.sbuf_tensor("rrow", [1, 512], F32))
            tl["rrow_b"] = Eb(nc.sbuf_tensor("rrow_b", [1, 512], BF16))
            tl["rb"] = Eb(nc.sbuf_tensor("rb", [128, 512], BF16))
            tl["oh"] = Eb(nc.sbuf_tensor("oh", [128, 2048], BF16))
            emit_attn1(prog, tl, dram, scr, upto=ATTN_UPTO, nheads=ATTN_NHEADS)
        if "attn" in parts:
         with ExitStack() as phc:
            Ec = phc.enter_context
            tl["codebuf"] = Ec(nc.sbuf_tensor("codebuf2", [128, 2048], BF16))
            tl["sc_s"] = Ec(nc.sbuf_tensor("sc_s2", [4, 2048], F32))
            tl["wsc"] = Ec(nc.sbuf_tensor("wsc2", [128, 2048], BF16))
            tl["wsc_ev"] = [0] * 16
            tl["wsc_free"] = [0]
            tl["o_g1"] = [Ec(nc.sbuf_tensor(f"og1_{i}", [128, 512], BF16)) for i in range(16)]
            tl["xbuf"] = [Ec(nc.sbuf_tensor(f"xbufW{i}", [128, 512], F32)) for i in range(4)]
            emit_wo_residual(prog, tl, dram, scr, x_in_ap)
        if "attn" not in parts:
            for dt in range(16):
                for g in range(NGRP):
                    prog.dma("sync",
                             bass.AP(x_out_ap.tensor, 128 * dt * T + 512 * g, [[T, 128], [1, 512]]),
                             bass.AP(x_in_ap.tensor, 128 * dt * T + 512 * g, [[T, 128], [1, 512]]),
                             inc="fout")
            prog.wait("gpsimd", "fout", prog.n("fout"))
            prog.barrier()
            return
        if "ffn" not in parts:
            for dt in range(16):
                for g in range(NGRP):
                    prog.dma("sync",
                             bass.AP(x_out_ap.tensor, 128 * dt * T + 512 * g, [[T, 128], [1, 512]]),
                             bass.AP(scr["x1a1"].tensor, 128 * dt * T + 512 * g, [[T, 128], [1, 512]]),
                             inc="fout")
            prog.wait("gpsimd", "fout", prog.n("fout"))
            prog.barrier()
            return
        tl["qkhat"] = Ea(nc.sbuf_tensor("gatt1", [128, 2048], BF16))
        tl["sqtile"] = Ea(nc.sbuf_tensor("silt1", [128, 2048], BF16))
        tl["upt"] = Ea(nc.sbuf_tensor("upt1", [128, 2048], BF16))
        tl["wbuf2"] = [Ea(nc.sbuf_tensor(f"l1wb2_{i}", [128, 2048], BF16)) for i in range(3)]
        cfg1 = dict(Wg=dram["Wg1"], Wu=dram["Wu1"], Wd=dram["Wd1"],
                    wd_scr=scr["wd_bf1"], gated=scr["gated1"], hid=5632,
                    res_in=scr["x1a1"], out=x_out_ap, lbl="f1",
                    quant=dict(sg=tl["sg1_s"], su=tl["su1_s"], sd=tl["sd1_s"], rs=tl["rs_b"]))
        emit_ffn(prog, tl, cfg1)
    with ExitStack() as phe:
        Ee = phe.enter_context
        tl["gatg"] = [Ee(nc.sbuf_tensor(f"gatg1_{i}", [128, 512], BF16)) for i in range(44)]
        tl["gwb"] = [Ee(nc.sbuf_tensor(f"gwb1_{i}", [128, 5632], BF16)) for i in range(2)]
        tl["gsum_b"] = Ee(nc.sbuf_tensor("gsum_b", [1, 512], BF16))
        tl["xbuf"] = [Ee(nc.sbuf_tensor(f"xbufF1_{i}", [128, 512], F32)) for i in range(16)]
        emit_ffn_f2(prog, tl, cfg1)
    prog.barrier()


def build_layer1(parts=("attn", "ffn")):
    nc = bass.Bass("TRN2", target_bir_lowering=False)
    dram = l1_dram(nc)
    scr = l1_scratch(nc)
    with ExitStack() as ctx:
        E = ctx.enter_context
        tl = {}
        alloc_common(nc, E, tl, dram)
        prog = Prog(nc, ctx)
        register_float_consts(prog, tl)
        for sname, dname, dts, shape in CMAP:
            if dts == "BF16":
                prog.dma("gpsimd", tl["consts"][sname][:], dram[dname][:], inc="c_ldg")
            else:
                prog.dma("sync", tl["consts"][sname][:], dram[dname][:], inc="c_ld")
        for e in ("tensor", "vector", "scalar"):
            prog.wait(e, "c_ld", prog.n("c_ld"))
            prog.wait(e, "c_ldg", prog.n("c_ldg"))
        prog.barrier()
        emit_layer1(prog, tl, nc, ctx, dram, scr, dram["x1_in"], dram["x2_out"], parts)
        prog.finalize(E(nc.Block()))
    return nc


def l1_in_map(h, cs, x1_fm):
    m = {"x1_in": np.ascontiguousarray(x1_fm)}
    for nm in ("q", "k", "v", "o"):
        m[f"Cw{nm}"] = h[f"Cw{nm}"]
        m[f"Sw{nm}"] = h[f"Sw{nm}"]
    for nm in ("bq", "bk", "bv", "bo"):
        m[nm] = h[nm]
    for nm in ("Wg1", "Wu1", "Wd1", "sg1", "su1", "sd1"):
        m[nm] = h[nm]
    for sname, dname, dts, shape in CMAP:
        m[dname] = cs[dname]
    return m


def build_fused():
    nc = bass.Bass("TRN2", target_bir_lowering=False)
    dram = {}
    dram["x_in"] = nc.dram_tensor("x_in", [D, T], F32, kind="ExternalInput").ap()
    specs = [("Wqkvzp", [D, 8192], F32), ("Wb", [D, 32], F32), ("Wa", [D, 32], F32),
             ("convw", [6144, 4], F32), ("A_log", [128, 1], F32), ("dt_bias", [128, 1], F32),
             ("normw", [1, 64], F32), ("Woutp", [D, D], F32),
             ("Wg0", [D, 8192], F32), ("Wu0", [D, 8192], F32), ("Wd0", [8192, D], F32)]
    for nm, shape, dt in specs:
        dram[nm] = nc.dram_tensor(nm, shape, dt, kind="ExternalInput").ap()
    for nm in ("q", "k", "v", "o"):
        dram[f"Cw{nm}"] = nc.dram_tensor(f"Cw{nm}", [D, D], F32, kind="ExternalInput").ap()
        dram[f"Sw{nm}"] = nc.dram_tensor(f"Sw{nm}", [64, D], F32, kind="ExternalInput").ap()
    for nm in ("bq", "bk", "bv", "bo"):
        dram[nm] = nc.dram_tensor(nm, [128, 16], F32, kind="ExternalInput").ap()
    dram["Wg1"] = nc.dram_tensor("Wg1", [D, 5632], I32, kind="ExternalInput").ap()
    dram["Wu1"] = nc.dram_tensor("Wu1", [D, 5632], I32, kind="ExternalInput").ap()
    dram["Wd1"] = nc.dram_tensor("Wd1", [5632, D], I32, kind="ExternalInput").ap()
    dram["sg1"] = nc.dram_tensor("sg1", [128, 44], F32, kind="ExternalInput").ap()
    dram["su1"] = nc.dram_tensor("su1", [128, 44], F32, kind="ExternalInput").ap()
    dram["sd1"] = nc.dram_tensor("sd1", [128, 16], F32, kind="ExternalInput").ap()
    for sname, dname, dt, shape in CMAP:
        dram[dname] = nc.dram_tensor(dname, shape, F32, kind="ExternalInput").ap()
    dram["x2_out"] = nc.dram_tensor("x2_out", [D, T], F32, kind="ExternalOutput").ap()
    x1_scr = nc.dram_tensor("x1_mid", [D, T], F32, kind="Internal").ap()
    scr0 = l0_scratch(nc)
    scr1 = l1_scratch(nc)
    with ExitStack() as ctx:
        E = ctx.enter_context
        tl = {}
        alloc_common(nc, E, tl, dram)
        prog = Prog(nc, ctx)
        register_float_consts(prog, tl)
        emit_const_loads(prog, tl, dram)
        prog.barrier()
        emit_layer0(prog, tl, nc, ctx, dram, scr0, dram["x_in"], x1_scr)
        emit_layer1(prog, tl, nc, ctx, dram, scr1, x1_scr, dram["x2_out"])
        prog.finalize(E(nc.Block()))
    return nc


def kernel_fused(**inputs):
    h = host_prep(inputs)
    cs = host_consts()
    nc = build_fused()
    in_maps = []
    for b in range(8):
        m = l0_in_map(inputs, h, cs, b)
        m2 = l1_in_map(h, cs, np.zeros((1,), np.float32))
        del m2["x1_in"]
        m.update(m2)
        in_maps.append(m)
    res = run_bass_kernel_spmd(nc, in_maps, core_ids=list(range(8)))
    out = np.stack([res.results[b]["x2_out"].T for b in range(8)])
    return np.ascontiguousarray(out)


# version 03
# speedup vs baseline: 1.0888x; 1.0888x over previous
import numpy as np
import ml_dtypes
from contextlib import ExitStack
import concourse.bass as bass
import concourse.mybir as mybir
from concourse.bass_utils import run_bass_kernel_spmd

F32 = mybir.dt.float32
BF16 = mybir.dt.bfloat16
I32 = mybir.dt.int32
AF = mybir.ActivationFunctionType
ALU = mybir.AluOpType
AX = mybir.AxisListType

T = 2048
D = 2048
NH = 32
NPAIR = 16
C = 64
NCHUNK = T // C
GRP = 512
NGRP = T // GRP
CPG = GRP // C
NEG = -1.0e30
ATTN_UPTO = 3
ATTN_NHEADS = 16
EPS_LN = 1e-5
EPS_DN = 1e-6


def host_consts():
    c = {}
    c["ident"] = np.eye(128, dtype=np.float32)
    c["identF"] = np.eye(128, dtype=np.float32)
    ii = np.zeros((128, 64), np.float32)
    ii[0:64] = np.eye(64)
    ii[64:128] = np.eye(64)
    c["eye2"] = ii
    c["ones_col"] = np.ones((128, 1), np.float32)
    c["ones_row"] = np.ones((1, 128), np.float32)
    s4 = np.zeros((4, 128), np.float32)
    for k in range(4):
        s4[k, 32 * k:32 * (k + 1)] = 1.0
    c["sel4_c"] = s4
    c["neg128_row"] = np.full((1, 128), -128.0, np.float32)
    c["ones_row_fc"] = np.ones((1, 128), np.float32)
    c["ones128_c"] = np.ones((128, 128), np.float32)
    bd = np.zeros((128, 2), np.float32)
    bd[0:64, 0] = 1.0
    bd[64:128, 1] = 1.0
    c["bdones_c"] = bd
    ind2 = np.zeros((2, 128), np.float32)
    ind2[0, 0:64] = 1.0
    ind2[1, 64:128] = 1.0
    c["ind2_c"] = ind2
    sel = np.zeros((NPAIR, 32, 128), np.float32)
    for p in range(NPAIR):
        sel[p, 2 * p, 0:64] = 1.0
        sel[p, 2 * p + 1, 64:128] = 1.0
    c["selP"] = np.ascontiguousarray(sel.transpose(1, 0, 2).reshape(32, NPAIR * 128))
    c["selPn"] = np.ascontiguousarray((-sel).transpose(1, 0, 2).reshape(32, NPAIR * 128))
    ohLn = np.zeros((NPAIR, 32, 64), np.float32)
    ohLB = np.zeros((NPAIR, 32, 64), np.float32)
    for p in range(NPAIR):
        ohLn[p, p, :] = -1.0
        ohLB[p, 16 + p, :] = 1.0
    c["ohLn"] = np.ascontiguousarray(ohLn.transpose(1, 0, 2).reshape(32, NPAIR * 64))
    c["ohLB"] = np.ascontiguousarray(ohLB.transpose(1, 0, 2).reshape(32, NPAIR * 64))
    j = np.arange(64)
    mstrict = np.where(j[:, None] < j[None, :], 0.0, NEG).astype(np.float32)
    mincl = np.where(j[:, None] <= j[None, :], 0.0, NEG).astype(np.float32)
    mstrictT = np.where(j[:, None] > j[None, :], 0.0, NEG).astype(np.float32)
    c["mask_E1"] = np.concatenate([mstrict, mstrict], 0)
    c["mask_E2"] = np.concatenate([mincl, mincl], 0)
    c["mask_E1T"] = np.concatenate([mstrictT, mstrictT], 0)
    return c


def host_prep(inputs):
    h = {}
    asc = np.ascontiguousarray
    wq = inputs["L0_attn_w_qkvz"]
    rows = []
    for off in (0, 64, 128, 192):
        for hh in range(NH):
            rows.append(np.arange(hh * 256 + off, hh * 256 + off + 64))
    perm = np.concatenate(rows)
    h["Wqkvzp"] = asc(wq[perm].T)
    wba = inputs["L0_attn_w_ba"]
    h["Wb"] = asc(wba[0::2].T)
    h["Wa"] = asc(wba[1::2].T)
    h["convw"] = asc(inputs["L0_attn_conv_w"][:, 0, :])
    h["A_log"] = asc(np.tile(inputs["L0_attn_A_log"].reshape(1, 32), (4, 1)).reshape(128, 1))
    h["dt_bias"] = asc(np.tile(inputs["L0_attn_dt_bias"].reshape(1, 32), (4, 1)).reshape(128, 1))
    h["normw"] = asc(inputs["L0_attn_norm_weight"].reshape(1, 64))
    h["Woutp"] = asc(inputs["L0_attn_w_out"].T)
    h["Wg0"] = asc(inputs["L0_ffn_w_gate"].T)
    h["Wu0"] = asc(inputs["L0_ffn_w_up"].T)
    h["Wd0"] = asc(inputs["L0_ffn_w_down"].T)
    for nm in ("wq", "wk", "wv", "wo"):
        h[f"C{nm}"] = asc(inputs[f"L1_attn_{nm}_codes"])
        h[f"S{nm}"] = asc(inputs[f"L1_attn_{nm}_scales"])
    h["bq"] = asc(inputs["L1_attn_bq"].reshape(16, 128).T)
    h["bk"] = asc(inputs["L1_attn_bk"].reshape(16, 128).T)
    h["bv"] = asc(inputs["L1_attn_bv"].reshape(16, 128).T)
    h["bo"] = asc(inputs["L1_attn_bo"].reshape(16, 128).T)
    h["Wg1"] = asc(inputs["L1_ffn_wq_gate"].T)
    h["Wu1"] = asc(inputs["L1_ffn_wq_up"].T)
    h["Wd1"] = asc(inputs["L1_ffn_wq_down"].T)
    h["sg1"] = asc(inputs["L1_ffn_s_gate"].reshape(44, 128).T)
    h["su1"] = asc(inputs["L1_ffn_s_up"].reshape(44, 128).T)
    h["sd1"] = asc(inputs["L1_ffn_s_down"].reshape(16, 128).T)
    return h


class Ctr:
    def __init__(self, ctx, nc, name):
        self.h = ctx.enter_context(nc.semaphore(name))
        self.n = 0


class Prog:
    ENGINES = ("sync", "gpsimd", "tensor", "vector", "scalar")

    def __init__(self, nc, ctx):
        self.nc = nc
        self.ctx = ctx
        self.streams = {e: [] for e in self.ENGINES}
        self.sems = {}
        self.swdge = {}
        self._holders = []
        self.sync_runs = {}

    def sem(self, name):
        if name not in self.sems:
            self.sems[name] = Ctr(self.ctx, self.nc, name)
        return self.sems[name]

    def n(self, name):
        return self.sem(name).n

    def _close_sync_runs(self):
        for nm, rec in self.sync_runs.items():
            if rec["open"]:
                s = self.sems[nm]
                v = s.n
                self.streams["sync"].append(lambda e, s=s, v=v: e.wait_ge(s.h, v))
                rec["bounds"].append(v)
                rec["open"] = False

    def milestone(self, eng, main_sem, ms_sem, k=1):
        self.wait(eng, main_sem, self.n(main_sem))
        s = self.sem(ms_sem)
        s.n += k
        self.streams[eng].append(lambda e, s=s, k=k: e.sem_inc(s.h, k))
        return s.n

    def op(self, eng, fn, inc=None, inck=1):
        if eng == "sync":
            self._close_sync_runs()
        if inc is not None:
            if isinstance(inc, str):
                inc = [(inc, inck)]
            hs = []
            ret = None
            for nm, k in inc:
                s = self.sem(nm)
                s.n += k
                hs.append((s.h, k))
                ret = s.n

            def run(e, fn=fn, hs=hs):
                i = fn(e)
                for h, k in hs:
                    i = i.then_inc(h, k)
                return i
            self.streams[eng].append(run)
            return ret
        self.streams[eng].append(lambda e, fn=fn: fn(e))
        return None

    def dma(self, eng, out, in_, inc):
        if eng == "sync":
            s = self.sem(inc)
            s.n += 16
            rec = self.sync_runs.setdefault(inc, {"bounds": [], "open": False})
            rec["open"] = True
            self.streams[eng].append(
                lambda e, o=out, i=in_, s=s: e.dma_start(out=o, in_=i).then_inc(s.h, 16))
            return s.n
        if eng == "gpsimd":
            s = self.sem(inc)
            s.n += 16
            v = s.n

            def run(e, o=out, i=in_, s=s, v=v):
                e.dma_start(out=o, in_=i).then_inc(s.h, 16)
                return e.wait_ge(s.h, v)
            self.streams[eng].append(run)
            return v
        return self.op(eng, lambda e, o=out, i=in_: e.dma_start(out=o, in_=i), inc=inc, inck=16)

    def wait(self, eng, sem, val=None):
        if eng == "sync":
            self._close_sync_runs()
        s = self.sem(sem)
        v = s.n if val is None else val
        if v > 0:
            if sem in self.sync_runs and eng != "sync":
                holder = [v]
                self._holders.append((sem, holder))
                self.streams[eng].append(lambda e, s=s, h=holder: e.wait_ge(s.h, h[0]))
            else:
                self.streams[eng].append(lambda e, s=s, v=v: e.wait_ge(s.h, v))
        return v

    def barrier(self, engines=None):
        engines = list(engines or self.ENGINES)
        if "sync" in engines:
            self._close_sync_runs()
        vals = {}
        for e in engines:
            s = self.sem("ph_" + e)
            s.n += 1
            vals[e] = s.n
            self.streams[e].append(lambda en, s=s: en.drain().then_inc(s.h, 1))
        for e in engines:
            for f in engines:
                if f != e:
                    sf = self.sem("ph_" + f)
                    self.streams[e].append(lambda en, s=sf, v=vals[f]: en.wait_ge(s.h, v))
        return vals

    def finalize(self, block):
        self._close_sync_runs()
        for sem, holder in self._holders:
            bounds = self.sync_runs[sem]["bounds"]
            v = holder[0]
            holder[0] = next((b for b in bounds if b >= v), bounds[-1] if bounds else v)
        for name in self.ENGINES:
            stream = self.streams[name]

            def runner(eng, stream=stream):
                for fn in stream:
                    fn(eng)
            getattr(block, name)(runner)


def register_float_consts(prog, tl, values=(EPS_DN, EPS_LN, 1.0 / 64.0, 0.08838834764831845,
                                             1.0 / 128.0, 1.0 / D)):
    nc = prog.nc
    tile = tl["fconst"]
    for i, v in enumerate(values):
        prog.op("vector", lambda e, i=i, v=v: e.memset(tile[:, i:i + 1], v), inc="dvS")
        nc.const_aps.register(F32, v, tile[:, i:i + 1])
    prog.wait("vector", "dvS", prog.n("dvS"))


def emit_scan(prog, tl):
    nc = prog.nc
    cs = tl["consts"]
    for p in range(NPAIR):
        prog.op("vector", lambda e, p=p: e.memset(tl["state"][p][:], 0.0), inc="dvS")
        n = prog.op("vector", lambda e, p=p: e.memset(tl["state_b"][p][:], 0.0), inc="dvS")
        tl["state_dv"][p] = n
    prog.op("tensor", lambda e: e.matmul(tl["psumD"][:, 0:64], tl["ones_row_b"][:], tl["normw_b"][:],
                                         start=True, stop=True), inc="mmS")
    prog.wait("vector", "mmS", prog.n("mmS"))
    prog.op("vector", lambda e: e.tensor_copy(tl["normwB"][:], tl["psumD"][:, 0:64]), inc="dvS")
    prog.barrier()
    for g in range(NGRP):
        for i in range(48):
            prog.dma("sync", tl["qk_g"][i][:],
                     bass.AP(tl["qkvhat_scr"].tensor, 128 * i * T + 512 * g, [[T, 128], [1, 512]]),
                     inc="grp_ld")
        for i in range(16):
            prog.dma("sync", tl["sigz_g"][i][:],
                     bass.AP(tl["sigz_scr"].tensor, 128 * i * T + 512 * g, [[T, 128], [1, 512]]),
                     inc="grp_ld")
        n = prog.n("grp_ld")
        prog.wait("tensor", "grp_ld", n)
        prog.wait("vector", "grp_ld", n)
        for lc in range(CPG):
            emit_chunk(prog, tl, g * CPG + lc, lc)
        emit_wout_group(prog, tl, g)
        prog.barrier(("tensor", "vector", "scalar", "sync", "gpsimd"))


def emit_chunk(prog, tl, cidx, lc):
    nc = prog.nc
    cs = tl["consts"]
    ts = slice(lc * C, lc * C + C)
    pct = tl["pct"]
    ws = tl["ws"]
    IA = tl["ident_s"][0:64, 0:64]
    IB = tl["ident_s"][64:128, 64:128]
    k_t = tl["qk_g"]
    kof, vof = 16, 32
    psm = tl["psum_sm"]
    pse = tl["psum_e"]
    ppw = [tl["psumA"], tl["psumB"], tl["psumC"], tl["psumD"]]

    def mm(fn, inc=None):
        return prog.op("tensor", fn, inc=inc)

    def vv(fn, inc=None):
        return prog.op("vector", fn, inc=inc)

    def ss(fn, inc=None):
        return prog.op("scalar", fn, inc=inc)

    prog.wait("sync", "mmS", prog.n("mmS"))
    prog.wait("sync", "dvS", prog.n("dvS"))

    def pct_src(name):
        return bass.AP(tl["stats_scr"][name].tensor, cidx * C, [[2 * T, 16], [T, 2], [1, 64]])

    def pc_src(name, rep):
        if rep:
            return bass.AP(tl["stats_scr"][name].tensor, cidx * C + C - 1,
                           [[T, 2], [0, 64], [2 * T, 16], [1, 1]])
        return bass.AP(tl["stats_scr"][name].tensor, cidx * C, [[T, 2], [1, 64], [2 * T, 16], [1, 1]])

    prog.dma("sync", pct[0:16, :], pct_src("lam"), inc="pc_ld")
    prog.dma("sync", pct[16:32, :], pct_src("lamb"), inc="pc_ld")
    pcT = tl["pcT_in"]
    for si, (nm, rep) in enumerate((("negbg", False), ("beta", False), ("gam", False),
                                    ("w", False), ("gamC", True))):
        srcnm = "gam" if nm == "gamC" else nm
        if rep:
            ap = bass.AP(tl["stats_scr"][srcnm].tensor, cidx * C + C - 1,
                         [[2 * T, 16], [T, 2], [0, 64], [1, 1]])
        else:
            ap = bass.AP(tl["stats_scr"][srcnm].tensor, cidx * C, [[2 * T, 16], [T, 2], [1, 64]])
        if rep:
            prog.dma("sync", pcT[16 * si:16 * (si + 1), :].rearrange("p (a b) -> p a b", b=1), ap,
                     inc="pc_ld")
        else:
            prog.dma("sync", pcT[16 * si:16 * (si + 1), :], ap, inc="pc_ld")
    npc = prog.n("pc_ld")
    prog.wait("tensor", "pc_ld", npc)
    prog.wait("tensor", "dvS", prog.n("dvS"))
    prog.wait("tensor", "st4ev", prog.n("st4ev"))
    nmm0 = prog.op("tensor", lambda e: e.matmul(tl["psumA"][:, 384:464], pcT[:],
                                                cs["identF_s"][0:80, 0:80], start=True, stop=True),
                   inc="mmS")
    prog.wait("vector", "mmS", nmm0)
    npcv = prog.op("vector", lambda e: e.tensor_copy(tl["pcall"][:], tl["psumA"][:, 384:464]),
                   inc="dvS")
    prog.wait("tensor", "dvS", npcv)
    prog.wait("vector", "pc_ld", npc)
    pcall = tl["pcall"]
    pcs = {nm: pcall[:, 16 * si:16 * (si + 1)]
           for si, nm in enumerate(("negbg", "beta", "gam", "w", "gamC"))}

    st1_dv = [0] * NPAIR
    st1_ev = [0] * NPAIR
    lamblk = tl["lam_w"][:, 64 * cidx:64 * cidx + 64]
    lambblk = tl["lamb_w"][:, 64 * cidx:64 * cidx + 64]

    for p in range(NPAIR):
        kt, qt, vt = k_t[kof + p], k_t[p], k_t[vof + p]
        KT_A, KT_B = kt[0:64, ts], kt[64:128, ts]
        QT_A, QT_B = qt[0:64, ts], qt[64:128, ts]
        w = ws[p]
        bank1 = psm if p % 2 == 0 else pse
        pg = bank1[:, 0:128]
        pv = bank1[:, 320:448]
        pe = bank1[:, 128:320]
        if p >= 2:
            prog.wait("tensor", "st1ev", st1_ev[p - 2])
        mm(lambda e, pg=pg, A=KT_A: e.matmul(pg[0:64, 0:64], A, A, start=True, stop=True))
        mm(lambda e, pg=pg, A=KT_A, Q=QT_A: e.matmul(pg[0:64, 64:128], A, Q, start=True, stop=True))
        mm(lambda e, pg=pg, B=KT_B: e.matmul(pg[64:128, 0:64], B, B, start=True, stop=True))
        mm(lambda e, pg=pg, B=KT_B, Q=QT_B: e.matmul(pg[64:128, 64:128], B, Q, start=True, stop=True))
        sel = cs["selP_s"][:, 128 * p:128 * (p + 1)]
        seln = cs["selPn_s"][:, 128 * p:128 * (p + 1)]
        ohn = cs["ohLn_s"][:, 64 * p:64 * (p + 1)]
        ohb = cs["ohLB_s"][:, 64 * p:64 * (p + 1)]
        mm(lambda e, pe=pe, s=sel, b=lambblk: e.matmul(pe[:, 0:64], s, b, start=True, stop=False))
        mm(lambda e, pe=pe, o=ohn: e.matmul(pe[:, 0:64], pct[:], o, start=False, stop=True))
        mm(lambda e, pe=pe, s=sel, b=lamblk: e.matmul(pe[:, 64:128], s, b, start=True, stop=False))
        mm(lambda e, pe=pe, o=ohn: e.matmul(pe[:, 64:128], pct[:], o, start=False, stop=True))
        mm(lambda e, pe=pe, s=seln, b=lamblk: e.matmul(pe[:, 128:192], s, b, start=True, stop=False))
        mm(lambda e, pe=pe, o=ohb: e.matmul(pe[:, 128:192], pct[:], o, start=False, stop=True))
        mm(lambda e, pv=pv, vt=vt: e.matmul(pv[0:64, 0:64], vt[0:64, ts], IA, start=True, stop=True))
        mm(lambda e, pv=pv, vt=vt: e.matmul(pv[64:128, 0:64], vt[64:128, ts], IB, start=True, stop=True))
        mm(lambda e, pv=pv, kt=kt: e.matmul(pv[0:64, 64:128], kt[0:64, ts], IA, start=True, stop=True))
        nmm = mm(lambda e, pv=pv, kt=kt: e.matmul(pv[64:128, 64:128], kt[64:128, ts], IB,
                                                  start=True, stop=True), inc="mmS")
        prog.wait("vector", "mmS", nmm)
        vv(lambda e, w=w, pe=pe: e.tensor_add(w["e1f"][:], pe[:, 0:64], cs["mask_E1_s"][:]))
        vv(lambda e, w=w, pe=pe: e.tensor_add(w["e2f"][:], pe[:, 64:128], cs["mask_E2_s"][:]))
        nv = vv(lambda e, w=w, pe=pe: e.tensor_add(w["e1tf"][:], pe[:, 128:192], cs["mask_E1T_s"][:]),
                inc="dvS")
        prog.wait("scalar", "dvS", nv)
        ss(lambda e, w=w: e.activation(w["E1"][:], w["e1f"][:], AF.Exp))
        ss(lambda e, w=w: e.activation(w["E2"][:], w["e2f"][:], AF.Exp))
        na = ss(lambda e, w=w: e.activation(w["E1T"][:], w["e1tf"][:], AF.Exp), inc="acS")
        prog.wait("vector", "acS", na)
        vv(lambda e, w=w, pg=pg: e.tensor_mul(w["NT"][:], pg[:, 0:64], w["E1"][:]))
        vv(lambda e, w=w, pg=pg: e.tensor_mul(w["NTT"][:], pg[:, 0:64], w["E1T"][:]))
        vv(lambda e, w=w, pg=pg: e.tensor_mul(w["G"][:], pg[:, 64:128], w["E2"][:]))
        pbeta = pcs["beta"][:, p:p + 1]
        pw_ = pcs["w"][:, p:p + 1]
        vv(lambda e, w=w, pv=pv, s=pbeta: e.tensor_scalar_mul(w["bV"][:], pv[:, 0:64], s))
        nw = vv(lambda e, w=w, pv=pv, s=pw_: e.tensor_scalar_mul(w["W"][:], pv[:, 64:128], s),
                inc="dvS")
        prog.wait("vector", "dvS", nw)
        st1_dv[p] = prog.op("vector",
                            lambda e, w=w: e.tensor_sub(w["PPT"][:], cs["eye2_s"][:], w["NT"][:]),
                            inc="dvS")
        st1_ev[p] = prog.milestone("vector", "dvS", "st1ev")

    seq = [("P2", "NTT", "NT"), ("P2T", "NT", "NTT"), ("P4", "P2T", "P2"), ("P4T", "P2", "P2T"),
           ("P8", "P4T", "P4"), ("P8T", "P4", "P4T"), ("P16", "P8T", "P8"), ("P16T", "P8", "P8T"),
           ("P32T", "P16", "P16T")]
    sq_ev = {}
    bank_free = {}
    for si, (dst, lh, rh) in enumerate(seq):
        for p in range(NPAIR):
            w = ws[p]
            bi = p % 4
            bank = ppw[bi][:, 0:64]
            if bi in bank_free:
                prog.wait("tensor", "sq_ev", bank_free[bi])
            if si <= 1:
                prog.wait("tensor", "dvS", st1_dv[p])
            else:
                prog.wait("tensor", "sq_ev", sq_ev[(si - 1, p)])
            mm(lambda e, b=bank, w=w, lh=lh, rh=rh: e.matmul(b[0:64, :], w[lh][0:64, :], w[rh][0:64, :],
                                                             start=True, stop=True))
            nmm = mm(lambda e, b=bank, w=w, lh=lh, rh=rh: e.matmul(b[64:128, :], w[lh][64:128, :],
                                                                   w[rh][64:128, :], start=True, stop=True),
                     inc="mmS")
            prog.wait("vector", "mmS", nmm)
            n = vv(lambda e, w=w, b=bank, dst=dst: e.tensor_copy(w[dst][:], b[:]), inc="sq_ev")
            sq_ev[(si, p)] = n
            bank_free[bi] = n

    pr_ev = dict((p, 0) for p in range(NPAIR))
    bank_free3 = {}
    for fi, fk in enumerate(("P2T", "P4T", "P8T", "P16T", "P32T")):
        for p in range(NPAIR):
            w = ws[p]
            bi = p % 4
            bank = ppw[bi][:, 64:128]
            if bi in bank_free3:
                prog.wait("tensor", "pr_ev", bank_free3[bi])
            prog.wait("tensor", "sq_ev", sq_ev[({0: 1, 1: 3, 2: 5, 3: 7, 4: 8}[fi], p)])
            if fi == 0:
                prog.wait("tensor", "dvS", st1_dv[p])
            else:
                prog.wait("tensor", "pr_ev", pr_ev[p])
            mm(lambda e, b=bank, w=w, fk=fk: e.matmul(b[0:64, :], w[fk][0:64, :], w["PPT"][0:64, :],
                                                      start=True, stop=True))
            nmm = mm(lambda e, b=bank, w=w, fk=fk: e.matmul(b[64:128, :], w[fk][64:128, :],
                                                            w["PPT"][64:128, :], start=True, stop=True),
                     inc="mmS")
            prog.wait("vector", "mmS", nmm)
            if fi == 0:
                prog.wait("vector", "dvS", st1_dv[p])
            else:
                prog.wait("vector", "pr_ev", pr_ev[p])
            n = vv(lambda e, w=w, b=bank: e.tensor_add(w["PPT"][:], b[:], w["PPT"][:]), inc="pr_ev")
            pr_ev[p] = n
            bank_free3[bi] = n

    st4_free = {}
    for w4 in range(4):
        pw = list(range(4 * w4, 4 * w4 + 4))
        pc_n = {}
        dv_n = {}
        mm_n = {}
        for p in pw:
            kt, qt = k_t[kof + p], k_t[p]
            KT_A, KT_B = kt[0:64, ts], kt[64:128, ts]
            QT_A, QT_B = qt[0:64, ts], qt[64:128, ts]
            Sb = tl["state_b"][p]
            bank = ppw[p % 4]
            pcn = bank[:, 128:192]
            pcq = bank[:, 192:256]
            if (p % 4) in st4_free:
                prog.wait("tensor", "st4ev", st4_free[p % 4])
            prog.wait("tensor", "dvS", tl["state_dv"][p])
            mm(lambda e, A=KT_A, S=Sb, pcn=pcn: e.matmul(pcn[0:64, :], A, S[0:64, :], start=True, stop=True))
            mm(lambda e, B=KT_B, S=Sb, pcn=pcn: e.matmul(pcn[64:128, :], B, S[64:128, :], start=True, stop=True))
            mm(lambda e, A=QT_A, S=Sb, pcq=pcq: e.matmul(pcq[0:64, :], A, S[0:64, :], start=True, stop=True))
            mm_n[p] = mm(lambda e, B=QT_B, S=Sb, pcq=pcq: e.matmul(pcq[64:128, :], B, S[64:128, :],
                                                                   start=True, stop=True), inc="mmS")
        for p in pw:
            w = ws[p]
            bank = ppw[p % 4]
            pcn = bank[:, 128:192]
            pcq = bank[:, 192:256]
            pnegbg = pcs["negbg"][:, p:p + 1]
            pgam = pcs["gam"][:, p:p + 1]
            prog.wait("vector", "mmS", mm_n[p])
            prog.wait("vector", "dvS", st1_dv[p])
            vv(lambda e, w=w, pcn=pcn, s=pnegbg: e.scalar_tensor_tensor(w["c"][:], pcn[:], s,
                                                                        w["bV"][:], ALU.mult, ALU.add))
            dv_n[p] = vv(lambda e, w=w, pcq=pcq, s=pgam: e.tensor_scalar_mul(w["CSQ"][:], pcq[:], s),
                         inc="dvS")
        for p in pw:
            w = ws[p]
            bank = ppw[p % 4]
            pdl = bank[:, 256:320]
            prog.wait("tensor", "dvS", dv_n[p])
            prog.wait("tensor", "pr_ev", pr_ev[p])
            mm(lambda e, w=w, pdl=pdl: e.matmul(pdl[0:64, :], w["PPT"][0:64, :], w["c"][0:64, :],
                                                start=True, stop=True))
            mm_n[p] = mm(lambda e, w=w, pdl=pdl: e.matmul(pdl[64:128, :], w["PPT"][64:128, :],
                                                          w["c"][64:128, :], start=True, stop=True),
                         inc="mmS")
        for p in pw:
            w = ws[p]
            bank = ppw[p % 4]
            pdl = bank[:, 256:320]
            prog.wait("vector", "mmS", mm_n[p])
            dv_n[p] = vv(lambda e, w=w, pdl=pdl: e.tensor_copy(w["delta"][:], pdl[:]), inc="dvS")
        for p in pw:
            w = ws[p]
            bank = ppw[p % 4]
            pom = bank[:, 320:384]
            psu = bank[:, 384:448]
            prog.wait("tensor", "dvS", dv_n[p])
            mm(lambda e, w=w, pom=pom: e.matmul(pom[0:64, :], w["G"][0:64, :], w["delta"][0:64, :],
                                                start=True, stop=True))
            mm(lambda e, w=w, pom=pom: e.matmul(pom[64:128, :], w["G"][64:128, :], w["delta"][64:128, :],
                                                start=True, stop=True))
            mm(lambda e, w=w, psu=psu: e.matmul(psu[0:64, :], w["W"][0:64, :], w["delta"][0:64, :],
                                                start=True, stop=True))
            mm_n[p] = mm(lambda e, w=w, psu=psu: e.matmul(psu[64:128, :], w["W"][64:128, :],
                                                          w["delta"][64:128, :], start=True, stop=True),
                         inc="mmS")
        for p in pw:
            w = ws[p]
            S = tl["state"][p]
            Sb = tl["state_b"][p]
            bank = ppw[p % 4]
            pom = bank[:, 320:384]
            psu = bank[:, 384:448]
            pgamC = pcs["gamC"][:, p:p + 1]
            prog.wait("vector", "mmS", mm_n[p])
            vv(lambda e, w=w, pom=pom: e.tensor_add(w["orow"][:], pom[:], w["CSQ"][:]))
            nst = vv(lambda e, S=S, psu=psu, s=pgamC: e.scalar_tensor_tensor(S[:], S[:], s,
                                                                             psu[:], ALU.mult, ALU.add),
                     inc="dvS")
            prog.wait("vector", "dvS", nst)
            ns = vv(lambda e, S=S, Sb=Sb: e.tensor_copy(Sb[:], S[:]), inc="dvS")
            tl["state_dv"][p] = ns
            dv_n[p] = ns
        for p in pw:
            w = ws[p]
            prog.wait("scalar", "dvS", dv_n[p])
            pc_n[p] = ss(lambda e, w=w: e.activation(w["osq"][:], w["orow"][:], AF.Square), inc="acS")
        for p in pw:
            w = ws[p]
            prog.wait("vector", "acS", pc_n[p])
            dv_n[p] = vv(lambda e, w=w: e.reduce_sum(w["var"][:], w["osq"][:], axis=AX.X), inc="dvS")
        for p in pw:
            w = ws[p]
            prog.wait("scalar", "dvS", dv_n[p])
            pc_n[p] = ss(lambda e, w=w: e.activation(w["rstd"][:], w["var"][:], AF.Sqrt,
                                                     scale=1.0 / 64.0, bias=EPS_DN), inc="acS")
        for p in pw:
            w = ws[p]
            prog.wait("vector", "acS", pc_n[p])
            nv = vv(lambda e, w=w: e.reciprocal(w["rinv"][:], w["rstd"][:]), inc="dvS")
            prog.wait("vector", "dvS", nv)
            dv_n[p] = vv(lambda e, w=w: e.scalar_tensor_tensor(w["on"][:], w["orow"][:], w["rinv"][:, 0:1],
                                                               tl["normwB"][:], ALU.mult, ALU.mult),
                         inc="dvS")
        for p in pw:
            w = ws[p]
            bank = ppw[p % 4]
            pof = bank[:, 448:512]
            prog.wait("tensor", "dvS", dv_n[p])
            mm(lambda e, w=w, pof=pof: e.matmul(pof[0:64, :], w["on"][0:64, :], IA, start=True, stop=True))
            mm_n[p] = mm(lambda e, w=w, pof=pof: e.matmul(pof[64:128, :], w["on"][64:128, :], IB,
                                                          start=True, stop=True), inc="mmS")
        for p in pw:
            bank = ppw[p % 4]
            pof = bank[:, 448:512]
            prog.wait("vector", "mmS", mm_n[p])
            n4 = vv(lambda e, p=p, pof=pof: e.tensor_mul(tl["o_g"][p][:, ts], pof[:],
                                                         tl["sigz_g"][p][:, ts]), inc="st4ev")
            st4_free[p % 4] = n4


def emit_wout_group(prog, tl, g):
    nc = prog.nc
    ps = [tl["psumA"], tl["psumB"]]
    xb = tl["xbuf"]
    og = tl["o_g"]
    x_in = tl["x_in"]
    x1a = tl["x1a_scr"]
    wop = tl["Woutp"]
    xl = {}
    for dt in range(4):
        xl[dt] = prog.dma("sync", xb[dt][:],
                          bass.AP(x_in.tensor, 128 * dt * T + 512 * g, [[T, 128], [1, 512]]),
                          inc="x_ld")
    prog.wait("tensor", "st4ev", prog.n("st4ev"))
    wo_ev = {}
    for dt in range(16):
        buf = tl["wbuf"][dt % 3]
        if dt >= 3:
            prog.wait("gpsimd", "wofree", prog.n("wofree") - 2)
        prog.dma("gpsimd", buf[:].rearrange("p (a b) -> p a b", b=128),
                 bass.AP(wop.tensor, dt * 128, [[D, 128], [128 * D, 16], [1, 128]]),
                 inc="wodma")
        prog.wait("tensor", "wodma", prog.n("wodma"))
        bank = ps[dt % 2]
        if dt >= 2:
            prog.wait("tensor", "dvS", wo_ev[dt - 2])
        for kt in range(16):
            prog.op("tensor", lambda e, b=bank, buf=buf, kt=kt, og=og:
                    e.matmul(b[:], buf[:, 128 * kt:128 * (kt + 1)], og[kt][:],
                             start=(kt == 0), stop=(kt == 15)),
                    inc="mmS" if kt == 15 else None)
        nmm = prog.n("mmS")
        prog.milestone("tensor", "mmS", "wofree")
        prog.wait("vector", "mmS", nmm)
        prog.wait("vector", "x_ld", xl[dt])
        nv = prog.op("vector", lambda e, b=bank, t_=xb[dt % 4]: e.tensor_add(
            t_[:], b[:], t_[:]), inc="dvS")
        wo_ev[dt] = nv
        prog.wait("sync", "dvS", nv)
        prog.dma("sync", bass.AP(x1a.tensor, 128 * dt * T + 512 * g, [[T, 128], [1, 512]]),
                 xb[dt % 4][:], inc="x1a_st")
        if dt + 4 < 16:
            prog.wait("sync", "x1a_st", prog.n("x1a_st"))
            xl[dt + 4] = prog.dma("sync", xb[dt % 4][:],
                                  bass.AP(x_in.tensor, 128 * (dt + 4) * T + 512 * g,
                                          [[T, 128], [1, 512]]), inc="x_ld")


def emit_ln(prog, tl, src_ap, xn_dst, label):
    nc = prog.nc
    pn = tl["psumC"]
    pb = tl["psumD"]
    lr = tl["lnrow"]
    xb16 = tl["xb16"]
    sqb = tl["sqb"]
    onescb = tl["ones_col_b"]
    onesrb = tl["ones_row_b"]
    ev_g = {}
    for g in range(NGRP):
        if g >= 1:
            prog.wait("gpsimd", "dvS", ev_g[g - 1])
        for kt in range(16):
            prog.dma("gpsimd", xb16[kt][:],
                     bass.AP(src_ap.tensor, 128 * kt * T + 512 * g, [[T, 128], [1, 512]]),
                     inc=label + "ld")
        nld = prog.n(label + "ld")
        prog.wait("scalar", label + "ld", nld)
        prog.wait("tensor", label + "ld", nld)
        if g >= 1:
            prog.wait("tensor", "dvS", ev_g[g - 1])
        for kt in range(16):
            prog.op("scalar", lambda e, kt=kt: e.activation(sqb[kt][:], xb16[kt][:], AF.Square),
                    inc="acS" if kt == 15 else None)
        na = prog.n("acS")
        for kt in range(16):
            prog.op("tensor", lambda e, kt=kt: e.matmul(pn[0:1, 0:512], onescb[:],
                                                        xb16[kt][:], start=(kt == 0), stop=(kt == 15)))
        prog.wait("tensor", "acS", na)
        for kt in range(16):
            prog.op("tensor", lambda e, kt=kt: e.matmul(pb[0:1, 0:512], onescb[:],
                                                        sqb[kt][:], start=(kt == 0), stop=(kt == 15)),
                    inc="mmS" if kt == 15 else None)
        nmm = prog.n("mmS")
        prog.wait("vector", "mmS", nmm)

        def vv(fn, inc="dvS"):
            return prog.op("vector", fn, inc=inc)

        def ss(fn, inc="acS"):
            return prog.op("scalar", fn, inc=inc)

        vv(lambda e: e.tensor_scalar_mul(lr[0][:], pn[0:1, 0:512], 1.0 / D))
        nv = vv(lambda e: e.tensor_scalar_mul(lr[1][:], pb[0:1, 0:512], 1.0 / D))
        prog.wait("vector", "dvS", nv)
        nv = vv(lambda e: e.tensor_mul(lr[2][:], lr[0][:], lr[0][:]))
        prog.wait("vector", "dvS", nv)
        nv = vv(lambda e: e.tensor_sub(lr[1][:], lr[1][:], lr[2][:]))
        prog.wait("scalar", "dvS", nv)
        na = ss(lambda e: e.activation(lr[2][:], lr[1][:], AF.Sqrt, bias=EPS_LN))
        prog.wait("vector", "acS", na)
        nv = vv(lambda e: e.reciprocal(lr[1][:], lr[2][:]))
        prog.wait("vector", "dvS", nv)
        nv = vv(lambda e: e.tensor_mul(lr[2][:], lr[0][:], lr[1][:]))
        prog.wait("vector", "dvS", nv)
        nv = vv(lambda e: e.tensor_copy(lr[3][:], lr[1][:]))
        prog.wait("vector", "dvS", nv)
        nv = vv(lambda e: e.tensor_copy(lr[4][:], lr[2][:]))
        prog.wait("tensor", "dvS", nv)
        prog.op("tensor", lambda e: e.matmul(pn[:, 0:512], onesrb[:], lr[3][:],
                                             start=True, stop=True))
        nmm = prog.op("tensor", lambda e: e.matmul(pb[:, 0:512], onesrb[:], lr[4][:],
                                                   start=True, stop=True), inc="mmS")
        prog.wait("vector", "mmS", nmm)
        for kt in range(16):
            vv(lambda e, kt=kt: e.tensor_mul(sqb[kt][:], xb16[kt][:], pn[:, 0:512]),
               inc=None)
        nv = vv(lambda e: e.tensor_copy(lr[0][:], lr[0][:]))
        prog.wait("vector", "dvS", nv)
        last = None
        for kt in range(16):
            last = vv(lambda e, kt=kt, g=g: e.tensor_sub(
                xn_dst[:, 2048 * kt + 512 * g:2048 * kt + 512 * (g + 1)],
                sqb[kt][:], pb[:, 0:512]), inc="dvS")
        ev_g[g] = last
    prog.barrier(("tensor", "vector", "scalar", "gpsimd"))


def emit_stats(prog, tl):
    nc = prog.nc
    xn = tl["xn"]
    ps = [tl["psumA"], tl["psumB"]]
    st = tl["st"]
    stg = tl["st_stage"]
    ev = {}
    for wi, (which, wt) in enumerate((("b", tl["wb"]), ("a", tl["wa"]))):
        for g in range(NGRP):
            i = wi * NGRP + g
            bank = ps[i % 2]
            if i >= 2:
                prog.wait("tensor", "sb_mv", 16 * (i - 1))
            for kt in range(16):
                prog.op("tensor", lambda e, b=bank, w=wt, kt=kt, g=g:
                        e.matmul(b[0:32, 0:512], w[:, 32 * kt:32 * (kt + 1)],
                                 xn[:, 2048 * kt + 512 * g: 2048 * kt + 512 * (g + 1)],
                                 start=(kt == 0), stop=(kt == 15)),
                        inc="mmS" if kt == 15 else None)
            n = prog.n("mmS")
            prog.wait("vector", "mmS", n)
            prog.wait("vector", "sb_mv", 16 * i)
            nv = prog.op("vector", lambda e, b=bank: e.tensor_copy(stg[:], b[0:32, 0:512]), inc="dvS")
            prog.wait("sync", "dvS", nv)
            dst = st["braw"] if which == "b" else st["araw"]
            prog.dma("sync", dst[32 * g:32 * (g + 1), :], stg[:], inc="sb_mv")
    prog.wait("scalar", "sb_mv", prog.n("sb_mv"))
    prog.wait("vector", "sb_mv", prog.n("sb_mv"))

    def vv(fn, inc="dvS"):
        return prog.op("vector", fn, inc=inc)

    def ss(fn, inc="acS"):
        return prog.op("scalar", fn, inc=inc)

    braw, araw = st["braw"], st["araw"]
    dtb = tl["dt_bias"][:, 0:1]
    na = ss(lambda e: e.activation(st["t0"][:], araw[:], AF.Abs, bias=dtb))
    prog.wait("scalar", "acS", na)
    na = ss(lambda e: e.activation(st["t1"][:], st["t0"][:], AF.Exp, scale=-1.0))
    prog.wait("scalar", "acS", na)
    na = ss(lambda e: e.activation(st["t1"][:], st["t1"][:], AF.Ln, bias=1.0))
    prog.wait("scalar", "acS", na)
    na = ss(lambda e: e.activation(st["t2"][:], araw[:], AF.Relu, bias=dtb))
    prog.wait("vector", "acS", na)
    nv = vv(lambda e: e.tensor_add(st["sp"][:], st["t1"][:], st["t2"][:]))
    prog.wait("scalar", "dvS", nv)
    na = ss(lambda e: e.activation(tl["negexpA"][:], tl["A_log"][:], AF.Exp))
    prog.wait("vector", "acS", na)
    prog.wait("vector", "dvS", nv)
    nea = tl["negexpA"][:, 0:1]
    nv = vv(lambda e, s=nea: e.tensor_scalar(st["lam"][:], st["sp"][:], s, -1.0, ALU.mult, ALU.mult))
    l3 = st["lam"][:].rearrange("p (c k) -> p c k", k=64)
    for sh in (1, 2, 4, 8, 16, 32):
        prog.wait("vector", "dvS", nv)
        nv = vv(lambda e, sh=sh, l3=l3: e.tensor_add(l3[:, :, sh:64], l3[:, :, sh:64], l3[:, :, 0:64 - sh]))
    ss(lambda e: e.activation(st["beta"][:], braw[:], AF.Sigmoid))
    na = ss(lambda e: e.activation(st["t0"][:], braw[:], AF.Abs))
    prog.wait("scalar", "acS", na)
    na = ss(lambda e: e.activation(st["t1"][:], st["t0"][:], AF.Exp, scale=-1.0))
    prog.wait("scalar", "acS", na)
    na = ss(lambda e: e.activation(st["t1"][:], st["t1"][:], AF.Ln, bias=1.0))
    prog.wait("scalar", "acS", na)
    prog.wait("scalar", "dvS", prog.n("dvS"))
    na = ss(lambda e: e.activation(st["t2"][:], braw[:], AF.Relu, scale=-1.0))
    prog.wait("vector", "acS", na)
    nv2 = vv(lambda e: e.tensor_add(st["t2"][:], st["t1"][:], st["t2"][:]))
    prog.wait("vector", "dvS", nv2)
    nv2 = vv(lambda e: e.tensor_sub(st["lamb"][:], st["lam"][:], st["t2"][:]))
    prog.wait("scalar", "dvS", prog.n("dvS"))
    na = ss(lambda e: e.activation(st["gam"][:], st["lam"][:], AF.Exp))
    prog.wait("vector", "acS", na)
    nv3 = vv(lambda e: e.tensor_mul(st["t0"][:], st["beta"][:], st["gam"][:]))
    prog.wait("vector", "dvS", nv3)
    nv3 = vv(lambda e: e.tensor_scalar_mul(st["negbg"][:], st["t0"][:], -1.0))
    _lamh = st["lam"] if not hasattr(st["lam"], "tensor") else st["lam"].tensor
    lamCsrc = bass.AP(_lamh, 63, [[512, 128], [64, 8], [0, 64]])
    nv4 = vv(lambda e, s=lamCsrc, l3=l3: e.tensor_tensor(
        st["t1"][:].rearrange("p (c k) -> p c k", k=64), s, l3[:, :, :], ALU.subtract))
    prog.wait("scalar", "dvS", nv4)
    na = ss(lambda e: e.activation(st["w"][:], st["t1"][:], AF.Exp))
    prog.wait("sync", "acS", na)
    prog.wait("sync", "dvS", prog.n("dvS"))
    for nm in ("lam", "lamb", "beta", "negbg", "gam", "w"):
        for tb in range(4):
            prog.dma("sync", bass.AP(tl["stats_scr"][nm].tensor, tb * 512, [[T, 32], [1, 512]]),
                     st[nm][32 * tb:32 * (tb + 1), :], inc="st_scr")
    prog.wait("sync", "st_scr", prog.n("st_scr"))
    prog.dma("sync", tl["lam_w"][:], tl["stats_scr"]["lam"][:], inc="st_scr")
    prog.dma("sync", tl["lamb_w"][:], tl["stats_scr"]["lamb"][:], inc="st_scr")
    prog.wait("tensor", "st_scr", prog.n("st_scr"))
    prog.barrier(("tensor", "vector", "scalar", "sync"))


def emit_qkvz(prog, tl):
    nc = prog.nc
    xn = tl["xn"]
    wbuf = tl["wbuf"]
    ps = [tl["psumA"], tl["psumB"]]
    pn = tl["psumC"]
    pb = tl["psumD"]
    raw = tl["qkraw"]
    hat = tl["qkhat"]
    sq = tl["sqtile"]
    prog.op("vector", lambda e: e.memset(raw[:, 0:3], 0.0), inc="dvS")
    prog.wait("tensor", "dvS", prog.n("dvS"))
    evac_g = {}
    for dt in range(64):
        sec = dt // 16
        buf = wbuf[dt % 3]
        if dt >= 3:
            prog.wait("gpsimd", "wfree", dt - 2)
        prog.dma("gpsimd", buf[:].rearrange("p (a b) -> p a b", b=128),
                 bass.AP(tl["Wqkvzp"].tensor, dt * 128, [[8192, 128], [128 * 8192, 16], [1, 128]]),
                 inc="wqdma")
        prog.wait("tensor", "wqdma", prog.n("wqdma"))
        nhs = prog.n("hat_st")
        prog.wait("vector", "hat_st", nhs)
        prog.wait("scalar", "hat_st", nhs)
        for g in range(NGRP):
            i = dt * NGRP + g
            bank = ps[i % 2]
            if i >= 2:
                prog.wait("tensor", "dvS", evac_g[i - 2])
            for kt in range(16):
                prog.op("tensor", lambda e, b=bank, buf=buf, kt=kt, g=g:
                        e.matmul(b[:], buf[:, 128 * kt:128 * (kt + 1)],
                                 xn[:, 2048 * kt + 512 * g: 2048 * kt + 512 * (g + 1)],
                                 start=(kt == 0), stop=(kt == 15)),
                        inc="mmS" if kt == 15 else None)
            n = prog.n("mmS")
            if g == NGRP - 1:
                prog.milestone("tensor", "mmS", "wfree")
            if sec == 3:
                prog.wait("scalar", "mmS", n)
                sN = prog.op("scalar", lambda e, b=bank, g=g:
                             e.activation(hat[:, 512 * g:512 * (g + 1)], b[:], AF.Sigmoid), inc="acS")
                evac_g[i] = prog.n("dvS")
                prog.wait("tensor", "acS", sN)
                prog.wait("sync", "acS", sN)
                prog.dma("sync", tl["sigz_scr"][128 * (dt - 48):128 * (dt - 47), 512 * g:512 * (g + 1)],
                         hat[:, 512 * g:512 * (g + 1)], inc="hat_st")
            else:
                prog.wait("vector", "mmS", n)
                evac_g[i] = prog.op("vector", lambda e, b=bank, g=g:
                                    e.tensor_copy(raw[:, 3 + 512 * g:3 + 512 * (g + 1)], b[:]), inc="dvS")
        if sec == 3:
            prog.barrier(("tensor", "vector", "scalar", "sync"))
            continue
        cwc = tl["convw_s"][:, 4 * dt:4 * dt + 4]
        nv = prog.n("dvS")
        prog.wait("vector", "dvS", nv)
        nv = prog.op("vector", lambda e, c=cwc: e.tensor_scalar_mul(hat[:], raw[:, 3:2051], c[:, 3:4]),
                     inc="dvS")
        for tap in (2, 1, 0):
            prog.wait("vector", "dvS", nv)
            nv = prog.op("vector", lambda e, tap=tap, c=cwc: e.scalar_tensor_tensor(
                hat[:], raw[:, tap:tap + 2048], c[:, tap:tap + 1], hat[:], ALU.mult, ALU.add), inc="dvS")
        if sec == 2:
            prog.wait("sync", "dvS", nv)
            prog.dma("sync", tl["qkvhat_scr"][128 * dt:128 * (dt + 1), :], hat[:], inc="hat_st")
        else:
            prog.wait("scalar", "dvS", nv)
            sN = prog.op("scalar", lambda e: e.activation(sq[:], hat[:], AF.Square), inc="acS")
            prog.wait("tensor", "acS", sN)
            for g in range(NGRP):
                gsl = slice(512 * g, 512 * (g + 1))
                nmm = prog.op("tensor", lambda e, gsl=gsl: e.matmul(
                    pn[0:2, 0:512], tl["bdones"][:], sq[:, gsl], start=True, stop=True), inc="mmS")
                prog.wait("scalar", "mmS", nmm)
                sN = prog.op("scalar", lambda e: e.activation(tl["nrm2"][:], pn[0:2, 0:512], AF.Sqrt,
                                                              bias=EPS_DN), inc="acS")
                prog.wait("vector", "acS", sN)
                nv = prog.op("vector", lambda e: e.reciprocal(tl["nrm"][:], tl["nrm2"][:]), inc="dvS")
                if sec == 0:
                    prog.wait("vector", "dvS", nv)
                    nv = prog.op("vector", lambda e: e.tensor_scalar_mul(tl["nrm"][:], tl["nrm"][:], 0.125),
                                 inc="dvS")
                prog.wait("tensor", "dvS", nv)
                nmm = prog.op("tensor", lambda e: e.matmul(pb[:, 0:512], tl["ind2"][:], tl["nrm"][:],
                                                           start=True, stop=True), inc="mmS")
                prog.wait("vector", "mmS", nmm)
                nv = prog.op("vector", lambda e, gsl=gsl: e.tensor_mul(hat[:, gsl], hat[:, gsl], pb[:, 0:512]),
                             inc="dvS")
                prog.wait("tensor", "dvS", nv)
            prog.wait("sync", "dvS", prog.n("dvS"))
            prog.dma("sync", tl["qkvhat_scr"][128 * dt:128 * (dt + 1), :], hat[:], inc="hat_st")
        prog.barrier(("tensor", "vector", "scalar", "sync"))
    prog.wait("sync", "hat_st", prog.n("hat_st"))
    prog.barrier()


def emit_ffn(prog, tl, cfg):
    nc = prog.nc
    hid = cfg["hid"]
    nht = hid // 128
    xn = tl["xn"]
    ps = [tl["psumA"], tl["psumB"], tl["psumC"], tl["psumD"]]
    gat = tl["qkhat"]
    sil = tl["sqtile"]
    lbl = cfg["lbl"]
    ev = {}
    for ht in range(nht):
        buf = tl["wbuf"][ht % 3]
        buf2 = tl["wbuf2"][ht % 3]
        if ht >= 3:
            prog.wait("gpsimd", lbl + "f1free", ht - 2)
        prog.dma("gpsimd", buf[:].rearrange("p (a b) -> p a b", b=128),
                 bass.AP(cfg["Wg"].tensor, ht * 128, [[hid, 128], [128 * hid, 16], [1, 128]]),
                 inc=lbl + "f1dma")
        prog.dma("gpsimd", buf2[:].rearrange("p (a b) -> p a b", b=128),
                 bass.AP(cfg["Wu"].tensor, ht * 128, [[hid, 128], [128 * hid, 16], [1, 128]]),
                 inc=lbl + "f1dma")
        prog.wait("tensor", lbl + "f1dma", prog.n(lbl + "f1dma"))
        for g in range(NGRP):
            i = ht * NGRP + g
            bg = ps[(2 * i) % 4]
            bu = ps[(2 * i + 1) % 4]
            qz = cfg.get("quant")
            if i >= 2:
                prog.wait("tensor", "dvS", ev[i - 2])
            for kt in range(16):
                prog.op("tensor", lambda e, b=bg, buf=buf, kt=kt, g=g:
                        e.matmul(b[:], buf[:, 128 * kt:128 * (kt + 1)],
                                 xn[:, 2048 * kt + 512 * g:2048 * kt + 512 * (g + 1)],
                                 start=(kt == 0), stop=(kt == 15 and not qz)))
            if qz:
                prog.op("tensor", lambda e, b=bg, g=g: e.matmul(
                    b[:], tl["neg128_row_b"][:], qz["rs"][:, 512 * g:512 * (g + 1)],
                    start=False, stop=True))
            for kt in range(16):
                prog.op("tensor", lambda e, b=bu, buf=buf2, kt=kt, g=g:
                        e.matmul(b[:], buf[:, 128 * kt:128 * (kt + 1)],
                                 xn[:, 2048 * kt + 512 * g:2048 * kt + 512 * (g + 1)],
                                 start=(kt == 0), stop=(kt == 15 and not qz)),
                        inc="mmS" if (kt == 15 and not qz) else None)
            if qz:
                prog.op("tensor", lambda e, b=bu, g=g: e.matmul(
                    b[:], tl["neg128_row_b"][:], qz["rs"][:, 512 * g:512 * (g + 1)],
                    start=False, stop=True), inc="mmS")
            if g == NGRP - 1:
                prog.milestone("tensor", "mmS", lbl + "f1free")
            nmm = prog.n("mmS")
            gsl = slice(512 * g, 512 * (g + 1))
            qz = cfg.get("quant")
            if qz:
                pass
            prog.wait("scalar", "mmS", nmm)
            if qz:
                sgc = qz["sg"][:, ht:ht + 1]
                suc = qz["su"][:, ht:ht + 1]
                na = prog.op("scalar", lambda e, b=bg, gsl=gsl, s=sgc: e.activation(
                    sil[:, gsl], b[:], AF.Silu, scale=s), inc="acS")
                prog.wait("vector", "acS", na)
                nv0 = prog.op("vector", lambda e, b=bu, gsl=gsl, s=suc: e.tensor_scalar_mul(
                    tl["upt"][:, gsl], b[:], s), inc="dvS")
                prog.wait("vector", "dvS", nv0)
                ev[i] = prog.op("vector", lambda e, gsl=gsl: e.tensor_mul(
                    gat[:, gsl], sil[:, gsl], tl["upt"][:, gsl]), inc="dvS")
            else:
                na = prog.op("scalar", lambda e, b=bg, gsl=gsl: e.activation(sil[:, gsl], b[:], AF.Silu),
                             inc="acS")
                prog.wait("vector", "acS", na)
                ev[i] = prog.op("vector", lambda e, b=bu, gsl=gsl: e.tensor_mul(gat[:, gsl], sil[:, gsl], b[:]),
                                inc="dvS")
        prog.wait("sync", "dvS", ev[ht * NGRP + NGRP - 1])
        prog.dma("sync", cfg["gated"][128 * ht:128 * (ht + 1), :], gat[:], inc=lbl + "f1st")
        prog.wait("vector", lbl + "f1st", prog.n(lbl + "f1st"))
    prog.barrier(("tensor", "vector", "scalar", "sync", "gpsimd"))
    for i in range(nht):
        buf = tl["wbuf"][i % 3]
        if i >= 3:
            prog.wait("gpsimd", lbl + "wdst", 16 * (i - 2))
        prog.dma("gpsimd", buf[:], cfg["Wd"][128 * i:128 * (i + 1), :], inc=lbl + "wddma")
        prog.wait("sync", lbl + "wddma", prog.n(lbl + "wddma"))
        prog.dma("sync", cfg["wd_scr"][128 * i:128 * (i + 1), :], buf[:], inc=lbl + "wdst")
    prog.wait("gpsimd", lbl + "wdst", prog.n(lbl + "wdst"))
    prog.barrier(("tensor", "vector", "scalar", "sync", "gpsimd"))


def emit_ffn_f2(prog, tl, cfg):
    nc = prog.nc
    hid = cfg["hid"]
    nht = hid // 128
    ps = [tl["psumA"], tl["psumB"], tl["psumC"], tl["psumD"]]
    lbl = cfg["lbl"]
    gatg = tl["gatg"]
    gwb = tl["gwb"]
    xb = tl["xbuf"]
    qz = cfg.get("quant")
    for g in range(NGRP):
        for i in range(nht):
            prog.dma("sync", gatg[i][:],
                     bass.AP(cfg["gated"].tensor, 128 * i * T + 512 * g, [[T, 128], [1, 512]]),
                     inc=lbl + "f2gld")
        for dtile in range(16):
            prog.dma("sync", xb[dtile][:],
                     bass.AP(cfg["res_in"].tensor, 128 * dtile * T + 512 * g, [[T, 128], [1, 512]]),
                     inc=lbl + "f2gld")
        n = prog.n(lbl + "f2gld")
        prog.wait("tensor", lbl + "f2gld", n)
        prog.wait("vector", lbl + "f2gld", n)
        if qz:
            for i in range(nht):
                prog.op("tensor", lambda e, i=i: e.matmul(
                    tl["psum_e"][0:1, 0:512], tl["ones_col_b"][:], gatg[i][:],
                    start=(i == 0), stop=(i == nht - 1)), inc="mmS" if i == nht - 1 else None)
            nms = prog.n("mmS")
            prog.wait("vector", "mmS", nms)
            nv = prog.op("vector", lambda e: e.tensor_copy(tl["gsum_b"][:], tl["psum_e"][0:1, 0:512]),
                         inc="dvS")
            prog.wait("tensor", "dvS", nv)
        ev2 = {}
        for dtile in range(16):
            wb1 = gwb[dtile % 2]
            if dtile >= 2:
                prog.wait("sync", lbl + "f2free", (g * 16 + dtile - 1))
            prog.dma("sync", wb1[:, 0:128 * nht].rearrange("p (a b) -> p a b", b=128),
                     bass.AP(cfg["wd_scr"].tensor, dtile * 128, [[D, 128], [128 * D, nht], [1, 128]]),
                     inc=lbl + "f2wld")
            prog.wait("tensor", lbl + "f2wld", prog.n(lbl + "f2wld"))
            bank = ps[dtile % 2]
            if dtile >= 2:
                prog.wait("tensor", "dvS", ev2[dtile - 2])
            for i in range(nht):
                prog.op("tensor", lambda e, b=bank, bs=wb1, iw=i, gg=gatg[i], i=i:
                        e.matmul(b[:], bs[:, 128 * iw:128 * (iw + 1)], gg[:],
                                 start=(i == 0), stop=(i == nht - 1 and not qz)),
                        inc="mmS" if (i == nht - 1 and not qz) else None)
            if qz:
                prog.op("tensor", lambda e, b=bank: e.matmul(
                    b[:], tl["neg128_row_b"][:], tl["gsum_b"][:], start=False, stop=True), inc="mmS")
            nmm = prog.n("mmS")
            prog.milestone("tensor", "mmS", lbl + "f2free")
            prog.wait("vector", "mmS", nmm)
            if qz:
                sdc = qz["sd"][:, dtile:dtile + 1]
                nv = prog.op("vector", lambda e, b=bank, t_=xb[dtile], s=sdc: e.scalar_tensor_tensor(
                    t_[:], b[:], s, t_[:], ALU.mult, ALU.add), inc="dvS")
            else:
                nv = prog.op("vector", lambda e, b=bank, t_=xb[dtile]: e.tensor_add(
                    t_[:], b[:], t_[:]), inc="dvS")
            ev2[dtile] = nv
            prog.wait("sync", "dvS", nv)
            prog.dma("sync", bass.AP(cfg["out"].tensor, 128 * dtile * T + 512 * g, [[T, 128], [1, 512]]),
                     xb[dtile][:], inc="fout")
        prog.barrier(("tensor", "vector", "sync"))
    prog.wait("gpsimd", "fout", prog.n("fout"))


CMAP = [("ident_s", "ident", "BF16", [128, 128]), ("identF_s", "identF", "F32", [128, 128]),
        ("sel4", "sel4_c", "BF16", [4, 128]), ("ones_row_f", "ones_row_fc", "F32", [1, 128]),
        ("ones128", "ones128_c", "BF16", [128, 128]),
        ("neg128_row_b", "neg128_row", "BF16", [1, 128]),
        ("eye2_s", "eye2", "BF16", [128, 64]),
        ("ones_col_b", "ones_col", "BF16", [128, 1]), ("ones_row_b", "ones_row", "BF16", [1, 128]),
        ("bdones", "bdones_c", "BF16", [128, 2]), ("ind2", "ind2_c", "F32", [2, 128]),
        ("selP_s", "selP", "F32", [32, NPAIR * 128]), ("selPn_s", "selPn", "F32", [32, NPAIR * 128]),
        ("ohLn_s", "ohLn", "F32", [32, NPAIR * 64]), ("ohLB_s", "ohLB", "F32", [32, NPAIR * 64]),
        ("mask_E1_s", "mask_E1", "F32", [128, 64]), ("mask_E2_s", "mask_E2", "F32", [128, 64]),
        ("mask_E1T_s", "mask_E1T", "F32", [128, 64])]


def l0_dram(nc):
    dram = {}
    dram["x_in"] = nc.dram_tensor("x_in", [D, T], F32, kind="ExternalInput").ap()
    specs = [("Wqkvzp", [D, 8192], F32), ("Wb", [D, 32], F32), ("Wa", [D, 32], F32),
             ("convw", [6144, 4], F32), ("A_log", [128, 1], F32), ("dt_bias", [128, 1], F32),
             ("normw", [1, 64], F32), ("Woutp", [D, D], F32),
             ("Wg0", [D, 8192], F32), ("Wu0", [D, 8192], F32), ("Wd0", [8192, D], F32)]
    for nm, shape, dt in specs:
        dram[nm] = nc.dram_tensor(nm, shape, dt, kind="ExternalInput").ap()
    for sname, dname, dt, shape in CMAP:
        dram[dname] = nc.dram_tensor(dname, shape, F32, kind="ExternalInput").ap()
    dram["x1_out"] = nc.dram_tensor("x1_out", [D, T], F32, kind="ExternalOutput").ap()
    return dram


def l0_scratch(nc, tag=""):
    scr = {}
    scr["qkvhat"] = nc.dram_tensor(f"qkvhat_scr{tag}", [6144, T], BF16, kind="Internal").ap()
    scr["sigz"] = nc.dram_tensor(f"sigz_scr{tag}", [D, T], BF16, kind="Internal").ap()
    scr["x1a"] = nc.dram_tensor(f"x1a_scr{tag}", [D, T], F32, kind="Internal").ap()
    scr["xn"] = nc.dram_tensor(f"xn_scr{tag}", [D, T], BF16, kind="Internal").ap()
    scr["gated"] = nc.dram_tensor(f"gated_scr{tag}", [8192, T], BF16, kind="Internal").ap()
    scr["wd_bf"] = nc.dram_tensor(f"wd_bf_scr{tag}", [8192, D], BF16, kind="Internal").ap()
    for nm in ("lam", "lamb", "beta", "negbg", "gam", "w"):
        scr[nm] = nc.dram_tensor(f"stat_{nm}{tag}", [32, T], F32, kind="Internal").ap()
    return scr


def alloc_common(nc, E, tl, dram):
    tl["consts"] = {}
    for sname, dname, dts, shape in CMAP:
        h = E(nc.sbuf_tensor(sname, shape, BF16 if dts == "BF16" else F32))
        tl["consts"][sname] = h
        tl[sname] = h
    tl["fconst"] = E(nc.sbuf_tensor("fconst", [128, 8], F32))
    tl["wbuf"] = [E(nc.sbuf_tensor(f"wbuf{i}", [128, 2048], BF16)) for i in range(3)]
    tl["psumA"] = E(nc.psum_tensor("psA", [128, 512], F32))
    tl["psumB"] = E(nc.psum_tensor("psB", [128, 512], F32))
    tl["psumC"] = E(nc.psum_tensor("psC", [128, 512], F32))
    tl["psumD"] = E(nc.psum_tensor("psD", [128, 512], F32))
    tl["psum_sm"] = E(nc.psum_tensor("psSM", [128, 512], F32))
    tl["psum_e"] = E(nc.psum_tensor("psE", [128, 512], F32))
    tl["A_log"] = E(nc.sbuf_tensor("A_log_s", [128, 1], F32))
    tl["dt_bias"] = E(nc.sbuf_tensor("dt_bias_s", [128, 1], F32))
    tl["negexpA"] = E(nc.sbuf_tensor("negexpA", [128, 1], F32))
    tl["normw_b"] = E(nc.sbuf_tensor("normw_b", [1, 64], BF16))
    tl["normwB"] = E(nc.sbuf_tensor("normwB", [128, 64], F32))
    tl["convw_s"] = E(nc.sbuf_tensor("convw_s", [128, 48 * 4], F32))
    tl["lnrow"] = ([E(nc.sbuf_tensor(f"lnrow{i}", [1, 512], F32)) for i in range(3)]
                   + [E(nc.sbuf_tensor(f"lnrow{i}", [1, 512], BF16)) for i in (3, 4)])


def emit_const_loads(prog, tl, dram):
    for sname, dname, dts, shape in CMAP:
        if dts == "BF16":
            prog.dma("gpsimd", tl["consts"][sname][:], dram[dname][:], inc="c_ldg")
        else:
            prog.dma("sync", tl["consts"][sname][:], dram[dname][:], inc="c_ld")
    prog.dma("sync", tl["convw_s"][:].rearrange("p (a b) -> p a b", b=4),
             bass.AP(dram["convw"].tensor, 0, [[4, 128], [128 * 4, 48], [1, 4]]), inc="c_ld")
    prog.dma("sync", tl["A_log"][:], dram["A_log"][:], inc="c_ld")
    prog.dma("sync", tl["dt_bias"][:], dram["dt_bias"][:], inc="c_ld")
    prog.dma("gpsimd", tl["normw_b"][:], dram["normw"][:], inc="c_ldg")
    for e in ("tensor", "vector", "scalar"):
        prog.wait(e, "c_ld", prog.n("c_ld"))
        prog.wait(e, "c_ldg", prog.n("c_ldg"))


def emit_layer0(prog, tl_c, nc, ctx, dram, scr, x_in_ap, x_out_ap):
    E = ctx.enter_context
    tl = dict(tl_c)
    tl["x_in"] = x_in_ap
    tl["Wqkvzp"] = dram["Wqkvzp"]
    tl["Woutp"] = dram["Woutp"]
    tl["qkvhat_scr"] = scr["qkvhat"]
    tl["sigz_scr"] = scr["sigz"]
    tl["x1a_scr"] = scr["x1a"]
    tl["stats_scr"] = {nm: scr[nm] for nm in ("lam", "lamb", "beta", "negbg", "gam", "w")}
    tl["lam_w"] = E(nc.sbuf_tensor("lam_w", [32, 2048], F32))
    tl["lamb_w"] = E(nc.sbuf_tensor("lamb_w", [32, 2048], F32))
    st = {}
    tl["st"] = st

    with ExitStack() as phb:
        Eb = phb.enter_context
        tl["xn"] = Eb(nc.sbuf_tensor("xn0", [128, 16 * 2048], BF16))
        for nm in ("braw", "araw", "t0", "t1", "t2", "sp", "lam", "lamb", "beta",
                   "negbg", "gam", "w"):
            st[nm] = Eb(nc.sbuf_tensor(f"st_{nm}", [128, 512], F32))
        tl["st_stage"] = Eb(nc.sbuf_tensor("st_stage", [32, 512], F32))
        tl["wb"] = Eb(nc.sbuf_tensor("wb_s", [128, 16 * 32], BF16))
        tl["wa"] = Eb(nc.sbuf_tensor("wa_s", [128, 16 * 32], BF16))
        prog.dma("gpsimd", tl["wb"][:].rearrange("p (a b) -> p a b", b=32),
                 bass.AP(dram["Wb"].tensor, 0, [[32, 128], [128 * 32, 16], [1, 32]]), inc="c_ldg")
        prog.dma("gpsimd", tl["wa"][:].rearrange("p (a b) -> p a b", b=32),
                 bass.AP(dram["Wa"].tensor, 0, [[32, 128], [128 * 32, 16], [1, 32]]), inc="c_ldg")
        prog.wait("tensor", "c_ldg", prog.n("c_ldg"))
        with ExitStack() as phln:
            El = phln.enter_context
            tl["xb16"] = [El(nc.sbuf_tensor(f"xb16_{i}", [128, 512], BF16)) for i in range(16)]
            tl["sqb"] = [El(nc.sbuf_tensor(f"sqb{i}", [128, 512], BF16)) for i in range(16)]
            emit_ln(prog, tl, x_in_ap, tl["xn"][:], "ln0")
            prog.barrier()
        tl["qkraw"] = Eb(nc.sbuf_tensor("qkraw", [128, 2051], BF16))
        tl["qkhat"] = Eb(nc.sbuf_tensor("qkhat", [128, 2048], BF16))
        tl["sqtile"] = Eb(nc.sbuf_tensor("sqtile", [128, 2048], BF16))
        tl["nrm2"] = Eb(nc.sbuf_tensor("nrm2", [2, 512], F32))
        tl["nrm"] = Eb(nc.sbuf_tensor("nrm", [2, 512], F32))
        prog.wait("sync", "dvS", prog.n("dvS"))
        for kt in range(16):
            prog.dma("sync", scr["xn"][128 * kt:128 * (kt + 1), :],
                     tl["xn"][:, 2048 * kt:2048 * (kt + 1)], inc="xn_st")
        emit_stats(prog, tl)
        emit_qkvz(prog, tl)
        prog.wait("sync", "xn_st", prog.n("xn_st"))
        prog.barrier()

    with ExitStack() as phc:
        Ec = phc.enter_context
        tl["qk_g"] = [Ec(nc.sbuf_tensor(f"qkg{i}", [128, GRP], BF16)) for i in range(48)]
        tl["sigz_g"] = [Ec(nc.sbuf_tensor(f"szg{i}", [128, GRP], BF16)) for i in range(16)]
        tl["o_g"] = [Ec(nc.sbuf_tensor(f"og{i}", [128, GRP], BF16)) for i in range(16)]
        tl["state"] = [Ec(nc.sbuf_tensor(f"Sst{i}", [128, 64], F32)) for i in range(NPAIR)]
        tl["state_b"] = [Ec(nc.sbuf_tensor(f"Sstb{i}", [128, 64], BF16)) for i in range(NPAIR)]
        tl["state_dv"] = [0] * NPAIR
        tl["xbuf"] = [Ec(nc.sbuf_tensor(f"xbufC{i}", [128, 512], F32)) for i in range(4)]
        tl["pct"] = Ec(nc.sbuf_tensor("pct", [32, 128], F32))
        tl["pcT_in"] = Ec(nc.sbuf_tensor("pcT_in", [80, 128], F32))
        tl["pcall"] = Ec(nc.sbuf_tensor("pcall", [128, 80], F32))
        wn_pp = ("NT", "NTT", "G", "P2", "P2T", "P4", "P4T", "P8", "P8T",
                 "P16", "P16T", "P32T", "PPT", "bV", "W")
        wn_s4b = ("c", "CSQ", "delta", "on")
        wn_s4f = ("orow", "osq")
        wn_s2f = ("e1f", "e2f", "e1tf")
        wn_s2b = ("E1", "E2", "E1T")
        slot4 = []
        for s in range(4):
            d = {}
            for nm in wn_s4b:
                d[nm] = Ec(nc.sbuf_tensor(f"s4_{s}_{nm}", [128, 64], BF16))
            for nm in wn_s4f:
                d[nm] = Ec(nc.sbuf_tensor(f"s4_{s}_{nm}", [128, 64], F32))
            for nm in ("var", "rstd", "rinv"):
                d[nm] = Ec(nc.sbuf_tensor(f"s4_{s}_{nm}", [128, 1], F32))
            slot4.append(d)
        slot2 = []
        for s in range(2):
            d = {}
            for nm in wn_s2f:
                d[nm] = Ec(nc.sbuf_tensor(f"s2_{s}_{nm}", [128, 64], F32))
            for nm in wn_s2b:
                d[nm] = Ec(nc.sbuf_tensor(f"s2_{s}_{nm}", [128, 64], BF16))
            slot2.append(d)
        ws = []
        for p in range(NPAIR):
            w = {}
            for nm in wn_pp:
                w[nm] = Ec(nc.sbuf_tensor(f"w{p}_{nm}", [128, 64], BF16))
            w.update(slot4[p % 4])
            w.update(slot2[p % 2])
            ws.append(w)
        tl["ws"] = ws
        emit_scan(prog, tl)
        prog.barrier()

    with ExitStack() as phd:
        Ed = phd.enter_context
        tl["xn"] = Ed(nc.sbuf_tensor("xn1", [128, 16 * 2048], BF16))
        tl["qkhat"] = Ed(nc.sbuf_tensor("gatt", [128, 2048], BF16))
        tl["wbuf2"] = [Ed(nc.sbuf_tensor(f"wbuf2_{i}", [128, 2048], BF16)) for i in range(3)]
        tl["sqtile"] = Ed(nc.sbuf_tensor("silt", [128, 2048], BF16))
        for kt in range(16):
            prog.dma("gpsimd", tl["xn"][:, 2048 * kt:2048 * (kt + 1)],
                     scr["xn"][128 * kt:128 * (kt + 1), :], inc="xn_ld")
        prog.wait("tensor", "xn_ld", prog.n("xn_ld"))
        cfg0 = dict(Wg=dram["Wg0"], Wu=dram["Wu0"], Wd=dram["Wd0"],
                    wd_scr=scr["wd_bf"], gated=scr["gated"], hid=8192,
                    res_in=scr["x1a"], out=x_out_ap, lbl="f0")
        emit_ffn(prog, tl, cfg0)
    with ExitStack() as phe:
        Ee = phe.enter_context
        tl["gatg"] = [Ee(nc.sbuf_tensor(f"gatg{i}", [128, 512], BF16)) for i in range(64)]
        tl["gwb"] = [Ee(nc.sbuf_tensor(f"gwb{i}", [128, 8192], BF16)) for i in range(2)]
        tl["xbuf"] = [Ee(nc.sbuf_tensor(f"xbufF{i}", [128, 512], F32)) for i in range(16)]
        emit_ffn_f2(prog, tl, cfg0)
    prog.barrier()


def build_layer0():
    nc = bass.Bass("TRN2", target_bir_lowering=False)
    dram = l0_dram(nc)
    scr = l0_scratch(nc)
    with ExitStack() as ctx:
        E = ctx.enter_context
        tl = {}
        alloc_common(nc, E, tl, dram)
        prog = Prog(nc, ctx)
        register_float_consts(prog, tl)
        emit_const_loads(prog, tl, dram)
        prog.barrier()
        emit_layer0(prog, tl, nc, ctx, dram, scr, dram["x_in"], dram["x1_out"])
        prog.finalize(E(nc.Block()))
    return nc


def l0_in_map(inputs, h, cs, b):
    m = {"x_in": np.ascontiguousarray(inputs["L0_attn_hidden_states"][b].T)}
    for nm in ("Wqkvzp", "Wb", "Wa", "convw", "A_log", "dt_bias", "Woutp"):
        m[nm] = h[nm]
    m["normw"] = h["normw"]
    m["Wg0"], m["Wu0"], m["Wd0"] = h["Wg0"], h["Wu0"], h["Wd0"]
    for sname, dname, dts, shape in CMAP:
        m[dname] = cs[dname]
    return m


def kernel(**inputs):
    return kernel_fused(**inputs)


def kernel_unfused(**inputs):
    h = host_prep(inputs)
    cs = host_consts()
    nc0 = build_layer0()
    in0 = [l0_in_map(inputs, h, cs, b) for b in range(8)]
    r0 = run_bass_kernel_spmd(nc0, in0, core_ids=list(range(8)))
    nc1 = build_layer1()
    in1 = [l1_in_map(h, cs, r0.results[b]["x1_out"]) for b in range(8)]
    r1 = run_bass_kernel_spmd(nc1, in1, core_ids=list(range(8)))
    out = np.stack([r1.results[b]["x2_out"].T for b in range(8)])
    return np.ascontiguousarray(out)


def emit_rowsum(prog, tl, rs_b):
    xn = tl["xn"]
    pn = tl["psumC"]
    onescb = tl["ones_col_b"]
    for g in range(NGRP):
        if g >= 1:
            prog.wait("tensor", "dvS", prog.n("dvS"))
        for kt in range(16):
            prog.op("tensor", lambda e, kt=kt, g=g: e.matmul(
                pn[0:1, 0:512], onescb[:],
                xn[:, 2048 * kt + 512 * g:2048 * kt + 512 * (g + 1)],
                start=(kt == 0), stop=(kt == 15)), inc="mmS" if kt == 15 else None)
        n = prog.n("mmS")
        prog.wait("vector", "mmS", n)
        nv = prog.op("vector", lambda e, rs=rs_b, g=g: e.tensor_copy(rs[:, 512 * g:512 * (g + 1)],
                                                                     pn[0:1, 0:512]), inc="dvS")
    prog.wait("tensor", "dvS", nv)


def emit_wscale_build(prog, tl, codes_ap, sc_ap, wsc, head_or_dt, lbl):
    h = head_or_dt
    cbuf = tl["codebuf"]
    scs = tl["sc_s"]
    pbanks = (tl["psum_sm"], tl["psum_e"])
    prog.wait("gpsimd", "dvS", tl["wsc_free"][0])
    prog.dma("gpsimd", cbuf[:].rearrange("p (a b) -> p a b", b=128),
             bass.AP(codes_ap.tensor, h * 128, [[2048, 128], [128 * 2048, 16], [1, 128]]),
             inc=lbl + "cdma")
    prog.dma("gpsimd", scs[:].rearrange("p (a b) -> p a b", b=128),
             bass.AP(sc_ap.tensor, h * 128, [[2048, 4], [4 * 2048, 16], [1, 128]]),
             inc=lbl + "cdma")
    prog.wait("tensor", lbl + "cdma", prog.n(lbl + "cdma"))
    for kt in range(16):
        pb = pbanks[kt % 2]
        par = (kt // 2) % 2
        if kt >= 2:
            prog.wait("tensor", "dvS", tl["wsc_ev"][kt - 2])
        nmm = prog.op("tensor", lambda e, kt=kt, par=par, pb=pb: e.matmul(
            pb[:, 128 * par:128 * (par + 1)], tl["sel4"][:], scs[:, 128 * kt:128 * (kt + 1)],
            start=True, stop=True), inc="mmS")
        prog.wait("vector", "mmS", nmm)
        tl["wsc_ev"][kt] = prog.op("vector", lambda e, kt=kt, par=par, pb=pb, w=wsc, c=cbuf: e.tensor_mul(
            w[:, 128 * kt:128 * (kt + 1)], c[:, 128 * kt:128 * (kt + 1)],
            pb[:, 128 * par:128 * (par + 1)]), inc="dvS")
    tl["wsc_free"][0] = tl["wsc_ev"][15]
    return tl["wsc_ev"][15]


def emit_attn1(prog, tl, dram, scr, upto=3, nheads=16):
    nc = prog.nc
    xn = tl["xn"]
    ps = [tl["psumA"], tl["psumB"]]
    pso = tl["psumC"]
    pse = tl["psum_e"]
    IA = tl["ident_s"]
    onescb = tl["ones_col_b"]
    onesrb = tl["ones_row_b"]
    for h in range(nheads):
        if upto <= 0:
            wev = emit_wscale_build(prog, tl, dram["Cwq"], dram["Swq"], tl["wsc"], h, "at")
            prog.wait("sync", "dvS", wev)
            prog.dma("sync", scr["o1"][128 * h:128 * (h + 1), :], tl["wsc"][:], inc="o_st")
            prog.wait("vector", "o_st", prog.n("o_st"))
            prog.barrier(("tensor", "vector", "scalar", "sync", "gpsimd"))
            continue
        qkv = {}
        for mi, mat in enumerate(("q", "k", "v")):
            wsc = tl["wsc"]
            wev = emit_wscale_build(prog, tl, dram[f"Cw{mat}"], dram[f"Sw{mat}"], wsc, h, "at")
            prog.wait("tensor", "dvS", wev)
            dst = tl[f"{mat}h"]
            bias = tl["bqkv"][mat][:, h:h + 1]
            prog.wait("tensor", "dvS", prog.n("dvS"))
            ev = None
            for g in range(NGRP):
                bank = ps[g % 2]
                if g >= 2:
                    prog.wait("tensor", "dvS", ev_g[g - 2])
                if g == 0:
                    ev_g = {}
                for kt in range(16):
                    prog.op("tensor", lambda e, b=bank, w=wsc, kt=kt, g=g: e.matmul(
                        b[:], w[:, 128 * kt:128 * (kt + 1)],
                        xn[:, 2048 * kt + 512 * g:2048 * kt + 512 * (g + 1)],
                        start=(kt == 0), stop=(kt == 15)), inc="mmS" if kt == 15 else None)
                nmm = prog.n("mmS")
                prog.wait("vector", "mmS", nmm)
                ev_g[g] = prog.op("vector", lambda e, b=bank, g=g, d=dst, bi=bias: e.tensor_scalar_add(
                    d[:, 512 * g:512 * (g + 1)], b[:], bi), inc="dvS")
            prog.wait("vector", "dvS", ev_g[3])
        if upto <= 1:
            nv = prog.n("dvS")
            prog.wait("sync", "dvS", nv)
            prog.dma("sync", scr["o1"][128 * h:128 * (h + 1), :], tl["vh"][:], inc="o_st")
            prog.wait("vector", "o_st", prog.n("o_st"))
            prog.barrier(("tensor", "vector", "scalar", "sync", "gpsimd"))
            continue
        vt = tl["vh"]
        vbanks = (tl["psumD"], tl["psum_sm"])
        for kt in range(16):
            vb = vbanks[kt % 2]
            if kt >= 2:
                prog.wait("tensor", "dvS", tl["vt_ev"][kt - 2])
            nmm = prog.op("tensor", lambda e, kt=kt, vb=vb, vt=vt: e.matmul(
                vb[:, 0:128], vt[:, 128 * kt:128 * (kt + 1)], IA[:],
                start=True, stop=True), inc="mmS")
            prog.wait("vector", "mmS", nmm)
            tl["vt_ev"][kt] = prog.op("vector", lambda e, kt=kt, vb=vb: e.tensor_copy(
                tl["VT"][kt][:], vb[:, 0:128]), inc="dvS")
        for mi, mat in enumerate(("q", "k")):
            src = tl[f"{mat}h"]
            prog.wait("scalar", "dvS", prog.n("dvS"))
            na = prog.op("scalar", lambda e, s=src: e.activation(tl["sqh"][:], s[:], AF.Square),
                         inc="acS")
            prog.wait("tensor", "acS", na)
            mdst = tl["mxq"] if mi == 0 else tl["mxk"]
            for g in range(NGRP):
                if g >= 1:
                    prog.wait("tensor", "dvS", prog.n("dvS"))
                prog.op("tensor", lambda e, g=g: e.matmul(
                    pse[:, 0:512], tl["ones128"][:], tl["sqh"][:, 512 * g:512 * (g + 1)],
                    start=True, stop=True), inc="mmS")
                nmm = prog.n("mmS")
                prog.wait("vector", "mmS", nmm)
                if g == 0:
                    prog.op("vector", lambda e, m=mdst: e.reduce_max(m[:, 0:1], pse[:, 0:512], axis=AX.X),
                            inc="dvS")
                else:
                    prog.wait("vector", "dvS", prog.n("dvS"))
                    prog.op("vector", lambda e, m=mdst: e.reduce_max(m[:, 1:2], pse[:, 0:512], axis=AX.X),
                            inc="dvS")
                    prog.wait("vector", "dvS", prog.n("dvS"))
                    prog.op("vector", lambda e, m=mdst: e.tensor_max(m[:, 0:1], m[:, 0:1], m[:, 1:2]),
                            inc="dvS")
        prog.wait("vector", "dvS", prog.n("dvS"))
        nv = prog.op("vector", lambda e: e.tensor_mul(tl["mxq"][:, 2:3], tl["mxq"][:, 0:1],
                                                      tl["mxk"][:, 0:1]), inc="dvS")
        prog.wait("scalar", "dvS", nv)
        na = prog.op("scalar", lambda e: e.activation(tl["mxq"][:, 3:4], tl["mxq"][:, 2:3], AF.Sqrt,
                                                      scale=1.0 / 128.0), inc="acS")
        prog.wait("vector", "acS", na)
        nv = prog.op("vector", lambda e: e.tensor_scalar(tl["negB"][:], tl["mxq"][:, 3:4], -1.0, -1.0,
                                                         ALU.mult, ALU.add), inc="dvS")
        prog.wait("scalar", "dvS", nv)
        prog.wait("tensor", "dvS", nv)
        if upto <= 2:
            nv = prog.n("dvS")
            prog.wait("sync", "dvS", nv)
            prog.dma("sync", scr["o1"][128 * h:128 * (h + 1), :], tl["vh"][:], inc="o_st")
            prog.wait("vector", "o_st", prog.n("o_st"))
            prog.barrier(("tensor", "vector", "scalar", "sync", "gpsimd"))
            continue
        for g in range(NGRP):
            qsl = slice(512 * g, 512 * (g + 1))
            for kt in range(16):
                par = kt % 2
                bank = ps[par]
                if kt >= 2:
                    prog.wait("tensor", "acS", tl["pt_ev"][kt - 2])
                nmm = prog.op("tensor", lambda e, b=bank, kt=kt, qsl=qsl: e.matmul(
                    b[:], tl["kh"][:, 128 * kt:128 * (kt + 1)], tl["qh"][:, qsl],
                    start=True, stop=True), inc="mmS")
                prog.wait("scalar", "mmS", nmm)
                tl["pt_ev"][kt] = prog.op("scalar", lambda e, b=bank, kt=kt: e.activation(
                    tl["PT"][kt][:], b[:], AF.Exp, bias=tl["negB"][:, 0:1], scale=0.08838834764831845),
                    inc="acS")
            na = prog.n("acS")
            prog.wait("tensor", "acS", na)
            for kt in range(16):
                prog.op("tensor", lambda e, kt=kt: e.matmul(
                    pse[0:1, 0:512], onescb[:], tl["PT"][kt][:],
                    start=(kt == 0), stop=(kt == 15)), inc=None)
            for kt in range(16):
                prog.op("tensor", lambda e, kt=kt: e.matmul(
                    pso[:], tl["VT"][kt][:], tl["PT"][kt][:],
                    start=(kt == 0), stop=(kt == 15)), inc="mmS" if kt == 15 else None)
            nmm = prog.n("mmS")
            prog.wait("vector", "mmS", nmm)
            nv = prog.op("vector", lambda e: e.reciprocal(tl["rrow"][:], pse[0:1, 0:512]), inc="dvS")
            prog.wait("vector", "dvS", nv)
            nv = prog.op("vector", lambda e: e.tensor_copy(tl["rrow_b"][:], tl["rrow"][:]), inc="dvS")
            prog.wait("tensor", "dvS", nv)
            nmm = prog.op("tensor", lambda e: e.matmul(tl["psumD"][:, 0:512], onesrb[:], tl["rrow_b"][:],
                                                       start=True, stop=True), inc="mmS")
            prog.wait("vector", "mmS", nmm)
            nv = prog.op("vector", lambda e: e.tensor_copy(tl["rb"][:], tl["psumD"][:, 0:512]), inc="dvS")
            prog.wait("vector", "dvS", nv)
            nv = prog.op("vector", lambda e, qsl=qsl: e.tensor_mul(tl["oh"][:, qsl], pso[:], tl["rb"][:]),
                         inc="dvS")
            prog.wait("tensor", "dvS", nv)
            prog.wait("scalar", "dvS", nv)
        nv = prog.n("dvS")
        prog.wait("sync", "dvS", nv)
        prog.dma("sync", scr["o1"][128 * h:128 * (h + 1), :], tl["oh"][:], inc="o_st")
        prog.wait("vector", "o_st", prog.n("o_st"))
        prog.barrier(("tensor", "vector", "scalar", "sync", "gpsimd"))


def emit_wo_residual(prog, tl, dram, scr, x_in_ap):
    nc = prog.nc
    ps = [tl["psumA"], tl["psumB"]]
    for g in range(NGRP):
        for i in range(16):
            prog.dma("sync", tl["o_g1"][i][:],
                     bass.AP(scr["o1"].tensor, 128 * i * T + 512 * g, [[T, 128], [1, 512]]),
                     inc="wo1_ld")
        n = prog.n("wo1_ld")
        prog.wait("tensor", "wo1_ld", n)
        ev2 = {}
        for dt in range(16):
            wev = emit_wscale_build(prog, tl, dram["Cwo"], dram["Swo"], tl["wsc"], dt, "wo")
            prog.wait("tensor", "dvS", wev)
            bank = ps[dt % 2]
            if dt >= 2:
                prog.wait("tensor", "dvS", ev2[dt - 2])
            for kt in range(16):
                prog.op("tensor", lambda e, b=bank, kt=kt: e.matmul(
                    b[:], tl["wsc"][:, 128 * kt:128 * (kt + 1)], tl["o_g1"][kt][:],
                    start=(kt == 0), stop=(kt == 15)), inc="mmS" if kt == 15 else None)
            nmm = prog.n("mmS")
            xb = tl["xbuf"][dt % 4]
            nx = prog.dma("sync", xb[:],
                          bass.AP(x_in_ap.tensor, 128 * dt * T + 512 * g, [[T, 128], [1, 512]]),
                          inc="x1_ld")
            prog.wait("vector", "mmS", nmm)
            prog.wait("vector", "x1_ld", nx)
            nv = prog.op("vector", lambda e, b=bank, xb=xb, dt=dt: e.scalar_tensor_tensor(
                xb[:], b[:], tl["bqkv"]["o"][:, dt:dt + 1] if False else tl["bo_s"][:, dt:dt + 1],
                xb[:], ALU.add, ALU.add), inc="dvS")
            ev2[dt] = nv
            prog.wait("sync", "dvS", nv)
            prog.dma("sync", bass.AP(scr["x1a1"].tensor, 128 * dt * T + 512 * g, [[T, 128], [1, 512]]),
                     xb[:], inc="x1a1_st")
        prog.barrier(("tensor", "vector", "scalar", "sync", "gpsimd"))


def l1_dram(nc):
    dram = {}
    dram["x1_in"] = nc.dram_tensor("x1_in", [D, T], F32, kind="ExternalInput").ap()
    for nm in ("q", "k", "v", "o"):
        dram[f"Cw{nm}"] = nc.dram_tensor(f"Cw{nm}", [D, D], F32, kind="ExternalInput").ap()
        dram[f"Sw{nm}"] = nc.dram_tensor(f"Sw{nm}", [64, D], F32, kind="ExternalInput").ap()
    for nm in ("bq", "bk", "bv", "bo"):
        dram[nm] = nc.dram_tensor(nm, [128, 16], F32, kind="ExternalInput").ap()
    dram["Wg1"] = nc.dram_tensor("Wg1", [D, 5632], I32, kind="ExternalInput").ap()
    dram["Wu1"] = nc.dram_tensor("Wu1", [D, 5632], I32, kind="ExternalInput").ap()
    dram["Wd1"] = nc.dram_tensor("Wd1", [5632, D], I32, kind="ExternalInput").ap()
    dram["sg1"] = nc.dram_tensor("sg1", [128, 44], F32, kind="ExternalInput").ap()
    dram["su1"] = nc.dram_tensor("su1", [128, 44], F32, kind="ExternalInput").ap()
    dram["sd1"] = nc.dram_tensor("sd1", [128, 16], F32, kind="ExternalInput").ap()
    for sname, dname, dt, shape in CMAP:
        dram[dname] = nc.dram_tensor(dname, shape, F32, kind="ExternalInput").ap()
    dram["x2_out"] = nc.dram_tensor("x2_out", [D, T], F32, kind="ExternalOutput").ap()
    return dram


def l1_scratch(nc, tag=""):
    scr = {}
    scr["o1"] = nc.dram_tensor(f"o1_scr{tag}", [D, T], BF16, kind="Internal").ap()
    scr["x1a1"] = nc.dram_tensor(f"x1a1_scr{tag}", [D, T], F32, kind="Internal").ap()
    scr["xn1"] = nc.dram_tensor(f"xn1_scr{tag}", [D, T], BF16, kind="Internal").ap()
    scr["gated1"] = nc.dram_tensor(f"gated1_scr{tag}", [5632, T], BF16, kind="Internal").ap()
    scr["wd_bf1"] = nc.dram_tensor(f"wd_bf1_scr{tag}", [5632, D], BF16, kind="Internal").ap()
    return scr


def emit_layer1(prog, tl_c, nc, ctx, dram, scr, x_in_ap, x_out_ap, parts=("attn", "ffn")):
    E = ctx.enter_context
    tl = dict(tl_c)
    tl["bqkv"] = {}
    for nm in ("q", "k", "v"):
        tl["bqkv"][nm] = E(nc.sbuf_tensor(f"b{nm}_s", [128, 16], F32))
        prog.dma("sync", tl["bqkv"][nm][:], dram[f"b{nm}"][:], inc="c_ld")
    tl["bo_s"] = E(nc.sbuf_tensor("bo_s", [128, 16], F32))
    prog.dma("sync", tl["bo_s"][:], dram["bo"][:], inc="c_ld")
    tl["sg1_s"] = E(nc.sbuf_tensor("sg1_s", [128, 44], F32))
    tl["su1_s"] = E(nc.sbuf_tensor("su1_s", [128, 44], F32))
    tl["sd1_s"] = E(nc.sbuf_tensor("sd1_s", [128, 16], F32))
    for nm in ("sg1", "su1", "sd1"):
        prog.dma("sync", tl[nm + "_s"][:], dram[nm][:], inc="c_ld")
    for e in ("tensor", "vector", "scalar"):
        prog.wait(e, "c_ld", prog.n("c_ld"))
    prog.barrier(("tensor", "vector", "scalar", "sync", "gpsimd"))

    with ExitStack() as pha:
        Ea = pha.enter_context
        tl["xn"] = Ea(nc.sbuf_tensor("xn1r", [128, 16 * 2048], BF16))
        tl["rs_b"] = Ea(nc.sbuf_tensor("rs_b", [1, 2048], BF16))
        with ExitStack() as phln:
            El = phln.enter_context
            tl["xb16"] = [El(nc.sbuf_tensor(f"l1xb{i}", [128, 512], BF16)) for i in range(16)]
            tl["sqb"] = [El(nc.sbuf_tensor(f"l1sq{i}", [128, 512], BF16)) for i in range(16)]
            emit_ln(prog, tl, x_in_ap, tl["xn"][:], "ln1")
            prog.barrier()
        emit_rowsum(prog, tl, tl["rs_b"])
        if "attn" in parts and ExitStack:
         with ExitStack() as phb:
            Eb = phb.enter_context
            tl["codebuf"] = Eb(nc.sbuf_tensor("codebuf", [128, 2048], BF16))
            tl["sc_s"] = Eb(nc.sbuf_tensor("sc_s", [4, 2048], BF16))
            tl["wsc"] = Eb(nc.sbuf_tensor("wsc", [128, 2048], BF16))
            tl["wsc_ev"] = [0] * 16
            tl["wsc_free"] = [0]
            tl["qh"] = Eb(nc.sbuf_tensor("qh", [128, 2048], BF16))
            tl["kh"] = Eb(nc.sbuf_tensor("kh", [128, 2048], BF16))
            tl["vh"] = Eb(nc.sbuf_tensor("vh", [128, 2048], BF16))
            tl["VT"] = [Eb(nc.sbuf_tensor(f"VTt{i}", [128, 128], BF16)) for i in range(16)]
            tl["vt_ev"] = [0] * 16
            tl["PT"] = [Eb(nc.sbuf_tensor(f"PTt{i}", [128, 512], BF16)) for i in range(16)]
            tl["pt_ev"] = [0] * 16
            tl["sqh"] = Eb(nc.sbuf_tensor("sqh", [128, 2048], BF16))
            tl["mxq"] = Eb(nc.sbuf_tensor("mxq", [128, 4], F32))
            tl["mxk"] = Eb(nc.sbuf_tensor("mxk", [128, 4], F32))
            tl["negB"] = Eb(nc.sbuf_tensor("negB", [128, 1], F32))
            tl["rrow"] = Eb(nc.sbuf_tensor("rrow", [1, 512], F32))
            tl["rrow_b"] = Eb(nc.sbuf_tensor("rrow_b", [1, 512], BF16))
            tl["rb"] = Eb(nc.sbuf_tensor("rb", [128, 512], BF16))
            tl["oh"] = Eb(nc.sbuf_tensor("oh", [128, 2048], BF16))
            emit_attn1(prog, tl, dram, scr, upto=ATTN_UPTO, nheads=ATTN_NHEADS)
        if "attn" in parts:
         with ExitStack() as phc:
            Ec = phc.enter_context
            tl["codebuf"] = Ec(nc.sbuf_tensor("codebuf2", [128, 2048], BF16))
            tl["sc_s"] = Ec(nc.sbuf_tensor("sc_s2", [4, 2048], BF16))
            tl["wsc"] = Ec(nc.sbuf_tensor("wsc2", [128, 2048], BF16))
            tl["wsc_ev"] = [0] * 16
            tl["wsc_free"] = [0]
            tl["o_g1"] = [Ec(nc.sbuf_tensor(f"og1_{i}", [128, 512], BF16)) for i in range(16)]
            tl["xbuf"] = [Ec(nc.sbuf_tensor(f"xbufW{i}", [128, 512], F32)) for i in range(4)]
            emit_wo_residual(prog, tl, dram, scr, x_in_ap)
        if "attn" not in parts:
            for dt in range(16):
                for g in range(NGRP):
                    prog.dma("sync",
                             bass.AP(x_out_ap.tensor, 128 * dt * T + 512 * g, [[T, 128], [1, 512]]),
                             bass.AP(x_in_ap.tensor, 128 * dt * T + 512 * g, [[T, 128], [1, 512]]),
                             inc="fout")
            prog.wait("gpsimd", "fout", prog.n("fout"))
            prog.barrier()
            return
        if "ffn" not in parts:
            for dt in range(16):
                for g in range(NGRP):
                    prog.dma("sync",
                             bass.AP(x_out_ap.tensor, 128 * dt * T + 512 * g, [[T, 128], [1, 512]]),
                             bass.AP(scr["x1a1"].tensor, 128 * dt * T + 512 * g, [[T, 128], [1, 512]]),
                             inc="fout")
            prog.wait("gpsimd", "fout", prog.n("fout"))
            prog.barrier()
            return
        tl["qkhat"] = Ea(nc.sbuf_tensor("gatt1", [128, 2048], BF16))
        tl["sqtile"] = Ea(nc.sbuf_tensor("silt1", [128, 2048], BF16))
        tl["upt"] = Ea(nc.sbuf_tensor("upt1", [128, 2048], BF16))
        tl["wbuf2"] = [Ea(nc.sbuf_tensor(f"l1wb2_{i}", [128, 2048], BF16)) for i in range(3)]
        cfg1 = dict(Wg=dram["Wg1"], Wu=dram["Wu1"], Wd=dram["Wd1"],
                    wd_scr=scr["wd_bf1"], gated=scr["gated1"], hid=5632,
                    res_in=scr["x1a1"], out=x_out_ap, lbl="f1",
                    quant=dict(sg=tl["sg1_s"], su=tl["su1_s"], sd=tl["sd1_s"], rs=tl["rs_b"]))
        emit_ffn(prog, tl, cfg1)
    with ExitStack() as phe:
        Ee = phe.enter_context
        tl["gatg"] = [Ee(nc.sbuf_tensor(f"gatg1_{i}", [128, 512], BF16)) for i in range(44)]
        tl["gwb"] = [Ee(nc.sbuf_tensor(f"gwb1_{i}", [128, 5632], BF16)) for i in range(2)]
        tl["gsum_b"] = Ee(nc.sbuf_tensor("gsum_b", [1, 512], BF16))
        tl["xbuf"] = [Ee(nc.sbuf_tensor(f"xbufF1_{i}", [128, 512], F32)) for i in range(16)]
        emit_ffn_f2(prog, tl, cfg1)
    prog.barrier()


def build_layer1(parts=("attn", "ffn")):
    nc = bass.Bass("TRN2", target_bir_lowering=False)
    dram = l1_dram(nc)
    scr = l1_scratch(nc)
    with ExitStack() as ctx:
        E = ctx.enter_context
        tl = {}
        alloc_common(nc, E, tl, dram)
        prog = Prog(nc, ctx)
        register_float_consts(prog, tl)
        for sname, dname, dts, shape in CMAP:
            if dts == "BF16":
                prog.dma("gpsimd", tl["consts"][sname][:], dram[dname][:], inc="c_ldg")
            else:
                prog.dma("sync", tl["consts"][sname][:], dram[dname][:], inc="c_ld")
        for e in ("tensor", "vector", "scalar"):
            prog.wait(e, "c_ld", prog.n("c_ld"))
            prog.wait(e, "c_ldg", prog.n("c_ldg"))
        prog.barrier()
        emit_layer1(prog, tl, nc, ctx, dram, scr, dram["x1_in"], dram["x2_out"], parts)
        prog.finalize(E(nc.Block()))
    return nc


def l1_in_map(h, cs, x1_fm):
    m = {"x1_in": np.ascontiguousarray(x1_fm)}
    for nm in ("q", "k", "v", "o"):
        m[f"Cw{nm}"] = h[f"Cw{nm}"]
        m[f"Sw{nm}"] = h[f"Sw{nm}"]
    for nm in ("bq", "bk", "bv", "bo"):
        m[nm] = h[nm]
    for nm in ("Wg1", "Wu1", "Wd1", "sg1", "su1", "sd1"):
        m[nm] = h[nm]
    for sname, dname, dts, shape in CMAP:
        m[dname] = cs[dname]
    return m


def build_fused():
    nc = bass.Bass("TRN2", target_bir_lowering=False)
    dram = {}
    dram["x_in"] = nc.dram_tensor("x_in", [D, T], F32, kind="ExternalInput").ap()
    specs = [("Wqkvzp", [D, 8192], F32), ("Wb", [D, 32], F32), ("Wa", [D, 32], F32),
             ("convw", [6144, 4], F32), ("A_log", [128, 1], F32), ("dt_bias", [128, 1], F32),
             ("normw", [1, 64], F32), ("Woutp", [D, D], F32),
             ("Wg0", [D, 8192], F32), ("Wu0", [D, 8192], F32), ("Wd0", [8192, D], F32)]
    for nm, shape, dt in specs:
        dram[nm] = nc.dram_tensor(nm, shape, dt, kind="ExternalInput").ap()
    for nm in ("q", "k", "v", "o"):
        dram[f"Cw{nm}"] = nc.dram_tensor(f"Cw{nm}", [D, D], F32, kind="ExternalInput").ap()
        dram[f"Sw{nm}"] = nc.dram_tensor(f"Sw{nm}", [64, D], F32, kind="ExternalInput").ap()
    for nm in ("bq", "bk", "bv", "bo"):
        dram[nm] = nc.dram_tensor(nm, [128, 16], F32, kind="ExternalInput").ap()
    dram["Wg1"] = nc.dram_tensor("Wg1", [D, 5632], I32, kind="ExternalInput").ap()
    dram["Wu1"] = nc.dram_tensor("Wu1", [D, 5632], I32, kind="ExternalInput").ap()
    dram["Wd1"] = nc.dram_tensor("Wd1", [5632, D], I32, kind="ExternalInput").ap()
    dram["sg1"] = nc.dram_tensor("sg1", [128, 44], F32, kind="ExternalInput").ap()
    dram["su1"] = nc.dram_tensor("su1", [128, 44], F32, kind="ExternalInput").ap()
    dram["sd1"] = nc.dram_tensor("sd1", [128, 16], F32, kind="ExternalInput").ap()
    for sname, dname, dt, shape in CMAP:
        dram[dname] = nc.dram_tensor(dname, shape, F32, kind="ExternalInput").ap()
    dram["x2_out"] = nc.dram_tensor("x2_out", [D, T], F32, kind="ExternalOutput").ap()
    x1_scr = nc.dram_tensor("x1_mid", [D, T], F32, kind="Internal").ap()
    scr0 = l0_scratch(nc)
    scr1 = l1_scratch(nc)
    with ExitStack() as ctx:
        E = ctx.enter_context
        tl = {}
        alloc_common(nc, E, tl, dram)
        prog = Prog(nc, ctx)
        register_float_consts(prog, tl)
        emit_const_loads(prog, tl, dram)
        prog.barrier()
        emit_layer0(prog, tl, nc, ctx, dram, scr0, dram["x_in"], x1_scr)
        emit_layer1(prog, tl, nc, ctx, dram, scr1, x1_scr, dram["x2_out"])
        prog.finalize(E(nc.Block()))
    return nc


def kernel_fused(**inputs):
    h = host_prep(inputs)
    cs = host_consts()
    nc = build_fused()
    in_maps = []
    for b in range(8):
        m = l0_in_map(inputs, h, cs, b)
        m2 = l1_in_map(h, cs, np.zeros((1,), np.float32))
        del m2["x1_in"]
        m.update(m2)
        in_maps.append(m)
    res = run_bass_kernel_spmd(nc, in_maps, core_ids=list(range(8)))
    out = np.stack([res.results[b]["x2_out"].T for b in range(8)])
    return np.ascontiguousarray(out)


# version 04
# speedup vs baseline: 1.1427x; 1.0495x over previous
import numpy as np
import ml_dtypes
from contextlib import ExitStack
import concourse.bass as bass
import concourse.mybir as mybir
from concourse.bass_utils import run_bass_kernel_spmd

F32 = mybir.dt.float32
BF16 = mybir.dt.bfloat16
I32 = mybir.dt.int32
AF = mybir.ActivationFunctionType
ALU = mybir.AluOpType
AX = mybir.AxisListType

T = 2048
D = 2048
NH = 32
NPAIR = 16
C = 64
NCHUNK = T // C
GRP = 512
NGRP = T // GRP
CPG = GRP // C
NEG = -1.0e30
ATTN_UPTO = 3
ATTN_NHEADS = 16
EPS_LN = 1e-5
EPS_DN = 1e-6


def host_consts():
    c = {}
    c["ident"] = np.eye(128, dtype=np.float32)
    c["identF"] = np.eye(128, dtype=np.float32)
    ii = np.zeros((128, 64), np.float32)
    ii[0:64] = np.eye(64)
    ii[64:128] = np.eye(64)
    c["eye2"] = ii
    c["ones_col"] = np.ones((128, 1), np.float32)
    c["ones_row"] = np.ones((1, 128), np.float32)
    s4 = np.zeros((4, 128), np.float32)
    for k in range(4):
        s4[k, 32 * k:32 * (k + 1)] = 1.0
    c["sel4_c"] = s4
    c["neg128_row"] = np.full((1, 128), -128.0, np.float32)
    c["ones_row_fc"] = np.ones((1, 128), np.float32)
    c["ones128_c"] = np.ones((128, 128), np.float32)
    bd = np.zeros((128, 2), np.float32)
    bd[0:64, 0] = 1.0
    bd[64:128, 1] = 1.0
    c["bdones_c"] = bd
    ind2 = np.zeros((2, 128), np.float32)
    ind2[0, 0:64] = 1.0
    ind2[1, 64:128] = 1.0
    c["ind2_c"] = ind2
    sel = np.zeros((NPAIR, 32, 128), np.float32)
    for p in range(NPAIR):
        sel[p, 2 * p, 0:64] = 1.0
        sel[p, 2 * p + 1, 64:128] = 1.0
    c["selP"] = np.ascontiguousarray(sel.transpose(1, 0, 2).reshape(32, NPAIR * 128))
    c["selPn"] = np.ascontiguousarray((-sel).transpose(1, 0, 2).reshape(32, NPAIR * 128))
    ohLn = np.zeros((NPAIR, 32, 64), np.float32)
    ohLB = np.zeros((NPAIR, 32, 64), np.float32)
    for p in range(NPAIR):
        ohLn[p, p, :] = -1.0
        ohLB[p, 16 + p, :] = 1.0
    c["ohLn"] = np.ascontiguousarray(ohLn.transpose(1, 0, 2).reshape(32, NPAIR * 64))
    c["ohLB"] = np.ascontiguousarray(ohLB.transpose(1, 0, 2).reshape(32, NPAIR * 64))
    j = np.arange(64)
    mstrict = np.where(j[:, None] < j[None, :], 0.0, NEG).astype(np.float32)
    mincl = np.where(j[:, None] <= j[None, :], 0.0, NEG).astype(np.float32)
    mstrictT = np.where(j[:, None] > j[None, :], 0.0, NEG).astype(np.float32)
    c["mask_E1"] = np.concatenate([mstrict, mstrict], 0)
    c["mask_E2"] = np.concatenate([mincl, mincl], 0)
    c["mask_E1T"] = np.concatenate([mstrictT, mstrictT], 0)
    return c


def host_prep(inputs):
    h = {}
    asc = np.ascontiguousarray
    wq = inputs["L0_attn_w_qkvz"]
    rows = []
    for off in (0, 64, 128, 192):
        for hh in range(NH):
            rows.append(np.arange(hh * 256 + off, hh * 256 + off + 64))
    perm = np.concatenate(rows)
    h["Wqkvzp"] = asc(wq[perm].T)
    wba = inputs["L0_attn_w_ba"]
    h["Wb"] = asc(wba[0::2].T)
    h["Wa"] = asc(wba[1::2].T)
    h["convw"] = asc(inputs["L0_attn_conv_w"][:, 0, :])
    h["A_log"] = asc(np.tile(inputs["L0_attn_A_log"].reshape(1, 32), (4, 1)).reshape(128, 1))
    h["dt_bias"] = asc(np.tile(inputs["L0_attn_dt_bias"].reshape(1, 32), (4, 1)).reshape(128, 1))
    h["normw"] = asc(inputs["L0_attn_norm_weight"].reshape(1, 64))
    h["Woutp"] = asc(inputs["L0_attn_w_out"].T)
    h["Wg0"] = asc(inputs["L0_ffn_w_gate"].T)
    h["Wu0"] = asc(inputs["L0_ffn_w_up"].T)
    h["Wd0"] = asc(inputs["L0_ffn_w_down"].T)
    for nm in ("wq", "wk", "wv", "wo"):
        h[f"C{nm}"] = asc(inputs[f"L1_attn_{nm}_codes"])
        h[f"S{nm}"] = asc(inputs[f"L1_attn_{nm}_scales"])
    h["bq"] = asc(inputs["L1_attn_bq"].reshape(16, 128).T)
    h["bk"] = asc(inputs["L1_attn_bk"].reshape(16, 128).T)
    h["bv"] = asc(inputs["L1_attn_bv"].reshape(16, 128).T)
    h["bo"] = asc(inputs["L1_attn_bo"].reshape(16, 128).T)
    h["Wg1"] = asc(inputs["L1_ffn_wq_gate"].T)
    h["Wu1"] = asc(inputs["L1_ffn_wq_up"].T)
    h["Wd1"] = asc(inputs["L1_ffn_wq_down"].T)
    h["sg1"] = asc(inputs["L1_ffn_s_gate"].reshape(44, 128).T)
    h["su1"] = asc(inputs["L1_ffn_s_up"].reshape(44, 128).T)
    h["sd1"] = asc(inputs["L1_ffn_s_down"].reshape(16, 128).T)
    return h


class Ctr:
    def __init__(self, ctx, nc, name):
        self.h = ctx.enter_context(nc.semaphore(name))
        self.n = 0


class Prog:
    ENGINES = ("sync", "gpsimd", "tensor", "vector", "scalar")

    def __init__(self, nc, ctx):
        self.nc = nc
        self.ctx = ctx
        self.streams = {e: [] for e in self.ENGINES}
        self.sems = {}
        self.swdge = {}
        self._holders = []
        self.sync_runs = {}

    def sem(self, name):
        if name not in self.sems:
            self.sems[name] = Ctr(self.ctx, self.nc, name)
        return self.sems[name]

    def n(self, name):
        return self.sem(name).n

    def _close_sync_runs(self):
        for nm, rec in self.sync_runs.items():
            if rec["open"]:
                s = self.sems[nm]
                v = s.n
                self.streams["sync"].append(lambda e, s=s, v=v: e.wait_ge(s.h, v))
                rec["bounds"].append(v)
                rec["open"] = False

    def milestone(self, eng, main_sem, ms_sem, k=1):
        self.wait(eng, main_sem, self.n(main_sem))
        s = self.sem(ms_sem)
        s.n += k
        self.streams[eng].append(lambda e, s=s, k=k: e.sem_inc(s.h, k))
        return s.n

    def op(self, eng, fn, inc=None, inck=1):
        if eng == "sync":
            self._close_sync_runs()
        if inc is not None:
            if isinstance(inc, str):
                inc = [(inc, inck)]
            hs = []
            ret = None
            for nm, k in inc:
                s = self.sem(nm)
                s.n += k
                hs.append((s.h, k))
                ret = s.n

            def run(e, fn=fn, hs=hs):
                i = fn(e)
                for h, k in hs:
                    i = i.then_inc(h, k)
                return i
            self.streams[eng].append(run)
            return ret
        self.streams[eng].append(lambda e, fn=fn: fn(e))
        return None

    def dma(self, eng, out, in_, inc):
        if eng == "sync":
            s = self.sem(inc)
            s.n += 16
            rec = self.sync_runs.setdefault(inc, {"bounds": [], "open": False})
            rec["open"] = True
            self.streams[eng].append(
                lambda e, o=out, i=in_, s=s: e.dma_start(out=o, in_=i).then_inc(s.h, 16))
            return s.n
        if eng == "gpsimd":
            s = self.sem(inc)
            s.n += 16
            v = s.n

            def run(e, o=out, i=in_, s=s, v=v):
                e.dma_start(out=o, in_=i).then_inc(s.h, 16)
                return e.wait_ge(s.h, v)
            self.streams[eng].append(run)
            return v
        return self.op(eng, lambda e, o=out, i=in_: e.dma_start(out=o, in_=i), inc=inc, inck=16)

    def wait(self, eng, sem, val=None):
        if eng == "sync":
            self._close_sync_runs()
        s = self.sem(sem)
        v = s.n if val is None else val
        if v > 0:
            if sem in self.sync_runs and eng != "sync":
                holder = [v]
                self._holders.append((sem, holder))
                self.streams[eng].append(lambda e, s=s, h=holder: e.wait_ge(s.h, h[0]))
            else:
                self.streams[eng].append(lambda e, s=s, v=v: e.wait_ge(s.h, v))
        return v

    def barrier(self, engines=None):
        engines = list(engines or self.ENGINES)
        if "sync" in engines:
            self._close_sync_runs()
        vals = {}
        for e in engines:
            s = self.sem("ph_" + e)
            s.n += 1
            vals[e] = s.n
            self.streams[e].append(lambda en, s=s: en.drain().then_inc(s.h, 1))
        for e in engines:
            for f in engines:
                if f != e:
                    sf = self.sem("ph_" + f)
                    self.streams[e].append(lambda en, s=sf, v=vals[f]: en.wait_ge(s.h, v))
        return vals

    def finalize(self, block):
        self._close_sync_runs()
        for sem, holder in self._holders:
            bounds = self.sync_runs[sem]["bounds"]
            v = holder[0]
            holder[0] = next((b for b in bounds if b >= v), bounds[-1] if bounds else v)
        for name in self.ENGINES:
            stream = self.streams[name]

            def runner(eng, stream=stream):
                for fn in stream:
                    fn(eng)
            getattr(block, name)(runner)


def register_float_consts(prog, tl, values=(EPS_DN, EPS_LN, 1.0 / 64.0, 0.08838834764831845,
                                             1.0 / 128.0, 1.0 / D)):
    nc = prog.nc
    tile = tl["fconst"]
    for i, v in enumerate(values):
        prog.op("vector", lambda e, i=i, v=v: e.memset(tile[:, i:i + 1], v), inc="dvS")
        nc.const_aps.register(F32, v, tile[:, i:i + 1])
    prog.wait("vector", "dvS", prog.n("dvS"))


def emit_scan(prog, tl):
    nc = prog.nc
    cs = tl["consts"]
    for p in range(NPAIR):
        prog.op("vector", lambda e, p=p: e.memset(tl["state"][p][:], 0.0), inc="dvS")
        n = prog.op("vector", lambda e, p=p: e.memset(tl["state_b"][p][:], 0.0), inc="dvS")
        tl["state_dv"][p] = n
    prog.op("tensor", lambda e: e.matmul(tl["psumD"][:, 0:64], tl["ones_row_b"][:], tl["normw_b"][:],
                                         start=True, stop=True), inc="mmS")
    prog.wait("vector", "mmS", prog.n("mmS"))
    prog.op("vector", lambda e: e.tensor_copy(tl["normwB"][:], tl["psumD"][:, 0:64]), inc="dvS")
    prog.barrier()
    for g in range(NGRP):
        for i in range(48):
            prog.dma("sync", tl["qk_g"][i][:],
                     bass.AP(tl["qkvhat_scr"].tensor, 128 * i * T + 512 * g, [[T, 128], [1, 512]]),
                     inc="grp_ld")
        for i in range(16):
            prog.dma("sync", tl["sigz_g"][i][:],
                     bass.AP(tl["sigz_scr"].tensor, 128 * i * T + 512 * g, [[T, 128], [1, 512]]),
                     inc="grp_ld")
        n = prog.n("grp_ld")
        prog.wait("tensor", "grp_ld", n)
        prog.wait("vector", "grp_ld", n)
        for lc in range(CPG):
            emit_chunk(prog, tl, g * CPG + lc, lc)
        emit_wout_group(prog, tl, g)
        prog.barrier(("tensor", "vector", "scalar", "sync", "gpsimd"))


def emit_chunk(prog, tl, cidx, lc):
    nc = prog.nc
    cs = tl["consts"]
    ts = slice(lc * C, lc * C + C)
    pct = tl["pct"]
    ws = tl["ws"]
    IA = tl["ident_s"][0:64, 0:64]
    IB = tl["ident_s"][64:128, 64:128]
    k_t = tl["qk_g"]
    kof, vof = 16, 32
    psm = tl["psum_sm"]
    pse = tl["psum_e"]
    ppw = [tl["psumA"], tl["psumB"], tl["psumC"], tl["psumD"]]

    def mm(fn, inc=None):
        return prog.op("tensor", fn, inc=inc)

    def vv(fn, inc=None):
        return prog.op("vector", fn, inc=inc)

    def ss(fn, inc=None):
        return prog.op("scalar", fn, inc=inc)

    prog.wait("sync", "mmS", prog.n("mmS"))
    prog.wait("sync", "dvS", prog.n("dvS"))

    def pct_src(name):
        return bass.AP(tl["stats_scr"][name].tensor, cidx * C, [[2 * T, 16], [T, 2], [1, 64]])

    def pc_src(name, rep):
        if rep:
            return bass.AP(tl["stats_scr"][name].tensor, cidx * C + C - 1,
                           [[T, 2], [0, 64], [2 * T, 16], [1, 1]])
        return bass.AP(tl["stats_scr"][name].tensor, cidx * C, [[T, 2], [1, 64], [2 * T, 16], [1, 1]])

    prog.dma("sync", pct[0:16, :], pct_src("lam"), inc="pc_ld")
    prog.dma("sync", pct[16:32, :], pct_src("lamb"), inc="pc_ld")
    pcT = tl["pcT_in"]
    for si, (nm, rep) in enumerate((("negbg", False), ("beta", False), ("gam", False),
                                    ("w", False), ("gamC", True))):
        srcnm = "gam" if nm == "gamC" else nm
        if rep:
            ap = bass.AP(tl["stats_scr"][srcnm].tensor, cidx * C + C - 1,
                         [[2 * T, 16], [T, 2], [0, 64], [1, 1]])
        else:
            ap = bass.AP(tl["stats_scr"][srcnm].tensor, cidx * C, [[2 * T, 16], [T, 2], [1, 64]])
        if rep:
            prog.dma("sync", pcT[16 * si:16 * (si + 1), :].rearrange("p (a b) -> p a b", b=1), ap,
                     inc="pc_ld")
        else:
            prog.dma("sync", pcT[16 * si:16 * (si + 1), :], ap, inc="pc_ld")
    npc = prog.n("pc_ld")
    prog.wait("tensor", "pc_ld", npc)
    prog.wait("tensor", "dvS", prog.n("dvS"))
    prog.wait("tensor", "st4ev", prog.n("st4ev"))
    nmm0 = prog.op("tensor", lambda e: e.matmul(tl["psumA"][:, 384:464], pcT[:],
                                                cs["identF_s"][0:80, 0:80], start=True, stop=True),
                   inc="mmS")
    prog.wait("vector", "mmS", nmm0)
    npcv = prog.op("vector", lambda e: e.tensor_copy(tl["pcall"][:], tl["psumA"][:, 384:464]),
                   inc="dvS")
    prog.wait("tensor", "dvS", npcv)
    prog.wait("vector", "pc_ld", npc)
    pcall = tl["pcall"]
    pcs = {nm: pcall[:, 16 * si:16 * (si + 1)]
           for si, nm in enumerate(("negbg", "beta", "gam", "w", "gamC"))}

    st1_dv = [0] * NPAIR
    st1_ev = [0] * NPAIR
    lamblk = tl["lam_w"][:, 64 * cidx:64 * cidx + 64]
    lambblk = tl["lamb_w"][:, 64 * cidx:64 * cidx + 64]

    for p in range(NPAIR):
        kt, qt, vt = k_t[kof + p], k_t[p], k_t[vof + p]
        KT_A, KT_B = kt[0:64, ts], kt[64:128, ts]
        QT_A, QT_B = qt[0:64, ts], qt[64:128, ts]
        w = ws[p]
        bank1 = psm if p % 2 == 0 else pse
        pg = bank1[:, 0:128]
        pv = bank1[:, 320:448]
        pe = bank1[:, 128:320]
        if p >= 2:
            prog.wait("tensor", "st1ev", st1_ev[p - 2])
        mm(lambda e, pg=pg, A=KT_A: e.matmul(pg[0:64, 0:64], A, A, start=True, stop=True))
        mm(lambda e, pg=pg, A=KT_A, Q=QT_A: e.matmul(pg[0:64, 64:128], A, Q, start=True, stop=True))
        mm(lambda e, pg=pg, B=KT_B: e.matmul(pg[64:128, 0:64], B, B, start=True, stop=True))
        mm(lambda e, pg=pg, B=KT_B, Q=QT_B: e.matmul(pg[64:128, 64:128], B, Q, start=True, stop=True))
        sel = cs["selP_s"][:, 128 * p:128 * (p + 1)]
        seln = cs["selPn_s"][:, 128 * p:128 * (p + 1)]
        ohn = cs["ohLn_s"][:, 64 * p:64 * (p + 1)]
        ohb = cs["ohLB_s"][:, 64 * p:64 * (p + 1)]
        mm(lambda e, pe=pe, s=sel, b=lambblk: e.matmul(pe[:, 0:64], s, b, start=True, stop=False))
        mm(lambda e, pe=pe, o=ohn: e.matmul(pe[:, 0:64], pct[:], o, start=False, stop=True))
        mm(lambda e, pe=pe, s=sel, b=lamblk: e.matmul(pe[:, 64:128], s, b, start=True, stop=False))
        mm(lambda e, pe=pe, o=ohn: e.matmul(pe[:, 64:128], pct[:], o, start=False, stop=True))
        mm(lambda e, pe=pe, s=seln, b=lamblk: e.matmul(pe[:, 128:192], s, b, start=True, stop=False))
        mm(lambda e, pe=pe, o=ohb: e.matmul(pe[:, 128:192], pct[:], o, start=False, stop=True))
        mm(lambda e, pv=pv, vt=vt: e.matmul(pv[0:64, 0:64], vt[0:64, ts], IA, start=True, stop=True))
        mm(lambda e, pv=pv, vt=vt: e.matmul(pv[64:128, 0:64], vt[64:128, ts], IB, start=True, stop=True))
        mm(lambda e, pv=pv, kt=kt: e.matmul(pv[0:64, 64:128], kt[0:64, ts], IA, start=True, stop=True))
        nmm = mm(lambda e, pv=pv, kt=kt: e.matmul(pv[64:128, 64:128], kt[64:128, ts], IB,
                                                  start=True, stop=True), inc="mmS")
        prog.wait("vector", "mmS", nmm)
        vv(lambda e, w=w, pe=pe: e.tensor_add(w["e1f"][:], pe[:, 0:64], cs["mask_E1_s"][:]))
        vv(lambda e, w=w, pe=pe: e.tensor_add(w["e2f"][:], pe[:, 64:128], cs["mask_E2_s"][:]))
        nv = vv(lambda e, w=w, pe=pe: e.tensor_add(w["e1tf"][:], pe[:, 128:192], cs["mask_E1T_s"][:]),
                inc="dvS")
        prog.wait("scalar", "dvS", nv)
        ss(lambda e, w=w: e.activation(w["E1"][:], w["e1f"][:], AF.Exp))
        ss(lambda e, w=w: e.activation(w["E2"][:], w["e2f"][:], AF.Exp))
        na = ss(lambda e, w=w: e.activation(w["E1T"][:], w["e1tf"][:], AF.Exp), inc="acS")
        prog.wait("vector", "acS", na)
        vv(lambda e, w=w, pg=pg: e.tensor_mul(w["NT"][:], pg[:, 0:64], w["E1"][:]))
        vv(lambda e, w=w, pg=pg: e.tensor_mul(w["NTT"][:], pg[:, 0:64], w["E1T"][:]))
        vv(lambda e, w=w, pg=pg: e.tensor_mul(w["G"][:], pg[:, 64:128], w["E2"][:]))
        pbeta = pcs["beta"][:, p:p + 1]
        pw_ = pcs["w"][:, p:p + 1]
        vv(lambda e, w=w, pv=pv, s=pbeta: e.tensor_scalar_mul(w["bV"][:], pv[:, 0:64], s))
        nw = vv(lambda e, w=w, pv=pv, s=pw_: e.tensor_scalar_mul(w["W"][:], pv[:, 64:128], s),
                inc="dvS")
        prog.wait("vector", "dvS", nw)
        st1_dv[p] = prog.op("vector",
                            lambda e, w=w: e.tensor_sub(w["PPT"][:], cs["eye2_s"][:], w["NT"][:]),
                            inc="dvS")
        st1_ev[p] = prog.milestone("vector", "dvS", "st1ev")

    seq = [("P2", "NTT", "NT"), ("P2T", "NT", "NTT"), ("P4", "P2T", "P2"), ("P4T", "P2", "P2T"),
           ("P8", "P4T", "P4"), ("P8T", "P4", "P4T"), ("P16", "P8T", "P8"), ("P16T", "P8", "P8T"),
           ("P32T", "P16", "P16T")]
    sq_ev = {}
    bank_free = {}
    for si, (dst, lh, rh) in enumerate(seq):
        for p in range(NPAIR):
            w = ws[p]
            bi = p % 4
            bank = ppw[bi][:, 0:64]
            if bi in bank_free:
                prog.wait("tensor", "sq_ev", bank_free[bi])
            if si <= 1:
                prog.wait("tensor", "dvS", st1_dv[p])
            else:
                prog.wait("tensor", "sq_ev", sq_ev[(si - 1, p)])
            mm(lambda e, b=bank, w=w, lh=lh, rh=rh: e.matmul(b[0:64, :], w[lh][0:64, :], w[rh][0:64, :],
                                                             start=True, stop=True))
            nmm = mm(lambda e, b=bank, w=w, lh=lh, rh=rh: e.matmul(b[64:128, :], w[lh][64:128, :],
                                                                   w[rh][64:128, :], start=True, stop=True),
                     inc="mmS")
            prog.wait("vector", "mmS", nmm)
            n = vv(lambda e, w=w, b=bank, dst=dst: e.tensor_copy(w[dst][:], b[:]), inc="sq_ev")
            sq_ev[(si, p)] = n
            bank_free[bi] = n

    pr_ev = dict((p, 0) for p in range(NPAIR))
    bank_free3 = {}
    for fi, fk in enumerate(("P2T", "P4T", "P8T", "P16T", "P32T")):
        for p in range(NPAIR):
            w = ws[p]
            bi = p % 4
            bank = ppw[bi][:, 64:128]
            if bi in bank_free3:
                prog.wait("tensor", "pr_ev", bank_free3[bi])
            prog.wait("tensor", "sq_ev", sq_ev[({0: 1, 1: 3, 2: 5, 3: 7, 4: 8}[fi], p)])
            if fi == 0:
                prog.wait("tensor", "dvS", st1_dv[p])
            else:
                prog.wait("tensor", "pr_ev", pr_ev[p])
            mm(lambda e, b=bank, w=w, fk=fk: e.matmul(b[0:64, :], w[fk][0:64, :], w["PPT"][0:64, :],
                                                      start=True, stop=True))
            nmm = mm(lambda e, b=bank, w=w, fk=fk: e.matmul(b[64:128, :], w[fk][64:128, :],
                                                            w["PPT"][64:128, :], start=True, stop=True),
                     inc="mmS")
            prog.wait("vector", "mmS", nmm)
            if fi == 0:
                prog.wait("vector", "dvS", st1_dv[p])
            else:
                prog.wait("vector", "pr_ev", pr_ev[p])
            n = vv(lambda e, w=w, b=bank: e.tensor_add(w["PPT"][:], b[:], w["PPT"][:]), inc="pr_ev")
            pr_ev[p] = n
            bank_free3[bi] = n

    st4_free = {}
    for w4 in range(4):
        pw = list(range(4 * w4, 4 * w4 + 4))
        pc_n = {}
        dv_n = {}
        mm_n = {}
        for p in pw:
            kt, qt = k_t[kof + p], k_t[p]
            KT_A, KT_B = kt[0:64, ts], kt[64:128, ts]
            QT_A, QT_B = qt[0:64, ts], qt[64:128, ts]
            Sb = tl["state_b"][p]
            bank = ppw[p % 4]
            pcn = bank[:, 128:192]
            pcq = bank[:, 192:256]
            if (p % 4) in st4_free:
                prog.wait("tensor", "st4ev", st4_free[p % 4])
            prog.wait("tensor", "dvS", tl["state_dv"][p])
            mm(lambda e, A=KT_A, S=Sb, pcn=pcn: e.matmul(pcn[0:64, :], A, S[0:64, :], start=True, stop=True))
            mm(lambda e, B=KT_B, S=Sb, pcn=pcn: e.matmul(pcn[64:128, :], B, S[64:128, :], start=True, stop=True))
            mm(lambda e, A=QT_A, S=Sb, pcq=pcq: e.matmul(pcq[0:64, :], A, S[0:64, :], start=True, stop=True))
            mm_n[p] = mm(lambda e, B=QT_B, S=Sb, pcq=pcq: e.matmul(pcq[64:128, :], B, S[64:128, :],
                                                                   start=True, stop=True), inc="mmS")
        for p in pw:
            w = ws[p]
            bank = ppw[p % 4]
            pcn = bank[:, 128:192]
            pcq = bank[:, 192:256]
            pnegbg = pcs["negbg"][:, p:p + 1]
            pgam = pcs["gam"][:, p:p + 1]
            prog.wait("vector", "mmS", mm_n[p])
            prog.wait("vector", "dvS", st1_dv[p])
            vv(lambda e, w=w, pcn=pcn, s=pnegbg: e.scalar_tensor_tensor(w["c"][:], pcn[:], s,
                                                                        w["bV"][:], ALU.mult, ALU.add))
            dv_n[p] = vv(lambda e, w=w, pcq=pcq, s=pgam: e.tensor_scalar_mul(w["CSQ"][:], pcq[:], s),
                         inc="dvS")
        for p in pw:
            w = ws[p]
            bank = ppw[p % 4]
            pdl = bank[:, 256:320]
            prog.wait("tensor", "dvS", dv_n[p])
            prog.wait("tensor", "pr_ev", pr_ev[p])
            mm(lambda e, w=w, pdl=pdl: e.matmul(pdl[0:64, :], w["PPT"][0:64, :], w["c"][0:64, :],
                                                start=True, stop=True))
            mm_n[p] = mm(lambda e, w=w, pdl=pdl: e.matmul(pdl[64:128, :], w["PPT"][64:128, :],
                                                          w["c"][64:128, :], start=True, stop=True),
                         inc="mmS")
        for p in pw:
            w = ws[p]
            bank = ppw[p % 4]
            pdl = bank[:, 256:320]
            prog.wait("vector", "mmS", mm_n[p])
            dv_n[p] = vv(lambda e, w=w, pdl=pdl: e.tensor_copy(w["delta"][:], pdl[:]), inc="dvS")
        for p in pw:
            w = ws[p]
            bank = ppw[p % 4]
            pom = bank[:, 320:384]
            psu = bank[:, 384:448]
            prog.wait("tensor", "dvS", dv_n[p])
            mm(lambda e, w=w, pom=pom: e.matmul(pom[0:64, :], w["G"][0:64, :], w["delta"][0:64, :],
                                                start=True, stop=True))
            mm(lambda e, w=w, pom=pom: e.matmul(pom[64:128, :], w["G"][64:128, :], w["delta"][64:128, :],
                                                start=True, stop=True))
            mm(lambda e, w=w, psu=psu: e.matmul(psu[0:64, :], w["W"][0:64, :], w["delta"][0:64, :],
                                                start=True, stop=True))
            mm_n[p] = mm(lambda e, w=w, psu=psu: e.matmul(psu[64:128, :], w["W"][64:128, :],
                                                          w["delta"][64:128, :], start=True, stop=True),
                         inc="mmS")
        for p in pw:
            w = ws[p]
            S = tl["state"][p]
            Sb = tl["state_b"][p]
            bank = ppw[p % 4]
            pom = bank[:, 320:384]
            psu = bank[:, 384:448]
            pgamC = pcs["gamC"][:, p:p + 1]
            prog.wait("vector", "mmS", mm_n[p])
            vv(lambda e, w=w, pom=pom: e.tensor_add(w["orow"][:], pom[:], w["CSQ"][:]))
            nst = vv(lambda e, S=S, psu=psu, s=pgamC: e.scalar_tensor_tensor(S[:], S[:], s,
                                                                             psu[:], ALU.mult, ALU.add),
                     inc="dvS")
            prog.wait("vector", "dvS", nst)
            ns = vv(lambda e, S=S, Sb=Sb: e.tensor_copy(Sb[:], S[:]), inc="dvS")
            tl["state_dv"][p] = ns
            dv_n[p] = ns
        for p in pw:
            w = ws[p]
            prog.wait("scalar", "dvS", dv_n[p])
            pc_n[p] = ss(lambda e, w=w: e.activation(w["osq"][:], w["orow"][:], AF.Square), inc="acS")
        for p in pw:
            w = ws[p]
            prog.wait("vector", "acS", pc_n[p])
            dv_n[p] = vv(lambda e, w=w: e.reduce_sum(w["var"][:], w["osq"][:], axis=AX.X), inc="dvS")
        for p in pw:
            w = ws[p]
            prog.wait("scalar", "dvS", dv_n[p])
            pc_n[p] = ss(lambda e, w=w: e.activation(w["rstd"][:], w["var"][:], AF.Sqrt,
                                                     scale=1.0 / 64.0, bias=EPS_DN), inc="acS")
        for p in pw:
            w = ws[p]
            prog.wait("vector", "acS", pc_n[p])
            nv = vv(lambda e, w=w: e.reciprocal(w["rinv"][:], w["rstd"][:]), inc="dvS")
            prog.wait("vector", "dvS", nv)
            dv_n[p] = vv(lambda e, w=w: e.scalar_tensor_tensor(w["on"][:], w["orow"][:], w["rinv"][:, 0:1],
                                                               tl["normwB"][:], ALU.mult, ALU.mult),
                         inc="dvS")
        for p in pw:
            w = ws[p]
            bank = ppw[p % 4]
            pof = bank[:, 448:512]
            prog.wait("tensor", "dvS", dv_n[p])
            mm(lambda e, w=w, pof=pof: e.matmul(pof[0:64, :], w["on"][0:64, :], IA, start=True, stop=True))
            mm_n[p] = mm(lambda e, w=w, pof=pof: e.matmul(pof[64:128, :], w["on"][64:128, :], IB,
                                                          start=True, stop=True), inc="mmS")
        for p in pw:
            bank = ppw[p % 4]
            pof = bank[:, 448:512]
            prog.wait("vector", "mmS", mm_n[p])
            n4 = vv(lambda e, p=p, pof=pof: e.tensor_mul(tl["o_g"][p][:, ts], pof[:],
                                                         tl["sigz_g"][p][:, ts]), inc="st4ev")
            st4_free[p % 4] = n4


def emit_wout_group(prog, tl, g):
    nc = prog.nc
    ps = [tl["psumA"], tl["psumB"]]
    xb = tl["xbuf"]
    og = tl["o_g"]
    x_in = tl["x_in"]
    x1a = tl["x1a_scr"]
    wop = tl["Woutp"]
    xl = {}
    for dt in range(4):
        xl[dt] = prog.dma("sync", xb[dt][:],
                          bass.AP(x_in.tensor, 128 * dt * T + 512 * g, [[T, 128], [1, 512]]),
                          inc="x_ld")
    prog.wait("tensor", "st4ev", prog.n("st4ev"))
    wo_ev = {}
    for dt in range(16):
        buf = tl["wbuf"][dt % 3]
        if dt >= 3:
            prog.wait("gpsimd", "wofree", prog.n("wofree") - 2)
        prog.dma("gpsimd", buf[:].rearrange("p (a b) -> p a b", b=128),
                 bass.AP(wop.tensor, dt * 128, [[D, 128], [128 * D, 16], [1, 128]]),
                 inc="wodma")
        prog.wait("tensor", "wodma", prog.n("wodma"))
        bank = ps[dt % 2]
        if dt >= 2:
            prog.wait("tensor", "dvS", wo_ev[dt - 2])
        for kt in range(16):
            prog.op("tensor", lambda e, b=bank, buf=buf, kt=kt, og=og:
                    e.matmul(b[:], buf[:, 128 * kt:128 * (kt + 1)], og[kt][:],
                             start=(kt == 0), stop=(kt == 15)),
                    inc="mmS" if kt == 15 else None)
        nmm = prog.n("mmS")
        prog.milestone("tensor", "mmS", "wofree")
        prog.wait("vector", "mmS", nmm)
        prog.wait("vector", "x_ld", xl[dt])
        nv = prog.op("vector", lambda e, b=bank, t_=xb[dt % 4]: e.tensor_add(
            t_[:], b[:], t_[:]), inc="dvS")
        wo_ev[dt] = nv
        prog.wait("sync", "dvS", nv)
        prog.dma("sync", bass.AP(x1a.tensor, 128 * dt * T + 512 * g, [[T, 128], [1, 512]]),
                 xb[dt % 4][:], inc="x1a_st")
        if dt + 4 < 16:
            prog.wait("sync", "x1a_st", prog.n("x1a_st"))
            xl[dt + 4] = prog.dma("sync", xb[dt % 4][:],
                                  bass.AP(x_in.tensor, 128 * (dt + 4) * T + 512 * g,
                                          [[T, 128], [1, 512]]), inc="x_ld")


def emit_ln(prog, tl, src_ap, xn_dst, label):
    nc = prog.nc
    pn = tl["psumC"]
    pb = tl["psumD"]
    lr = tl["lnrow"]
    xb16 = tl["xb16"]
    sqb = tl["sqb"]
    onescb = tl["ones_col_b"]
    onesrb = tl["ones_row_b"]
    ev_g = {}
    for g in range(NGRP):
        if g >= 1:
            prog.wait("gpsimd", "dvS", ev_g[g - 1])
        for kt in range(16):
            prog.dma("gpsimd", xb16[kt][:],
                     bass.AP(src_ap.tensor, 128 * kt * T + 512 * g, [[T, 128], [1, 512]]),
                     inc=label + "ld")
        nld = prog.n(label + "ld")
        prog.wait("scalar", label + "ld", nld)
        prog.wait("tensor", label + "ld", nld)
        if g >= 1:
            prog.wait("tensor", "dvS", ev_g[g - 1])
        for kt in range(16):
            prog.op("scalar", lambda e, kt=kt: e.activation(sqb[kt][:], xb16[kt][:], AF.Square),
                    inc="acS" if kt == 15 else None)
        na = prog.n("acS")
        for kt in range(16):
            prog.op("tensor", lambda e, kt=kt: e.matmul(pn[0:1, 0:512], onescb[:],
                                                        xb16[kt][:], start=(kt == 0), stop=(kt == 15)))
        prog.wait("tensor", "acS", na)
        for kt in range(16):
            prog.op("tensor", lambda e, kt=kt: e.matmul(pb[0:1, 0:512], onescb[:],
                                                        sqb[kt][:], start=(kt == 0), stop=(kt == 15)),
                    inc="mmS" if kt == 15 else None)
        nmm = prog.n("mmS")
        prog.wait("vector", "mmS", nmm)

        def vv(fn, inc="dvS"):
            return prog.op("vector", fn, inc=inc)

        def ss(fn, inc="acS"):
            return prog.op("scalar", fn, inc=inc)

        vv(lambda e: e.tensor_scalar_mul(lr[0][:], pn[0:1, 0:512], 1.0 / D))
        nv = vv(lambda e: e.tensor_scalar_mul(lr[1][:], pb[0:1, 0:512], 1.0 / D))
        prog.wait("vector", "dvS", nv)
        nv = vv(lambda e: e.tensor_mul(lr[2][:], lr[0][:], lr[0][:]))
        prog.wait("vector", "dvS", nv)
        nv = vv(lambda e: e.tensor_sub(lr[1][:], lr[1][:], lr[2][:]))
        prog.wait("scalar", "dvS", nv)
        na = ss(lambda e: e.activation(lr[2][:], lr[1][:], AF.Sqrt, bias=EPS_LN))
        prog.wait("vector", "acS", na)
        nv = vv(lambda e: e.reciprocal(lr[1][:], lr[2][:]))
        prog.wait("vector", "dvS", nv)
        nv = vv(lambda e: e.tensor_mul(lr[2][:], lr[0][:], lr[1][:]))
        prog.wait("vector", "dvS", nv)
        nv = vv(lambda e: e.tensor_copy(lr[3][:], lr[1][:]))
        prog.wait("vector", "dvS", nv)
        nv = vv(lambda e: e.tensor_copy(lr[4][:], lr[2][:]))
        prog.wait("tensor", "dvS", nv)
        prog.op("tensor", lambda e: e.matmul(pn[:, 0:512], onesrb[:], lr[3][:],
                                             start=True, stop=True))
        nmm = prog.op("tensor", lambda e: e.matmul(pb[:, 0:512], onesrb[:], lr[4][:],
                                                   start=True, stop=True), inc="mmS")
        prog.wait("vector", "mmS", nmm)
        for kt in range(16):
            vv(lambda e, kt=kt: e.tensor_mul(sqb[kt][:], xb16[kt][:], pn[:, 0:512]),
               inc=None)
        nv = vv(lambda e: e.tensor_copy(lr[0][:], lr[0][:]))
        prog.wait("vector", "dvS", nv)
        last = None
        for kt in range(16):
            last = vv(lambda e, kt=kt, g=g: e.tensor_sub(
                xn_dst[:, 2048 * kt + 512 * g:2048 * kt + 512 * (g + 1)],
                sqb[kt][:], pb[:, 0:512]), inc="dvS")
        ev_g[g] = last
    prog.barrier(("tensor", "vector", "scalar", "gpsimd"))


def emit_stats(prog, tl):
    nc = prog.nc
    xn = tl["xn"]
    ps = [tl["psumA"], tl["psumB"]]
    st = tl["st"]
    stg = tl["st_stage"]
    ev = {}
    for wi, (which, wt) in enumerate((("b", tl["wb"]), ("a", tl["wa"]))):
        for g in range(NGRP):
            i = wi * NGRP + g
            bank = ps[i % 2]
            if i >= 2:
                prog.wait("tensor", "sb_mv", 16 * (i - 1))
            for kt in range(16):
                prog.op("tensor", lambda e, b=bank, w=wt, kt=kt, g=g:
                        e.matmul(b[0:32, 0:512], w[:, 32 * kt:32 * (kt + 1)],
                                 xn[:, 2048 * kt + 512 * g: 2048 * kt + 512 * (g + 1)],
                                 start=(kt == 0), stop=(kt == 15)),
                        inc="mmS" if kt == 15 else None)
            n = prog.n("mmS")
            prog.wait("vector", "mmS", n)
            prog.wait("vector", "sb_mv", 16 * i)
            nv = prog.op("vector", lambda e, b=bank: e.tensor_copy(stg[:], b[0:32, 0:512]), inc="dvS")
            prog.wait("sync", "dvS", nv)
            dst = st["braw"] if which == "b" else st["araw"]
            prog.dma("sync", dst[32 * g:32 * (g + 1), :], stg[:], inc="sb_mv")
    prog.wait("scalar", "sb_mv", prog.n("sb_mv"))
    prog.wait("vector", "sb_mv", prog.n("sb_mv"))

    def vv(fn, inc="dvS"):
        return prog.op("vector", fn, inc=inc)

    def ss(fn, inc="acS"):
        return prog.op("scalar", fn, inc=inc)

    braw, araw = st["braw"], st["araw"]
    dtb = tl["dt_bias"][:, 0:1]
    na = ss(lambda e: e.activation(st["t0"][:], araw[:], AF.Abs, bias=dtb))
    prog.wait("scalar", "acS", na)
    na = ss(lambda e: e.activation(st["t1"][:], st["t0"][:], AF.Exp, scale=-1.0))
    prog.wait("scalar", "acS", na)
    na = ss(lambda e: e.activation(st["t1"][:], st["t1"][:], AF.Ln, bias=1.0))
    prog.wait("scalar", "acS", na)
    na = ss(lambda e: e.activation(st["t2"][:], araw[:], AF.Relu, bias=dtb))
    prog.wait("vector", "acS", na)
    nv = vv(lambda e: e.tensor_add(st["sp"][:], st["t1"][:], st["t2"][:]))
    prog.wait("scalar", "dvS", nv)
    na = ss(lambda e: e.activation(tl["negexpA"][:], tl["A_log"][:], AF.Exp))
    prog.wait("vector", "acS", na)
    prog.wait("vector", "dvS", nv)
    nea = tl["negexpA"][:, 0:1]
    nv = vv(lambda e, s=nea: e.tensor_scalar(st["lam"][:], st["sp"][:], s, -1.0, ALU.mult, ALU.mult))
    l3 = st["lam"][:].rearrange("p (c k) -> p c k", k=64)
    for sh in (1, 2, 4, 8, 16, 32):
        prog.wait("vector", "dvS", nv)
        nv = vv(lambda e, sh=sh, l3=l3: e.tensor_add(l3[:, :, sh:64], l3[:, :, sh:64], l3[:, :, 0:64 - sh]))
    ss(lambda e: e.activation(st["beta"][:], braw[:], AF.Sigmoid))
    na = ss(lambda e: e.activation(st["t0"][:], braw[:], AF.Abs))
    prog.wait("scalar", "acS", na)
    na = ss(lambda e: e.activation(st["t1"][:], st["t0"][:], AF.Exp, scale=-1.0))
    prog.wait("scalar", "acS", na)
    na = ss(lambda e: e.activation(st["t1"][:], st["t1"][:], AF.Ln, bias=1.0))
    prog.wait("scalar", "acS", na)
    prog.wait("scalar", "dvS", prog.n("dvS"))
    na = ss(lambda e: e.activation(st["t2"][:], braw[:], AF.Relu, scale=-1.0))
    prog.wait("vector", "acS", na)
    nv2 = vv(lambda e: e.tensor_add(st["t2"][:], st["t1"][:], st["t2"][:]))
    prog.wait("vector", "dvS", nv2)
    nv2 = vv(lambda e: e.tensor_sub(st["lamb"][:], st["lam"][:], st["t2"][:]))
    prog.wait("scalar", "dvS", prog.n("dvS"))
    na = ss(lambda e: e.activation(st["gam"][:], st["lam"][:], AF.Exp))
    prog.wait("vector", "acS", na)
    nv3 = vv(lambda e: e.tensor_mul(st["t0"][:], st["beta"][:], st["gam"][:]))
    prog.wait("vector", "dvS", nv3)
    nv3 = vv(lambda e: e.tensor_scalar_mul(st["negbg"][:], st["t0"][:], -1.0))
    _lamh = st["lam"] if not hasattr(st["lam"], "tensor") else st["lam"].tensor
    lamCsrc = bass.AP(_lamh, 63, [[512, 128], [64, 8], [0, 64]])
    nv4 = vv(lambda e, s=lamCsrc, l3=l3: e.tensor_tensor(
        st["t1"][:].rearrange("p (c k) -> p c k", k=64), s, l3[:, :, :], ALU.subtract))
    prog.wait("scalar", "dvS", nv4)
    na = ss(lambda e: e.activation(st["w"][:], st["t1"][:], AF.Exp))
    prog.wait("sync", "acS", na)
    prog.wait("sync", "dvS", prog.n("dvS"))
    for nm in ("lam", "lamb", "beta", "negbg", "gam", "w"):
        for tb in range(4):
            prog.dma("sync", bass.AP(tl["stats_scr"][nm].tensor, tb * 512, [[T, 32], [1, 512]]),
                     st[nm][32 * tb:32 * (tb + 1), :], inc="st_scr")
    prog.wait("sync", "st_scr", prog.n("st_scr"))
    prog.dma("sync", tl["lam_w"][:], tl["stats_scr"]["lam"][:], inc="st_scr")
    prog.dma("sync", tl["lamb_w"][:], tl["stats_scr"]["lamb"][:], inc="st_scr")
    prog.wait("tensor", "st_scr", prog.n("st_scr"))
    prog.barrier(("tensor", "vector", "scalar", "sync"))


def emit_qkvz(prog, tl):
    nc = prog.nc
    xn = tl["xn"]
    wbuf = tl["wbuf"]
    ps = [tl["psumA"], tl["psumB"]]
    pn = tl["psumC"]
    pb = tl["psumD"]
    raw = tl["qkraw"]
    hat = tl["qkhat"]
    sq = tl["sqtile"]
    prog.op("vector", lambda e: e.memset(raw[:, 0:3], 0.0), inc="dvS")
    prog.wait("tensor", "dvS", prog.n("dvS"))
    evac_g = {}
    for dt in range(64):
        sec = dt // 16
        buf = wbuf[dt % 3]
        if dt >= 3:
            prog.wait("gpsimd", "wfree", dt - 2)
        prog.dma("gpsimd", buf[:].rearrange("p (a b) -> p a b", b=128),
                 bass.AP(tl["Wqkvzp"].tensor, dt * 128, [[8192, 128], [128 * 8192, 16], [1, 128]]),
                 inc="wqdma")
        prog.wait("tensor", "wqdma", prog.n("wqdma"))
        nhs = prog.n("hat_st")
        prog.wait("vector", "hat_st", nhs)
        prog.wait("scalar", "hat_st", nhs)
        for g in range(NGRP):
            i = dt * NGRP + g
            bank = ps[i % 2]
            if i >= 2:
                prog.wait("tensor", "dvS", evac_g[i - 2])
            for kt in range(16):
                prog.op("tensor", lambda e, b=bank, buf=buf, kt=kt, g=g:
                        e.matmul(b[:], buf[:, 128 * kt:128 * (kt + 1)],
                                 xn[:, 2048 * kt + 512 * g: 2048 * kt + 512 * (g + 1)],
                                 start=(kt == 0), stop=(kt == 15)),
                        inc="mmS" if kt == 15 else None)
            n = prog.n("mmS")
            if g == NGRP - 1:
                prog.milestone("tensor", "mmS", "wfree")
            if sec == 3:
                prog.wait("scalar", "mmS", n)
                sN = prog.op("scalar", lambda e, b=bank, g=g:
                             e.activation(hat[:, 512 * g:512 * (g + 1)], b[:], AF.Sigmoid), inc="acS")
                evac_g[i] = prog.n("dvS")
                prog.wait("tensor", "acS", sN)
                prog.wait("sync", "acS", sN)
                prog.dma("sync", tl["sigz_scr"][128 * (dt - 48):128 * (dt - 47), 512 * g:512 * (g + 1)],
                         hat[:, 512 * g:512 * (g + 1)], inc="hat_st")
            else:
                prog.wait("vector", "mmS", n)
                evac_g[i] = prog.op("vector", lambda e, b=bank, g=g:
                                    e.tensor_copy(raw[:, 3 + 512 * g:3 + 512 * (g + 1)], b[:]), inc="dvS")
        if sec == 3:
            prog.barrier(("tensor", "vector", "scalar", "sync"))
            continue
        cwc = tl["convw_s"][:, 4 * dt:4 * dt + 4]
        nv = prog.n("dvS")
        prog.wait("vector", "dvS", nv)
        nv = prog.op("vector", lambda e, c=cwc: e.tensor_scalar_mul(hat[:], raw[:, 3:2051], c[:, 3:4]),
                     inc="dvS")
        for tap in (2, 1, 0):
            prog.wait("vector", "dvS", nv)
            nv = prog.op("vector", lambda e, tap=tap, c=cwc: e.scalar_tensor_tensor(
                hat[:], raw[:, tap:tap + 2048], c[:, tap:tap + 1], hat[:], ALU.mult, ALU.add), inc="dvS")
        if sec == 2:
            prog.wait("sync", "dvS", nv)
            prog.dma("sync", tl["qkvhat_scr"][128 * dt:128 * (dt + 1), :], hat[:], inc="hat_st")
        else:
            prog.wait("scalar", "dvS", nv)
            sN = prog.op("scalar", lambda e: e.activation(sq[:], hat[:], AF.Square), inc="acS")
            prog.wait("tensor", "acS", sN)
            for g in range(NGRP):
                gsl = slice(512 * g, 512 * (g + 1))
                nmm = prog.op("tensor", lambda e, gsl=gsl: e.matmul(
                    pn[0:2, 0:512], tl["bdones"][:], sq[:, gsl], start=True, stop=True), inc="mmS")
                prog.wait("scalar", "mmS", nmm)
                sN = prog.op("scalar", lambda e: e.activation(tl["nrm2"][:], pn[0:2, 0:512], AF.Sqrt,
                                                              bias=EPS_DN), inc="acS")
                prog.wait("vector", "acS", sN)
                nv = prog.op("vector", lambda e: e.reciprocal(tl["nrm"][:], tl["nrm2"][:]), inc="dvS")
                if sec == 0:
                    prog.wait("vector", "dvS", nv)
                    nv = prog.op("vector", lambda e: e.tensor_scalar_mul(tl["nrm"][:], tl["nrm"][:], 0.125),
                                 inc="dvS")
                prog.wait("tensor", "dvS", nv)
                nmm = prog.op("tensor", lambda e: e.matmul(pb[:, 0:512], tl["ind2"][:], tl["nrm"][:],
                                                           start=True, stop=True), inc="mmS")
                prog.wait("vector", "mmS", nmm)
                nv = prog.op("vector", lambda e, gsl=gsl: e.tensor_mul(hat[:, gsl], hat[:, gsl], pb[:, 0:512]),
                             inc="dvS")
                prog.wait("tensor", "dvS", nv)
            prog.wait("sync", "dvS", prog.n("dvS"))
            prog.dma("sync", tl["qkvhat_scr"][128 * dt:128 * (dt + 1), :], hat[:], inc="hat_st")
        prog.barrier(("tensor", "vector", "scalar", "sync"))
    prog.wait("sync", "hat_st", prog.n("hat_st"))
    prog.barrier()


def emit_ffn(prog, tl, cfg):
    nc = prog.nc
    hid = cfg["hid"]
    nht = hid // 128
    xn = tl["xn"]
    ps = [tl["psumA"], tl["psumB"], tl["psumC"], tl["psumD"]]
    gat = tl["qkhat"]
    sil = tl["sqtile"]
    lbl = cfg["lbl"]
    ev = {}
    for ht in range(nht):
        buf = tl["wbuf"][ht % 3]
        buf2 = tl["wbuf2"][ht % 3]
        if ht >= 3:
            prog.wait("gpsimd", lbl + "f1free", ht - 2)
        prog.dma("gpsimd", buf[:].rearrange("p (a b) -> p a b", b=128),
                 bass.AP(cfg["Wg"].tensor, ht * 128, [[hid, 128], [128 * hid, 16], [1, 128]]),
                 inc=lbl + "f1dma")
        prog.dma("gpsimd", buf2[:].rearrange("p (a b) -> p a b", b=128),
                 bass.AP(cfg["Wu"].tensor, ht * 128, [[hid, 128], [128 * hid, 16], [1, 128]]),
                 inc=lbl + "f1dma")
        prog.wait("tensor", lbl + "f1dma", prog.n(lbl + "f1dma"))
        for g in range(NGRP):
            i = ht * NGRP + g
            bg = ps[(2 * i) % 4]
            bu = ps[(2 * i + 1) % 4]
            qz = cfg.get("quant")
            if i >= 2:
                prog.wait("tensor", "dvS", ev[i - 2])
            for kt in range(16):
                prog.op("tensor", lambda e, b=bg, buf=buf, kt=kt, g=g:
                        e.matmul(b[:], buf[:, 128 * kt:128 * (kt + 1)],
                                 xn[:, 2048 * kt + 512 * g:2048 * kt + 512 * (g + 1)],
                                 start=(kt == 0), stop=(kt == 15 and not qz)))
            if qz:
                prog.op("tensor", lambda e, b=bg, g=g: e.matmul(
                    b[:], tl["neg128_row_b"][:], qz["rs"][:, 512 * g:512 * (g + 1)],
                    start=False, stop=True))
            for kt in range(16):
                prog.op("tensor", lambda e, b=bu, buf=buf2, kt=kt, g=g:
                        e.matmul(b[:], buf[:, 128 * kt:128 * (kt + 1)],
                                 xn[:, 2048 * kt + 512 * g:2048 * kt + 512 * (g + 1)],
                                 start=(kt == 0), stop=(kt == 15 and not qz)),
                        inc="mmS" if (kt == 15 and not qz) else None)
            if qz:
                prog.op("tensor", lambda e, b=bu, g=g: e.matmul(
                    b[:], tl["neg128_row_b"][:], qz["rs"][:, 512 * g:512 * (g + 1)],
                    start=False, stop=True), inc="mmS")
            if g == NGRP - 1:
                prog.milestone("tensor", "mmS", lbl + "f1free")
            nmm = prog.n("mmS")
            gsl = slice(512 * g, 512 * (g + 1))
            qz = cfg.get("quant")
            if qz:
                pass
            prog.wait("scalar", "mmS", nmm)
            if qz:
                sgc = qz["sg"][:, ht:ht + 1]
                suc = qz["su"][:, ht:ht + 1]
                na = prog.op("scalar", lambda e, b=bg, gsl=gsl, s=sgc: e.activation(
                    sil[:, gsl], b[:], AF.Silu, scale=s), inc="acS")
                prog.wait("vector", "acS", na)
                nv0 = prog.op("vector", lambda e, b=bu, gsl=gsl, s=suc: e.tensor_scalar_mul(
                    tl["upt"][:, gsl], b[:], s), inc="dvS")
                prog.wait("vector", "dvS", nv0)
                ev[i] = prog.op("vector", lambda e, gsl=gsl: e.tensor_mul(
                    gat[:, gsl], sil[:, gsl], tl["upt"][:, gsl]), inc="dvS")
            else:
                na = prog.op("scalar", lambda e, b=bg, gsl=gsl: e.activation(sil[:, gsl], b[:], AF.Silu),
                             inc="acS")
                prog.wait("vector", "acS", na)
                ev[i] = prog.op("vector", lambda e, b=bu, gsl=gsl: e.tensor_mul(gat[:, gsl], sil[:, gsl], b[:]),
                                inc="dvS")
        prog.wait("sync", "dvS", ev[ht * NGRP + NGRP - 1])
        prog.dma("sync", cfg["gated"][128 * ht:128 * (ht + 1), :], gat[:], inc=lbl + "f1st")
        prog.wait("vector", lbl + "f1st", prog.n(lbl + "f1st"))
        wdb = tl["wdprep"][ht % 2]
        if ht >= 2:
            prog.wait("gpsimd", lbl + "wdst", 16 * (ht - 1))
        prog.dma("gpsimd", wdb[:], cfg["Wd"][128 * ht:128 * (ht + 1), :], inc=lbl + "wddma")
        prog.wait("sync", lbl + "wddma", prog.n(lbl + "wddma"))
        prog.dma("sync", cfg["wd_scr"][128 * ht:128 * (ht + 1), :], wdb[:], inc=lbl + "wdst")
    prog.wait("gpsimd", lbl + "wdst", prog.n(lbl + "wdst"))
    prog.barrier(("tensor", "vector", "scalar", "sync", "gpsimd"))


def emit_ffn_f2(prog, tl, cfg):
    nc = prog.nc
    hid = cfg["hid"]
    nht = hid // 128
    ps = [tl["psumA"], tl["psumB"], tl["psumC"], tl["psumD"]]
    lbl = cfg["lbl"]
    gatg = tl["gatg"]
    gwb = tl["gwb"]
    xb = tl["xbuf"]
    qz = cfg.get("quant")
    for g in range(NGRP):
        for i in range(nht):
            prog.dma("sync", gatg[i][:],
                     bass.AP(cfg["gated"].tensor, 128 * i * T + 512 * g, [[T, 128], [1, 512]]),
                     inc=lbl + "f2gld")
        for dtile in range(16):
            prog.dma("sync", xb[dtile][:],
                     bass.AP(cfg["res_in"].tensor, 128 * dtile * T + 512 * g, [[T, 128], [1, 512]]),
                     inc=lbl + "f2gld")
        n = prog.n(lbl + "f2gld")
        prog.wait("tensor", lbl + "f2gld", n)
        prog.wait("vector", lbl + "f2gld", n)
        if qz:
            for i in range(nht):
                prog.op("tensor", lambda e, i=i: e.matmul(
                    tl["psum_e"][0:1, 0:512], tl["ones_col_b"][:], gatg[i][:],
                    start=(i == 0), stop=(i == nht - 1)), inc="mmS" if i == nht - 1 else None)
            nms = prog.n("mmS")
            prog.wait("vector", "mmS", nms)
            nv = prog.op("vector", lambda e: e.tensor_copy(tl["gsum_b"][:], tl["psum_e"][0:1, 0:512]),
                         inc="dvS")
            prog.wait("tensor", "dvS", nv)
        ev2 = {}
        for dtile in range(16):
            wb1 = gwb[dtile % 2]
            if dtile >= 2:
                prog.wait("sync", lbl + "f2free", (g * 16 + dtile - 1))
            prog.dma("sync", wb1[:, 0:128 * nht].rearrange("p (a b) -> p a b", b=128),
                     bass.AP(cfg["wd_scr"].tensor, dtile * 128, [[D, 128], [128 * D, nht], [1, 128]]),
                     inc=lbl + "f2wld")
            prog.wait("tensor", lbl + "f2wld", prog.n(lbl + "f2wld"))
            bank = ps[dtile % 2]
            if dtile >= 2:
                prog.wait("tensor", "dvS", ev2[dtile - 2])
            for i in range(nht):
                prog.op("tensor", lambda e, b=bank, bs=wb1, iw=i, gg=gatg[i], i=i:
                        e.matmul(b[:], bs[:, 128 * iw:128 * (iw + 1)], gg[:],
                                 start=(i == 0), stop=(i == nht - 1 and not qz)),
                        inc="mmS" if (i == nht - 1 and not qz) else None)
            if qz:
                prog.op("tensor", lambda e, b=bank: e.matmul(
                    b[:], tl["neg128_row_b"][:], tl["gsum_b"][:], start=False, stop=True), inc="mmS")
            nmm = prog.n("mmS")
            prog.milestone("tensor", "mmS", lbl + "f2free")
            prog.wait("vector", "mmS", nmm)
            if qz:
                sdc = qz["sd"][:, dtile:dtile + 1]
                nv = prog.op("vector", lambda e, b=bank, t_=xb[dtile], s=sdc: e.scalar_tensor_tensor(
                    t_[:], b[:], s, t_[:], ALU.mult, ALU.add), inc="dvS")
            else:
                nv = prog.op("vector", lambda e, b=bank, t_=xb[dtile]: e.tensor_add(
                    t_[:], b[:], t_[:]), inc="dvS")
            ev2[dtile] = nv
            prog.wait("sync", "dvS", nv)
            prog.dma("sync", bass.AP(cfg["out"].tensor, 128 * dtile * T + 512 * g, [[T, 128], [1, 512]]),
                     xb[dtile][:], inc="fout")
        prog.barrier(("tensor", "vector", "sync"))
    prog.wait("gpsimd", "fout", prog.n("fout"))


CMAP = [("ident_s", "ident", "BF16", [128, 128]), ("identF_s", "identF", "F32", [128, 128]),
        ("sel4", "sel4_c", "BF16", [4, 128]), ("ones_row_f", "ones_row_fc", "F32", [1, 128]),
        ("ones128", "ones128_c", "BF16", [128, 128]),
        ("neg128_row_b", "neg128_row", "BF16", [1, 128]),
        ("eye2_s", "eye2", "BF16", [128, 64]),
        ("ones_col_b", "ones_col", "BF16", [128, 1]), ("ones_row_b", "ones_row", "BF16", [1, 128]),
        ("bdones", "bdones_c", "BF16", [128, 2]), ("ind2", "ind2_c", "F32", [2, 128]),
        ("selP_s", "selP", "F32", [32, NPAIR * 128]), ("selPn_s", "selPn", "F32", [32, NPAIR * 128]),
        ("ohLn_s", "ohLn", "F32", [32, NPAIR * 64]), ("ohLB_s", "ohLB", "F32", [32, NPAIR * 64]),
        ("mask_E1_s", "mask_E1", "F32", [128, 64]), ("mask_E2_s", "mask_E2", "F32", [128, 64]),
        ("mask_E1T_s", "mask_E1T", "F32", [128, 64])]


def l0_dram(nc):
    dram = {}
    dram["x_in"] = nc.dram_tensor("x_in", [D, T], F32, kind="ExternalInput").ap()
    specs = [("Wqkvzp", [D, 8192], F32), ("Wb", [D, 32], F32), ("Wa", [D, 32], F32),
             ("convw", [6144, 4], F32), ("A_log", [128, 1], F32), ("dt_bias", [128, 1], F32),
             ("normw", [1, 64], F32), ("Woutp", [D, D], F32),
             ("Wg0", [D, 8192], F32), ("Wu0", [D, 8192], F32), ("Wd0", [8192, D], F32)]
    for nm, shape, dt in specs:
        dram[nm] = nc.dram_tensor(nm, shape, dt, kind="ExternalInput").ap()
    for sname, dname, dt, shape in CMAP:
        dram[dname] = nc.dram_tensor(dname, shape, F32, kind="ExternalInput").ap()
    dram["x1_out"] = nc.dram_tensor("x1_out", [D, T], F32, kind="ExternalOutput").ap()
    return dram


def l0_scratch(nc, tag=""):
    scr = {}
    scr["qkvhat"] = nc.dram_tensor(f"qkvhat_scr{tag}", [6144, T], BF16, kind="Internal").ap()
    scr["sigz"] = nc.dram_tensor(f"sigz_scr{tag}", [D, T], BF16, kind="Internal").ap()
    scr["x1a"] = nc.dram_tensor(f"x1a_scr{tag}", [D, T], F32, kind="Internal").ap()
    scr["xn"] = nc.dram_tensor(f"xn_scr{tag}", [D, T], BF16, kind="Internal").ap()
    scr["gated"] = nc.dram_tensor(f"gated_scr{tag}", [8192, T], BF16, kind="Internal").ap()
    scr["wd_bf"] = nc.dram_tensor(f"wd_bf_scr{tag}", [8192, D], BF16, kind="Internal").ap()
    for nm in ("lam", "lamb", "beta", "negbg", "gam", "w"):
        scr[nm] = nc.dram_tensor(f"stat_{nm}{tag}", [32, T], F32, kind="Internal").ap()
    return scr


def alloc_common(nc, E, tl, dram):
    tl["consts"] = {}
    for sname, dname, dts, shape in CMAP:
        h = E(nc.sbuf_tensor(sname, shape, BF16 if dts == "BF16" else F32))
        tl["consts"][sname] = h
        tl[sname] = h
    tl["fconst"] = E(nc.sbuf_tensor("fconst", [128, 8], F32))
    tl["wbuf"] = [E(nc.sbuf_tensor(f"wbuf{i}", [128, 2048], BF16)) for i in range(3)]
    tl["psumA"] = E(nc.psum_tensor("psA", [128, 512], F32))
    tl["psumB"] = E(nc.psum_tensor("psB", [128, 512], F32))
    tl["psumC"] = E(nc.psum_tensor("psC", [128, 512], F32))
    tl["psumD"] = E(nc.psum_tensor("psD", [128, 512], F32))
    tl["psum_sm"] = E(nc.psum_tensor("psSM", [128, 512], F32))
    tl["psum_e"] = E(nc.psum_tensor("psE", [128, 512], F32))
    tl["A_log"] = E(nc.sbuf_tensor("A_log_s", [128, 1], F32))
    tl["dt_bias"] = E(nc.sbuf_tensor("dt_bias_s", [128, 1], F32))
    tl["negexpA"] = E(nc.sbuf_tensor("negexpA", [128, 1], F32))
    tl["normw_b"] = E(nc.sbuf_tensor("normw_b", [1, 64], BF16))
    tl["normwB"] = E(nc.sbuf_tensor("normwB", [128, 64], F32))
    tl["convw_s"] = E(nc.sbuf_tensor("convw_s", [128, 48 * 4], F32))
    tl["lnrow"] = ([E(nc.sbuf_tensor(f"lnrow{i}", [1, 512], F32)) for i in range(3)]
                   + [E(nc.sbuf_tensor(f"lnrow{i}", [1, 512], BF16)) for i in (3, 4)])


def emit_const_loads(prog, tl, dram):
    for sname, dname, dts, shape in CMAP:
        if dts == "BF16":
            prog.dma("gpsimd", tl["consts"][sname][:], dram[dname][:], inc="c_ldg")
        else:
            prog.dma("sync", tl["consts"][sname][:], dram[dname][:], inc="c_ld")
    prog.dma("sync", tl["convw_s"][:].rearrange("p (a b) -> p a b", b=4),
             bass.AP(dram["convw"].tensor, 0, [[4, 128], [128 * 4, 48], [1, 4]]), inc="c_ld")
    prog.dma("sync", tl["A_log"][:], dram["A_log"][:], inc="c_ld")
    prog.dma("sync", tl["dt_bias"][:], dram["dt_bias"][:], inc="c_ld")
    prog.dma("gpsimd", tl["normw_b"][:], dram["normw"][:], inc="c_ldg")
    for e in ("tensor", "vector", "scalar"):
        prog.wait(e, "c_ld", prog.n("c_ld"))
        prog.wait(e, "c_ldg", prog.n("c_ldg"))


def emit_layer0(prog, tl_c, nc, ctx, dram, scr, x_in_ap, x_out_ap):
    E = ctx.enter_context
    tl = dict(tl_c)
    tl["x_in"] = x_in_ap
    tl["Wqkvzp"] = dram["Wqkvzp"]
    tl["Woutp"] = dram["Woutp"]
    tl["qkvhat_scr"] = scr["qkvhat"]
    tl["sigz_scr"] = scr["sigz"]
    tl["x1a_scr"] = scr["x1a"]
    tl["stats_scr"] = {nm: scr[nm] for nm in ("lam", "lamb", "beta", "negbg", "gam", "w")}
    tl["lam_w"] = E(nc.sbuf_tensor("lam_w", [32, 2048], F32))
    tl["lamb_w"] = E(nc.sbuf_tensor("lamb_w", [32, 2048], F32))
    st = {}
    tl["st"] = st

    with ExitStack() as phb:
        Eb = phb.enter_context
        tl["xn"] = Eb(nc.sbuf_tensor("xn0", [128, 16 * 2048], BF16))
        for nm in ("braw", "araw", "t0", "t1", "t2", "sp", "lam", "lamb", "beta",
                   "negbg", "gam", "w"):
            st[nm] = Eb(nc.sbuf_tensor(f"st_{nm}", [128, 512], F32))
        tl["st_stage"] = Eb(nc.sbuf_tensor("st_stage", [32, 512], F32))
        tl["wb"] = Eb(nc.sbuf_tensor("wb_s", [128, 16 * 32], BF16))
        tl["wa"] = Eb(nc.sbuf_tensor("wa_s", [128, 16 * 32], BF16))
        prog.dma("gpsimd", tl["wb"][:].rearrange("p (a b) -> p a b", b=32),
                 bass.AP(dram["Wb"].tensor, 0, [[32, 128], [128 * 32, 16], [1, 32]]), inc="c_ldg")
        prog.dma("gpsimd", tl["wa"][:].rearrange("p (a b) -> p a b", b=32),
                 bass.AP(dram["Wa"].tensor, 0, [[32, 128], [128 * 32, 16], [1, 32]]), inc="c_ldg")
        prog.wait("tensor", "c_ldg", prog.n("c_ldg"))
        with ExitStack() as phln:
            El = phln.enter_context
            tl["xb16"] = [El(nc.sbuf_tensor(f"xb16_{i}", [128, 512], BF16)) for i in range(16)]
            tl["sqb"] = [El(nc.sbuf_tensor(f"sqb{i}", [128, 512], BF16)) for i in range(16)]
            emit_ln(prog, tl, x_in_ap, tl["xn"][:], "ln0")
            prog.barrier()
        tl["qkraw"] = Eb(nc.sbuf_tensor("qkraw", [128, 2051], BF16))
        tl["qkhat"] = Eb(nc.sbuf_tensor("qkhat", [128, 2048], BF16))
        tl["sqtile"] = Eb(nc.sbuf_tensor("sqtile", [128, 2048], BF16))
        tl["nrm2"] = Eb(nc.sbuf_tensor("nrm2", [2, 512], F32))
        tl["nrm"] = Eb(nc.sbuf_tensor("nrm", [2, 512], F32))
        prog.wait("sync", "dvS", prog.n("dvS"))
        for kt in range(16):
            prog.dma("sync", scr["xn"][128 * kt:128 * (kt + 1), :],
                     tl["xn"][:, 2048 * kt:2048 * (kt + 1)], inc="xn_st")
        emit_stats(prog, tl)
        emit_qkvz(prog, tl)
        prog.wait("sync", "xn_st", prog.n("xn_st"))
        prog.barrier()

    with ExitStack() as phc:
        Ec = phc.enter_context
        tl["qk_g"] = [Ec(nc.sbuf_tensor(f"qkg{i}", [128, GRP], BF16)) for i in range(48)]
        tl["sigz_g"] = [Ec(nc.sbuf_tensor(f"szg{i}", [128, GRP], BF16)) for i in range(16)]
        tl["o_g"] = [Ec(nc.sbuf_tensor(f"og{i}", [128, GRP], BF16)) for i in range(16)]
        tl["state"] = [Ec(nc.sbuf_tensor(f"Sst{i}", [128, 64], F32)) for i in range(NPAIR)]
        tl["state_b"] = [Ec(nc.sbuf_tensor(f"Sstb{i}", [128, 64], BF16)) for i in range(NPAIR)]
        tl["state_dv"] = [0] * NPAIR
        tl["xbuf"] = [Ec(nc.sbuf_tensor(f"xbufC{i}", [128, 512], F32)) for i in range(4)]
        tl["pct"] = Ec(nc.sbuf_tensor("pct", [32, 128], F32))
        tl["pcT_in"] = Ec(nc.sbuf_tensor("pcT_in", [80, 128], F32))
        tl["pcall"] = Ec(nc.sbuf_tensor("pcall", [128, 80], F32))
        wn_pp = ("NT", "NTT", "G", "P2", "P2T", "P4", "P4T", "P8", "P8T",
                 "P16", "P16T", "P32T", "PPT", "bV", "W")
        wn_s4b = ("c", "CSQ", "delta", "on")
        wn_s4f = ("orow", "osq")
        wn_s2f = ("e1f", "e2f", "e1tf")
        wn_s2b = ("E1", "E2", "E1T")
        slot4 = []
        for s in range(4):
            d = {}
            for nm in wn_s4b:
                d[nm] = Ec(nc.sbuf_tensor(f"s4_{s}_{nm}", [128, 64], BF16))
            for nm in wn_s4f:
                d[nm] = Ec(nc.sbuf_tensor(f"s4_{s}_{nm}", [128, 64], F32))
            for nm in ("var", "rstd", "rinv"):
                d[nm] = Ec(nc.sbuf_tensor(f"s4_{s}_{nm}", [128, 1], F32))
            slot4.append(d)
        slot2 = []
        for s in range(2):
            d = {}
            for nm in wn_s2f:
                d[nm] = Ec(nc.sbuf_tensor(f"s2_{s}_{nm}", [128, 64], F32))
            for nm in wn_s2b:
                d[nm] = Ec(nc.sbuf_tensor(f"s2_{s}_{nm}", [128, 64], BF16))
            slot2.append(d)
        ws = []
        for p in range(NPAIR):
            w = {}
            for nm in wn_pp:
                w[nm] = Ec(nc.sbuf_tensor(f"w{p}_{nm}", [128, 64], BF16))
            w.update(slot4[p % 4])
            w.update(slot2[p % 2])
            ws.append(w)
        tl["ws"] = ws
        emit_scan(prog, tl)
        prog.barrier()

    with ExitStack() as phd:
        Ed = phd.enter_context
        tl["xn"] = Ed(nc.sbuf_tensor("xn1", [128, 16 * 2048], BF16))
        tl["qkhat"] = Ed(nc.sbuf_tensor("gatt", [128, 2048], BF16))
        tl["wbuf2"] = [Ed(nc.sbuf_tensor(f"wbuf2_{i}", [128, 2048], BF16)) for i in range(3)]
        tl["wdprep"] = [Ed(nc.sbuf_tensor(f"wdprep{i}", [128, 2048], BF16)) for i in range(2)]
        tl["sqtile"] = Ed(nc.sbuf_tensor("silt", [128, 2048], BF16))
        for kt in range(16):
            prog.dma("gpsimd", tl["xn"][:, 2048 * kt:2048 * (kt + 1)],
                     scr["xn"][128 * kt:128 * (kt + 1), :], inc="xn_ld")
        prog.wait("tensor", "xn_ld", prog.n("xn_ld"))
        cfg0 = dict(Wg=dram["Wg0"], Wu=dram["Wu0"], Wd=dram["Wd0"],
                    wd_scr=scr["wd_bf"], gated=scr["gated"], hid=8192,
                    res_in=scr["x1a"], out=x_out_ap, lbl="f0")
        emit_ffn(prog, tl, cfg0)
    with ExitStack() as phe:
        Ee = phe.enter_context
        tl["gatg"] = [Ee(nc.sbuf_tensor(f"gatg{i}", [128, 512], BF16)) for i in range(64)]
        tl["gwb"] = [Ee(nc.sbuf_tensor(f"gwb{i}", [128, 8192], BF16)) for i in range(2)]
        tl["xbuf"] = [Ee(nc.sbuf_tensor(f"xbufF{i}", [128, 512], F32)) for i in range(16)]
        emit_ffn_f2(prog, tl, cfg0)
    prog.barrier()


def build_layer0():
    nc = bass.Bass("TRN2", target_bir_lowering=False)
    dram = l0_dram(nc)
    scr = l0_scratch(nc)
    with ExitStack() as ctx:
        E = ctx.enter_context
        tl = {}
        alloc_common(nc, E, tl, dram)
        prog = Prog(nc, ctx)
        register_float_consts(prog, tl)
        emit_const_loads(prog, tl, dram)
        prog.barrier()
        emit_layer0(prog, tl, nc, ctx, dram, scr, dram["x_in"], dram["x1_out"])
        prog.finalize(E(nc.Block()))
    return nc


def l0_in_map(inputs, h, cs, b):
    m = {"x_in": np.ascontiguousarray(inputs["L0_attn_hidden_states"][b].T)}
    for nm in ("Wqkvzp", "Wb", "Wa", "convw", "A_log", "dt_bias", "Woutp"):
        m[nm] = h[nm]
    m["normw"] = h["normw"]
    m["Wg0"], m["Wu0"], m["Wd0"] = h["Wg0"], h["Wu0"], h["Wd0"]
    for sname, dname, dts, shape in CMAP:
        m[dname] = cs[dname]
    return m


def kernel(**inputs):
    return kernel_fused(**inputs)


def kernel_unfused(**inputs):
    h = host_prep(inputs)
    cs = host_consts()
    nc0 = build_layer0()
    in0 = [l0_in_map(inputs, h, cs, b) for b in range(8)]
    r0 = run_bass_kernel_spmd(nc0, in0, core_ids=list(range(8)))
    nc1 = build_layer1()
    in1 = [l1_in_map(h, cs, r0.results[b]["x1_out"]) for b in range(8)]
    r1 = run_bass_kernel_spmd(nc1, in1, core_ids=list(range(8)))
    out = np.stack([r1.results[b]["x2_out"].T for b in range(8)])
    return np.ascontiguousarray(out)


def emit_rowsum(prog, tl, rs_b):
    xn = tl["xn"]
    pn = tl["psumC"]
    onescb = tl["ones_col_b"]
    for g in range(NGRP):
        if g >= 1:
            prog.wait("tensor", "dvS", prog.n("dvS"))
        for kt in range(16):
            prog.op("tensor", lambda e, kt=kt, g=g: e.matmul(
                pn[0:1, 0:512], onescb[:],
                xn[:, 2048 * kt + 512 * g:2048 * kt + 512 * (g + 1)],
                start=(kt == 0), stop=(kt == 15)), inc="mmS" if kt == 15 else None)
        n = prog.n("mmS")
        prog.wait("vector", "mmS", n)
        nv = prog.op("vector", lambda e, rs=rs_b, g=g: e.tensor_copy(rs[:, 512 * g:512 * (g + 1)],
                                                                     pn[0:1, 0:512]), inc="dvS")
    prog.wait("tensor", "dvS", nv)


def emit_wscale_build(prog, tl, codes_ap, sc_ap, wsc, head_or_dt, lbl):
    h = head_or_dt
    cbuf = tl["codebuf"]
    scs = tl["sc_s"]
    pbanks = (tl["psum_sm"], tl["psum_e"])
    prog.wait("gpsimd", "dvS", tl["wsc_free"][0])
    prog.dma("gpsimd", cbuf[:].rearrange("p (a b) -> p a b", b=128),
             bass.AP(codes_ap.tensor, h * 128, [[2048, 128], [128 * 2048, 16], [1, 128]]),
             inc=lbl + "cdma")
    prog.dma("gpsimd", scs[:].rearrange("p (a b) -> p a b", b=128),
             bass.AP(sc_ap.tensor, h * 128, [[2048, 4], [4 * 2048, 16], [1, 128]]),
             inc=lbl + "cdma")
    prog.wait("tensor", lbl + "cdma", prog.n(lbl + "cdma"))
    for kt in range(16):
        pb = pbanks[kt % 2]
        par = (kt // 2) % 2
        if kt >= 2:
            prog.wait("tensor", "dvS", tl["wsc_ev"][kt - 2])
        nmm = prog.op("tensor", lambda e, kt=kt, par=par, pb=pb: e.matmul(
            pb[:, 128 * par:128 * (par + 1)], tl["sel4"][:], scs[:, 128 * kt:128 * (kt + 1)],
            start=True, stop=True), inc="mmS")
        prog.wait("vector", "mmS", nmm)
        tl["wsc_ev"][kt] = prog.op("vector", lambda e, kt=kt, par=par, pb=pb, w=wsc, c=cbuf: e.tensor_mul(
            w[:, 128 * kt:128 * (kt + 1)], c[:, 128 * kt:128 * (kt + 1)],
            pb[:, 128 * par:128 * (par + 1)]), inc="dvS")
    tl["wsc_free"][0] = tl["wsc_ev"][15]
    return tl["wsc_ev"][15]


def emit_qkv1(prog, tl, dram, scr):
    nc = prog.nc
    xn = tl["xn"]
    ps = [tl["psumA"], tl["psumB"]]
    ev_bank = {0: 0, 1: 0}
    st_n = {}
    for mi, mat in enumerate(("q", "k", "v")):
        for h in range(16):
            j = mi * 16 + h
            wsc = tl["wsc2"][j % 2]
            tl["codebuf"] = tl["cb2"][j % 2]
            tl["sc_s"] = tl["scs2"][j % 2]
            wev = emit_wscale_build(prog, tl, dram[f"Cw{mat}"], dram[f"Sw{mat}"], wsc, h, "at")
            prog.wait("tensor", "dvS", wev)
            bias = tl["bqkv"][mat][:, h:h + 1]
            stage = tl["qkvstage"][j % 2]
            ev_g = {}
            for g in range(NGRP):
                gi = j * NGRP + g
                bank = ps[gi % 2]
                if g >= 2:
                    prog.wait("tensor", "dvS", ev_g[g - 2])
                else:
                    prog.wait("tensor", "dvS", ev_bank[gi % 2])
                for kt in range(16):
                    prog.op("tensor", lambda e, b=bank, w=wsc, kt=kt, g=g: e.matmul(
                        b[:], w[:, 128 * kt:128 * (kt + 1)],
                        xn[:, 2048 * kt + 512 * g:2048 * kt + 512 * (g + 1)],
                        start=(kt == 0), stop=(kt == 15)), inc="mmS" if kt == 15 else None)
                nmm = prog.n("mmS")
                prog.wait("vector", "mmS", nmm)
                if g == 0 and j >= 2:
                    prog.wait("vector", "qkv_st", st_n[j - 2])
                ev_g[g] = prog.op("vector", lambda e, b=bank, g=g, bi=bias, s=stage: e.tensor_scalar_add(
                    s[:, 512 * g:512 * (g + 1)], b[:], bi), inc="dvS")
                ev_bank[gi % 2] = ev_g[g]
            nv = ev_g[3]
            prog.wait("sync", "dvS", nv)
            prog.dma("sync", scr["qkv1"][mat][128 * h:128 * (h + 1), :], stage[:], inc="qkv_st")
            st_n[j] = prog.n("qkv_st")
    prog.wait("vector", "qkv_st", st_n[47])
    prog.barrier(("tensor", "vector", "scalar", "sync", "gpsimd"))


def emit_attn_scores(prog, tl, dram, scr):
    nc = prog.nc
    ps = [tl["psumA"], tl["psumB"]]
    pso = tl["psumC"]
    pse = tl["psum_e"]
    IA = tl["ident_s"]
    onescb = tl["ones_col_b"]
    onesrb = tl["ones_row_b"]
    hd_free = {}
    ld_n = {}
    st_n = {}
    for mat, dst in (("q", tl["qh2"][0]), ("k", tl["kh2"][0]), ("v", tl["vh2"][0])):
        prog.dma("sync", dst[:], scr["qkv1"][mat][0:128, :], inc="hd_ld")
    ld_n[0] = prog.n("hd_ld")
    for h in range(16):
        par = h % 2
        qh, kh, vh = tl["qh2"][par], tl["kh2"][par], tl["vh2"][par]
        if h + 1 < 16:
            if h >= 1:
                prog.wait("sync", "mmS", hd_free[h - 1])
            for mat, dst in (("q", tl["qh2"][1 - par]), ("k", tl["kh2"][1 - par]),
                             ("v", tl["vh2"][1 - par])):
                prog.dma("sync", dst[:], scr["qkv1"][mat][128 * (h + 1):128 * (h + 2), :],
                         inc="hd_ld")
            ld_n[h + 1] = prog.n("hd_ld")
        prog.wait("tensor", "hd_ld", ld_n[h])
        prog.wait("scalar", "hd_ld", ld_n[h])
        vbanks = (tl["psumD"], tl["psum_sm"])
        for kt in range(16):
            vb = vbanks[kt % 2]
            if kt >= 2:
                prog.wait("tensor", "dvS", tl["vt_ev"][kt - 2])
            nmm = prog.op("tensor", lambda e, kt=kt, vb=vb, vt=vh: e.matmul(
                vb[:, 0:128], vt[:, 128 * kt:128 * (kt + 1)], IA[:],
                start=True, stop=True), inc="mmS")
            prog.wait("vector", "mmS", nmm)
            tl["vt_ev"][kt] = prog.op("vector", lambda e, kt=kt, vb=vb, par=par: e.tensor_copy(
                tl["VT2"][par][kt][:], vb[:, 0:128]), inc="dvS")
        for mi, mat in enumerate(("q", "k")):
            src = qh if mi == 0 else kh
            prog.wait("scalar", "dvS", prog.n("dvS"))
            na = prog.op("scalar", lambda e, s=src: e.activation(tl["sqh"][:], s[:], AF.Square),
                         inc="acS")
            prog.wait("tensor", "acS", na)
            mdst = tl["mxq"] if mi == 0 else tl["mxk"]
            for g in range(NGRP):
                if g >= 1:
                    prog.wait("tensor", "dvS", prog.n("dvS"))
                prog.op("tensor", lambda e, g=g: e.matmul(
                    pse[:, 0:512], tl["ones128"][:], tl["sqh"][:, 512 * g:512 * (g + 1)],
                    start=True, stop=True), inc="mmS")
                nmm = prog.n("mmS")
                prog.wait("vector", "mmS", nmm)
                if g == 0:
                    prog.op("vector", lambda e, m=mdst: e.reduce_max(m[:, 0:1], pse[:, 0:512], axis=AX.X),
                            inc="dvS")
                else:
                    prog.wait("vector", "dvS", prog.n("dvS"))
                    prog.op("vector", lambda e, m=mdst: e.reduce_max(m[:, 1:2], pse[:, 0:512], axis=AX.X),
                            inc="dvS")
                    prog.wait("vector", "dvS", prog.n("dvS"))
                    prog.op("vector", lambda e, m=mdst: e.tensor_max(m[:, 0:1], m[:, 0:1], m[:, 1:2]),
                            inc="dvS")
        prog.wait("vector", "dvS", prog.n("dvS"))
        nv = prog.op("vector", lambda e: e.tensor_mul(tl["mxq"][:, 2:3], tl["mxq"][:, 0:1],
                                                      tl["mxk"][:, 0:1]), inc="dvS")
        prog.wait("scalar", "dvS", nv)
        na = prog.op("scalar", lambda e: e.activation(tl["mxq"][:, 3:4], tl["mxq"][:, 2:3], AF.Sqrt,
                                                      scale=1.0 / 128.0), inc="acS")
        prog.wait("vector", "acS", na)
        nv = prog.op("vector", lambda e, par=par: e.tensor_scalar(tl["negB2"][par][:], tl["mxq"][:, 3:4],
                                                                  -1.0, -1.0, ALU.mult, ALU.add), inc="dvS")
        prog.wait("scalar", "dvS", nv)
        prog.wait("tensor", "dvS", nv)
        for g in range(NGRP):
            qsl = slice(512 * g, 512 * (g + 1))
            for kt in range(16):
                bank = ps[kt % 2]
                if kt >= 2:
                    prog.wait("tensor", "acS", tl["pt_ev"][kt - 2])
                nmm = prog.op("tensor", lambda e, b=bank, kt=kt, qsl=qsl, kh=kh, qh=qh: e.matmul(
                    b[:], kh[:, 128 * kt:128 * (kt + 1)], qh[:, qsl],
                    start=True, stop=True), inc="mmS")
                prog.wait("scalar", "mmS", nmm)
                tl["pt_ev"][kt] = prog.op("scalar", lambda e, b=bank, kt=kt, par=par: e.activation(
                    tl["PT"][kt][:], b[:], AF.Exp, bias=tl["negB2"][par][:, 0:1],
                    scale=0.08838834764831845), inc="acS")
            na = prog.n("acS")
            prog.wait("tensor", "acS", na)
            for kt in range(16):
                prog.op("tensor", lambda e, kt=kt: e.matmul(
                    pse[0:1, 0:512], onescb[:], tl["PT"][kt][:],
                    start=(kt == 0), stop=(kt == 15)), inc=None)
            for kt in range(16):
                prog.op("tensor", lambda e, kt=kt, par=par: e.matmul(
                    pso[:], tl["VT2"][par][kt][:], tl["PT"][kt][:],
                    start=(kt == 0), stop=(kt == 15)), inc="mmS" if kt == 15 else None)
            nmm = prog.n("mmS")
            prog.wait("vector", "mmS", nmm)
            nv = prog.op("vector", lambda e: e.reciprocal(tl["rrow"][:], pse[0:1, 0:512]), inc="dvS")
            prog.wait("vector", "dvS", nv)
            nv = prog.op("vector", lambda e: e.tensor_copy(tl["rrow_b"][:], tl["rrow"][:]), inc="dvS")
            prog.wait("tensor", "dvS", nv)
            nmm = prog.op("tensor", lambda e: e.matmul(tl["psumD"][:, 0:512], onesrb[:], tl["rrow_b"][:],
                                                       start=True, stop=True), inc="mmS")
            prog.wait("vector", "mmS", nmm)
            nv = prog.op("vector", lambda e: e.tensor_copy(tl["rb"][:], tl["psumD"][:, 0:512]), inc="dvS")
            prog.wait("vector", "dvS", nv)
            if g == 0 and h >= 2:
                prog.wait("vector", "o_st", st_n[h - 2])
            nv = prog.op("vector", lambda e, qsl=qsl, par=par: e.tensor_mul(tl["oh2"][par][:, qsl], pso[:],
                                                                            tl["rb"][:]), inc="dvS")
            prog.wait("tensor", "dvS", nv)
            prog.wait("scalar", "dvS", nv)
        hd_free[h] = prog.n("mmS")
        nv = prog.n("dvS")
        prog.wait("gpsimd", "dvS", nv)
        prog.dma("gpsimd", scr["o1"][128 * h:128 * (h + 1), :], tl["oh2"][par][:], inc="o_st")
        st_n[h] = prog.n("o_st")
    prog.wait("vector", "o_st", st_n[15])
    prog.barrier(("tensor", "vector", "scalar", "sync", "gpsimd"))


def emit_wo_residual(prog, tl, dram, scr, x_in_ap):
    nc = prog.nc
    ps = [tl["psumA"], tl["psumB"]]
    for g in range(NGRP):
        for i in range(16):
            prog.dma("sync", tl["o_g1"][i][:],
                     bass.AP(scr["o1"].tensor, 128 * i * T + 512 * g, [[T, 128], [1, 512]]),
                     inc="wo1_ld")
        n = prog.n("wo1_ld")
        prog.wait("tensor", "wo1_ld", n)
        ev2 = {}
        for dt in range(16):
            wev = emit_wscale_build(prog, tl, dram["Cwo"], dram["Swo"], tl["wsc"], dt, "wo")
            prog.wait("tensor", "dvS", wev)
            bank = ps[dt % 2]
            if dt >= 2:
                prog.wait("tensor", "dvS", ev2[dt - 2])
            for kt in range(16):
                prog.op("tensor", lambda e, b=bank, kt=kt: e.matmul(
                    b[:], tl["wsc"][:, 128 * kt:128 * (kt + 1)], tl["o_g1"][kt][:],
                    start=(kt == 0), stop=(kt == 15)), inc="mmS" if kt == 15 else None)
            nmm = prog.n("mmS")
            xb = tl["xbuf"][dt % 4]
            nx = prog.dma("sync", xb[:],
                          bass.AP(x_in_ap.tensor, 128 * dt * T + 512 * g, [[T, 128], [1, 512]]),
                          inc="x1_ld")
            prog.wait("vector", "mmS", nmm)
            prog.wait("vector", "x1_ld", nx)
            nv = prog.op("vector", lambda e, b=bank, xb=xb, dt=dt: e.scalar_tensor_tensor(
                xb[:], b[:], tl["bqkv"]["o"][:, dt:dt + 1] if False else tl["bo_s"][:, dt:dt + 1],
                xb[:], ALU.add, ALU.add), inc="dvS")
            ev2[dt] = nv
            prog.wait("sync", "dvS", nv)
            prog.dma("sync", bass.AP(scr["x1a1"].tensor, 128 * dt * T + 512 * g, [[T, 128], [1, 512]]),
                     xb[:], inc="x1a1_st")
        prog.barrier(("tensor", "vector", "scalar", "sync", "gpsimd"))


def l1_dram(nc):
    dram = {}
    dram["x1_in"] = nc.dram_tensor("x1_in", [D, T], F32, kind="ExternalInput").ap()
    for nm in ("q", "k", "v", "o"):
        dram[f"Cw{nm}"] = nc.dram_tensor(f"Cw{nm}", [D, D], F32, kind="ExternalInput").ap()
        dram[f"Sw{nm}"] = nc.dram_tensor(f"Sw{nm}", [64, D], F32, kind="ExternalInput").ap()
    for nm in ("bq", "bk", "bv", "bo"):
        dram[nm] = nc.dram_tensor(nm, [128, 16], F32, kind="ExternalInput").ap()
    dram["Wg1"] = nc.dram_tensor("Wg1", [D, 5632], I32, kind="ExternalInput").ap()
    dram["Wu1"] = nc.dram_tensor("Wu1", [D, 5632], I32, kind="ExternalInput").ap()
    dram["Wd1"] = nc.dram_tensor("Wd1", [5632, D], I32, kind="ExternalInput").ap()
    dram["sg1"] = nc.dram_tensor("sg1", [128, 44], F32, kind="ExternalInput").ap()
    dram["su1"] = nc.dram_tensor("su1", [128, 44], F32, kind="ExternalInput").ap()
    dram["sd1"] = nc.dram_tensor("sd1", [128, 16], F32, kind="ExternalInput").ap()
    for sname, dname, dt, shape in CMAP:
        dram[dname] = nc.dram_tensor(dname, shape, F32, kind="ExternalInput").ap()
    dram["x2_out"] = nc.dram_tensor("x2_out", [D, T], F32, kind="ExternalOutput").ap()
    return dram


def l1_scratch(nc, tag=""):
    scr = {}
    scr["o1"] = nc.dram_tensor(f"o1_scr{tag}", [D, T], BF16, kind="Internal").ap()
    scr["qkv1"] = {m: nc.dram_tensor(f"qkv1_{m}{tag}", [D, T], BF16, kind="Internal").ap()
                   for m in ("q", "k", "v")}
    scr["x1a1"] = nc.dram_tensor(f"x1a1_scr{tag}", [D, T], F32, kind="Internal").ap()
    scr["xn1"] = nc.dram_tensor(f"xn1_scr{tag}", [D, T], BF16, kind="Internal").ap()
    scr["gated1"] = nc.dram_tensor(f"gated1_scr{tag}", [5632, T], BF16, kind="Internal").ap()
    scr["wd_bf1"] = nc.dram_tensor(f"wd_bf1_scr{tag}", [5632, D], BF16, kind="Internal").ap()
    return scr


def emit_layer1(prog, tl_c, nc, ctx, dram, scr, x_in_ap, x_out_ap, parts=("attn", "ffn")):
    E = ctx.enter_context
    tl = dict(tl_c)
    tl["bqkv"] = {}
    for nm in ("q", "k", "v"):
        tl["bqkv"][nm] = E(nc.sbuf_tensor(f"b{nm}_s", [128, 16], F32))
        prog.dma("sync", tl["bqkv"][nm][:], dram[f"b{nm}"][:], inc="c_ld")
    tl["bo_s"] = E(nc.sbuf_tensor("bo_s", [128, 16], F32))
    prog.dma("sync", tl["bo_s"][:], dram["bo"][:], inc="c_ld")
    tl["sg1_s"] = E(nc.sbuf_tensor("sg1_s", [128, 44], F32))
    tl["su1_s"] = E(nc.sbuf_tensor("su1_s", [128, 44], F32))
    tl["sd1_s"] = E(nc.sbuf_tensor("sd1_s", [128, 16], F32))
    for nm in ("sg1", "su1", "sd1"):
        prog.dma("sync", tl[nm + "_s"][:], dram[nm][:], inc="c_ld")
    for e in ("tensor", "vector", "scalar"):
        prog.wait(e, "c_ld", prog.n("c_ld"))
    prog.barrier(("tensor", "vector", "scalar", "sync", "gpsimd"))

    with ExitStack() as pha:
        Ea = pha.enter_context
        tl["xn"] = Ea(nc.sbuf_tensor("xn1r", [128, 16 * 2048], BF16))
        tl["rs_b"] = Ea(nc.sbuf_tensor("rs_b", [1, 2048], BF16))
        with ExitStack() as phln:
            El = phln.enter_context
            tl["xb16"] = [El(nc.sbuf_tensor(f"l1xb{i}", [128, 512], BF16)) for i in range(16)]
            tl["sqb"] = [El(nc.sbuf_tensor(f"l1sq{i}", [128, 512], BF16)) for i in range(16)]
            emit_ln(prog, tl, x_in_ap, tl["xn"][:], "ln1")
            prog.barrier()
        emit_rowsum(prog, tl, tl["rs_b"])
        if "attn" in parts and ExitStack:
         with ExitStack() as phb:
            Eb = phb.enter_context
            tl["cb2"] = [Eb(nc.sbuf_tensor(f"codebuf{i}", [128, 2048], BF16)) for i in range(2)]
            tl["scs2"] = [Eb(nc.sbuf_tensor(f"sc_s{i}", [4, 2048], BF16)) for i in range(2)]
            tl["wsc2"] = [Eb(nc.sbuf_tensor(f"wsc{i}", [128, 2048], BF16)) for i in range(2)]
            tl["wsc_ev"] = [0] * 16
            tl["wsc_free"] = [0]
            tl["qkvstage"] = [Eb(nc.sbuf_tensor(f"qkvstage{i}", [128, 2048], BF16))
                              for i in range(2)]
            emit_qkv1(prog, tl, dram, scr)
         with ExitStack() as phb2:
            Eb = phb2.enter_context
            tl["qh2"] = [Eb(nc.sbuf_tensor(f"qh{i}", [128, 2048], BF16)) for i in range(2)]
            tl["kh2"] = [Eb(nc.sbuf_tensor(f"kh{i}", [128, 2048], BF16)) for i in range(2)]
            tl["vh2"] = [Eb(nc.sbuf_tensor(f"vh{i}", [128, 2048], BF16)) for i in range(2)]
            tl["VT2"] = [[Eb(nc.sbuf_tensor(f"VT{j}t{i}", [128, 128], BF16)) for i in range(16)]
                         for j in range(2)]
            tl["vt_ev"] = [0] * 16
            tl["PT"] = [Eb(nc.sbuf_tensor(f"PTt{i}", [128, 512], BF16)) for i in range(16)]
            tl["pt_ev"] = [0] * 16
            tl["sqh"] = Eb(nc.sbuf_tensor("sqh", [128, 2048], BF16))
            tl["mxq"] = Eb(nc.sbuf_tensor("mxq", [128, 4], F32))
            tl["mxk"] = Eb(nc.sbuf_tensor("mxk", [128, 4], F32))
            tl["negB2"] = [Eb(nc.sbuf_tensor(f"negB{i}", [128, 1], F32)) for i in range(2)]
            tl["rrow"] = Eb(nc.sbuf_tensor("rrow", [1, 512], F32))
            tl["rrow_b"] = Eb(nc.sbuf_tensor("rrow_b", [1, 512], BF16))
            tl["rb"] = Eb(nc.sbuf_tensor("rb", [128, 512], BF16))
            tl["oh2"] = [Eb(nc.sbuf_tensor(f"oh{i}", [128, 2048], BF16)) for i in range(2)]
            emit_attn_scores(prog, tl, dram, scr)
        if "attn" in parts:
         with ExitStack() as phc:
            Ec = phc.enter_context
            tl["codebuf"] = Ec(nc.sbuf_tensor("codebuf2", [128, 2048], BF16))
            tl["sc_s"] = Ec(nc.sbuf_tensor("sc_s2w", [4, 2048], BF16))
            tl["wsc"] = Ec(nc.sbuf_tensor("wsc2w", [128, 2048], BF16))
            tl["wsc_ev"] = [0] * 16
            tl["wsc_free"] = [0]
            tl["o_g1"] = [Ec(nc.sbuf_tensor(f"og1_{i}", [128, 512], BF16)) for i in range(16)]
            tl["xbuf"] = [Ec(nc.sbuf_tensor(f"xbufW{i}", [128, 512], F32)) for i in range(4)]
            emit_wo_residual(prog, tl, dram, scr, x_in_ap)
        if "attn" not in parts:
            for dt in range(16):
                for g in range(NGRP):
                    prog.dma("sync",
                             bass.AP(x_out_ap.tensor, 128 * dt * T + 512 * g, [[T, 128], [1, 512]]),
                             bass.AP(x_in_ap.tensor, 128 * dt * T + 512 * g, [[T, 128], [1, 512]]),
                             inc="fout")
            prog.wait("gpsimd", "fout", prog.n("fout"))
            prog.barrier()
            return
        if "ffn" not in parts:
            for dt in range(16):
                for g in range(NGRP):
                    prog.dma("sync",
                             bass.AP(x_out_ap.tensor, 128 * dt * T + 512 * g, [[T, 128], [1, 512]]),
                             bass.AP(scr["x1a1"].tensor, 128 * dt * T + 512 * g, [[T, 128], [1, 512]]),
                             inc="fout")
            prog.wait("gpsimd", "fout", prog.n("fout"))
            prog.barrier()
            return
        tl["qkhat"] = Ea(nc.sbuf_tensor("gatt1", [128, 2048], BF16))
        tl["sqtile"] = Ea(nc.sbuf_tensor("silt1", [128, 2048], BF16))
        tl["upt"] = Ea(nc.sbuf_tensor("upt1", [128, 2048], BF16))
        tl["wbuf2"] = [Ea(nc.sbuf_tensor(f"l1wb2_{i}", [128, 2048], BF16)) for i in range(3)]
        tl["wdprep"] = [Ea(nc.sbuf_tensor(f"l1wdp{i}", [128, 2048], BF16)) for i in range(2)]
        cfg1 = dict(Wg=dram["Wg1"], Wu=dram["Wu1"], Wd=dram["Wd1"],
                    wd_scr=scr["wd_bf1"], gated=scr["gated1"], hid=5632,
                    res_in=scr["x1a1"], out=x_out_ap, lbl="f1",
                    quant=dict(sg=tl["sg1_s"], su=tl["su1_s"], sd=tl["sd1_s"], rs=tl["rs_b"]))
        emit_ffn(prog, tl, cfg1)
    with ExitStack() as phe:
        Ee = phe.enter_context
        tl["gatg"] = [Ee(nc.sbuf_tensor(f"gatg1_{i}", [128, 512], BF16)) for i in range(44)]
        tl["gwb"] = [Ee(nc.sbuf_tensor(f"gwb1_{i}", [128, 5632], BF16)) for i in range(2)]
        tl["gsum_b"] = Ee(nc.sbuf_tensor("gsum_b", [1, 512], BF16))
        tl["xbuf"] = [Ee(nc.sbuf_tensor(f"xbufF1_{i}", [128, 512], F32)) for i in range(16)]
        emit_ffn_f2(prog, tl, cfg1)
    prog.barrier()


def build_layer1(parts=("attn", "ffn")):
    nc = bass.Bass("TRN2", target_bir_lowering=False)
    dram = l1_dram(nc)
    scr = l1_scratch(nc)
    with ExitStack() as ctx:
        E = ctx.enter_context
        tl = {}
        alloc_common(nc, E, tl, dram)
        prog = Prog(nc, ctx)
        register_float_consts(prog, tl)
        for sname, dname, dts, shape in CMAP:
            if dts == "BF16":
                prog.dma("gpsimd", tl["consts"][sname][:], dram[dname][:], inc="c_ldg")
            else:
                prog.dma("sync", tl["consts"][sname][:], dram[dname][:], inc="c_ld")
        for e in ("tensor", "vector", "scalar"):
            prog.wait(e, "c_ld", prog.n("c_ld"))
            prog.wait(e, "c_ldg", prog.n("c_ldg"))
        prog.barrier()
        emit_layer1(prog, tl, nc, ctx, dram, scr, dram["x1_in"], dram["x2_out"], parts)
        prog.finalize(E(nc.Block()))
    return nc


def l1_in_map(h, cs, x1_fm):
    m = {"x1_in": np.ascontiguousarray(x1_fm)}
    for nm in ("q", "k", "v", "o"):
        m[f"Cw{nm}"] = h[f"Cw{nm}"]
        m[f"Sw{nm}"] = h[f"Sw{nm}"]
    for nm in ("bq", "bk", "bv", "bo"):
        m[nm] = h[nm]
    for nm in ("Wg1", "Wu1", "Wd1", "sg1", "su1", "sd1"):
        m[nm] = h[nm]
    for sname, dname, dts, shape in CMAP:
        m[dname] = cs[dname]
    return m


def build_fused():
    nc = bass.Bass("TRN2", target_bir_lowering=False)
    dram = {}
    dram["x_in"] = nc.dram_tensor("x_in", [D, T], F32, kind="ExternalInput").ap()
    specs = [("Wqkvzp", [D, 8192], F32), ("Wb", [D, 32], F32), ("Wa", [D, 32], F32),
             ("convw", [6144, 4], F32), ("A_log", [128, 1], F32), ("dt_bias", [128, 1], F32),
             ("normw", [1, 64], F32), ("Woutp", [D, D], F32),
             ("Wg0", [D, 8192], F32), ("Wu0", [D, 8192], F32), ("Wd0", [8192, D], F32)]
    for nm, shape, dt in specs:
        dram[nm] = nc.dram_tensor(nm, shape, dt, kind="ExternalInput").ap()
    for nm in ("q", "k", "v", "o"):
        dram[f"Cw{nm}"] = nc.dram_tensor(f"Cw{nm}", [D, D], F32, kind="ExternalInput").ap()
        dram[f"Sw{nm}"] = nc.dram_tensor(f"Sw{nm}", [64, D], F32, kind="ExternalInput").ap()
    for nm in ("bq", "bk", "bv", "bo"):
        dram[nm] = nc.dram_tensor(nm, [128, 16], F32, kind="ExternalInput").ap()
    dram["Wg1"] = nc.dram_tensor("Wg1", [D, 5632], I32, kind="ExternalInput").ap()
    dram["Wu1"] = nc.dram_tensor("Wu1", [D, 5632], I32, kind="ExternalInput").ap()
    dram["Wd1"] = nc.dram_tensor("Wd1", [5632, D], I32, kind="ExternalInput").ap()
    dram["sg1"] = nc.dram_tensor("sg1", [128, 44], F32, kind="ExternalInput").ap()
    dram["su1"] = nc.dram_tensor("su1", [128, 44], F32, kind="ExternalInput").ap()
    dram["sd1"] = nc.dram_tensor("sd1", [128, 16], F32, kind="ExternalInput").ap()
    for sname, dname, dt, shape in CMAP:
        dram[dname] = nc.dram_tensor(dname, shape, F32, kind="ExternalInput").ap()
    dram["x2_out"] = nc.dram_tensor("x2_out", [D, T], F32, kind="ExternalOutput").ap()
    x1_scr = nc.dram_tensor("x1_mid", [D, T], F32, kind="Internal").ap()
    scr0 = l0_scratch(nc)
    scr1 = l1_scratch(nc)
    with ExitStack() as ctx:
        E = ctx.enter_context
        tl = {}
        alloc_common(nc, E, tl, dram)
        prog = Prog(nc, ctx)
        register_float_consts(prog, tl)
        emit_const_loads(prog, tl, dram)
        prog.barrier()
        emit_layer0(prog, tl, nc, ctx, dram, scr0, dram["x_in"], x1_scr)
        emit_layer1(prog, tl, nc, ctx, dram, scr1, x1_scr, dram["x2_out"])
        prog.finalize(E(nc.Block()))
    return nc


def kernel_fused(**inputs):
    h = host_prep(inputs)
    cs = host_consts()
    nc = build_fused()
    in_maps = []
    for b in range(8):
        m = l0_in_map(inputs, h, cs, b)
        m2 = l1_in_map(h, cs, np.zeros((1,), np.float32))
        del m2["x1_in"]
        m.update(m2)
        in_maps.append(m)
    res = run_bass_kernel_spmd(nc, in_maps, core_ids=list(range(8)))
    out = np.stack([res.results[b]["x2_out"].T for b in range(8)])
    return np.ascontiguousarray(out)


# version 05 (final)
# speedup vs baseline: 1.1560x; 1.0116x over previous
import numpy as np
import ml_dtypes
from contextlib import ExitStack
import concourse.bass as bass
import concourse.mybir as mybir
from concourse.bass_utils import run_bass_kernel_spmd

F32 = mybir.dt.float32
BF16 = mybir.dt.bfloat16
I32 = mybir.dt.int32
AF = mybir.ActivationFunctionType
ALU = mybir.AluOpType
AX = mybir.AxisListType

T = 2048
D = 2048
NH = 32
NPAIR = 16
C = 64
NCHUNK = T // C
GRP = 512
NGRP = T // GRP
CPG = GRP // C
NEG = -1.0e30
ATTN_UPTO = 3
ATTN_NHEADS = 16
EPS_LN = 1e-5
EPS_DN = 1e-6


def host_consts():
    c = {}
    c["ident"] = np.eye(128, dtype=np.float32)
    c["identF"] = np.eye(128, dtype=np.float32)
    ii = np.zeros((128, 64), np.float32)
    ii[0:64] = np.eye(64)
    ii[64:128] = np.eye(64)
    c["eye2"] = ii
    c["ones_col"] = np.ones((128, 1), np.float32)
    c["ones_row"] = np.ones((1, 128), np.float32)
    s4 = np.zeros((4, 128), np.float32)
    for k in range(4):
        s4[k, 32 * k:32 * (k + 1)] = 1.0
    c["sel4_c"] = s4
    c["neg128_row"] = np.full((1, 128), -128.0, np.float32)
    c["ones_row_fc"] = np.ones((1, 128), np.float32)
    c["ones128_c"] = np.ones((128, 128), np.float32)
    bd = np.zeros((128, 2), np.float32)
    bd[0:64, 0] = 1.0
    bd[64:128, 1] = 1.0
    c["bdones_c"] = bd
    ind2 = np.zeros((2, 128), np.float32)
    ind2[0, 0:64] = 1.0
    ind2[1, 64:128] = 1.0
    c["ind2_c"] = ind2
    sel = np.zeros((NPAIR, 32, 128), np.float32)
    for p in range(NPAIR):
        sel[p, 2 * p, 0:64] = 1.0
        sel[p, 2 * p + 1, 64:128] = 1.0
    c["selP"] = np.ascontiguousarray(sel.transpose(1, 0, 2).reshape(32, NPAIR * 128))
    c["selPn"] = np.ascontiguousarray((-sel).transpose(1, 0, 2).reshape(32, NPAIR * 128))
    ohLn = np.zeros((NPAIR, 32, 64), np.float32)
    ohLB = np.zeros((NPAIR, 32, 64), np.float32)
    for p in range(NPAIR):
        ohLn[p, p, :] = -1.0
        ohLB[p, 16 + p, :] = 1.0
    c["ohLn"] = np.ascontiguousarray(ohLn.transpose(1, 0, 2).reshape(32, NPAIR * 64))
    c["ohLB"] = np.ascontiguousarray(ohLB.transpose(1, 0, 2).reshape(32, NPAIR * 64))
    j = np.arange(64)
    mstrict = np.where(j[:, None] < j[None, :], 0.0, NEG).astype(np.float32)
    mincl = np.where(j[:, None] <= j[None, :], 0.0, NEG).astype(np.float32)
    mstrictT = np.where(j[:, None] > j[None, :], 0.0, NEG).astype(np.float32)
    c["mask_E1"] = np.concatenate([mstrict, mstrict], 0)
    c["mask_E2"] = np.concatenate([mincl, mincl], 0)
    c["mask_E1T"] = np.concatenate([mstrictT, mstrictT], 0)
    return c


def host_prep(inputs):
    h = {}
    asc = np.ascontiguousarray
    wq = inputs["L0_attn_w_qkvz"]
    rows = []
    for off in (0, 64, 128, 192):
        for hh in range(NH):
            rows.append(np.arange(hh * 256 + off, hh * 256 + off + 64))
    perm = np.concatenate(rows)
    h["Wqkvzp"] = asc(wq[perm].T)
    wba = inputs["L0_attn_w_ba"]
    h["Wb"] = asc(wba[0::2].T)
    h["Wa"] = asc(wba[1::2].T)
    h["convw"] = asc(inputs["L0_attn_conv_w"][:, 0, :])
    h["A_log"] = asc(np.tile(inputs["L0_attn_A_log"].reshape(1, 32), (4, 1)).reshape(128, 1))
    h["dt_bias"] = asc(np.tile(inputs["L0_attn_dt_bias"].reshape(1, 32), (4, 1)).reshape(128, 1))
    h["normw"] = asc(inputs["L0_attn_norm_weight"].reshape(1, 64))
    h["Woutp"] = asc(inputs["L0_attn_w_out"].T)
    h["Wg0"] = asc(inputs["L0_ffn_w_gate"].T)
    h["Wu0"] = asc(inputs["L0_ffn_w_up"].T)
    h["Wd0"] = asc(inputs["L0_ffn_w_down"].T)
    for nm in ("wq", "wk", "wv", "wo"):
        h[f"C{nm}"] = asc(inputs[f"L1_attn_{nm}_codes"])
        h[f"S{nm}"] = asc(inputs[f"L1_attn_{nm}_scales"])
    h["bq"] = asc(inputs["L1_attn_bq"].reshape(16, 128).T)
    h["bk"] = asc(inputs["L1_attn_bk"].reshape(16, 128).T)
    h["bv"] = asc(inputs["L1_attn_bv"].reshape(16, 128).T)
    h["bo"] = asc(inputs["L1_attn_bo"].reshape(16, 128).T)
    h["Wg1"] = asc(inputs["L1_ffn_wq_gate"].T)
    h["Wu1"] = asc(inputs["L1_ffn_wq_up"].T)
    h["Wd1"] = asc(inputs["L1_ffn_wq_down"].T)
    h["sg1"] = asc(inputs["L1_ffn_s_gate"].reshape(44, 128).T)
    h["su1"] = asc(inputs["L1_ffn_s_up"].reshape(44, 128).T)
    h["sd1"] = asc(inputs["L1_ffn_s_down"].reshape(16, 128).T)
    return h


class Ctr:
    def __init__(self, ctx, nc, name):
        self.h = ctx.enter_context(nc.semaphore(name))
        self.n = 0


class Prog:
    ENGINES = ("sync", "gpsimd", "tensor", "vector", "scalar")

    def __init__(self, nc, ctx):
        self.nc = nc
        self.ctx = ctx
        self.streams = {e: [] for e in self.ENGINES}
        self.sems = {}
        self.swdge = {}
        self._holders = []
        self.sync_runs = {}

    def sem(self, name):
        if name not in self.sems:
            self.sems[name] = Ctr(self.ctx, self.nc, name)
        return self.sems[name]

    def n(self, name):
        return self.sem(name).n

    def _close_sync_runs(self):
        for nm, rec in self.sync_runs.items():
            if rec["open"]:
                s = self.sems[nm]
                v = s.n
                self.streams["sync"].append(lambda e, s=s, v=v: e.wait_ge(s.h, v))
                rec["bounds"].append(v)
                rec["open"] = False

    def milestone(self, eng, main_sem, ms_sem, k=1):
        self.wait(eng, main_sem, self.n(main_sem))
        s = self.sem(ms_sem)
        s.n += k
        self.streams[eng].append(lambda e, s=s, k=k: e.sem_inc(s.h, k))
        return s.n

    def op(self, eng, fn, inc=None, inck=1):
        if eng == "sync":
            self._close_sync_runs()
        if inc is not None:
            if isinstance(inc, str):
                inc = [(inc, inck)]
            hs = []
            ret = None
            for nm, k in inc:
                s = self.sem(nm)
                s.n += k
                hs.append((s.h, k))
                ret = s.n

            def run(e, fn=fn, hs=hs):
                i = fn(e)
                for h, k in hs:
                    i = i.then_inc(h, k)
                return i
            self.streams[eng].append(run)
            return ret
        self.streams[eng].append(lambda e, fn=fn: fn(e))
        return None

    def dma(self, eng, out, in_, inc):
        if eng == "sync":
            s = self.sem(inc)
            s.n += 16
            rec = self.sync_runs.setdefault(inc, {"bounds": [], "open": False})
            rec["open"] = True
            self.streams[eng].append(
                lambda e, o=out, i=in_, s=s: e.dma_start(out=o, in_=i).then_inc(s.h, 16))
            return s.n
        if eng == "gpsimd":
            s = self.sem(inc)
            s.n += 16
            v = s.n

            def run(e, o=out, i=in_, s=s, v=v):
                e.dma_start(out=o, in_=i).then_inc(s.h, 16)
                return e.wait_ge(s.h, v)
            self.streams[eng].append(run)
            return v
        return self.op(eng, lambda e, o=out, i=in_: e.dma_start(out=o, in_=i), inc=inc, inck=16)

    def wait(self, eng, sem, val=None):
        if eng == "sync":
            self._close_sync_runs()
        s = self.sem(sem)
        v = s.n if val is None else val
        if v > 0:
            if sem in self.sync_runs and eng != "sync":
                holder = [v]
                self._holders.append((sem, holder))
                self.streams[eng].append(lambda e, s=s, h=holder: e.wait_ge(s.h, h[0]))
            else:
                self.streams[eng].append(lambda e, s=s, v=v: e.wait_ge(s.h, v))
        return v

    def barrier(self, engines=None):
        engines = list(engines or self.ENGINES)
        if "sync" in engines:
            self._close_sync_runs()
        vals = {}
        for e in engines:
            s = self.sem("ph_" + e)
            s.n += 1
            vals[e] = s.n
            self.streams[e].append(lambda en, s=s: en.drain().then_inc(s.h, 1))
        for e in engines:
            for f in engines:
                if f != e:
                    sf = self.sem("ph_" + f)
                    self.streams[e].append(lambda en, s=sf, v=vals[f]: en.wait_ge(s.h, v))
        return vals

    def finalize(self, block):
        self._close_sync_runs()
        for sem, holder in self._holders:
            bounds = self.sync_runs[sem]["bounds"]
            v = holder[0]
            holder[0] = next((b for b in bounds if b >= v), bounds[-1] if bounds else v)
        for name in self.ENGINES:
            stream = self.streams[name]

            def runner(eng, stream=stream):
                for fn in stream:
                    fn(eng)
            getattr(block, name)(runner)


def register_float_consts(prog, tl, values=(EPS_DN, EPS_LN, 1.0 / 64.0, 0.08838834764831845,
                                             1.0 / 128.0, 1.0 / D)):
    nc = prog.nc
    tile = tl["fconst"]
    for i, v in enumerate(values):
        prog.op("vector", lambda e, i=i, v=v: e.memset(tile[:, i:i + 1], v), inc="dvS")
        nc.const_aps.register(F32, v, tile[:, i:i + 1])
    prog.wait("vector", "dvS", prog.n("dvS"))


def emit_scan(prog, tl):
    nc = prog.nc
    cs = tl["consts"]
    for p in range(NPAIR):
        prog.op("vector", lambda e, p=p: e.memset(tl["state"][p][:], 0.0), inc="dvS")
        n = prog.op("vector", lambda e, p=p: e.memset(tl["state_b"][p][:], 0.0), inc="dvS")
        tl["state_dv"][p] = n
    prog.op("tensor", lambda e: e.matmul(tl["psumD"][:, 0:64], tl["ones_row_b"][:], tl["normw_b"][:],
                                         start=True, stop=True), inc="mmS")
    prog.wait("vector", "mmS", prog.n("mmS"))
    prog.op("vector", lambda e: e.tensor_copy(tl["normwB"][:], tl["psumD"][:, 0:64]), inc="dvS")
    prog.barrier()
    for g in range(NGRP):
        for i in range(48):
            prog.dma("sync", tl["qk_g"][i][:],
                     bass.AP(tl["qkvhat_scr"].tensor, 128 * i * T + 512 * g, [[T, 128], [1, 512]]),
                     inc="grp_ld")
        for i in range(16):
            prog.dma("sync", tl["sigz_g"][i][:],
                     bass.AP(tl["sigz_scr"].tensor, 128 * i * T + 512 * g, [[T, 128], [1, 512]]),
                     inc="grp_ld")
        n = prog.n("grp_ld")
        prog.wait("tensor", "grp_ld", n)
        prog.wait("vector", "grp_ld", n)
        for lc in range(CPG):
            emit_chunk(prog, tl, g * CPG + lc, lc)
        emit_wout_group(prog, tl, g)
        prog.barrier(("tensor", "vector", "scalar", "sync", "gpsimd"))


def emit_chunk(prog, tl, cidx, lc):
    nc = prog.nc
    cs = tl["consts"]
    ts = slice(lc * C, lc * C + C)
    pct = tl["pct"]
    ws = tl["ws"]
    IA = tl["ident_s"][0:64, 0:64]
    IB = tl["ident_s"][64:128, 64:128]
    k_t = tl["qk_g"]
    kof, vof = 16, 32
    psm = tl["psum_sm"]
    pse = tl["psum_e"]
    ppw = [tl["psumA"], tl["psumB"], tl["psumC"], tl["psumD"]]

    def mm(fn, inc=None):
        return prog.op("tensor", fn, inc=inc)

    def vv(fn, inc=None):
        return prog.op("vector", fn, inc=inc)

    def ss(fn, inc=None):
        return prog.op("scalar", fn, inc=inc)

    prog.wait("sync", "mmS", prog.n("mmS"))
    prog.wait("sync", "dvS", prog.n("dvS"))

    def pct_src(name):
        return bass.AP(tl["stats_scr"][name].tensor, cidx * C, [[2 * T, 16], [T, 2], [1, 64]])

    def pc_src(name, rep):
        if rep:
            return bass.AP(tl["stats_scr"][name].tensor, cidx * C + C - 1,
                           [[T, 2], [0, 64], [2 * T, 16], [1, 1]])
        return bass.AP(tl["stats_scr"][name].tensor, cidx * C, [[T, 2], [1, 64], [2 * T, 16], [1, 1]])

    prog.dma("sync", pct[0:16, :], pct_src("lam"), inc="pc_ld")
    prog.dma("sync", pct[16:32, :], pct_src("lamb"), inc="pc_ld")
    pcT = tl["pcT_in"]
    for si, (nm, rep) in enumerate((("negbg", False), ("beta", False), ("gam", False),
                                    ("w", False), ("gamC", True))):
        srcnm = "gam" if nm == "gamC" else nm
        if rep:
            ap = bass.AP(tl["stats_scr"][srcnm].tensor, cidx * C + C - 1,
                         [[2 * T, 16], [T, 2], [0, 64], [1, 1]])
        else:
            ap = bass.AP(tl["stats_scr"][srcnm].tensor, cidx * C, [[2 * T, 16], [T, 2], [1, 64]])
        if rep:
            prog.dma("sync", pcT[16 * si:16 * (si + 1), :].rearrange("p (a b) -> p a b", b=1), ap,
                     inc="pc_ld")
        else:
            prog.dma("sync", pcT[16 * si:16 * (si + 1), :], ap, inc="pc_ld")
    npc = prog.n("pc_ld")
    prog.wait("tensor", "pc_ld", npc)
    prog.wait("tensor", "dvS", prog.n("dvS"))
    prog.wait("tensor", "st4ev", prog.n("st4ev"))
    nmm0 = prog.op("tensor", lambda e: e.matmul(tl["psumA"][:, 384:464], pcT[:],
                                                cs["identF_s"][0:80, 0:80], start=True, stop=True),
                   inc="mmS")
    prog.wait("vector", "mmS", nmm0)
    npcv = prog.op("vector", lambda e: e.tensor_copy(tl["pcall"][:], tl["psumA"][:, 384:464]),
                   inc="dvS")
    prog.wait("tensor", "dvS", npcv)
    prog.wait("vector", "pc_ld", npc)
    pcall = tl["pcall"]
    pcs = {nm: pcall[:, 16 * si:16 * (si + 1)]
           for si, nm in enumerate(("negbg", "beta", "gam", "w", "gamC"))}

    st1_dv = [0] * NPAIR
    st1_ev = [0] * NPAIR
    lamblk = tl["lam_w"][:, 64 * cidx:64 * cidx + 64]
    lambblk = tl["lamb_w"][:, 64 * cidx:64 * cidx + 64]

    for p in range(NPAIR):
        kt, qt, vt = k_t[kof + p], k_t[p], k_t[vof + p]
        KT_A, KT_B = kt[0:64, ts], kt[64:128, ts]
        QT_A, QT_B = qt[0:64, ts], qt[64:128, ts]
        w = ws[p]
        bank1 = psm if p % 2 == 0 else pse
        pg = bank1[:, 0:128]
        pv = bank1[:, 320:448]
        pe = bank1[:, 128:320]
        if p >= 2:
            prog.wait("tensor", "st1ev", st1_ev[p - 2])
        mm(lambda e, pg=pg, A=KT_A: e.matmul(pg[0:64, 0:64], A, A, start=True, stop=True))
        mm(lambda e, pg=pg, A=KT_A, Q=QT_A: e.matmul(pg[0:64, 64:128], A, Q, start=True, stop=True))
        mm(lambda e, pg=pg, B=KT_B: e.matmul(pg[64:128, 0:64], B, B, start=True, stop=True))
        mm(lambda e, pg=pg, B=KT_B, Q=QT_B: e.matmul(pg[64:128, 64:128], B, Q, start=True, stop=True))
        sel = cs["selP_s"][:, 128 * p:128 * (p + 1)]
        seln = cs["selPn_s"][:, 128 * p:128 * (p + 1)]
        ohn = cs["ohLn_s"][:, 64 * p:64 * (p + 1)]
        ohb = cs["ohLB_s"][:, 64 * p:64 * (p + 1)]
        mm(lambda e, pe=pe, s=sel, b=lambblk: e.matmul(pe[:, 0:64], s, b, start=True, stop=False))
        mm(lambda e, pe=pe, o=ohn: e.matmul(pe[:, 0:64], pct[:], o, start=False, stop=True))
        mm(lambda e, pe=pe, s=sel, b=lamblk: e.matmul(pe[:, 64:128], s, b, start=True, stop=False))
        mm(lambda e, pe=pe, o=ohn: e.matmul(pe[:, 64:128], pct[:], o, start=False, stop=True))
        mm(lambda e, pe=pe, s=seln, b=lamblk: e.matmul(pe[:, 128:192], s, b, start=True, stop=False))
        mm(lambda e, pe=pe, o=ohb: e.matmul(pe[:, 128:192], pct[:], o, start=False, stop=True))
        mm(lambda e, pv=pv, vt=vt: e.matmul(pv[0:64, 0:64], vt[0:64, ts], IA, start=True, stop=True))
        mm(lambda e, pv=pv, vt=vt: e.matmul(pv[64:128, 0:64], vt[64:128, ts], IB, start=True, stop=True))
        mm(lambda e, pv=pv, kt=kt: e.matmul(pv[0:64, 64:128], kt[0:64, ts], IA, start=True, stop=True))
        nmm = mm(lambda e, pv=pv, kt=kt: e.matmul(pv[64:128, 64:128], kt[64:128, ts], IB,
                                                  start=True, stop=True), inc="mmS")
        prog.wait("vector", "mmS", nmm)
        vv(lambda e, w=w, pe=pe: e.tensor_add(w["e1f"][:], pe[:, 0:64], cs["mask_E1_s"][:]))
        vv(lambda e, w=w, pe=pe: e.tensor_add(w["e2f"][:], pe[:, 64:128], cs["mask_E2_s"][:]))
        nv = vv(lambda e, w=w, pe=pe: e.tensor_add(w["e1tf"][:], pe[:, 128:192], cs["mask_E1T_s"][:]),
                inc="dvS")
        prog.wait("scalar", "dvS", nv)
        ss(lambda e, w=w: e.activation(w["E1"][:], w["e1f"][:], AF.Exp))
        ss(lambda e, w=w: e.activation(w["E2"][:], w["e2f"][:], AF.Exp))
        na = ss(lambda e, w=w: e.activation(w["E1T"][:], w["e1tf"][:], AF.Exp), inc="acS")
        prog.wait("vector", "acS", na)
        vv(lambda e, w=w, pg=pg: e.tensor_mul(w["NT"][:], pg[:, 0:64], w["E1"][:]))
        vv(lambda e, w=w, pg=pg: e.tensor_mul(w["NTT"][:], pg[:, 0:64], w["E1T"][:]))
        vv(lambda e, w=w, pg=pg: e.tensor_mul(w["G"][:], pg[:, 64:128], w["E2"][:]))
        pbeta = pcs["beta"][:, p:p + 1]
        pw_ = pcs["w"][:, p:p + 1]
        vv(lambda e, w=w, pv=pv, s=pbeta: e.tensor_scalar_mul(w["bV"][:], pv[:, 0:64], s))
        nw = vv(lambda e, w=w, pv=pv, s=pw_: e.tensor_scalar_mul(w["W"][:], pv[:, 64:128], s),
                inc="dvS")
        prog.wait("vector", "dvS", nw)
        st1_dv[p] = prog.op("vector",
                            lambda e, w=w: e.tensor_sub(w["PPT"][:], cs["eye2_s"][:], w["NT"][:]),
                            inc="dvS")
        st1_ev[p] = prog.milestone("vector", "dvS", "st1ev")

    seq = [("P2", "NTT", "NT"), ("P2T", "NT", "NTT"), ("P4", "P2T", "P2"), ("P4T", "P2", "P2T"),
           ("P8", "P4T", "P4"), ("P8T", "P4", "P4T"), ("P16", "P8T", "P8"), ("P16T", "P8", "P8T"),
           ("P32T", "P16", "P16T")]
    sq_ev = {}
    bank_free = {}
    for si, (dst, lh, rh) in enumerate(seq):
        for p in range(NPAIR):
            w = ws[p]
            bi = p % 4
            bank = ppw[bi][:, 0:64]
            if bi in bank_free:
                prog.wait("tensor", "sq_ev", bank_free[bi])
            if si <= 1:
                prog.wait("tensor", "dvS", st1_dv[p])
            else:
                prog.wait("tensor", "sq_ev", sq_ev[(si - 1, p)])
            mm(lambda e, b=bank, w=w, lh=lh, rh=rh: e.matmul(b[0:64, :], w[lh][0:64, :], w[rh][0:64, :],
                                                             start=True, stop=True))
            nmm = mm(lambda e, b=bank, w=w, lh=lh, rh=rh: e.matmul(b[64:128, :], w[lh][64:128, :],
                                                                   w[rh][64:128, :], start=True, stop=True),
                     inc="mmS")
            prog.wait("vector", "mmS", nmm)
            n = vv(lambda e, w=w, b=bank, dst=dst: e.tensor_copy(w[dst][:], b[:]), inc="sq_ev")
            sq_ev[(si, p)] = n
            bank_free[bi] = n

    pr_ev = dict((p, 0) for p in range(NPAIR))
    bank_free3 = {}
    for fi, fk in enumerate(("P2T", "P4T", "P8T", "P16T", "P32T")):
        for p in range(NPAIR):
            w = ws[p]
            bi = p % 4
            bank = ppw[bi][:, 64:128]
            if bi in bank_free3:
                prog.wait("tensor", "pr_ev", bank_free3[bi])
            prog.wait("tensor", "sq_ev", sq_ev[({0: 1, 1: 3, 2: 5, 3: 7, 4: 8}[fi], p)])
            if fi == 0:
                prog.wait("tensor", "dvS", st1_dv[p])
            else:
                prog.wait("tensor", "pr_ev", pr_ev[p])
            mm(lambda e, b=bank, w=w, fk=fk: e.matmul(b[0:64, :], w[fk][0:64, :], w["PPT"][0:64, :],
                                                      start=True, stop=True))
            nmm = mm(lambda e, b=bank, w=w, fk=fk: e.matmul(b[64:128, :], w[fk][64:128, :],
                                                            w["PPT"][64:128, :], start=True, stop=True),
                     inc="mmS")
            prog.wait("vector", "mmS", nmm)
            if fi == 0:
                prog.wait("vector", "dvS", st1_dv[p])
            else:
                prog.wait("vector", "pr_ev", pr_ev[p])
            n = vv(lambda e, w=w, b=bank: e.tensor_add(w["PPT"][:], b[:], w["PPT"][:]), inc="pr_ev")
            pr_ev[p] = n
            bank_free3[bi] = n

    st4_free = {}
    for w4 in range(4):
        pw = list(range(4 * w4, 4 * w4 + 4))
        pc_n = {}
        dv_n = {}
        mm_n = {}
        for p in pw:
            kt, qt = k_t[kof + p], k_t[p]
            KT_A, KT_B = kt[0:64, ts], kt[64:128, ts]
            QT_A, QT_B = qt[0:64, ts], qt[64:128, ts]
            Sb = tl["state_b"][p]
            bank = ppw[p % 4]
            pcn = bank[:, 128:192]
            pcq = bank[:, 192:256]
            if (p % 4) in st4_free:
                prog.wait("tensor", "st4ev", st4_free[p % 4])
            prog.wait("tensor", "dvS", tl["state_dv"][p])
            mm(lambda e, A=KT_A, S=Sb, pcn=pcn: e.matmul(pcn[0:64, :], A, S[0:64, :], start=True, stop=True))
            mm(lambda e, B=KT_B, S=Sb, pcn=pcn: e.matmul(pcn[64:128, :], B, S[64:128, :], start=True, stop=True))
            mm(lambda e, A=QT_A, S=Sb, pcq=pcq: e.matmul(pcq[0:64, :], A, S[0:64, :], start=True, stop=True))
            mm_n[p] = mm(lambda e, B=QT_B, S=Sb, pcq=pcq: e.matmul(pcq[64:128, :], B, S[64:128, :],
                                                                   start=True, stop=True), inc="mmS")
        for p in pw:
            w = ws[p]
            bank = ppw[p % 4]
            pcn = bank[:, 128:192]
            pcq = bank[:, 192:256]
            pnegbg = pcs["negbg"][:, p:p + 1]
            pgam = pcs["gam"][:, p:p + 1]
            prog.wait("vector", "mmS", mm_n[p])
            prog.wait("vector", "dvS", st1_dv[p])
            vv(lambda e, w=w, pcn=pcn, s=pnegbg: e.scalar_tensor_tensor(w["c"][:], pcn[:], s,
                                                                        w["bV"][:], ALU.mult, ALU.add))
            dv_n[p] = vv(lambda e, w=w, pcq=pcq, s=pgam: e.tensor_scalar_mul(w["CSQ"][:], pcq[:], s),
                         inc="dvS")
        for p in pw:
            w = ws[p]
            bank = ppw[p % 4]
            pdl = bank[:, 256:320]
            prog.wait("tensor", "dvS", dv_n[p])
            prog.wait("tensor", "pr_ev", pr_ev[p])
            mm(lambda e, w=w, pdl=pdl: e.matmul(pdl[0:64, :], w["PPT"][0:64, :], w["c"][0:64, :],
                                                start=True, stop=True))
            mm_n[p] = mm(lambda e, w=w, pdl=pdl: e.matmul(pdl[64:128, :], w["PPT"][64:128, :],
                                                          w["c"][64:128, :], start=True, stop=True),
                         inc="mmS")
        for p in pw:
            w = ws[p]
            bank = ppw[p % 4]
            pdl = bank[:, 256:320]
            prog.wait("vector", "mmS", mm_n[p])
            dv_n[p] = vv(lambda e, w=w, pdl=pdl: e.tensor_copy(w["delta"][:], pdl[:]), inc="dvS")
        for p in pw:
            w = ws[p]
            bank = ppw[p % 4]
            pom = bank[:, 320:384]
            psu = bank[:, 384:448]
            prog.wait("tensor", "dvS", dv_n[p])
            mm(lambda e, w=w, pom=pom: e.matmul(pom[0:64, :], w["G"][0:64, :], w["delta"][0:64, :],
                                                start=True, stop=True))
            mm(lambda e, w=w, pom=pom: e.matmul(pom[64:128, :], w["G"][64:128, :], w["delta"][64:128, :],
                                                start=True, stop=True))
            mm(lambda e, w=w, psu=psu: e.matmul(psu[0:64, :], w["W"][0:64, :], w["delta"][0:64, :],
                                                start=True, stop=True))
            mm_n[p] = mm(lambda e, w=w, psu=psu: e.matmul(psu[64:128, :], w["W"][64:128, :],
                                                          w["delta"][64:128, :], start=True, stop=True),
                         inc="mmS")
        for p in pw:
            w = ws[p]
            S = tl["state"][p]
            Sb = tl["state_b"][p]
            bank = ppw[p % 4]
            pom = bank[:, 320:384]
            psu = bank[:, 384:448]
            pgamC = pcs["gamC"][:, p:p + 1]
            prog.wait("vector", "mmS", mm_n[p])
            vv(lambda e, w=w, pom=pom: e.tensor_add(w["orow"][:], pom[:], w["CSQ"][:]))
            nst = vv(lambda e, S=S, psu=psu, s=pgamC: e.scalar_tensor_tensor(S[:], S[:], s,
                                                                             psu[:], ALU.mult, ALU.add),
                     inc="dvS")
            prog.wait("vector", "dvS", nst)
            ns = vv(lambda e, S=S, Sb=Sb: e.tensor_copy(Sb[:], S[:]), inc="dvS")
            tl["state_dv"][p] = ns
            dv_n[p] = ns
        for p in pw:
            w = ws[p]
            prog.wait("scalar", "dvS", dv_n[p])
            pc_n[p] = ss(lambda e, w=w: e.activation(w["osq"][:], w["orow"][:], AF.Square), inc="acS")
        for p in pw:
            w = ws[p]
            prog.wait("vector", "acS", pc_n[p])
            dv_n[p] = vv(lambda e, w=w: e.reduce_sum(w["var"][:], w["osq"][:], axis=AX.X), inc="dvS")
        for p in pw:
            w = ws[p]
            prog.wait("scalar", "dvS", dv_n[p])
            pc_n[p] = ss(lambda e, w=w: e.activation(w["rstd"][:], w["var"][:], AF.Sqrt,
                                                     scale=1.0 / 64.0, bias=EPS_DN), inc="acS")
        for p in pw:
            w = ws[p]
            prog.wait("vector", "acS", pc_n[p])
            nv = vv(lambda e, w=w: e.reciprocal(w["rinv"][:], w["rstd"][:]), inc="dvS")
            prog.wait("vector", "dvS", nv)
            dv_n[p] = vv(lambda e, w=w: e.scalar_tensor_tensor(w["on"][:], w["orow"][:], w["rinv"][:, 0:1],
                                                               tl["normwB"][:], ALU.mult, ALU.mult),
                         inc="dvS")
        for p in pw:
            w = ws[p]
            bank = ppw[p % 4]
            pof = bank[:, 448:512]
            prog.wait("tensor", "dvS", dv_n[p])
            mm(lambda e, w=w, pof=pof: e.matmul(pof[0:64, :], w["on"][0:64, :], IA, start=True, stop=True))
            mm_n[p] = mm(lambda e, w=w, pof=pof: e.matmul(pof[64:128, :], w["on"][64:128, :], IB,
                                                          start=True, stop=True), inc="mmS")
        for p in pw:
            bank = ppw[p % 4]
            pof = bank[:, 448:512]
            prog.wait("vector", "mmS", mm_n[p])
            n4 = vv(lambda e, p=p, pof=pof: e.tensor_mul(tl["o_g"][p][:, ts], pof[:],
                                                         tl["sigz_g"][p][:, ts]), inc="st4ev")
            st4_free[p % 4] = n4


def emit_wout_group(prog, tl, g):
    nc = prog.nc
    ps = [tl["psumA"], tl["psumB"]]
    xb = tl["xbuf"]
    og = tl["o_g"]
    x_in = tl["x_in"]
    x1a = tl["x1a_scr"]
    wop = tl["Woutp"]
    xl = {}
    for dt in range(4):
        xl[dt] = prog.dma("sync", xb[dt][:],
                          bass.AP(x_in.tensor, 128 * dt * T + 512 * g, [[T, 128], [1, 512]]),
                          inc="x_ld")
    prog.wait("tensor", "st4ev", prog.n("st4ev"))
    wo_ev = {}
    for dt in range(16):
        buf = tl["wbuf"][dt % 3]
        if dt >= 3:
            prog.wait("gpsimd", "wofree", prog.n("wofree") - 2)
        prog.dma("gpsimd", buf[:].rearrange("p (a b) -> p a b", b=128),
                 bass.AP(wop.tensor, dt * 128, [[D, 128], [128 * D, 16], [1, 128]]),
                 inc="wodma")
        prog.wait("tensor", "wodma", prog.n("wodma"))
        bank = ps[dt % 2]
        if dt >= 2:
            prog.wait("tensor", "dvS", wo_ev[dt - 2])
        for kt in range(16):
            prog.op("tensor", lambda e, b=bank, buf=buf, kt=kt, og=og:
                    e.matmul(b[:], buf[:, 128 * kt:128 * (kt + 1)], og[kt][:],
                             start=(kt == 0), stop=(kt == 15)),
                    inc="mmS" if kt == 15 else None)
        nmm = prog.n("mmS")
        prog.milestone("tensor", "mmS", "wofree")
        prog.wait("vector", "mmS", nmm)
        prog.wait("vector", "x_ld", xl[dt])
        nv = prog.op("vector", lambda e, b=bank, t_=xb[dt % 4]: e.tensor_add(
            t_[:], b[:], t_[:]), inc="dvS")
        wo_ev[dt] = nv
        prog.wait("sync", "dvS", nv)
        prog.dma("sync", bass.AP(x1a.tensor, 128 * dt * T + 512 * g, [[T, 128], [1, 512]]),
                 xb[dt % 4][:], inc="x1a_st")
        if dt + 4 < 16:
            prog.wait("sync", "x1a_st", prog.n("x1a_st"))
            xl[dt + 4] = prog.dma("sync", xb[dt % 4][:],
                                  bass.AP(x_in.tensor, 128 * (dt + 4) * T + 512 * g,
                                          [[T, 128], [1, 512]]), inc="x_ld")


def emit_ln(prog, tl, src_ap, xn_dst, label):
    nc = prog.nc
    pn = tl["psumC"]
    pb = tl["psumD"]
    lr = tl["lnrow"]
    xb16 = tl["xb16"]
    sqb = tl["sqb"]
    onescb = tl["ones_col_b"]
    onesrb = tl["ones_row_b"]
    ev_g = {}
    for g in range(NGRP):
        if g >= 1:
            prog.wait("gpsimd", "dvS", ev_g[g - 1])
        for kt in range(16):
            prog.dma("gpsimd", xb16[kt][:],
                     bass.AP(src_ap.tensor, 128 * kt * T + 512 * g, [[T, 128], [1, 512]]),
                     inc=label + "ld")
        nld = prog.n(label + "ld")
        prog.wait("scalar", label + "ld", nld)
        prog.wait("tensor", label + "ld", nld)
        if g >= 1:
            prog.wait("tensor", "dvS", ev_g[g - 1])
        for kt in range(16):
            prog.op("scalar", lambda e, kt=kt: e.activation(sqb[kt][:], xb16[kt][:], AF.Square),
                    inc="acS" if kt == 15 else None)
        na = prog.n("acS")
        for kt in range(16):
            prog.op("tensor", lambda e, kt=kt: e.matmul(pn[0:1, 0:512], onescb[:],
                                                        xb16[kt][:], start=(kt == 0), stop=(kt == 15)))
        prog.wait("tensor", "acS", na)
        for kt in range(16):
            prog.op("tensor", lambda e, kt=kt: e.matmul(pb[0:1, 0:512], onescb[:],
                                                        sqb[kt][:], start=(kt == 0), stop=(kt == 15)),
                    inc="mmS" if kt == 15 else None)
        nmm = prog.n("mmS")
        prog.wait("vector", "mmS", nmm)

        def vv(fn, inc="dvS"):
            return prog.op("vector", fn, inc=inc)

        def ss(fn, inc="acS"):
            return prog.op("scalar", fn, inc=inc)

        vv(lambda e: e.tensor_scalar_mul(lr[0][:], pn[0:1, 0:512], 1.0 / D))
        nv = vv(lambda e: e.tensor_scalar_mul(lr[1][:], pb[0:1, 0:512], 1.0 / D))
        prog.wait("vector", "dvS", nv)
        nv = vv(lambda e: e.tensor_mul(lr[2][:], lr[0][:], lr[0][:]))
        prog.wait("vector", "dvS", nv)
        nv = vv(lambda e: e.tensor_sub(lr[1][:], lr[1][:], lr[2][:]))
        prog.wait("scalar", "dvS", nv)
        na = ss(lambda e: e.activation(lr[2][:], lr[1][:], AF.Sqrt, bias=EPS_LN))
        prog.wait("vector", "acS", na)
        nv = vv(lambda e: e.reciprocal(lr[1][:], lr[2][:]))
        prog.wait("vector", "dvS", nv)
        nv = vv(lambda e: e.tensor_mul(lr[2][:], lr[0][:], lr[1][:]))
        prog.wait("vector", "dvS", nv)
        nv = vv(lambda e: e.tensor_copy(lr[3][:], lr[1][:]))
        prog.wait("vector", "dvS", nv)
        nv = vv(lambda e: e.tensor_copy(lr[4][:], lr[2][:]))
        prog.wait("tensor", "dvS", nv)
        prog.op("tensor", lambda e: e.matmul(pn[:, 0:512], onesrb[:], lr[3][:],
                                             start=True, stop=True))
        nmm = prog.op("tensor", lambda e: e.matmul(pb[:, 0:512], onesrb[:], lr[4][:],
                                                   start=True, stop=True), inc="mmS")
        prog.wait("vector", "mmS", nmm)
        for kt in range(16):
            vv(lambda e, kt=kt: e.tensor_mul(sqb[kt][:], xb16[kt][:], pn[:, 0:512]),
               inc=None)
        nv = vv(lambda e: e.tensor_copy(lr[0][:], lr[0][:]))
        prog.wait("vector", "dvS", nv)
        last = None
        for kt in range(16):
            last = vv(lambda e, kt=kt, g=g: e.tensor_sub(
                xn_dst[:, 2048 * kt + 512 * g:2048 * kt + 512 * (g + 1)],
                sqb[kt][:], pb[:, 0:512]), inc="dvS")
        ev_g[g] = last
    prog.barrier(("tensor", "vector", "scalar", "gpsimd"))


def emit_stats(prog, tl):
    nc = prog.nc
    xn = tl["xn"]
    ps = [tl["psumA"], tl["psumB"]]
    st = tl["st"]
    stg = tl["st_stage"]
    ev = {}
    for wi, (which, wt) in enumerate((("b", tl["wb"]), ("a", tl["wa"]))):
        for g in range(NGRP):
            i = wi * NGRP + g
            bank = ps[i % 2]
            if i >= 2:
                prog.wait("tensor", "sb_mv", 16 * (i - 1))
            for kt in range(16):
                prog.op("tensor", lambda e, b=bank, w=wt, kt=kt, g=g:
                        e.matmul(b[0:32, 0:512], w[:, 32 * kt:32 * (kt + 1)],
                                 xn[:, 2048 * kt + 512 * g: 2048 * kt + 512 * (g + 1)],
                                 start=(kt == 0), stop=(kt == 15)),
                        inc="mmS" if kt == 15 else None)
            n = prog.n("mmS")
            prog.wait("vector", "mmS", n)
            prog.wait("vector", "sb_mv", 16 * i)
            nv = prog.op("vector", lambda e, b=bank: e.tensor_copy(stg[:], b[0:32, 0:512]), inc="dvS")
            prog.wait("sync", "dvS", nv)
            dst = st["braw"] if which == "b" else st["araw"]
            prog.dma("sync", dst[32 * g:32 * (g + 1), :], stg[:], inc="sb_mv")
    prog.wait("scalar", "sb_mv", prog.n("sb_mv"))
    prog.wait("vector", "sb_mv", prog.n("sb_mv"))

    def vv(fn, inc="dvS"):
        return prog.op("vector", fn, inc=inc)

    def ss(fn, inc="acS"):
        return prog.op("scalar", fn, inc=inc)

    braw, araw = st["braw"], st["araw"]
    dtb = tl["dt_bias"][:, 0:1]
    na = ss(lambda e: e.activation(st["t0"][:], araw[:], AF.Abs, bias=dtb))
    prog.wait("scalar", "acS", na)
    na = ss(lambda e: e.activation(st["t1"][:], st["t0"][:], AF.Exp, scale=-1.0))
    prog.wait("scalar", "acS", na)
    na = ss(lambda e: e.activation(st["t1"][:], st["t1"][:], AF.Ln, bias=1.0))
    prog.wait("scalar", "acS", na)
    na = ss(lambda e: e.activation(st["t2"][:], araw[:], AF.Relu, bias=dtb))
    prog.wait("vector", "acS", na)
    nv = vv(lambda e: e.tensor_add(st["sp"][:], st["t1"][:], st["t2"][:]))
    prog.wait("scalar", "dvS", nv)
    na = ss(lambda e: e.activation(tl["negexpA"][:], tl["A_log"][:], AF.Exp))
    prog.wait("vector", "acS", na)
    prog.wait("vector", "dvS", nv)
    nea = tl["negexpA"][:, 0:1]
    nv = vv(lambda e, s=nea: e.tensor_scalar(st["lam"][:], st["sp"][:], s, -1.0, ALU.mult, ALU.mult))
    l3 = st["lam"][:].rearrange("p (c k) -> p c k", k=64)
    for sh in (1, 2, 4, 8, 16, 32):
        prog.wait("vector", "dvS", nv)
        nv = vv(lambda e, sh=sh, l3=l3: e.tensor_add(l3[:, :, sh:64], l3[:, :, sh:64], l3[:, :, 0:64 - sh]))
    ss(lambda e: e.activation(st["beta"][:], braw[:], AF.Sigmoid))
    na = ss(lambda e: e.activation(st["t0"][:], braw[:], AF.Abs))
    prog.wait("scalar", "acS", na)
    na = ss(lambda e: e.activation(st["t1"][:], st["t0"][:], AF.Exp, scale=-1.0))
    prog.wait("scalar", "acS", na)
    na = ss(lambda e: e.activation(st["t1"][:], st["t1"][:], AF.Ln, bias=1.0))
    prog.wait("scalar", "acS", na)
    prog.wait("scalar", "dvS", prog.n("dvS"))
    na = ss(lambda e: e.activation(st["t2"][:], braw[:], AF.Relu, scale=-1.0))
    prog.wait("vector", "acS", na)
    nv2 = vv(lambda e: e.tensor_add(st["t2"][:], st["t1"][:], st["t2"][:]))
    prog.wait("vector", "dvS", nv2)
    nv2 = vv(lambda e: e.tensor_sub(st["lamb"][:], st["lam"][:], st["t2"][:]))
    prog.wait("scalar", "dvS", prog.n("dvS"))
    na = ss(lambda e: e.activation(st["gam"][:], st["lam"][:], AF.Exp))
    prog.wait("vector", "acS", na)
    nv3 = vv(lambda e: e.tensor_mul(st["t0"][:], st["beta"][:], st["gam"][:]))
    prog.wait("vector", "dvS", nv3)
    nv3 = vv(lambda e: e.tensor_scalar_mul(st["negbg"][:], st["t0"][:], -1.0))
    _lamh = st["lam"] if not hasattr(st["lam"], "tensor") else st["lam"].tensor
    lamCsrc = bass.AP(_lamh, 63, [[512, 128], [64, 8], [0, 64]])
    nv4 = vv(lambda e, s=lamCsrc, l3=l3: e.tensor_tensor(
        st["t1"][:].rearrange("p (c k) -> p c k", k=64), s, l3[:, :, :], ALU.subtract))
    prog.wait("scalar", "dvS", nv4)
    na = ss(lambda e: e.activation(st["w"][:], st["t1"][:], AF.Exp))
    prog.wait("sync", "acS", na)
    prog.wait("sync", "dvS", prog.n("dvS"))
    for nm in ("lam", "lamb", "beta", "negbg", "gam", "w"):
        for tb in range(4):
            prog.dma("sync", bass.AP(tl["stats_scr"][nm].tensor, tb * 512, [[T, 32], [1, 512]]),
                     st[nm][32 * tb:32 * (tb + 1), :], inc="st_scr")
    prog.wait("sync", "st_scr", prog.n("st_scr"))
    prog.dma("sync", tl["lam_w"][:], tl["stats_scr"]["lam"][:], inc="st_scr")
    prog.dma("sync", tl["lamb_w"][:], tl["stats_scr"]["lamb"][:], inc="st_scr")
    prog.wait("tensor", "st_scr", prog.n("st_scr"))
    prog.barrier(("tensor", "vector", "scalar", "sync"))


def emit_qkvz(prog, tl):
    nc = prog.nc
    xn = tl["xn"]
    wbuf = tl["wbuf"]
    ps = [tl["psumA"], tl["psumB"]]
    pn = tl["psumC"]
    pb = tl["psumD"]
    prog.op("vector", lambda e: e.memset(tl["qkraw2"][0][:, 0:3], 0.0), inc="dvS")
    prog.op("vector", lambda e: e.memset(tl["qkraw2"][1][:, 0:3], 0.0), inc="dvS")
    prog.wait("tensor", "dvS", prog.n("dvS"))
    evac_g = {}
    st_n = {}
    for dt in range(64):
        sec = dt // 16
        par = dt % 2
        raw = tl["qkraw2"][par]
        hat = tl["qkhat2"][par]
        sq = tl["sqt2"][par]
        buf = wbuf[dt % 3]
        if dt >= 3:
            prog.wait("gpsimd", "wfree", dt - 2)
        prog.dma("gpsimd", buf[:].rearrange("p (a b) -> p a b", b=128),
                 bass.AP(tl["Wqkvzp"].tensor, dt * 128, [[8192, 128], [128 * 8192, 16], [1, 128]]),
                 inc="wqdma")
        prog.wait("tensor", "wqdma", prog.n("wqdma"))
        if dt >= 2:
            prog.wait("vector", "hat_st", st_n[dt - 2])
            if sec == 3:
                prog.wait("scalar", "hat_st", st_n[dt - 2])
        for g in range(NGRP):
            i = dt * NGRP + g
            bank = ps[i % 2]
            if i >= 2:
                prog.wait("tensor", "dvS", evac_g[i - 2])
            for kt in range(16):
                prog.op("tensor", lambda e, b=bank, buf=buf, kt=kt, g=g:
                        e.matmul(b[:], buf[:, 128 * kt:128 * (kt + 1)],
                                 xn[:, 2048 * kt + 512 * g: 2048 * kt + 512 * (g + 1)],
                                 start=(kt == 0), stop=(kt == 15)),
                        inc="mmS" if kt == 15 else None)
            n = prog.n("mmS")
            if g == NGRP - 1:
                prog.milestone("tensor", "mmS", "wfree")
            if sec == 3:
                prog.wait("scalar", "mmS", n)
                sN = prog.op("scalar", lambda e, b=bank, g=g, hat=hat:
                             e.activation(hat[:, 512 * g:512 * (g + 1)], b[:], AF.Sigmoid), inc="acS")
                evac_g[i] = prog.n("dvS")
                prog.wait("tensor", "acS", sN)
                prog.wait("sync", "acS", sN)
                prog.dma("sync", tl["sigz_scr"][128 * (dt - 48):128 * (dt - 47), 512 * g:512 * (g + 1)],
                         hat[:, 512 * g:512 * (g + 1)], inc="hat_st")
            else:
                prog.wait("vector", "mmS", n)
                evac_g[i] = prog.op("vector", lambda e, b=bank, g=g, raw=raw:
                                    e.tensor_copy(raw[:, 3 + 512 * g:3 + 512 * (g + 1)], b[:]), inc="dvS")
        if sec == 3:
            st_n[dt] = prog.n("hat_st")
            continue
        cwc = tl["convw_s"][:, 4 * dt:4 * dt + 4]
        nv = prog.n("dvS")
        prog.wait("vector", "dvS", nv)
        nv = prog.op("vector", lambda e, c=cwc, hat=hat, raw=raw: e.tensor_scalar_mul(
            hat[:], raw[:, 3:2051], c[:, 3:4]), inc="dvS")
        for tap in (2, 1, 0):
            prog.wait("vector", "dvS", nv)
            nv = prog.op("vector", lambda e, tap=tap, c=cwc, hat=hat, raw=raw: e.scalar_tensor_tensor(
                hat[:], raw[:, tap:tap + 2048], c[:, tap:tap + 1], hat[:], ALU.mult, ALU.add), inc="dvS")
        if sec == 2:
            prog.wait("sync", "dvS", nv)
            prog.dma("sync", tl["qkvhat_scr"][128 * dt:128 * (dt + 1), :], hat[:], inc="hat_st")
        else:
            prog.wait("scalar", "dvS", nv)
            sN = prog.op("scalar", lambda e, sq=sq, hat=hat: e.activation(sq[:], hat[:], AF.Square),
                         inc="acS")
            prog.wait("tensor", "acS", sN)
            for g in range(NGRP):
                gsl = slice(512 * g, 512 * (g + 1))
                nmm = prog.op("tensor", lambda e, gsl=gsl, sq=sq: e.matmul(
                    pn[0:2, 0:512], tl["bdones"][:], sq[:, gsl], start=True, stop=True), inc="mmS")
                prog.wait("scalar", "mmS", nmm)
                sN = prog.op("scalar", lambda e: e.activation(tl["nrm2"][:], pn[0:2, 0:512], AF.Sqrt,
                                                              bias=EPS_DN), inc="acS")
                prog.wait("vector", "acS", sN)
                nv = prog.op("vector", lambda e: e.reciprocal(tl["nrm"][:], tl["nrm2"][:]), inc="dvS")
                if sec == 0:
                    prog.wait("vector", "dvS", nv)
                    nv = prog.op("vector", lambda e: e.tensor_scalar_mul(tl["nrm"][:], tl["nrm"][:], 0.125),
                                 inc="dvS")
                prog.wait("tensor", "dvS", nv)
                nmm = prog.op("tensor", lambda e: e.matmul(pb[:, 0:512], tl["ind2"][:], tl["nrm"][:],
                                                           start=True, stop=True), inc="mmS")
                prog.wait("vector", "mmS", nmm)
                nv = prog.op("vector", lambda e, gsl=gsl, hat=hat: e.tensor_mul(
                    hat[:, gsl], hat[:, gsl], pb[:, 0:512]), inc="dvS")
                prog.wait("tensor", "dvS", nv)
            prog.wait("sync", "dvS", prog.n("dvS"))
            prog.dma("sync", tl["qkvhat_scr"][128 * dt:128 * (dt + 1), :], hat[:], inc="hat_st")
        st_n[dt] = prog.n("hat_st")
    prog.wait("sync", "hat_st", prog.n("hat_st"))
    prog.barrier()


def emit_ffn(prog, tl, cfg):
    nc = prog.nc
    hid = cfg["hid"]
    nht = hid // 128
    xn = tl["xn"]
    ps = [tl["psumA"], tl["psumB"], tl["psumC"], tl["psumD"]]
    gat = tl["qkhat"]
    sil = tl["sqtile"]
    lbl = cfg["lbl"]
    ev = {}
    for ht in range(nht):
        buf = tl["wbuf"][ht % 3]
        buf2 = tl["wbuf2"][ht % 3]
        if ht >= 3:
            prog.wait("gpsimd", lbl + "f1free", ht - 2)
        prog.dma("gpsimd", buf[:].rearrange("p (a b) -> p a b", b=128),
                 bass.AP(cfg["Wg"].tensor, ht * 128, [[hid, 128], [128 * hid, 16], [1, 128]]),
                 inc=lbl + "f1dma")
        prog.dma("gpsimd", buf2[:].rearrange("p (a b) -> p a b", b=128),
                 bass.AP(cfg["Wu"].tensor, ht * 128, [[hid, 128], [128 * hid, 16], [1, 128]]),
                 inc=lbl + "f1dma")
        prog.wait("tensor", lbl + "f1dma", prog.n(lbl + "f1dma"))
        for g in range(NGRP):
            i = ht * NGRP + g
            bg = ps[(2 * i) % 4]
            bu = ps[(2 * i + 1) % 4]
            qz = cfg.get("quant")
            if i >= 2:
                prog.wait("tensor", "dvS", ev[i - 2])
            for kt in range(16):
                prog.op("tensor", lambda e, b=bg, buf=buf, kt=kt, g=g:
                        e.matmul(b[:], buf[:, 128 * kt:128 * (kt + 1)],
                                 xn[:, 2048 * kt + 512 * g:2048 * kt + 512 * (g + 1)],
                                 start=(kt == 0), stop=(kt == 15 and not qz)))
            if qz:
                prog.op("tensor", lambda e, b=bg, g=g: e.matmul(
                    b[:], tl["neg128_row_b"][:], qz["rs"][:, 512 * g:512 * (g + 1)],
                    start=False, stop=True))
            for kt in range(16):
                prog.op("tensor", lambda e, b=bu, buf=buf2, kt=kt, g=g:
                        e.matmul(b[:], buf[:, 128 * kt:128 * (kt + 1)],
                                 xn[:, 2048 * kt + 512 * g:2048 * kt + 512 * (g + 1)],
                                 start=(kt == 0), stop=(kt == 15 and not qz)),
                        inc="mmS" if (kt == 15 and not qz) else None)
            if qz:
                prog.op("tensor", lambda e, b=bu, g=g: e.matmul(
                    b[:], tl["neg128_row_b"][:], qz["rs"][:, 512 * g:512 * (g + 1)],
                    start=False, stop=True), inc="mmS")
            if g == NGRP - 1:
                prog.milestone("tensor", "mmS", lbl + "f1free")
            nmm = prog.n("mmS")
            gsl = slice(512 * g, 512 * (g + 1))
            qz = cfg.get("quant")
            if qz:
                pass
            prog.wait("scalar", "mmS", nmm)
            if qz:
                sgc = qz["sg"][:, ht:ht + 1]
                suc = qz["su"][:, ht:ht + 1]
                na = prog.op("scalar", lambda e, b=bg, gsl=gsl, s=sgc: e.activation(
                    sil[:, gsl], b[:], AF.Silu, scale=s), inc="acS")
                prog.wait("vector", "acS", na)
                nv0 = prog.op("vector", lambda e, b=bu, gsl=gsl, s=suc: e.tensor_scalar_mul(
                    tl["upt"][:, gsl], b[:], s), inc="dvS")
                prog.wait("vector", "dvS", nv0)
                ev[i] = prog.op("vector", lambda e, gsl=gsl: e.tensor_mul(
                    gat[:, gsl], sil[:, gsl], tl["upt"][:, gsl]), inc="dvS")
            else:
                na = prog.op("scalar", lambda e, b=bg, gsl=gsl: e.activation(sil[:, gsl], b[:], AF.Silu),
                             inc="acS")
                prog.wait("vector", "acS", na)
                ev[i] = prog.op("vector", lambda e, b=bu, gsl=gsl: e.tensor_mul(gat[:, gsl], sil[:, gsl], b[:]),
                                inc="dvS")
        prog.wait("sync", "dvS", ev[ht * NGRP + NGRP - 1])
        prog.dma("sync", cfg["gated"][128 * ht:128 * (ht + 1), :], gat[:], inc=lbl + "f1st")
        prog.wait("vector", lbl + "f1st", prog.n(lbl + "f1st"))
        wdb = tl["wdprep"][ht % 2]
        if ht >= 2:
            prog.wait("gpsimd", lbl + "wdst", 16 * (ht - 1))
        prog.dma("gpsimd", wdb[:], cfg["Wd"][128 * ht:128 * (ht + 1), :], inc=lbl + "wddma")
        prog.wait("sync", lbl + "wddma", prog.n(lbl + "wddma"))
        prog.dma("sync", cfg["wd_scr"][128 * ht:128 * (ht + 1), :], wdb[:], inc=lbl + "wdst")
    prog.wait("gpsimd", lbl + "wdst", prog.n(lbl + "wdst"))
    prog.barrier(("tensor", "vector", "scalar", "sync", "gpsimd"))


def emit_ffn_f2(prog, tl, cfg):
    nc = prog.nc
    hid = cfg["hid"]
    nht = hid // 128
    ps = [tl["psumA"], tl["psumB"], tl["psumC"], tl["psumD"]]
    lbl = cfg["lbl"]
    gatg = tl["gatg"]
    gwb = tl["gwb"]
    xb = tl["xbuf"]
    qz = cfg.get("quant")
    for g in range(NGRP):
        for i in range(nht):
            prog.dma("sync", gatg[i][:],
                     bass.AP(cfg["gated"].tensor, 128 * i * T + 512 * g, [[T, 128], [1, 512]]),
                     inc=lbl + "f2gld")
        for dtile in range(16):
            prog.dma("sync", xb[dtile][:],
                     bass.AP(cfg["res_in"].tensor, 128 * dtile * T + 512 * g, [[T, 128], [1, 512]]),
                     inc=lbl + "f2gld")
        n = prog.n(lbl + "f2gld")
        prog.wait("tensor", lbl + "f2gld", n)
        prog.wait("vector", lbl + "f2gld", n)
        if qz:
            for i in range(nht):
                prog.op("tensor", lambda e, i=i: e.matmul(
                    tl["psum_e"][0:1, 0:512], tl["ones_col_b"][:], gatg[i][:],
                    start=(i == 0), stop=(i == nht - 1)), inc="mmS" if i == nht - 1 else None)
            nms = prog.n("mmS")
            prog.wait("vector", "mmS", nms)
            nv = prog.op("vector", lambda e: e.tensor_copy(tl["gsum_b"][:], tl["psum_e"][0:1, 0:512]),
                         inc="dvS")
            prog.wait("tensor", "dvS", nv)
        ev2 = {}
        for dtile in range(16):
            wb1 = gwb[dtile % 2]
            if dtile >= 2:
                prog.wait("sync", lbl + "f2free", (g * 16 + dtile - 1))
            prog.dma("sync", wb1[:, 0:128 * nht].rearrange("p (a b) -> p a b", b=128),
                     bass.AP(cfg["wd_scr"].tensor, dtile * 128, [[D, 128], [128 * D, nht], [1, 128]]),
                     inc=lbl + "f2wld")
            prog.wait("tensor", lbl + "f2wld", prog.n(lbl + "f2wld"))
            bank = ps[dtile % 2]
            if dtile >= 2:
                prog.wait("tensor", "dvS", ev2[dtile - 2])
            for i in range(nht):
                prog.op("tensor", lambda e, b=bank, bs=wb1, iw=i, gg=gatg[i], i=i:
                        e.matmul(b[:], bs[:, 128 * iw:128 * (iw + 1)], gg[:],
                                 start=(i == 0), stop=(i == nht - 1 and not qz)),
                        inc="mmS" if (i == nht - 1 and not qz) else None)
            if qz:
                prog.op("tensor", lambda e, b=bank: e.matmul(
                    b[:], tl["neg128_row_b"][:], tl["gsum_b"][:], start=False, stop=True), inc="mmS")
            nmm = prog.n("mmS")
            prog.milestone("tensor", "mmS", lbl + "f2free")
            prog.wait("vector", "mmS", nmm)
            if qz:
                sdc = qz["sd"][:, dtile:dtile + 1]
                nv = prog.op("vector", lambda e, b=bank, t_=xb[dtile], s=sdc: e.scalar_tensor_tensor(
                    t_[:], b[:], s, t_[:], ALU.mult, ALU.add), inc="dvS")
            else:
                nv = prog.op("vector", lambda e, b=bank, t_=xb[dtile]: e.tensor_add(
                    t_[:], b[:], t_[:]), inc="dvS")
            ev2[dtile] = nv
            prog.wait("sync", "dvS", nv)
            prog.dma("sync", bass.AP(cfg["out"].tensor, 128 * dtile * T + 512 * g, [[T, 128], [1, 512]]),
                     xb[dtile][:], inc="fout")
        prog.barrier(("tensor", "vector", "sync"))
    prog.wait("gpsimd", "fout", prog.n("fout"))


CMAP = [("ident_s", "ident", "BF16", [128, 128]), ("identF_s", "identF", "F32", [128, 128]),
        ("sel4", "sel4_c", "BF16", [4, 128]), ("ones_row_f", "ones_row_fc", "F32", [1, 128]),
        ("ones128", "ones128_c", "BF16", [128, 128]),
        ("neg128_row_b", "neg128_row", "BF16", [1, 128]),
        ("eye2_s", "eye2", "BF16", [128, 64]),
        ("ones_col_b", "ones_col", "BF16", [128, 1]), ("ones_row_b", "ones_row", "BF16", [1, 128]),
        ("bdones", "bdones_c", "BF16", [128, 2]), ("ind2", "ind2_c", "F32", [2, 128]),
        ("selP_s", "selP", "F32", [32, NPAIR * 128]), ("selPn_s", "selPn", "F32", [32, NPAIR * 128]),
        ("ohLn_s", "ohLn", "F32", [32, NPAIR * 64]), ("ohLB_s", "ohLB", "F32", [32, NPAIR * 64]),
        ("mask_E1_s", "mask_E1", "F32", [128, 64]), ("mask_E2_s", "mask_E2", "F32", [128, 64]),
        ("mask_E1T_s", "mask_E1T", "F32", [128, 64])]


def l0_dram(nc):
    dram = {}
    dram["x_in"] = nc.dram_tensor("x_in", [D, T], F32, kind="ExternalInput").ap()
    specs = [("Wqkvzp", [D, 8192], F32), ("Wb", [D, 32], F32), ("Wa", [D, 32], F32),
             ("convw", [6144, 4], F32), ("A_log", [128, 1], F32), ("dt_bias", [128, 1], F32),
             ("normw", [1, 64], F32), ("Woutp", [D, D], F32),
             ("Wg0", [D, 8192], F32), ("Wu0", [D, 8192], F32), ("Wd0", [8192, D], F32)]
    for nm, shape, dt in specs:
        dram[nm] = nc.dram_tensor(nm, shape, dt, kind="ExternalInput").ap()
    for sname, dname, dt, shape in CMAP:
        dram[dname] = nc.dram_tensor(dname, shape, F32, kind="ExternalInput").ap()
    dram["x1_out"] = nc.dram_tensor("x1_out", [D, T], F32, kind="ExternalOutput").ap()
    return dram


def l0_scratch(nc, tag=""):
    scr = {}
    scr["qkvhat"] = nc.dram_tensor(f"qkvhat_scr{tag}", [6144, T], BF16, kind="Internal").ap()
    scr["sigz"] = nc.dram_tensor(f"sigz_scr{tag}", [D, T], BF16, kind="Internal").ap()
    scr["x1a"] = nc.dram_tensor(f"x1a_scr{tag}", [D, T], F32, kind="Internal").ap()
    scr["xn"] = nc.dram_tensor(f"xn_scr{tag}", [D, T], BF16, kind="Internal").ap()
    scr["gated"] = nc.dram_tensor(f"gated_scr{tag}", [8192, T], BF16, kind="Internal").ap()
    scr["wd_bf"] = nc.dram_tensor(f"wd_bf_scr{tag}", [8192, D], BF16, kind="Internal").ap()
    for nm in ("lam", "lamb", "beta", "negbg", "gam", "w"):
        scr[nm] = nc.dram_tensor(f"stat_{nm}{tag}", [32, T], F32, kind="Internal").ap()
    return scr


def alloc_common(nc, E, tl, dram):
    tl["consts"] = {}
    for sname, dname, dts, shape in CMAP:
        h = E(nc.sbuf_tensor(sname, shape, BF16 if dts == "BF16" else F32))
        tl["consts"][sname] = h
        tl[sname] = h
    tl["fconst"] = E(nc.sbuf_tensor("fconst", [128, 8], F32))
    tl["wbuf"] = [E(nc.sbuf_tensor(f"wbuf{i}", [128, 2048], BF16)) for i in range(3)]
    tl["psumA"] = E(nc.psum_tensor("psA", [128, 512], F32))
    tl["psumB"] = E(nc.psum_tensor("psB", [128, 512], F32))
    tl["psumC"] = E(nc.psum_tensor("psC", [128, 512], F32))
    tl["psumD"] = E(nc.psum_tensor("psD", [128, 512], F32))
    tl["psum_sm"] = E(nc.psum_tensor("psSM", [128, 512], F32))
    tl["psum_e"] = E(nc.psum_tensor("psE", [128, 512], F32))
    tl["A_log"] = E(nc.sbuf_tensor("A_log_s", [128, 1], F32))
    tl["dt_bias"] = E(nc.sbuf_tensor("dt_bias_s", [128, 1], F32))
    tl["negexpA"] = E(nc.sbuf_tensor("negexpA", [128, 1], F32))
    tl["normw_b"] = E(nc.sbuf_tensor("normw_b", [1, 64], BF16))
    tl["normwB"] = E(nc.sbuf_tensor("normwB", [128, 64], F32))
    tl["convw_s"] = E(nc.sbuf_tensor("convw_s", [128, 48 * 4], F32))
    tl["lnrow"] = ([E(nc.sbuf_tensor(f"lnrow{i}", [1, 512], F32)) for i in range(3)]
                   + [E(nc.sbuf_tensor(f"lnrow{i}", [1, 512], BF16)) for i in (3, 4)])


def emit_const_loads(prog, tl, dram):
    for sname, dname, dts, shape in CMAP:
        if dts == "BF16":
            prog.dma("gpsimd", tl["consts"][sname][:], dram[dname][:], inc="c_ldg")
        else:
            prog.dma("sync", tl["consts"][sname][:], dram[dname][:], inc="c_ld")
    prog.dma("sync", tl["convw_s"][:].rearrange("p (a b) -> p a b", b=4),
             bass.AP(dram["convw"].tensor, 0, [[4, 128], [128 * 4, 48], [1, 4]]), inc="c_ld")
    prog.dma("sync", tl["A_log"][:], dram["A_log"][:], inc="c_ld")
    prog.dma("sync", tl["dt_bias"][:], dram["dt_bias"][:], inc="c_ld")
    prog.dma("gpsimd", tl["normw_b"][:], dram["normw"][:], inc="c_ldg")
    for e in ("tensor", "vector", "scalar"):
        prog.wait(e, "c_ld", prog.n("c_ld"))
        prog.wait(e, "c_ldg", prog.n("c_ldg"))


def emit_layer0(prog, tl_c, nc, ctx, dram, scr, x_in_ap, x_out_ap):
    E = ctx.enter_context
    tl = dict(tl_c)
    tl["x_in"] = x_in_ap
    tl["Wqkvzp"] = dram["Wqkvzp"]
    tl["Woutp"] = dram["Woutp"]
    tl["qkvhat_scr"] = scr["qkvhat"]
    tl["sigz_scr"] = scr["sigz"]
    tl["x1a_scr"] = scr["x1a"]
    tl["stats_scr"] = {nm: scr[nm] for nm in ("lam", "lamb", "beta", "negbg", "gam", "w")}
    tl["lam_w"] = E(nc.sbuf_tensor("lam_w", [32, 2048], F32))
    tl["lamb_w"] = E(nc.sbuf_tensor("lamb_w", [32, 2048], F32))
    st = {}
    tl["st"] = st

    with ExitStack() as phb:
        Eb = phb.enter_context
        tl["xn"] = Eb(nc.sbuf_tensor("xn0", [128, 16 * 2048], BF16))
        for nm in ("braw", "araw", "t0", "t1", "t2", "sp", "lam", "lamb", "beta",
                   "negbg", "gam", "w"):
            st[nm] = Eb(nc.sbuf_tensor(f"st_{nm}", [128, 512], F32))
        tl["st_stage"] = Eb(nc.sbuf_tensor("st_stage", [32, 512], F32))
        tl["wb"] = Eb(nc.sbuf_tensor("wb_s", [128, 16 * 32], BF16))
        tl["wa"] = Eb(nc.sbuf_tensor("wa_s", [128, 16 * 32], BF16))
        prog.dma("gpsimd", tl["wb"][:].rearrange("p (a b) -> p a b", b=32),
                 bass.AP(dram["Wb"].tensor, 0, [[32, 128], [128 * 32, 16], [1, 32]]), inc="c_ldg")
        prog.dma("gpsimd", tl["wa"][:].rearrange("p (a b) -> p a b", b=32),
                 bass.AP(dram["Wa"].tensor, 0, [[32, 128], [128 * 32, 16], [1, 32]]), inc="c_ldg")
        prog.wait("tensor", "c_ldg", prog.n("c_ldg"))
        with ExitStack() as phln:
            El = phln.enter_context
            tl["xb16"] = [El(nc.sbuf_tensor(f"xb16_{i}", [128, 512], BF16)) for i in range(16)]
            tl["sqb"] = [El(nc.sbuf_tensor(f"sqb{i}", [128, 512], BF16)) for i in range(16)]
            emit_ln(prog, tl, x_in_ap, tl["xn"][:], "ln0")
            prog.barrier()
        tl["qkraw2"] = [Eb(nc.sbuf_tensor(f"qkraw{i}", [128, 2051], BF16)) for i in range(2)]
        tl["qkhat2"] = [Eb(nc.sbuf_tensor(f"qkhat{i}", [128, 2048], BF16)) for i in range(2)]
        tl["sqt2"] = [Eb(nc.sbuf_tensor(f"sqt{i}", [128, 2048], BF16)) for i in range(2)]
        tl["nrm2"] = Eb(nc.sbuf_tensor("nrm2", [2, 512], F32))
        tl["nrm"] = Eb(nc.sbuf_tensor("nrm", [2, 512], F32))
        prog.wait("sync", "dvS", prog.n("dvS"))
        for kt in range(16):
            prog.dma("sync", scr["xn"][128 * kt:128 * (kt + 1), :],
                     tl["xn"][:, 2048 * kt:2048 * (kt + 1)], inc="xn_st")
        emit_stats(prog, tl)
        emit_qkvz(prog, tl)
        prog.wait("sync", "xn_st", prog.n("xn_st"))
        prog.barrier()

    with ExitStack() as phc:
        Ec = phc.enter_context
        tl["qk_g"] = [Ec(nc.sbuf_tensor(f"qkg{i}", [128, GRP], BF16)) for i in range(48)]
        tl["sigz_g"] = [Ec(nc.sbuf_tensor(f"szg{i}", [128, GRP], BF16)) for i in range(16)]
        tl["o_g"] = [Ec(nc.sbuf_tensor(f"og{i}", [128, GRP], BF16)) for i in range(16)]
        tl["state"] = [Ec(nc.sbuf_tensor(f"Sst{i}", [128, 64], F32)) for i in range(NPAIR)]
        tl["state_b"] = [Ec(nc.sbuf_tensor(f"Sstb{i}", [128, 64], BF16)) for i in range(NPAIR)]
        tl["state_dv"] = [0] * NPAIR
        tl["xbuf"] = [Ec(nc.sbuf_tensor(f"xbufC{i}", [128, 512], F32)) for i in range(4)]
        tl["pct"] = Ec(nc.sbuf_tensor("pct", [32, 128], F32))
        tl["pcT_in"] = Ec(nc.sbuf_tensor("pcT_in", [80, 128], F32))
        tl["pcall"] = Ec(nc.sbuf_tensor("pcall", [128, 80], F32))
        wn_pp = ("NT", "NTT", "G", "P2", "P2T", "P4", "P4T", "P8", "P8T",
                 "P16", "P16T", "P32T", "PPT", "bV", "W")
        wn_s4b = ("c", "CSQ", "delta", "on")
        wn_s4f = ("orow", "osq")
        wn_s2f = ("e1f", "e2f", "e1tf")
        wn_s2b = ("E1", "E2", "E1T")
        slot4 = []
        for s in range(4):
            d = {}
            for nm in wn_s4b:
                d[nm] = Ec(nc.sbuf_tensor(f"s4_{s}_{nm}", [128, 64], BF16))
            for nm in wn_s4f:
                d[nm] = Ec(nc.sbuf_tensor(f"s4_{s}_{nm}", [128, 64], F32))
            for nm in ("var", "rstd", "rinv"):
                d[nm] = Ec(nc.sbuf_tensor(f"s4_{s}_{nm}", [128, 1], F32))
            slot4.append(d)
        slot2 = []
        for s in range(2):
            d = {}
            for nm in wn_s2f:
                d[nm] = Ec(nc.sbuf_tensor(f"s2_{s}_{nm}", [128, 64], F32))
            for nm in wn_s2b:
                d[nm] = Ec(nc.sbuf_tensor(f"s2_{s}_{nm}", [128, 64], BF16))
            slot2.append(d)
        ws = []
        for p in range(NPAIR):
            w = {}
            for nm in wn_pp:
                w[nm] = Ec(nc.sbuf_tensor(f"w{p}_{nm}", [128, 64], BF16))
            w.update(slot4[p % 4])
            w.update(slot2[p % 2])
            ws.append(w)
        tl["ws"] = ws
        emit_scan(prog, tl)
        prog.barrier()

    with ExitStack() as phd:
        Ed = phd.enter_context
        tl["xn"] = Ed(nc.sbuf_tensor("xn1", [128, 16 * 2048], BF16))
        tl["qkhat"] = Ed(nc.sbuf_tensor("gatt", [128, 2048], BF16))
        tl["wbuf2"] = [Ed(nc.sbuf_tensor(f"wbuf2_{i}", [128, 2048], BF16)) for i in range(3)]
        tl["wdprep"] = [Ed(nc.sbuf_tensor(f"wdprep{i}", [128, 2048], BF16)) for i in range(2)]
        tl["sqtile"] = Ed(nc.sbuf_tensor("silt", [128, 2048], BF16))
        for kt in range(16):
            prog.dma("gpsimd", tl["xn"][:, 2048 * kt:2048 * (kt + 1)],
                     scr["xn"][128 * kt:128 * (kt + 1), :], inc="xn_ld")
        prog.wait("tensor", "xn_ld", prog.n("xn_ld"))
        cfg0 = dict(Wg=dram["Wg0"], Wu=dram["Wu0"], Wd=dram["Wd0"],
                    wd_scr=scr["wd_bf"], gated=scr["gated"], hid=8192,
                    res_in=scr["x1a"], out=x_out_ap, lbl="f0")
        emit_ffn(prog, tl, cfg0)
    with ExitStack() as phe:
        Ee = phe.enter_context
        tl["gatg"] = [Ee(nc.sbuf_tensor(f"gatg{i}", [128, 512], BF16)) for i in range(64)]
        tl["gwb"] = [Ee(nc.sbuf_tensor(f"gwb{i}", [128, 8192], BF16)) for i in range(2)]
        tl["xbuf"] = [Ee(nc.sbuf_tensor(f"xbufF{i}", [128, 512], F32)) for i in range(16)]
        emit_ffn_f2(prog, tl, cfg0)
    prog.barrier()


def build_layer0():
    nc = bass.Bass("TRN2", target_bir_lowering=False)
    dram = l0_dram(nc)
    scr = l0_scratch(nc)
    with ExitStack() as ctx:
        E = ctx.enter_context
        tl = {}
        alloc_common(nc, E, tl, dram)
        prog = Prog(nc, ctx)
        register_float_consts(prog, tl)
        emit_const_loads(prog, tl, dram)
        prog.barrier()
        emit_layer0(prog, tl, nc, ctx, dram, scr, dram["x_in"], dram["x1_out"])
        prog.finalize(E(nc.Block()))
    return nc


def l0_in_map(inputs, h, cs, b):
    m = {"x_in": np.ascontiguousarray(inputs["L0_attn_hidden_states"][b].T)}
    for nm in ("Wqkvzp", "Wb", "Wa", "convw", "A_log", "dt_bias", "Woutp"):
        m[nm] = h[nm]
    m["normw"] = h["normw"]
    m["Wg0"], m["Wu0"], m["Wd0"] = h["Wg0"], h["Wu0"], h["Wd0"]
    for sname, dname, dts, shape in CMAP:
        m[dname] = cs[dname]
    return m


def kernel(**inputs):
    return kernel_fused(**inputs)


def kernel_unfused(**inputs):
    h = host_prep(inputs)
    cs = host_consts()
    nc0 = build_layer0()
    in0 = [l0_in_map(inputs, h, cs, b) for b in range(8)]
    r0 = run_bass_kernel_spmd(nc0, in0, core_ids=list(range(8)))
    nc1 = build_layer1()
    in1 = [l1_in_map(h, cs, r0.results[b]["x1_out"]) for b in range(8)]
    r1 = run_bass_kernel_spmd(nc1, in1, core_ids=list(range(8)))
    out = np.stack([r1.results[b]["x2_out"].T for b in range(8)])
    return np.ascontiguousarray(out)


def emit_rowsum(prog, tl, rs_b):
    xn = tl["xn"]
    pn = tl["psumC"]
    onescb = tl["ones_col_b"]
    for g in range(NGRP):
        if g >= 1:
            prog.wait("tensor", "dvS", prog.n("dvS"))
        for kt in range(16):
            prog.op("tensor", lambda e, kt=kt, g=g: e.matmul(
                pn[0:1, 0:512], onescb[:],
                xn[:, 2048 * kt + 512 * g:2048 * kt + 512 * (g + 1)],
                start=(kt == 0), stop=(kt == 15)), inc="mmS" if kt == 15 else None)
        n = prog.n("mmS")
        prog.wait("vector", "mmS", n)
        nv = prog.op("vector", lambda e, rs=rs_b, g=g: e.tensor_copy(rs[:, 512 * g:512 * (g + 1)],
                                                                     pn[0:1, 0:512]), inc="dvS")
    prog.wait("tensor", "dvS", nv)


def emit_wscale_build(prog, tl, codes_ap, sc_ap, wsc, head_or_dt, lbl):
    h = head_or_dt
    cbuf = tl["codebuf"]
    scs = tl["sc_s"]
    pbanks = (tl["psum_sm"], tl["psum_e"])
    prog.wait("gpsimd", "dvS", tl["wsc_free"][0])
    prog.dma("gpsimd", cbuf[:].rearrange("p (a b) -> p a b", b=128),
             bass.AP(codes_ap.tensor, h * 128, [[2048, 128], [128 * 2048, 16], [1, 128]]),
             inc=lbl + "cdma")
    prog.dma("gpsimd", scs[:].rearrange("p (a b) -> p a b", b=128),
             bass.AP(sc_ap.tensor, h * 128, [[2048, 4], [4 * 2048, 16], [1, 128]]),
             inc=lbl + "cdma")
    prog.wait("tensor", lbl + "cdma", prog.n(lbl + "cdma"))
    for kt in range(16):
        pb = pbanks[kt % 2]
        par = (kt // 2) % 2
        if kt >= 2:
            prog.wait("tensor", "dvS", tl["wsc_ev"][kt - 2])
        nmm = prog.op("tensor", lambda e, kt=kt, par=par, pb=pb: e.matmul(
            pb[:, 128 * par:128 * (par + 1)], tl["sel4"][:], scs[:, 128 * kt:128 * (kt + 1)],
            start=True, stop=True), inc="mmS")
        prog.wait("vector", "mmS", nmm)
        tl["wsc_ev"][kt] = prog.op("vector", lambda e, kt=kt, par=par, pb=pb, w=wsc, c=cbuf: e.tensor_mul(
            w[:, 128 * kt:128 * (kt + 1)], c[:, 128 * kt:128 * (kt + 1)],
            pb[:, 128 * par:128 * (par + 1)]), inc="dvS")
    tl["wsc_free"][0] = tl["wsc_ev"][15]
    return tl["wsc_ev"][15]


def emit_qkv1(prog, tl, dram, scr):
    nc = prog.nc
    xn = tl["xn"]
    ps = [tl["psumA"], tl["psumB"]]
    ev_bank = {0: 0, 1: 0}
    st_n = {}
    for mi, mat in enumerate(("q", "k", "v")):
        for h in range(16):
            j = mi * 16 + h
            wsc = tl["wsc2"][j % 2]
            tl["codebuf"] = tl["cb2"][j % 2]
            tl["sc_s"] = tl["scs2"][j % 2]
            wev = emit_wscale_build(prog, tl, dram[f"Cw{mat}"], dram[f"Sw{mat}"], wsc, h, "at")
            prog.wait("tensor", "dvS", wev)
            bias = tl["bqkv"][mat][:, h:h + 1]
            stage = tl["qkvstage"][j % 2]
            ev_g = {}
            for g in range(NGRP):
                gi = j * NGRP + g
                bank = ps[gi % 2]
                if g >= 2:
                    prog.wait("tensor", "dvS", ev_g[g - 2])
                else:
                    prog.wait("tensor", "dvS", ev_bank[gi % 2])
                for kt in range(16):
                    prog.op("tensor", lambda e, b=bank, w=wsc, kt=kt, g=g: e.matmul(
                        b[:], w[:, 128 * kt:128 * (kt + 1)],
                        xn[:, 2048 * kt + 512 * g:2048 * kt + 512 * (g + 1)],
                        start=(kt == 0), stop=(kt == 15)), inc="mmS" if kt == 15 else None)
                nmm = prog.n("mmS")
                prog.wait("vector", "mmS", nmm)
                if g == 0 and j >= 2:
                    prog.wait("vector", "qkv_st", st_n[j - 2])
                ev_g[g] = prog.op("vector", lambda e, b=bank, g=g, bi=bias, s=stage: e.tensor_scalar_add(
                    s[:, 512 * g:512 * (g + 1)], b[:], bi), inc="dvS")
                ev_bank[gi % 2] = ev_g[g]
            nv = ev_g[3]
            prog.wait("sync", "dvS", nv)
            prog.dma("sync", scr["qkv1"][mat][128 * h:128 * (h + 1), :], stage[:], inc="qkv_st")
            st_n[j] = prog.n("qkv_st")
    prog.wait("vector", "qkv_st", st_n[47])
    prog.barrier(("tensor", "vector", "scalar", "sync", "gpsimd"))


def emit_attn_scores(prog, tl, dram, scr):
    nc = prog.nc
    ps = [tl["psumA"], tl["psumB"]]
    pso = tl["psumC"]
    pse = tl["psum_e"]
    IA = tl["ident_s"]
    onescb = tl["ones_col_b"]
    onesrb = tl["ones_row_b"]
    hd_free = {}
    ld_n = {}
    st_n = {}
    for mat, dst in (("q", tl["qh2"][0]), ("k", tl["kh2"][0]), ("v", tl["vh2"][0])):
        prog.dma("sync", dst[:], scr["qkv1"][mat][0:128, :], inc="hd_ld")
    ld_n[0] = prog.n("hd_ld")
    for h in range(16):
        par = h % 2
        qh, kh, vh = tl["qh2"][par], tl["kh2"][par], tl["vh2"][par]
        if h + 1 < 16:
            if h >= 1:
                prog.wait("sync", "mmS", hd_free[h - 1])
            for mat, dst in (("q", tl["qh2"][1 - par]), ("k", tl["kh2"][1 - par]),
                             ("v", tl["vh2"][1 - par])):
                prog.dma("sync", dst[:], scr["qkv1"][mat][128 * (h + 1):128 * (h + 2), :],
                         inc="hd_ld")
            ld_n[h + 1] = prog.n("hd_ld")
        prog.wait("tensor", "hd_ld", ld_n[h])
        prog.wait("scalar", "hd_ld", ld_n[h])
        vbanks = (tl["psumD"], tl["psum_sm"])
        for kt in range(16):
            vb = vbanks[kt % 2]
            if kt >= 2:
                prog.wait("tensor", "dvS", tl["vt_ev"][kt - 2])
            nmm = prog.op("tensor", lambda e, kt=kt, vb=vb, vt=vh: e.matmul(
                vb[:, 0:128], vt[:, 128 * kt:128 * (kt + 1)], IA[:],
                start=True, stop=True), inc="mmS")
            prog.wait("vector", "mmS", nmm)
            tl["vt_ev"][kt] = prog.op("vector", lambda e, kt=kt, vb=vb, par=par: e.tensor_copy(
                tl["VT2"][par][kt][:], vb[:, 0:128]), inc="dvS")
        for mi, mat in enumerate(("q", "k")):
            src = qh if mi == 0 else kh
            prog.wait("scalar", "dvS", prog.n("dvS"))
            na = prog.op("scalar", lambda e, s=src: e.activation(tl["sqh"][:], s[:], AF.Square),
                         inc="acS")
            prog.wait("tensor", "acS", na)
            mdst = tl["mxq"] if mi == 0 else tl["mxk"]
            for g in range(NGRP):
                if g >= 1:
                    prog.wait("tensor", "dvS", prog.n("dvS"))
                prog.op("tensor", lambda e, g=g: e.matmul(
                    pse[:, 0:512], tl["ones128"][:], tl["sqh"][:, 512 * g:512 * (g + 1)],
                    start=True, stop=True), inc="mmS")
                nmm = prog.n("mmS")
                prog.wait("vector", "mmS", nmm)
                if g == 0:
                    prog.op("vector", lambda e, m=mdst: e.reduce_max(m[:, 0:1], pse[:, 0:512], axis=AX.X),
                            inc="dvS")
                else:
                    prog.wait("vector", "dvS", prog.n("dvS"))
                    prog.op("vector", lambda e, m=mdst: e.reduce_max(m[:, 1:2], pse[:, 0:512], axis=AX.X),
                            inc="dvS")
                    prog.wait("vector", "dvS", prog.n("dvS"))
                    prog.op("vector", lambda e, m=mdst: e.tensor_max(m[:, 0:1], m[:, 0:1], m[:, 1:2]),
                            inc="dvS")
        prog.wait("vector", "dvS", prog.n("dvS"))
        nv = prog.op("vector", lambda e: e.tensor_mul(tl["mxq"][:, 2:3], tl["mxq"][:, 0:1],
                                                      tl["mxk"][:, 0:1]), inc="dvS")
        prog.wait("scalar", "dvS", nv)
        na = prog.op("scalar", lambda e: e.activation(tl["mxq"][:, 3:4], tl["mxq"][:, 2:3], AF.Sqrt,
                                                      scale=1.0 / 128.0), inc="acS")
        prog.wait("vector", "acS", na)
        nv = prog.op("vector", lambda e, par=par: e.tensor_scalar(tl["negB2"][par][:], tl["mxq"][:, 3:4],
                                                                  -1.0, -1.0, ALU.mult, ALU.add), inc="dvS")
        prog.wait("scalar", "dvS", nv)
        prog.wait("tensor", "dvS", nv)
        for g in range(NGRP):
            qsl = slice(512 * g, 512 * (g + 1))
            for kt in range(16):
                bank = ps[kt % 2]
                if kt >= 2:
                    prog.wait("tensor", "acS", tl["pt_ev"][kt - 2])
                nmm = prog.op("tensor", lambda e, b=bank, kt=kt, qsl=qsl, kh=kh, qh=qh: e.matmul(
                    b[:], kh[:, 128 * kt:128 * (kt + 1)], qh[:, qsl],
                    start=True, stop=True), inc="mmS")
                prog.wait("scalar", "mmS", nmm)
                tl["pt_ev"][kt] = prog.op("scalar", lambda e, b=bank, kt=kt, par=par: e.activation(
                    tl["PT"][kt][:], b[:], AF.Exp, bias=tl["negB2"][par][:, 0:1],
                    scale=0.08838834764831845), inc="acS")
            na = prog.n("acS")
            prog.wait("tensor", "acS", na)
            for kt in range(16):
                prog.op("tensor", lambda e, kt=kt: e.matmul(
                    pse[0:1, 0:512], onescb[:], tl["PT"][kt][:],
                    start=(kt == 0), stop=(kt == 15)), inc=None)
            for kt in range(16):
                prog.op("tensor", lambda e, kt=kt, par=par: e.matmul(
                    pso[:], tl["VT2"][par][kt][:], tl["PT"][kt][:],
                    start=(kt == 0), stop=(kt == 15)), inc="mmS" if kt == 15 else None)
            nmm = prog.n("mmS")
            prog.wait("vector", "mmS", nmm)
            nv = prog.op("vector", lambda e: e.reciprocal(tl["rrow"][:], pse[0:1, 0:512]), inc="dvS")
            prog.wait("vector", "dvS", nv)
            nv = prog.op("vector", lambda e: e.tensor_copy(tl["rrow_b"][:], tl["rrow"][:]), inc="dvS")
            prog.wait("tensor", "dvS", nv)
            nmm = prog.op("tensor", lambda e: e.matmul(tl["psumD"][:, 0:512], onesrb[:], tl["rrow_b"][:],
                                                       start=True, stop=True), inc="mmS")
            prog.wait("vector", "mmS", nmm)
            nv = prog.op("vector", lambda e: e.tensor_copy(tl["rb"][:], tl["psumD"][:, 0:512]), inc="dvS")
            prog.wait("vector", "dvS", nv)
            if g == 0 and h >= 2:
                prog.wait("vector", "o_st", st_n[h - 2])
            nv = prog.op("vector", lambda e, qsl=qsl, par=par: e.tensor_mul(tl["oh2"][par][:, qsl], pso[:],
                                                                            tl["rb"][:]), inc="dvS")
            prog.wait("tensor", "dvS", nv)
            prog.wait("scalar", "dvS", nv)
        hd_free[h] = prog.n("mmS")
        nv = prog.n("dvS")
        prog.wait("gpsimd", "dvS", nv)
        prog.dma("gpsimd", scr["o1"][128 * h:128 * (h + 1), :], tl["oh2"][par][:], inc="o_st")
        st_n[h] = prog.n("o_st")
    prog.wait("vector", "o_st", st_n[15])
    prog.barrier(("tensor", "vector", "scalar", "sync", "gpsimd"))


def emit_wo_residual(prog, tl, dram, scr, x_in_ap):
    nc = prog.nc
    ps = [tl["psumA"], tl["psumB"]]
    for g in range(NGRP):
        for i in range(16):
            prog.dma("sync", tl["o_g1"][i][:],
                     bass.AP(scr["o1"].tensor, 128 * i * T + 512 * g, [[T, 128], [1, 512]]),
                     inc="wo1_ld")
        n = prog.n("wo1_ld")
        prog.wait("tensor", "wo1_ld", n)
        ev2 = {}
        for dt in range(16):
            wev = emit_wscale_build(prog, tl, dram["Cwo"], dram["Swo"], tl["wsc"], dt, "wo")
            prog.wait("tensor", "dvS", wev)
            bank = ps[dt % 2]
            if dt >= 2:
                prog.wait("tensor", "dvS", ev2[dt - 2])
            for kt in range(16):
                prog.op("tensor", lambda e, b=bank, kt=kt: e.matmul(
                    b[:], tl["wsc"][:, 128 * kt:128 * (kt + 1)], tl["o_g1"][kt][:],
                    start=(kt == 0), stop=(kt == 15)), inc="mmS" if kt == 15 else None)
            nmm = prog.n("mmS")
            xb = tl["xbuf"][dt % 4]
            nx = prog.dma("sync", xb[:],
                          bass.AP(x_in_ap.tensor, 128 * dt * T + 512 * g, [[T, 128], [1, 512]]),
                          inc="x1_ld")
            prog.wait("vector", "mmS", nmm)
            prog.wait("vector", "x1_ld", nx)
            nv = prog.op("vector", lambda e, b=bank, xb=xb, dt=dt: e.scalar_tensor_tensor(
                xb[:], b[:], tl["bqkv"]["o"][:, dt:dt + 1] if False else tl["bo_s"][:, dt:dt + 1],
                xb[:], ALU.add, ALU.add), inc="dvS")
            ev2[dt] = nv
            prog.wait("sync", "dvS", nv)
            prog.dma("sync", bass.AP(scr["x1a1"].tensor, 128 * dt * T + 512 * g, [[T, 128], [1, 512]]),
                     xb[:], inc="x1a1_st")
        prog.barrier(("tensor", "vector", "scalar", "sync", "gpsimd"))


def l1_dram(nc):
    dram = {}
    dram["x1_in"] = nc.dram_tensor("x1_in", [D, T], F32, kind="ExternalInput").ap()
    for nm in ("q", "k", "v", "o"):
        dram[f"Cw{nm}"] = nc.dram_tensor(f"Cw{nm}", [D, D], F32, kind="ExternalInput").ap()
        dram[f"Sw{nm}"] = nc.dram_tensor(f"Sw{nm}", [64, D], F32, kind="ExternalInput").ap()
    for nm in ("bq", "bk", "bv", "bo"):
        dram[nm] = nc.dram_tensor(nm, [128, 16], F32, kind="ExternalInput").ap()
    dram["Wg1"] = nc.dram_tensor("Wg1", [D, 5632], I32, kind="ExternalInput").ap()
    dram["Wu1"] = nc.dram_tensor("Wu1", [D, 5632], I32, kind="ExternalInput").ap()
    dram["Wd1"] = nc.dram_tensor("Wd1", [5632, D], I32, kind="ExternalInput").ap()
    dram["sg1"] = nc.dram_tensor("sg1", [128, 44], F32, kind="ExternalInput").ap()
    dram["su1"] = nc.dram_tensor("su1", [128, 44], F32, kind="ExternalInput").ap()
    dram["sd1"] = nc.dram_tensor("sd1", [128, 16], F32, kind="ExternalInput").ap()
    for sname, dname, dt, shape in CMAP:
        dram[dname] = nc.dram_tensor(dname, shape, F32, kind="ExternalInput").ap()
    dram["x2_out"] = nc.dram_tensor("x2_out", [D, T], F32, kind="ExternalOutput").ap()
    return dram


def l1_scratch(nc, tag=""):
    scr = {}
    scr["o1"] = nc.dram_tensor(f"o1_scr{tag}", [D, T], BF16, kind="Internal").ap()
    scr["qkv1"] = {m: nc.dram_tensor(f"qkv1_{m}{tag}", [D, T], BF16, kind="Internal").ap()
                   for m in ("q", "k", "v")}
    scr["x1a1"] = nc.dram_tensor(f"x1a1_scr{tag}", [D, T], F32, kind="Internal").ap()
    scr["xn1"] = nc.dram_tensor(f"xn1_scr{tag}", [D, T], BF16, kind="Internal").ap()
    scr["gated1"] = nc.dram_tensor(f"gated1_scr{tag}", [5632, T], BF16, kind="Internal").ap()
    scr["wd_bf1"] = nc.dram_tensor(f"wd_bf1_scr{tag}", [5632, D], BF16, kind="Internal").ap()
    return scr


def emit_layer1(prog, tl_c, nc, ctx, dram, scr, x_in_ap, x_out_ap, parts=("attn", "ffn")):
    E = ctx.enter_context
    tl = dict(tl_c)
    tl["bqkv"] = {}
    for nm in ("q", "k", "v"):
        tl["bqkv"][nm] = E(nc.sbuf_tensor(f"b{nm}_s", [128, 16], F32))
        prog.dma("sync", tl["bqkv"][nm][:], dram[f"b{nm}"][:], inc="c_ld")
    tl["bo_s"] = E(nc.sbuf_tensor("bo_s", [128, 16], F32))
    prog.dma("sync", tl["bo_s"][:], dram["bo"][:], inc="c_ld")
    tl["sg1_s"] = E(nc.sbuf_tensor("sg1_s", [128, 44], F32))
    tl["su1_s"] = E(nc.sbuf_tensor("su1_s", [128, 44], F32))
    tl["sd1_s"] = E(nc.sbuf_tensor("sd1_s", [128, 16], F32))
    for nm in ("sg1", "su1", "sd1"):
        prog.dma("sync", tl[nm + "_s"][:], dram[nm][:], inc="c_ld")
    for e in ("tensor", "vector", "scalar"):
        prog.wait(e, "c_ld", prog.n("c_ld"))
    prog.barrier(("tensor", "vector", "scalar", "sync", "gpsimd"))

    with ExitStack() as pha:
        Ea = pha.enter_context
        tl["xn"] = Ea(nc.sbuf_tensor("xn1r", [128, 16 * 2048], BF16))
        tl["rs_b"] = Ea(nc.sbuf_tensor("rs_b", [1, 2048], BF16))
        with ExitStack() as phln:
            El = phln.enter_context
            tl["xb16"] = [El(nc.sbuf_tensor(f"l1xb{i}", [128, 512], BF16)) for i in range(16)]
            tl["sqb"] = [El(nc.sbuf_tensor(f"l1sq{i}", [128, 512], BF16)) for i in range(16)]
            emit_ln(prog, tl, x_in_ap, tl["xn"][:], "ln1")
            prog.barrier()
        emit_rowsum(prog, tl, tl["rs_b"])
        if "attn" in parts and ExitStack:
         with ExitStack() as phb:
            Eb = phb.enter_context
            tl["cb2"] = [Eb(nc.sbuf_tensor(f"codebuf{i}", [128, 2048], BF16)) for i in range(2)]
            tl["scs2"] = [Eb(nc.sbuf_tensor(f"sc_s{i}", [4, 2048], BF16)) for i in range(2)]
            tl["wsc2"] = [Eb(nc.sbuf_tensor(f"wsc{i}", [128, 2048], BF16)) for i in range(2)]
            tl["wsc_ev"] = [0] * 16
            tl["wsc_free"] = [0]
            tl["qkvstage"] = [Eb(nc.sbuf_tensor(f"qkvstage{i}", [128, 2048], BF16))
                              for i in range(2)]
            emit_qkv1(prog, tl, dram, scr)
         with ExitStack() as phb2:
            Eb = phb2.enter_context
            tl["qh2"] = [Eb(nc.sbuf_tensor(f"qh{i}", [128, 2048], BF16)) for i in range(2)]
            tl["kh2"] = [Eb(nc.sbuf_tensor(f"kh{i}", [128, 2048], BF16)) for i in range(2)]
            tl["vh2"] = [Eb(nc.sbuf_tensor(f"vh{i}", [128, 2048], BF16)) for i in range(2)]
            tl["VT2"] = [[Eb(nc.sbuf_tensor(f"VT{j}t{i}", [128, 128], BF16)) for i in range(16)]
                         for j in range(2)]
            tl["vt_ev"] = [0] * 16
            tl["PT"] = [Eb(nc.sbuf_tensor(f"PTt{i}", [128, 512], BF16)) for i in range(16)]
            tl["pt_ev"] = [0] * 16
            tl["sqh"] = Eb(nc.sbuf_tensor("sqh", [128, 2048], BF16))
            tl["mxq"] = Eb(nc.sbuf_tensor("mxq", [128, 4], F32))
            tl["mxk"] = Eb(nc.sbuf_tensor("mxk", [128, 4], F32))
            tl["negB2"] = [Eb(nc.sbuf_tensor(f"negB{i}", [128, 1], F32)) for i in range(2)]
            tl["rrow"] = Eb(nc.sbuf_tensor("rrow", [1, 512], F32))
            tl["rrow_b"] = Eb(nc.sbuf_tensor("rrow_b", [1, 512], BF16))
            tl["rb"] = Eb(nc.sbuf_tensor("rb", [128, 512], BF16))
            tl["oh2"] = [Eb(nc.sbuf_tensor(f"oh{i}", [128, 2048], BF16)) for i in range(2)]
            emit_attn_scores(prog, tl, dram, scr)
        if "attn" in parts:
         with ExitStack() as phc:
            Ec = phc.enter_context
            tl["codebuf"] = Ec(nc.sbuf_tensor("codebuf2", [128, 2048], BF16))
            tl["sc_s"] = Ec(nc.sbuf_tensor("sc_s2w", [4, 2048], BF16))
            tl["wsc"] = Ec(nc.sbuf_tensor("wsc2w", [128, 2048], BF16))
            tl["wsc_ev"] = [0] * 16
            tl["wsc_free"] = [0]
            tl["o_g1"] = [Ec(nc.sbuf_tensor(f"og1_{i}", [128, 512], BF16)) for i in range(16)]
            tl["xbuf"] = [Ec(nc.sbuf_tensor(f"xbufW{i}", [128, 512], F32)) for i in range(4)]
            emit_wo_residual(prog, tl, dram, scr, x_in_ap)
        if "attn" not in parts:
            for dt in range(16):
                for g in range(NGRP):
                    prog.dma("sync",
                             bass.AP(x_out_ap.tensor, 128 * dt * T + 512 * g, [[T, 128], [1, 512]]),
                             bass.AP(x_in_ap.tensor, 128 * dt * T + 512 * g, [[T, 128], [1, 512]]),
                             inc="fout")
            prog.wait("gpsimd", "fout", prog.n("fout"))
            prog.barrier()
            return
        if "ffn" not in parts:
            for dt in range(16):
                for g in range(NGRP):
                    prog.dma("sync",
                             bass.AP(x_out_ap.tensor, 128 * dt * T + 512 * g, [[T, 128], [1, 512]]),
                             bass.AP(scr["x1a1"].tensor, 128 * dt * T + 512 * g, [[T, 128], [1, 512]]),
                             inc="fout")
            prog.wait("gpsimd", "fout", prog.n("fout"))
            prog.barrier()
            return
        tl["qkhat"] = Ea(nc.sbuf_tensor("gatt1", [128, 2048], BF16))
        tl["sqtile"] = Ea(nc.sbuf_tensor("silt1", [128, 2048], BF16))
        tl["upt"] = Ea(nc.sbuf_tensor("upt1", [128, 2048], BF16))
        tl["wbuf2"] = [Ea(nc.sbuf_tensor(f"l1wb2_{i}", [128, 2048], BF16)) for i in range(3)]
        tl["wdprep"] = [Ea(nc.sbuf_tensor(f"l1wdp{i}", [128, 2048], BF16)) for i in range(2)]
        cfg1 = dict(Wg=dram["Wg1"], Wu=dram["Wu1"], Wd=dram["Wd1"],
                    wd_scr=scr["wd_bf1"], gated=scr["gated1"], hid=5632,
                    res_in=scr["x1a1"], out=x_out_ap, lbl="f1",
                    quant=dict(sg=tl["sg1_s"], su=tl["su1_s"], sd=tl["sd1_s"], rs=tl["rs_b"]))
        emit_ffn(prog, tl, cfg1)
    with ExitStack() as phe:
        Ee = phe.enter_context
        tl["gatg"] = [Ee(nc.sbuf_tensor(f"gatg1_{i}", [128, 512], BF16)) for i in range(44)]
        tl["gwb"] = [Ee(nc.sbuf_tensor(f"gwb1_{i}", [128, 5632], BF16)) for i in range(2)]
        tl["gsum_b"] = Ee(nc.sbuf_tensor("gsum_b", [1, 512], BF16))
        tl["xbuf"] = [Ee(nc.sbuf_tensor(f"xbufF1_{i}", [128, 512], F32)) for i in range(16)]
        emit_ffn_f2(prog, tl, cfg1)
    prog.barrier()


def build_layer1(parts=("attn", "ffn")):
    nc = bass.Bass("TRN2", target_bir_lowering=False)
    dram = l1_dram(nc)
    scr = l1_scratch(nc)
    with ExitStack() as ctx:
        E = ctx.enter_context
        tl = {}
        alloc_common(nc, E, tl, dram)
        prog = Prog(nc, ctx)
        register_float_consts(prog, tl)
        for sname, dname, dts, shape in CMAP:
            if dts == "BF16":
                prog.dma("gpsimd", tl["consts"][sname][:], dram[dname][:], inc="c_ldg")
            else:
                prog.dma("sync", tl["consts"][sname][:], dram[dname][:], inc="c_ld")
        for e in ("tensor", "vector", "scalar"):
            prog.wait(e, "c_ld", prog.n("c_ld"))
            prog.wait(e, "c_ldg", prog.n("c_ldg"))
        prog.barrier()
        emit_layer1(prog, tl, nc, ctx, dram, scr, dram["x1_in"], dram["x2_out"], parts)
        prog.finalize(E(nc.Block()))
    return nc


def l1_in_map(h, cs, x1_fm):
    m = {"x1_in": np.ascontiguousarray(x1_fm)}
    for nm in ("q", "k", "v", "o"):
        m[f"Cw{nm}"] = h[f"Cw{nm}"]
        m[f"Sw{nm}"] = h[f"Sw{nm}"]
    for nm in ("bq", "bk", "bv", "bo"):
        m[nm] = h[nm]
    for nm in ("Wg1", "Wu1", "Wd1", "sg1", "su1", "sd1"):
        m[nm] = h[nm]
    for sname, dname, dts, shape in CMAP:
        m[dname] = cs[dname]
    return m


def build_fused():
    nc = bass.Bass("TRN2", target_bir_lowering=False)
    dram = {}
    dram["x_in"] = nc.dram_tensor("x_in", [D, T], F32, kind="ExternalInput").ap()
    specs = [("Wqkvzp", [D, 8192], F32), ("Wb", [D, 32], F32), ("Wa", [D, 32], F32),
             ("convw", [6144, 4], F32), ("A_log", [128, 1], F32), ("dt_bias", [128, 1], F32),
             ("normw", [1, 64], F32), ("Woutp", [D, D], F32),
             ("Wg0", [D, 8192], F32), ("Wu0", [D, 8192], F32), ("Wd0", [8192, D], F32)]
    for nm, shape, dt in specs:
        dram[nm] = nc.dram_tensor(nm, shape, dt, kind="ExternalInput").ap()
    for nm in ("q", "k", "v", "o"):
        dram[f"Cw{nm}"] = nc.dram_tensor(f"Cw{nm}", [D, D], F32, kind="ExternalInput").ap()
        dram[f"Sw{nm}"] = nc.dram_tensor(f"Sw{nm}", [64, D], F32, kind="ExternalInput").ap()
    for nm in ("bq", "bk", "bv", "bo"):
        dram[nm] = nc.dram_tensor(nm, [128, 16], F32, kind="ExternalInput").ap()
    dram["Wg1"] = nc.dram_tensor("Wg1", [D, 5632], I32, kind="ExternalInput").ap()
    dram["Wu1"] = nc.dram_tensor("Wu1", [D, 5632], I32, kind="ExternalInput").ap()
    dram["Wd1"] = nc.dram_tensor("Wd1", [5632, D], I32, kind="ExternalInput").ap()
    dram["sg1"] = nc.dram_tensor("sg1", [128, 44], F32, kind="ExternalInput").ap()
    dram["su1"] = nc.dram_tensor("su1", [128, 44], F32, kind="ExternalInput").ap()
    dram["sd1"] = nc.dram_tensor("sd1", [128, 16], F32, kind="ExternalInput").ap()
    for sname, dname, dt, shape in CMAP:
        dram[dname] = nc.dram_tensor(dname, shape, F32, kind="ExternalInput").ap()
    dram["x2_out"] = nc.dram_tensor("x2_out", [D, T], F32, kind="ExternalOutput").ap()
    x1_scr = nc.dram_tensor("x1_mid", [D, T], F32, kind="Internal").ap()
    scr0 = l0_scratch(nc)
    scr1 = l1_scratch(nc)
    with ExitStack() as ctx:
        E = ctx.enter_context
        tl = {}
        alloc_common(nc, E, tl, dram)
        prog = Prog(nc, ctx)
        register_float_consts(prog, tl)
        emit_const_loads(prog, tl, dram)
        prog.barrier()
        emit_layer0(prog, tl, nc, ctx, dram, scr0, dram["x_in"], x1_scr)
        emit_layer1(prog, tl, nc, ctx, dram, scr1, x1_scr, dram["x2_out"])
        prog.finalize(E(nc.Block()))
    return nc


def kernel_fused(**inputs):
    h = host_prep(inputs)
    cs = host_consts()
    nc = build_fused()
    in_maps = []
    for b in range(8):
        m = l0_in_map(inputs, h, cs, b)
        m2 = l1_in_map(h, cs, np.zeros((1,), np.float32))
        del m2["x1_in"]
        m.update(m2)
        in_maps.append(m)
    res = run_bass_kernel_spmd(nc, in_maps, core_ids=list(range(8)))
    out = np.stack([res.results[b]["x2_out"].T for b in range(8)])
    return np.ascontiguousarray(out)
